# version 02
# baseline (speedup 1.0000x reference)
import numpy as np
import ml_dtypes
from contextlib import ExitStack

import concourse.bass as bass
import concourse.mybir as mybir
from concourse.bass import IndirectOffsetOnAxis
from concourse.bass_utils import run_bass_kernel_spmd

F32 = mybir.dt.float32
BF16 = mybir.dt.bfloat16
AF = mybir.ActivationFunctionType
ALU = mybir.AluOpType
AX = mybir.AxisListType
NPBF = ml_dtypes.bfloat16

NC = 8
T = 16384
D = 1024
E = 2048
ES = E // NC
TSH = T // NC
EXT = 128
TSE = TSH + EXT
NST = 16
DTR = 64
WIN = 512
HEADS = 8
DH = 64
INNER = 512
HID = 2730
HIDP = 2752
NH1 = 2 * HIDP
QKS = 8.0
EPS = 1e-5
STRIP = WIN + 128
NEG = np.float32(-3.0e38)

ENGINES = ("sync", "vector", "scalar", "tensor", "gpsimd")

YW = T + 2048
CHK_C = [(0, 128), (128, 512), (640, 512), (1152, 512), (1664, 512)]
IXA = 0
IXB = 8
IXC = 40
IXK = 120
IXV = 124
IXKR = 130
IXVR = 134
IXN = 140


def idx_table_for(c):
    p = np.arange(128, dtype=np.int64)
    t = np.zeros((128, IXN), np.int64)
    for b in range(8):
        t[:, IXA + b] = (b * 128 + p) * 8 + c
    for ch in range(16):
        for tb in range(2):
            t[:, IXB + ch * 2 + tb] = (c * ES + tb * 128 + p) * (YW // 1024) \
                + ch
    for k in range(16):
        t[:, IXC + k] = (k * 128 + p) * (YW // 128) + \
            (c * 16 - 1 if c > 0 else YW // 128 - 1)
        for ci in range(1, 5):
            t[:, IXC + ci * 16 + k] = (k * 128 + p) * (YW // 512) + c * 4 \
                + (ci - 1)
    for kt in range(4):
        t[:, IXK + kt] = (kt * 128 + p) * 9 + (c + 1)
        t[:, IXKR + kt] = (kt * 128 + p) * 9 + c
    for it in range(6):
        t[:, IXV + it] = (c + 1) * 768 + it * 128 + p
        t[:, IXVR + it] = c * 768 + it * 128 + p
    return np.ascontiguousarray(t.astype(np.uint32))


class Prog:

    LIM = 16000

    def __init__(self, nc, stack, nsem=10):
        self.nc = nc
        self.sems = [stack.enter_context(nc.semaphore(f"sq{i}"))
                     for i in range(nsem)]
        self.sem = self.sems[0]
        self.ops = []
        self.cnt_ = 0

    def op(self, ename, builder, dma=False):
        inc = 16 if dma else 1
        self.ops.append((ename, builder, inc, inc))
        self.cnt_ += inc

    def site(self, ename, builders):
        for j, b in enumerate(builders):
            en = ename
            if isinstance(b, tuple):
                en, b = b
            cnt = 16 if j == len(builders) - 1 else 0
            self.ops.append((en, b, 16, cnt))
            self.cnt_ += cnt

    def dma(self, out, in_):
        self.op("sync", lambda e, o=out, i=in_: e.dma_start(out=o, in_=i), dma=True)

    def dmag(self, out, in_):
        self.op("gpsimd", lambda e, o=out, i=in_: e.dma_start(out=o, in_=i), dma=True)

    def ind(self, out, out_idx, in_, in_idx):
        oo = IndirectOffsetOnAxis(ap=out_idx, axis=0) if out_idx is not None \
            else None
        io = IndirectOffsetOnAxis(ap=in_idx, axis=0) if in_idx is not None \
            else None
        self.op("gpsimd",
                lambda e, o=out, i=in_, oo=oo, io=io: e.indirect_dma_start(
                    out=o, out_offset=oo, in_=i, in_offset=io), dma=True)

    def v(self, builder):
        self.op("vector", builder)

    def s(self, builder):
        self.op("scalar", builder)

    def t(self, builder):
        self.op("tensor", builder)

    def g(self, builder):
        self.op("gpsimd", builder)

    def finalize(self, block):
        LIM = Prog.LIM
        recs = []
        ep = 0
        base = [0]
        c = 0
        for e, b, stamp, cnt in self.ops:
            if c + cnt - base[ep] > LIM:
                ep += 1
                base.append(c)
                assert ep < len(self.sems), "need more epoch semaphores"
            recs.append((e, b, stamp, ep, c - base[ep]))
            c += cnt
        totals = [0] * (ep + 1)
        c2 = 0
        for (e, b, stamp, cnt), (e2, b2, st2, k, lp) in zip(self.ops, recs):
            totals[k] += cnt
        for ename in ENGINES:
            mine = [(b, inc, k, lp) for (e, b, inc, k, lp) in recs
                    if e == ename]
            if not mine and ename != "sync":
                continue

            def body(eng, mine=mine, last=(ename == "sync"), ep=ep):
                seen = -1
                for b, inc, k, lp in mine:
                    if k > seen:
                        for kk in range(max(seen, 0), k):
                            eng.wait_ge(self.sems[kk], totals[kk])
                        seen = k
                    if lp:
                        eng.wait_ge(self.sems[k], lp)
                    b(eng).then_inc(self.sems[k], inc)
                if last:
                    for kk in range(ep + 1):
                        eng.wait_ge(self.sems[kk], totals[kk])

            getattr(block, ename)(body)
        return c


def mm(ps_ap, lhsT_ap, rhs_ap, start, stop):
    return lambda e: e.matmul(ps_ap, lhsT_ap, rhs_ap, start=start, stop=stop)


class Consts:
    cur = None

    def __init__(self, P, nc, stack):
        self.P = P
        self.t = stack.enter_context(nc.sbuf_tensor("cpool", [128, 32], F32))
        self.vals = {}
        Consts.cur = self

    def col(self, v):
        v = float(v)
        if v not in self.vals:
            i = len(self.vals)
            assert i < 32
            self.vals[v] = i
            self.P.v(mset(self.t[:, i:i + 1], v))
        return self.t[:, self.vals[v]:self.vals[v] + 1]


def act(out, in_, func, bias=0.0, scale=1.0, accum=None):
    np_ = in_.shape[0]
    if isinstance(bias, float) and bias != 0.0:
        bias = Consts.cur.col(bias)[0:np_, :] if np_ < 128 else             Consts.cur.col(bias)
    if isinstance(scale, float) and scale != 1.0:
        scale = Consts.cur.col(scale)[0:np_, :] if np_ < 128 else             Consts.cur.col(scale)
    return lambda e: e.activation(out, in_, func, bias=bias, scale=scale,
                                  accum_out=accum)


def tt(out, a, b, op):
    return lambda e: e.tensor_tensor(out, a, b, op)


def ts(out, a, s1, s2, op0, op1=ALU.bypass):
    if s2 is None:
        return lambda e: e.tensor_scalar(out, a, s1, None, op0)
    return lambda e: e.tensor_scalar(out, a, s1, s2, op0, op1)


def stt(out, in0, scalar, in1, op0, op1):
    return lambda e: e.scalar_tensor_tensor(out, in0, scalar, in1, op0, op1)


def red(out, in_, op, negate=False):
    return lambda e: e.tensor_reduce(out, in_, AX.X, op, negate=negate)


def cp(out, in_):
    return lambda e: e.tensor_copy(out, in_)


def rcp(out, in_):
    return lambda e: e.reciprocal(out, in_)


def mset(ap_, v):
    return lambda e: e.memset(ap_, v)


def tts(out, d0, d1, init):
    return lambda e: e.tensor_tensor_scan(out, d0, d1, init, ALU.mult, ALU.add)


def trans(ps_ap, in_ap, ident_ap):
    return lambda e: e.transpose(ps_ap, in_ap, ident_ap)


def vec_cols(P, dst, src1d, nb, cast=False):
    for b in range(nb):
        src = src1d[b * 128:(b + 1) * 128][:, None]
        if cast:
            P.dmag(dst[:, b:b + 1], src)
        else:
            P.dma(dst[:, b:b + 1], src)


def scan_f(total, step):
    o = 0
    while o < total:
        s = min(step, total - o)
        yield o, s
        o += s


def emit_fm_ln(P, sc, x_view, nb, F, fo, d_out, n_out, ones128, sq_t, ps_row,
               ps_bc, row_sb, shift_upper_from=None, out_dtype_note=None):
    for k in range(nb):
        P.t(mm(ps_row[0:1, 0:F], ones128[:, 0:1], x_view(k),
               start=(k == 0), stop=(k == nb - 1)))
    P.s(act(row_sb[0:1, 0:F], ps_row[0:1, 0:F], AF.Identity, scale=1.0 / D))
    P.t(mm(ps_bc[:, 0:F], ones128[0:1, 0:128], row_sb[0:1, 0:F], True, True))
    for k in range(nb):
        P.v(tt(d_out[:, k, fo:fo + F], x_view(k), ps_bc[:, 0:F], ALU.subtract))
    for k in range(nb):
        P.s(act(sq_t[:, k, 0:F], d_out[:, k, fo:fo + F], AF.Square))
    for k in range(nb):
        P.t(mm(ps_row[0:1, 0:F], ones128[:, 0:1], sq_t[:, k, 0:F],
               start=(k == 0), stop=(k == nb - 1)))
    P.s(act(row_sb[1:2, 0:F], ps_row[0:1, 0:F], AF.Ln, bias=EPS, scale=1.0 / D))
    P.s(act(row_sb[1:2, 0:F], row_sb[1:2, 0:F], AF.Exp, scale=-0.5))
    P.t(mm(ps_bc[:, 0:F], ones128[0:1, 0:128], row_sb[1:2, 0:F], True, True))
    if n_out is not None:
        for k in range(nb):
            P.v(tt(n_out[:, k, fo:fo + F], d_out[:, k, fo:fo + F],
                   ps_bc[:, 0:F], ALU.mult))


def build_A():
    nc = bass.Bass("TRN2", target_bir_lowering=False, debug=False)
    xa_d = nc.dram_tensor("xa", [TSH, D], F32, kind="ExternalInput").ap()
    id_d = nc.dram_tensor("ident", [128, 128], BF16, kind="ExternalInput").ap()
    xn_d = nc.dram_tensor("xn", [D, TSH], BF16, kind="ExternalOutput").ap()
    with ExitStack() as st:
        P = Prog(nc, st)
        Consts(P, nc, st)
        x_s = st.enter_context(nc.sbuf_tensor("x_s", [128, 16, D], F32))
        sqs = st.enter_context(nc.sbuf_tensor("sqs", [128, D], F32))
        xnb = st.enter_context(nc.sbuf_tensor("xnb", [128, 16, D], BF16))
        colc = st.enter_context(nc.sbuf_tensor("colc", [128, 4], F32))
        ident = st.enter_context(nc.sbuf_tensor("ident_s", [128, 128], BF16))
        xnf = st.enter_context(nc.sbuf_tensor("xnf", [128, 8, TSH], BF16))
        pst = st.enter_context(nc.psum_tensor("pst", [128, 512], BF16))
        blk = st.enter_context(nc.Block())

        P.dma(x_s[:, :, :], xa_d.rearrange("(n p) d -> p n d", p=128))
        P.dma(ident[:, :], id_d)
        for i in range(16):
            xt = x_s[:, i, :]
            for rep in range(2):
                P.v(red(colc[:, 0:1], xt, ALU.add))
                P.v(ts(colc[:, 0:1], colc[:, 0:1], 1.0 / D, None, ALU.mult))
                P.v(ts(xt, xt, colc[:, 0:1], None, ALU.subtract))
                P.s(act(sqs[:, :], xt, AF.Square, accum=colc[:, 2:3]))
                P.s(act(colc[:, 2:3], colc[:, 2:3], AF.Ln, bias=EPS,
                        scale=1.0 / D))
                P.s(act(colc[:, 2:3], colc[:, 2:3], AF.Exp, scale=-0.5))
                if rep == 0:
                    P.v(ts(xt, xt, colc[:, 2:3], None, ALU.mult))
                else:
                    P.v(ts(xnb[:, i, :], xt, colc[:, 2:3], None, ALU.mult))
        for i in range(16):
            for dbq in range(2):
                for j in range(4):
                    db = dbq * 4 + j
                    P.t(trans(pst[:, j * 128:(j + 1) * 128],
                              xnb[:, i, db * 128:(db + 1) * 128], ident[:, :]))
                for j in range(4):
                    db = dbq * 4 + j
                    P.s(act(xnf[:, db, i * 128:(i + 1) * 128],
                            pst[:, j * 128:(j + 1) * 128], AF.Copy))
        P.dma(xn_d.rearrange("(b p) l -> p b l", p=128), xnf[:, :, :])
        P.finalize(blk)
    return nc


def build_B0():
    nc = bass.Bass("TRN2", target_bir_lowering=False, debug=False)
    xn_d = nc.dram_tensor("xnf", [D, T], BF16, kind="ExternalInput").ap()
    w_d = nc.dram_tensor("w_in_slT", [D, 2 * ES], F32, kind="ExternalInput").ap()
    lnw_d = nc.dram_tensor("lnw", [D], F32, kind="ExternalInput").ap()
    lnb_d = nc.dram_tensor("lnb", [D], F32, kind="ExternalInput").ap()
    cw_d = nc.dram_tensor("cw", [ES, 4], F32, kind="ExternalInput").ap()
    cb_d = nc.dram_tensor("cb", [ES], F32, kind="ExternalInput").ap()
    wxp_d = nc.dram_tensor("wxpT", [ES, 96], F32, kind="ExternalInput").ap()
    u_d = nc.dram_tensor("u32", [ES, T], F32, kind="ExternalOutput").ap()
    z_d = nc.dram_tensor("z32", [ES, T], F32, kind="ExternalOutput").ap()
    xp_d = nc.dram_tensor("xdblp", [96, T], F32, kind="ExternalOutput").ap()

    LC = 2048
    NCH = T // LC
    with ExitStack() as st:
        P = Prog(nc, st)
        Consts(P, nc, st)
        wt = st.enter_context(nc.sbuf_tensor("wt", [128, 8, 512], BF16))
        lnw_s = st.enter_context(nc.sbuf_tensor("lnw_s", [128, 8], F32))
        lnbb = st.enter_context(nc.sbuf_tensor("lnbb", [128, 8], BF16))
        bias_s = st.enter_context(nc.sbuf_tensor("bias_s", [128, 4], F32))
        cw_s = st.enter_context(nc.sbuf_tensor("cw_s", [128, 2, 4], F32))
        cb_s = st.enter_context(nc.sbuf_tensor("cb_s", [128, 2], F32))
        wxt = st.enter_context(nc.sbuf_tensor("wxt", [128, 2, 96], BF16))
        xnc = st.enter_context(nc.sbuf_tensor("xnc", [128, 8, LC], BF16))
        xs3 = st.enter_context(nc.sbuf_tensor("xs3", [128, 2, LC + 3], F32))
        zb = st.enter_context(nc.sbuf_tensor("zb", [128, 2, LC], F32))
        uc = st.enter_context(nc.sbuf_tensor("uc", [128, 2, LC], F32))
        ub = st.enter_context(nc.sbuf_tensor("ub", [128, 2, LC], BF16))
        ps = st.enter_context(nc.psum_tensor("ps", [128, 512], F32))
        blk = st.enter_context(nc.Block())

        P.dmag(wt[:, :, :], w_d.rearrange("(b p) r -> p b r", p=128))
        vec_cols(P, lnw_s, lnw_d, 8)
        vec_cols(P, lnbb, lnb_d, 8, cast=True)
        P.dma(cw_s[:, :, :], cw_d.rearrange("(b p) k -> p b k", p=128))
        vec_cols(P, cb_s, cb_d, 2)
        P.dmag(wxt[:, :, :], wxp_d.rearrange("(b p) r -> p b r", p=128))
        for mg in range(4):
            for k in range(8):
                P.t(mm(ps[0:128, mg:mg + 1],
                       wt[:, k, mg * 128:(mg + 1) * 128],
                       lnbb[:, k:k + 1], k == 0, k == 7))
        P.s(act(bias_s[:, :], ps[0:128, 0:4], AF.Copy))
        for db in range(8):
            P.v(ts(wt[:, db, :], wt[:, db, :], lnw_s[:, db:db + 1], None,
                   ALU.mult))
        P.v(mset(xs3[:, :, 0:3], 0.0))

        for c in range(NCH):
            P.dma(xnc[:, :, :],
                  xn_d.rearrange("(b p) l -> p b l", p=128)[:, :,
                                                           c * LC:(c + 1) * LC])
            for m in range(4):
                for fo, fs in scan_f(LC, 512):
                    for k in range(8):
                        P.t(mm(ps[:, 0:fs], wt[:, k, m * 128:(m + 1) * 128],
                               xnc[:, k, fo:fo + fs], k == 0, k == 7))
                    if m < 2:
                        P.s(act(xs3[:, m, 3 + fo:3 + fo + fs], ps[:, 0:fs],
                                AF.Identity, bias=bias_s[:, m:m + 1]))
                    else:
                        P.s(act(zb[:, m - 2, fo:fo + fs], ps[:, 0:fs],
                                AF.Identity, bias=bias_s[:, m:m + 1]))
            for tb in range(2):
                P.v(ts(uc[:, tb, :], xs3[:, tb, 0:LC], cw_s[:, tb, 0:1],
                       cb_s[:, tb:tb + 1], ALU.mult, ALU.add))
                for k in range(1, 4):
                    P.v(stt(uc[:, tb, :], xs3[:, tb, k:k + LC],
                            cw_s[:, tb, k:k + 1], uc[:, tb, :],
                            ALU.mult, ALU.add))
                P.v(cp(xs3[:, tb, 0:3], xs3[:, tb, LC:LC + 3]))
                P.s(act(uc[:, tb, :], uc[:, tb, :], AF.Silu))
                P.v(cp(ub[:, tb, :], uc[:, tb, :]))
            P.dma(u_d.rearrange("(b p) l -> p b l", p=128)[:, :,
                                                           c * LC:(c + 1) * LC],
                  uc[:, :, :])
            P.dma(z_d.rearrange("(b p) l -> p b l", p=128)[:, :,
                                                           c * LC:(c + 1) * LC],
                  zb[:, :, :])
            for fo, fs in scan_f(LC, 512):
                for k in range(2):
                    P.t(mm(ps[0:96, 0:fs], wxt[:, k, :], ub[:, k, fo:fo + fs],
                           k == 0, k == 1))
                P.s(act(zb[0:96, 0, fo:fo + fs], ps[0:96, 0:fs], AF.Copy))
            P.dma(xp_d[:, c * LC:(c + 1) * LC], zb[0:96, 0, 0:LC])
        P.finalize(blk)
    return nc


def build_B1():
    nc = bass.Bass("TRN2", target_bir_lowering=False, debug=False)
    u_d = nc.dram_tensor("u32", [ES, T], F32, kind="ExternalInput").ap()
    z_d = nc.dram_tensor("z32", [ES, T], F32, kind="ExternalInput").ap()
    xp_d = nc.dram_tensor("xdblp", [NC, 96, T], F32, kind="ExternalInput").ap()
    wdt_d = nc.dram_tensor("wdtT", [DTR, ES], F32, kind="ExternalInput").ap()
    bdt_d = nc.dram_tensor("bdt", [ES], F32, kind="ExternalInput").ap()
    alog_d = nc.dram_tensor("alog", [ES, NST], F32, kind="ExternalInput").ap()
    dsk_d = nc.dram_tensor("dskip", [ES], F32, kind="ExternalInput").ap()
    yg_d = nc.dram_tensor("yg", [ES, T], BF16, kind="ExternalOutput").ap()
    bc_d = nc.dram_tensor("bc_scr", [32, T], F32).ap()

    LC = 1024
    NCH = T // LC
    NG = 4
    with ExitStack() as st:
        P = Prog(nc, st)
        Consts(P, nc, st)
        wdtt = st.enter_context(nc.sbuf_tensor("wdtt", [64, 2, 128], BF16))
        bdt_s = st.enter_context(nc.sbuf_tensor("bdt_s", [128, 2], F32))
        acol = st.enter_context(nc.sbuf_tensor("acol", [128, 2, NST], F32))
        dsk_s = st.enter_context(nc.sbuf_tensor("dsk_s", [128, 2], F32))
        ones1 = st.enter_context(nc.sbuf_tensor("ones1", [1, 128], F32))
        xdb = st.enter_context(nc.sbuf_tensor("xdb", [96, 8, LC], F32))
        xds = st.enter_context(nc.sbuf_tensor("xds", [96, LC], F32))
        dtrb = st.enter_context(nc.sbuf_tensor("dtrb", [64, LC], BF16))
        dtpre = st.enter_context(nc.sbuf_tensor("dtpre", [128, 2, LC], F32))
        spt = st.enter_context(nc.sbuf_tensor("spt", [128, LC], F32))
        dtb = st.enter_context(nc.sbuf_tensor("dtb", [128, 2, LC], F32))
        uch = st.enter_context(nc.sbuf_tensor("uch", [128, 2, LC], F32))
        zch = st.enter_context(nc.sbuf_tensor("zch", [128, 2, LC], F32))
        dtu = st.enter_context(nc.sbuf_tensor("dtu", [128, 2, LC], F32))
        bcf = st.enter_context(nc.sbuf_tensor("bcf", [1, NG * LC], F32))
        dA = st.enter_context(nc.sbuf_tensor("dA", [128, NG * LC], F32))
        dBu = st.enter_context(nc.sbuf_tensor("dBu", [128, NG * LC], F32))
        hb = st.enter_context(nc.sbuf_tensor("hb", [128, NG * LC], F32))
        fixc = st.enter_context(nc.sbuf_tensor("fixc", [128, NG], F32))
        carry = st.enter_context(nc.sbuf_tensor("carry", [128, 2, NST], F32))
        yb4 = st.enter_context(nc.sbuf_tensor("yb4", [128, NG, LC], F32))
        yb = st.enter_context(nc.sbuf_tensor("yb", [128, LC], F32))
        ygb = st.enter_context(nc.sbuf_tensor("ygb", [128, LC], BF16))
        ps = st.enter_context(nc.psum_tensor("ps", [128, NG * LC], F32))
        blk = st.enter_context(nc.Block())

        P.dmag(wdtt[:, :, :], wdt_d.rearrange("r (b p) -> r b p", p=128))
        vec_cols(P, bdt_s, bdt_d, 2)
        P.dma(acol[:, :, :], alog_d.rearrange("(b p) n -> p b n", p=128))
        vec_cols(P, dsk_s, dsk_d, 2)
        P.v(mset(ones1[:, :], 1.0))
        P.v(mset(carry[:, :, :], 0.0))
        for tb in range(2):
            P.s(act(acol[:, tb, :], acol[:, tb, :], AF.Exp))
            P.v(ts(acol[:, tb, :], acol[:, tb, :], -1.0, None, ALU.mult))

        for c in range(NCH):
            lo = c * LC
            P.dma(xdb[:, :, :], xp_d.transpose([1, 0, 2])[:, :, lo:lo + LC])
            P.v(tt(xds[:, :], xdb[:, 0, :], xdb[:, 1, :], ALU.add))
            for j in range(2, 8):
                P.v(tt(xds[:, :], xds[:, :], xdb[:, j, :], ALU.add))
            P.dma(bc_d[:, lo:lo + LC], xds[64:96, :])
            P.v(cp(dtrb[:, :], xds[0:64, :]))
            for mg in range(2):
                for fo, fs in scan_f(LC, 512):
                    P.t(mm(ps[0:128, fo:fo + fs], wdtt[:, mg, :],
                           dtrb[:, fo:fo + fs], True, True))
                P.s(act(dtpre[:, mg, :], ps[0:128, 0:LC], AF.Identity,
                        bias=bdt_s[:, mg:mg + 1]))
            for tb in range(2):
                P.s(act(spt[:, :], dtpre[:, tb, :], AF.Abs))
                P.s(act(spt[:, :], spt[:, :], AF.Exp, scale=-1.0))
                P.s(act(spt[:, :], spt[:, :], AF.Ln, bias=1.0))
                P.s(act(dtb[:, tb, :], dtpre[:, tb, :], AF.Relu))
                P.v(tt(dtb[:, tb, :], dtb[:, tb, :], spt[:, :], ALU.add))
            P.dma(uch[:, :, :],
                  u_d.rearrange("(b p) l -> p b l", p=128)[:, :, lo:lo + LC])
            P.dma(zch[:, :, :],
                  z_d.rearrange("(b p) l -> p b l", p=128)[:, :, lo:lo + LC])
            for tb in range(2):
                P.v(tt(dtu[:, tb, :], dtb[:, tb, :], uch[:, tb, :], ALU.mult))
            for tb in range(2):
                for g in range(NST // NG):
                    P.dma(bcf.ap().rearrange("p (n l) -> p n l", n=NG),
                          bc_d[g * NG:(g + 1) * NG, lo:lo + LC][None, :, :])
                    for fo, fs in scan_f(NG * LC, 512):
                        P.t(mm(ps[:, fo:fo + fs], ones1[0:1, :],
                               bcf[0:1, fo:fo + fs], True, True))
                    for n in range(NG):
                        P.s(act(dA[:, n * LC:(n + 1) * LC], dtb[:, tb, :],
                                AF.Exp, scale=acol[:, tb, g * NG + n:
                                                   g * NG + n + 1]))
                    P.v(tt(dBu.ap().rearrange("p (n l) -> p n l", n=NG),
                           dtu[:, tb, None, :].broadcast_to([128, NG, LC]),
                           ps.ap().rearrange("p (n l) -> p n l", n=NG),
                           ALU.mult))
                    P.v(tt(fixc[:, 0:NG],
                           dA.ap().rearrange("p (n l) -> p n l", n=NG)[:, :, 0],
                           carry[:, tb, g * NG:(g + 1) * NG], ALU.mult))
                    P.v(tt(dBu.ap().rearrange("p (n l) -> p n l",
                                              n=NG)[:, :, 0],
                           dBu.ap().rearrange("p (n l) -> p n l",
                                              n=NG)[:, :, 0],
                           fixc[:, 0:NG], ALU.add))
                    P.v(mset(dA.ap().rearrange("p (n l) -> p n l",
                                               n=NG)[:, :, 0:1], 0.0))
                    P.v(tts(hb[:, :], dA[:, :], dBu[:, :], 0.0))
                    P.v(cp(carry[:, tb, g * NG:(g + 1) * NG],
                           hb.ap().rearrange("p (n l) -> p n l",
                                             n=NG)[:, :, LC - 1]))
                    P.dma(bcf.ap().rearrange("p (n l) -> p n l", n=NG),
                          bc_d[16 + g * NG:16 + (g + 1) * NG,
                               lo:lo + LC][None, :, :])
                    for fo, fs in scan_f(NG * LC, 512):
                        P.t(mm(ps[:, fo:fo + fs], ones1[0:1, :],
                               bcf[0:1, fo:fo + fs], True, True))
                    P.v(tt(dA[:, :], hb[:, :], ps[:, 0:NG * LC], ALU.mult))
                    P.v(red(yb4[:, g, :],
                            dA.ap().rearrange("p (n l) -> p l n", n=NG),
                            ALU.add))
                P.v(red(yb[:, :],
                        yb4.ap().rearrange("p g l -> p l g"), ALU.add))
                P.v(ts(spt[:, :], uch[:, tb, :], dsk_s[:, tb:tb + 1], None,
                       ALU.mult))
                P.v(tt(yb[:, :], yb[:, :], spt[:, :], ALU.add))
                P.s(act(spt[:, :], zch[:, tb, :], AF.Silu))
                P.v(tt(ygb[:, :], yb[:, :], spt[:, :], ALU.mult))
                P.dma(yg_d.rearrange("(b p) l -> p b l", p=128)[:, tb,
                                                               lo:lo + LC],
                      ygb[:, :])
        P.finalize(blk)
    return nc


def fm_ln(P, x_view, F, dbuf, sqb, ones128, onesr, ps_row, ps_bc, rows, nrm_view=None):
    for k in range(8):
        P.t(mm(ps_row[0:1, 0:F], ones128[:, 0:1], x_view(k), k == 0, k == 7))
    P.s(act(rows[0:1, 0, 0:F], ps_row[0:1, 0:F], AF.Identity, scale=1.0 / D))
    P.t(mm(ps_bc[:, 0:F], onesr[0:1, :], rows[0:1, 0, 0:F], True, True))
    for k in range(8):
        P.v(tt(dbuf[:, k, 0:F], x_view(k), ps_bc[:, 0:F], ALU.subtract))
    for k in range(8):
        P.s(act(sqb[:, k, 0:F], dbuf[:, k, 0:F], AF.Square))
    for k in range(8):
        P.t(mm(ps_row[0:1, 0:F], ones128[:, 0:1], sqb[:, k, 0:F], k == 0,
               k == 7))
    P.s(act(rows[0:1, 1, 0:F], ps_row[0:1, 0:F], AF.Ln, bias=EPS,
            scale=1.0 / D))
    P.s(act(rows[0:1, 1, 0:F], rows[0:1, 1, 0:F], AF.Exp, scale=-0.5))
    P.t(mm(ps_bc[:, 0:F], onesr[0:1, :], rows[0:1, 1, 0:F], True, True))
    if nrm_view is not None:
        for k in range(8):
            P.v(tt(nrm_view(k), dbuf[:, k, 0:F], ps_bc[:, 0:F], ALU.mult))


def qkv_chunk(P, B, fo, fs, q_d, k_d, vT_d):
    xbpc, qf, vb, vtb, sqh, rqr = (B["xbpc"], B["qf"], B["vb"],
                                   B["vtb"], B["sqh"], B["rqr"])
    wq = B.get("wq")
    t1, t2, t3, qo, tabc = B["t1"], B["t2"], B["t3"], B["qo"], B["tabc"]
    ones128, onesr, oneh, ident = (B["ones128"], B["onesr"], B["oneh"],
                                   B["ident"])
    ps_mm, ps_bc, ps_row, ps_t = B["ps_mm"], B["ps_bc"], B["ps_row"], B["ps_t"]

    def mm_group(m):
        if "wq_d" in B:
            P.dmag(B["wqm"][:, :, :],
                   B["wq_d"].rearrange("(b p) c -> p b c", p=128)
                   [:, :, m * 128:(m + 1) * 128])
            w_ap = lambda k: B["wqm"][:, k, :]
        else:
            w_ap = lambda k: wq[:, k, m * 128:(m + 1) * 128]
        for k in range(8):
            rhs = (xbpc[:, k, 1:1 + fs] if k < 4 else xbpc[:, k, 0:fs])
            P.t(mm(ps_mm[:, 0:fs], w_ap(k), rhs, k == 0, k == 7))

    def one_side(scol, csi, sni, out_d):
        for tl in range(4):
            P.s(act(sqh[:, tl, 0:fs], qf[:, tl, 0:fs], AF.Square))
        for tl in range(4):
            for hh in range(2):
                b = 64 * hh
                P.t(mm(ps_row[0:1, 0:fs], ones128[b:b + 64, 0:1],
                       sqh[b:b + 64, tl, 0:fs], True, True))
                P.s(act(rqr[0:1, hh, 0:fs], ps_row[0:1, 0:fs], AF.Copy))
            P.v(ts(rqr[0:1, :, 0:fs], rqr[0:1, :, 0:fs], 1e-24, None,
                   ALU.max))
            P.s(act(rqr[0:1, :, 0:fs], rqr[0:1, :, 0:fs], AF.Ln))
            P.s(act(rqr[0:1, :, 0:fs], rqr[0:1, :, 0:fs], AF.Exp, scale=-0.5))
            P.t(mm(ps_bc[:, 0:fs], oneh[0:1, 0, :], rqr[0:1, 0, 0:fs],
                   True, False))
            P.t(mm(ps_bc[:, 0:fs], oneh[0:1, 1, :],
                   rqr[0:1, 1, 0:fs], False, True))
            P.v(tt(t1[:, 0:fs], qf[:, tl, 0:fs], ps_bc[:, 0:fs], ALU.mult))
            P.v(ts(t1[:, 0:fs], t1[:, 0:fs], scol, None, ALU.mult))
            P.v(tt(t2[:, 0:fs], t1[:, 0:fs], tabc[:, csi, 0:fs], ALU.mult))
            P.t(mm(ps_bc[:, 0:fs], B["rotm"][:, :], t1[:, 0:fs], True, True))
            P.v(tt(t3[:, 0:fs], ps_bc[:, 0:fs], tabc[:, sni, 0:fs], ALU.mult))
            P.v(tt(qo[:, 0:fs], t2[:, 0:fs], t3[:, 0:fs], ALU.add))
            P.dma(out_d[tl * 128:(tl + 1) * 128, fo:fo + fs], qo[:, 0:fs])

    for m in range(4):
        mm_group(m)
        P.s(act(qf[:, m, 0:fs], ps_mm[:, 0:fs], AF.Copy))
    one_side(B["qsr"], 0, 1, q_d)
    for m in range(4, 8):
        mm_group(m)
        P.s(act(qf[:, m - 4, 0:fs], ps_mm[:, 0:fs], AF.Copy))
    one_side(B["ksr"], 2, 3, k_d)
    for m in range(8, 12):
        mm_group(m)
        P.s(act(vb[:, m - 8, 0:fs], ps_mm[:, 0:fs], AF.Copy))
    ntile = fs // 128
    for vt_ in range(4):
        for j in range(ntile):
            P.t(trans(ps_t[:, j * 128:(j + 1) * 128],
                      vb[:, vt_, j * 128:(j + 1) * 128], ident[:, :]))
        for j in range(ntile):
            P.s(act(vtb[:, j, vt_ * 128:(vt_ + 1) * 128],
                    ps_t[:, j * 128:(j + 1) * 128], AF.Copy))
    P.dma(vT_d[fo:fo + fs, :].rearrange("(j p) c -> p j c", p=128),
          vtb[:, 0:ntile, :])
    for k in range(8):
        P.v(cp(xbpc[:, k, 0:1], xbpc[:, k, fs:fs + 1]))


def alloc_qkv_bufs(nc, st, name, qf=None, sqh=None, vb=None, vtb=None,
                   rqr=None, t123=None, qo=None, tabc=None):
    B = {}
    B["xbpc"] = st.enter_context(
        nc.sbuf_tensor(name + "xbpc", [128, 8, 513], BF16))
    if qf is None:
        qf = st.enter_context(nc.sbuf_tensor(name + "qf", [128, 4, 512], F32))
    if sqh is None:
        sqh = st.enter_context(
            nc.sbuf_tensor(name + "sqh", [128, 4, 512], F32))
    B["qf"], B["sqh"] = qf, sqh
    if vb is None:
        vb = st.enter_context(nc.sbuf_tensor(name + "vb", [128, 4, 512], BF16))
    if vtb is None:
        vtb = st.enter_context(
            nc.sbuf_tensor(name + "vtb", [128, 4, 512], BF16))
    B["vb"], B["vtb"] = vb, vtb
    B["rqr"] = rqr if rqr is not None else st.enter_context(
        nc.sbuf_tensor(name + "rqr", [1, 2, 512], F32))
    if t123 is not None:
        B["t1"], B["t2"], B["t3"] = t123
    else:
        B["t1"] = st.enter_context(nc.sbuf_tensor(name + "t1", [128, 512], F32))
        B["t2"] = st.enter_context(nc.sbuf_tensor(name + "t2", [128, 512], F32))
        B["t3"] = st.enter_context(nc.sbuf_tensor(name + "t3", [128, 512], F32))
    B["qo"] = qo if qo is not None else st.enter_context(
        nc.sbuf_tensor(name + "qo", [128, 512], BF16))
    B["tabc"] = tabc if tabc is not None else st.enter_context(
        nc.sbuf_tensor(name + "tabc", [128, 4, 512], F32))
    return B


def build_C():
    nc = bass.Bass("TRN2", target_bir_lowering=False, debug=False)
    xe_d = nc.dram_tensor("xe", [D, TSE], F32, kind="ExternalInput").ap()
    yg_d = nc.dram_tensor("yge", [E, TSE], BF16, kind="ExternalInput").ap()
    wo_d = nc.dram_tensor("woutT", [E, D], F32, kind="ExternalInput").ap()
    wq_d = nc.dram_tensor("wqkv", [D, 3 * INNER], F32, kind="ExternalInput").ap()
    qs_d = nc.dram_tensor("qsr", [128, 1], F32, kind="ExternalInput").ap()
    ks_d = nc.dram_tensor("ksr", [128, 1], F32, kind="ExternalInput").ap()
    tab_d = nc.dram_tensor("tabs", [4, 128, TSE], F32, kind="ExternalInput").ap()
    oneh_d = nc.dram_tensor("oneh", [1, 2, 128], F32, kind="ExternalInput").ap()
    rot_d = nc.dram_tensor("rotm", [128, 128], F32, kind="ExternalInput").ap()
    id_d = nc.dram_tensor("ident", [128, 128], BF16, kind="ExternalInput").ap()
    x1_d = nc.dram_tensor("x1e", [D, TSE], F32, kind="ExternalOutput").ap()
    xb_d = nc.dram_tensor("xbe", [D, TSE], F32, kind="ExternalOutput").ap()
    q_od = nc.dram_tensor("q0", [INNER, TSE], BF16, kind="ExternalOutput").ap()
    k_od = nc.dram_tensor("k0", [INNER, TSE], BF16, kind="ExternalOutput").ap()
    v_od = nc.dram_tensor("vT0", [TSE, INNER], BF16, kind="ExternalOutput").ap()

    with ExitStack() as st:
        P = Prog(nc, st)
        Consts(P, nc, st)
        xs = st.enter_context(nc.sbuf_tensor("xs", [128, 8, TSE], F32))
        warena = st.enter_context(nc.sbuf_tensor("warena", [128, 16384], BF16))
        wo = warena.ap().rearrange("p (b c) -> p b c", b=16)
        ygc = st.enter_context(nc.sbuf_tensor("ygc", [128, 16, 512], BF16))
        dbuf = st.enter_context(nc.sbuf_tensor("dbuf", [128, 8, 512], F32))
        bobuf = st.enter_context(nc.sbuf_tensor("bobuf", [128, 8, 512], F32))
        sqb = st.enter_context(nc.sbuf_tensor("sqb", [128, 8, 512], F32))
        rows = st.enter_context(nc.sbuf_tensor("rows", [1, 2, 512], F32))
        ones128 = st.enter_context(nc.sbuf_tensor("ones128", [128, 1], F32))
        onesr = st.enter_context(nc.sbuf_tensor("onesr", [1, 128], F32))
        oneh = st.enter_context(nc.sbuf_tensor("oneh_s", [1, 2, 128], F32))
        ident = st.enter_context(nc.sbuf_tensor("ident_s", [128, 128], BF16))
        qsr = st.enter_context(nc.sbuf_tensor("qsr_s", [128, 1], F32))
        ksr = st.enter_context(nc.sbuf_tensor("ksr_s", [128, 1], F32))
        rotm = st.enter_context(nc.sbuf_tensor("rotm_s", [128, 128], F32))
        B = alloc_qkv_bufs(nc, st, "q0", qf=dbuf[:, 0:4, :],
                           sqh=sqb[:, 0:4, :])
        ps_mm = st.enter_context(nc.psum_tensor("ps_mm", [128, 512], F32))
        ps_bc = st.enter_context(nc.psum_tensor("ps_bc", [128, 512], F32))
        ps_row = st.enter_context(nc.psum_tensor("ps_row", [1, 512], F32))
        ps_t = st.enter_context(nc.psum_tensor("ps_t", [128, 512], BF16))
        blk = st.enter_context(nc.Block())

        B.update(dict(ones128=ones128, onesr=onesr, oneh=oneh, ident=ident,
                      rotm=rotm,
                      ps_mm=ps_mm, ps_bc=ps_bc, ps_row=ps_row, ps_t=ps_t,
                      qsr=qsr[:, 0:1], ksr=ksr[:, 0:1],
                      wq=warena[:, 0:12288].rearrange("p (b c) -> p b c", b=8)))
        P.dma(xs[:, :, :], xe_d.rearrange("(b p) l -> p b l", p=128))
        P.dmag(warena.ap().rearrange("p (b c) -> p b c", b=16),
               wo_d.rearrange("(b p) c -> p b c", p=128))
        P.v(mset(ones128[:, :], 1.0))
        P.v(mset(onesr[:, :], 1.0))
        P.dma(oneh[:, :, :], oneh_d)
        P.dma(ident[:, :], id_d)
        P.dma(qsr[:, :], qs_d)
        P.dma(ksr[:, :], ks_d)
        P.dma(rotm[:, :], rot_d)

        for fo, fs in scan_f(TSE, 512):
            P.dma(ygc[:, :, 0:fs],
                  yg_d.rearrange("(b p) l -> p b l", p=128)[:, :, fo:fo + fs])
            fm_ln(P, lambda k: xs[:, k, fo:fo + fs], fs, dbuf, sqb,
                  ones128, onesr, ps_row, ps_bc, rows,
                  nrm_view=lambda k: dbuf[:, k, 0:fs])
            for m in range(8):
                for k in range(16):
                    P.t(mm(ps_mm[:, 0:fs], wo[:, k, m * 128:(m + 1) * 128],
                           ygc[:, k, 0:fs], k == 0, k == 15))
                P.v(tt(bobuf[:, m, 0:fs], ps_mm[:, 0:fs], dbuf[:, m, 0:fs],
                       ALU.add))
            fm_ln(P, lambda k: bobuf[:, k, 0:fs], fs, bobuf, sqb,
                  ones128, onesr, ps_row, ps_bc, rows,
                  nrm_view=lambda k: bobuf[:, k, 0:fs])
            for m in range(8):
                P.v(tt(xs[:, m, fo:fo + fs], bobuf[:, m, 0:fs],
                       xs[:, m, fo:fo + fs], ALU.add))
            P.dma(x1_d.rearrange("(b p) l -> p b l", p=128)[:, :, fo:fo + fs],
                  xs[:, :, fo:fo + fs])
        P.dmag(B["wq"], wq_d.rearrange("(b p) c -> p b c", p=128))
        P.v(mset(B["xbpc"][:, :, 0:1], 0.0))
        for ci, (fo, fs) in enumerate(scan_f(TSE, 512)):
            P.dma(B["tabc"][:, :, 0:fs],
                  tab_d[:, :, fo:fo + fs].transpose([1, 0, 2]))
            fm_ln(P, lambda k: xs[:, k, fo:fo + fs], fs, dbuf, sqb,
                  ones128, onesr, ps_row, ps_bc, rows,
                  nrm_view=lambda k: dbuf[:, k, 0:fs])
            for k in range(8):
                P.v(cp(B["xbpc"][:, k, 1:1 + fs], dbuf[:, k, 0:fs]))
            P.dma(xb_d.rearrange("(b p) l -> p b l", p=128)[:, :, fo:fo + fs],
                  dbuf[:, :, 0:fs])
            qkv_chunk(P, B, fo, fs, q_od, k_od, v_od)
        P.finalize(blk)
    return nc


def build_D(last):
    nc = bass.Bass("TRN2", target_bir_lowering=False, debug=False)
    qi_d = nc.dram_tensor("qi", [INNER, TSE], BF16, kind="ExternalInput").ap()
    ki_d = nc.dram_tensor("ki", [INNER, TSE], BF16, kind="ExternalInput").ap()
    vi_d = nc.dram_tensor("vTi", [TSE, INNER], BF16, kind="ExternalInput").ap()
    kt_d = nc.dram_tensor("ktail", [INNER, 768], BF16, kind="ExternalInput").ap()
    vt_d = nc.dram_tensor("vtail", [768, INNER], BF16, kind="ExternalInput").ap()
    mk_d = nc.dram_tensor("masks", [17, 128, STRIP], F32,
                          kind="ExternalInput").ap()
    xe_d = nc.dram_tensor("x1e", [D, TSE], F32, kind="ExternalInput").ap()
    wo_d = nc.dram_tensor("woA", [INNER, D], F32, kind="ExternalInput").ap()
    w1_d = nc.dram_tensor("w1f", [D, NH1], F32, kind="ExternalInput").ap()
    b1_d = nc.dram_tensor("b1p", [NH1], F32, kind="ExternalInput").ap()
    w2_d = nc.dram_tensor("w2f", [2816, D], F32, kind="ExternalInput").ap()
    b2_d = nc.dram_tensor("b2v", [D], F32, kind="ExternalInput").ap()
    lg_d = nc.dram_tensor("lgv", [D], F32, kind="ExternalInput").ap()
    lb_d = nc.dram_tensor("lbv", [D], F32, kind="ExternalInput").ap()
    em_d = nc.dram_tensor("extm", [128, 8, 128], F32, kind="ExternalInput").ap()
    id_d = nc.dram_tensor("ident", [128, 128], BF16, kind="ExternalInput").ap()
    x1o_d = nc.dram_tensor("x1o", [D, TSE], F32, kind="ExternalOutput").ap()
    x1b_d = nc.dram_tensor("x1b_scr", [D, TSE], BF16).ap()
    if not last:
        wq_d = nc.dram_tensor("wqkv", [D, 3 * INNER], F32,
                              kind="ExternalInput").ap()
        qs_d = nc.dram_tensor("qsr", [128, 1], F32, kind="ExternalInput").ap()
        ks_d = nc.dram_tensor("ksr", [128, 1], F32, kind="ExternalInput").ap()
        tab_d = nc.dram_tensor("tabs", [4, 128, TSE], F32,
                               kind="ExternalInput").ap()
        oneh_d = nc.dram_tensor("oneh", [1, 2, 128], F32,
                                kind="ExternalInput").ap()
        rot_d = nc.dram_tensor("rotm", [128, 128], F32,
                               kind="ExternalInput").ap()
        q_od = nc.dram_tensor("qo1", [INNER, TSE], BF16,
                              kind="ExternalOutput").ap()
        k_od = nc.dram_tensor("ko1", [INNER, TSE], BF16,
                              kind="ExternalOutput").ap()
        v_od = nc.dram_tensor("vTo1", [TSE, INNER], BF16,
                              kind="ExternalOutput").ap()
    else:
        x0_d = nc.dram_tensor("x1own", [D, TSH], F32,
                              kind="ExternalInput").ap()
        i32_d = nc.dram_tensor("iden32", [128, 128], F32,
                               kind="ExternalInput").ap()
        out_d = nc.dram_tensor("outp", [TSH, D], F32,
                               kind="ExternalOutput").ap()
        xf_d = nc.dram_tensor("xf_scr", [D, TSE], F32).ap()

    with ExitStack() as st:
        P = Prog(nc, st)
        Consts(P, nc, st)
        xm = st.enter_context(nc.sbuf_tensor("xm", [128, 8, TSE + 1], F32))
        qbt = st.enter_context(nc.sbuf_tensor("qbt", [128, 4, 128], BF16))
        kstr = st.enter_context(nc.sbuf_tensor("kstr", [128, 4, STRIP], BF16))
        vstr = st.enter_context(nc.sbuf_tensor("vstr", [128, 5, 512], BF16))
        maskc = st.enter_context(nc.sbuf_tensor("maskc", [128, STRIP], F32))
        simb = st.enter_context(nc.sbuf_tensor("simb", [128, STRIP], F32))
        pbuf = st.enter_context(nc.sbuf_tensor("pbuf", [128, STRIP], BF16))
        ptb = st.enter_context(nc.sbuf_tensor("ptb", [128, 5, 128], BF16))
        cols = st.enter_context(nc.sbuf_tensor("colsd", [128, 4], F32))
        obf = st.enter_context(nc.sbuf_tensor("obf", [128, 4, 128], BF16))
        rbuf = st.enter_context(nc.sbuf_tensor("rbuf", [128, 8, 512], F32))
        dbuf = st.enter_context(nc.sbuf_tensor("dbuf", [128, 8, 512], F32))
        sqb = st.enter_context(nc.sbuf_tensor("sqb", [128, 8, 512], F32))
        xbm = st.enter_context(nc.sbuf_tensor("xbm", [128, 8, 512], BF16))
        hbf = st.enter_context(nc.sbuf_tensor("hbf", [128, 22, 512], BF16))
        wmg = st.enter_context(nc.sbuf_tensor("wmg", [128, 8, 128], BF16))
        w2g = st.enter_context(nc.sbuf_tensor("w2g", [128, 22, 128], BF16))
        h1b = st.enter_context(nc.sbuf_tensor("h1b", [128, 512], F32))
        gb = st.enter_context(nc.sbuf_tensor("gb", [64, 512], F32))
        b1e = st.enter_context(nc.sbuf_tensor("b1e", [128, 44], F32))
        b1pv = st.enter_context(nc.sbuf_tensor("b1pv", [128, 44], F32))
        lgc = st.enter_context(nc.sbuf_tensor("lgc", [128, 8], F32))
        lbb = st.enter_context(nc.sbuf_tensor("lbb", [128, 8], BF16))
        b2c = st.enter_context(nc.sbuf_tensor("b2c", [128, 8], F32))
        extm = st.enter_context(nc.sbuf_tensor("extm_s", [128, 8, 128], F32))
        rows = st.enter_context(nc.sbuf_tensor("rows", [1, 2, 512], F32))
        ones128 = st.enter_context(nc.sbuf_tensor("ones128", [128, 1], F32))
        onesr = st.enter_context(nc.sbuf_tensor("onesr", [1, 128], F32))
        ident = st.enter_context(nc.sbuf_tensor("ident_s", [128, 128], BF16))
        wob = st.enter_context(nc.sbuf_tensor("wob", [128, 4, D], BF16))
        ps_sim = st.enter_context(nc.psum_tensor("ps_sim", [128, STRIP], F32))
        ps_pt = st.enter_context(nc.psum_tensor("ps_pt", [128, STRIP], BF16))
        ps_o = st.enter_context(nc.psum_tensor("ps_o", [128, 128], F32))
        ps_mm = st.enter_context(nc.psum_tensor("ps_mm", [128, 512], F32))
        ps_bc = st.enter_context(nc.psum_tensor("ps_bc", [128, 512], F32))
        ps_row = st.enter_context(nc.psum_tensor("ps_row", [1, 512], F32))
        ps_t = st.enter_context(nc.psum_tensor("ps_t", [128, 512], BF16))

        if not last:
            qsr = st.enter_context(nc.sbuf_tensor("qsr_s", [128, 1], F32))
            ksr = st.enter_context(nc.sbuf_tensor("ksr_s", [128, 1], F32))
            rotm = st.enter_context(nc.sbuf_tensor("rotm_s", [128, 128], F32))
            oneh = st.enter_context(nc.sbuf_tensor("oneh_s", [1, 2, 128], F32))
            wqm = st.enter_context(nc.sbuf_tensor("wqm", [128, 8, 128], BF16))
            B = alloc_qkv_bufs(nc, st, "qn", qf=dbuf[:, 0:4, :],
                               sqh=sqb[:, 0:4, :], vb=xbm[:, 0:4, :],
                               vtb=xbm[:, 4:8, :], rqr=rows,
                               t123=(rbuf[:, 4, :], rbuf[:, 5, :],
                                     rbuf[:, 6, :]),
                               qo=hbf[:, 0, 0:512], tabc=rbuf[:, 0:4, :])
            B.update(dict(ones128=ones128, onesr=onesr, oneh=oneh,
                          ident=ident, rotm=rotm, ps_mm=ps_mm, ps_bc=ps_bc,
                          ps_row=ps_row, ps_t=ps_t, qsr=qsr[:, 0:1],
                          ksr=ksr[:, 0:1], wq_d=wq_d, wqm=wqm))
        else:
            iden32 = st.enter_context(
                nc.sbuf_tensor("iden32_s", [128, 128], F32))
            otb = st.enter_context(nc.sbuf_tensor("otb", [128, D], F32))
        blk = st.enter_context(nc.Block())

        P.dma(xm[:, :, 1:TSE + 1], xe_d.rearrange("(b p) l -> p b l", p=128))
        P.v(mset(xm[:, :, 0:1], 0.0))
        P.dmag(wob[:, :, :], wo_d.rearrange("(b p) c -> p b c", p=128))
        P.v(mset(ones128[:, :], 1.0))
        P.v(mset(onesr[:, :], 1.0))
        P.dma(ident[:, :], id_d)
        P.dma(extm[:, :, :], em_d)
        vec_cols(P, lgc, lg_d, 8)
        vec_cols(P, lbb, lb_d, 8, cast=True)
        vec_cols(P, b2c, b2_d, 8)
        vec_cols(P, b1pv[:, 0:43], b1_d, 43)
        if not last:
            P.dma(qsr[:, :], qs_d)
            P.dma(ksr[:, :], ks_d)
            P.dma(rotm[:, :], rot_d)
            P.dma(oneh[:, :, :], oneh_d)
        else:
            P.dma(iden32[:, :], i32_d)

        for i in range(17):
            c0 = 128 * i + 128
            P.dma(maskc[:, :], mk_d[i, :, :])
            P.dma(qbt[:, :, :],
                  qi_d.rearrange("(b p) l -> p b l", p=128)
                  [:, :, 128 * i:128 * (i + 1)])
            nt = max(0, min(768 - c0, STRIP))
            if nt > 0:
                P.dma(kstr[:, :, 0:nt],
                      kt_d.rearrange("(b p) l -> p b l", p=128)
                      [:, :, c0:c0 + nt])
            if nt < STRIP:
                o0 = c0 + nt - 768 + 128
                P.dma(kstr[:, :, nt:STRIP],
                      ki_d.rearrange("(b p) l -> p b l", p=128)
                      [:, :, o0:o0 + STRIP - nt])
            for j in range(5):
                t0 = c0 + 128 * j
                if t0 < 768:
                    P.dma(vstr[:, j, :],
                          vt_d[t0:t0 + 128, :].rearrange("(a p) c -> p a c",
                                                         p=128)[:, 0, :])
                else:
                    o0 = t0 - 768 + 128
                    P.dma(vstr[:, j, :],
                          vi_d[o0:o0 + 128, :].rearrange("(a p) c -> p a c",
                                                         p=128)[:, 0, :])
            for h in range(8):
                tl, b = h // 2, 64 * (h % 2)
                P.t(mm(ps_sim[:, 0:512], qbt[b:b + 64, tl, :],
                       kstr[b:b + 64, tl, 0:512], True, True))
                P.t(mm(ps_sim[:, 512:STRIP], qbt[b:b + 64, tl, :],
                       kstr[b:b + 64, tl, 512:STRIP], True, True))
                P.v(tt(simb[:, :], ps_sim[:, :], maskc[:, :], ALU.add))
                P.v(red(cols[:, 0:1], simb[:, :], ALU.max, negate=True))
                P.s(act(pbuf[:, :], simb[:, :], AF.Exp, bias=cols[:, 0:1],
                        accum=cols[:, 1:2]))
                P.v(rcp(cols[:, 2:3], cols[:, 1:2]))
                P.v(ts(pbuf[:, :], pbuf[:, :], cols[:, 2:3], None, ALU.mult))
                for j in range(5):
                    P.t(trans(ps_pt[:, 128 * j:128 * (j + 1)],
                              pbuf[:, 128 * j:128 * (j + 1)], ident[:, :]))
                for j in range(5):
                    P.s(act(ptb[:, j, :], ps_pt[:, 128 * j:128 * (j + 1)],
                            AF.Copy))
                for j in range(5):
                    P.t(mm(ps_o[0:64, :], vstr[:, j, 64 * h:64 * h + 64],
                           ptb[:, j, :], j == 0, j == 4))
                P.s(act(obf[b:b + 64, tl, :], ps_o[0:64, :], AF.Copy))
            for m in range(8):
                for k in range(4):
                    P.t(mm(ps_mm[:, 0:128], wob[:, k, m * 128:(m + 1) * 128],
                           obf[:, k, :], k == 0, k == 3))
                P.v(tt(xm[:, m, 1 + 128 * i:1 + 128 * (i + 1)],
                       ps_mm[:, 0:128],
                       xm[:, m, 1 + 128 * i:1 + 128 * (i + 1)], ALU.add))
        for ci, (fo, fs) in enumerate(scan_f(TSE, 512)):
            def xv(k, fo=fo, fs=fs):
                return (xm[:, k, 1 + fo:1 + fo + fs] if k < 4
                        else xm[:, k, fo:fo + fs])
            fm_ln(P, xv, fs, dbuf, sqb, ones128, onesr, ps_row, ps_bc, rows,
                  nrm_view=None)
            for k in range(8):
                P.v(tt(xbm[:, k, 0:fs], dbuf[:, k, 0:fs], ps_bc[:, 0:fs],
                       ALU.mult))
            P.v(mset(hbf[64:128, 21, 0:fs], 0.0))
            for mg in range(43):
                P.dmag(wmg[:, :, :],
                       w1_d.rearrange("(b p) c -> p b c", p=128)
                       [:, :, mg * 128:(mg + 1) * 128])
                if ci == 0:
                    for k in range(8):
                        P.t(mm(ps_o[:, 0:1], wmg[:, k, :], lbb[:, k:k + 1],
                               k == 0, k == 7))
                    P.s(act(b1e[:, mg:mg + 1], ps_o[:, 0:1], AF.Copy))
                    P.v(tt(b1e[:, mg:mg + 1], b1e[:, mg:mg + 1],
                           b1pv[:, mg:mg + 1], ALU.add))
                for k in range(8):
                    P.v(ts(wmg[:, k, :], wmg[:, k, :], lgc[:, k:k + 1], None,
                           ALU.mult))
                for k in range(8):
                    P.t(mm(ps_mm[:, 0:fs], wmg[:, k, :], xbm[:, k, 0:fs],
                           k == 0, k == 7))
                P.s(act(h1b[:, 0:fs], ps_mm[:, 0:fs], AF.Identity,
                        bias=b1e[:, mg:mg + 1]))
                P.s(act(gb[:, 0:fs], h1b[64:128, 0:fs], AF.Gelu))
                P.v(tt(hbf[64 * (mg % 2):64 * (mg % 2) + 64, mg // 2, 0:fs],
                       h1b[0:64, 0:fs], gb[:, 0:fs], ALU.mult))
            for m in range(8):
                P.dmag(w2g[:, :, :],
                       w2_d.rearrange("(b p) c -> p b c", p=128)
                       [:, :, m * 128:(m + 1) * 128])
                for k in range(22):
                    P.t(mm(ps_mm[:, 0:fs], w2g[:, k, :], hbf[:, k, 0:fs],
                           k == 0, k == 21))
                P.s(act(rbuf[:, m, 0:fs], ps_mm[:, 0:fs], AF.Identity,
                        bias=b2c[:, m:m + 1]))
                P.v(tt(rbuf[:, m, 0:fs], rbuf[:, m, 0:fs],
                       xm[:, m, 1 + fo:1 + fo + fs], ALU.add))
            if fo < EXT:
                for m in range(8):
                    P.v(tt(rbuf[:, m, 0:EXT], rbuf[:, m, 0:EXT],
                           extm[:, m, :], ALU.mult))
            P.dma(x1o_d.rearrange("(b p) l -> p b l", p=128)[:, :, fo:fo + fs],
                  rbuf[:, :, 0:fs])
            if not last:
                P.dmag(x1b_d.rearrange("(b p) l -> p b l", p=128)
                       [:, :, fo:fo + fs], rbuf[:, :, 0:fs])
            else:
                P.dma(xf_d.rearrange("(b p) l -> p b l", p=128)
                      [:, :, fo:fo + fs], rbuf[:, :, 0:fs])
        if not last:
            for ci, (fo, fs) in enumerate(scan_f(TSE, 512)):
                P.dma(B["tabc"][:, :, 0:fs],
                      tab_d[:, :, fo:fo + fs].transpose([1, 0, 2]))
                if fo == 0:
                    P.dma(B["xbpc"][:, :, 1:1 + fs],
                          x1b_d.rearrange("(b p) l -> p b l", p=128)
                          [:, :, 0:fs])
                    P.v(mset(B["xbpc"][:, :, 0:1], 0.0))
                else:
                    P.dma(B["xbpc"][:, :, 0:1 + fs],
                          x1b_d.rearrange("(b p) l -> p b l", p=128)
                          [:, :, fo - 1:fo + fs])
                qkv_chunk(P, B, fo, fs, q_od, k_od, v_od)
        else:
            for fo, fs in scan_f(TSH, 512):
                P.dma(rbuf[:, :, 0:fs],
                      xf_d.rearrange("(b p) l -> p b l", p=128)
                      [:, :, EXT + fo:EXT + fo + fs])
                fm_ln(P, lambda k, fs=fs: rbuf[:, k, 0:fs], fs, dbuf, sqb,
                      ones128, onesr, ps_row, ps_bc, rows,
                      nrm_view=lambda k, fs=fs: dbuf[:, k, 0:fs])
                P.dma(sqb[:, :, 0:fs],
                      x0_d.rearrange("(b p) l -> p b l", p=128)
                      [:, :, fo:fo + fs])
                for m in range(8):
                    P.v(tt(dbuf[:, m, 0:fs], dbuf[:, m, 0:fs],
                           sqb[:, m, 0:fs], ALU.add))
                for j in range(fs // 128):
                    for m in range(8):
                        P.t(mm(ps_mm[:, 0:128], iden32[:, :],
                               dbuf[:, m, 128 * j:128 * (j + 1)],
                               True, True) if False else
                            trans(ps_mm[:, 0:128],
                                  dbuf[:, m, 128 * j:128 * (j + 1)],
                                  iden32[:, :]))
                        P.s(act(otb[:, m * 128:(m + 1) * 128],
                                ps_mm[:, 0:128], AF.Copy))
                    P.dma(out_d[fo + 128 * j:fo + 128 * (j + 1), :],
                          otb[:, :])
        P.finalize(blk)
    return nc


def rotary_tables():
    half = np.arange(0, DH, 2, dtype=np.float64)
    inv_freq = 1.0 / 10000.0 ** (half / DH)
    t = np.arange(T, dtype=np.float64)
    freqs = t[:, None] * inv_freq[None, :]
    freqs = np.concatenate([freqs, freqs], -1)
    base = (half + 0.4 * DH) / (1.4 * DH)
    power = (t - T // 2) / (WIN // 2)
    scale = base[None, :] ** power[:, None]
    scale = np.concatenate([scale, scale], -1)
    return np.cos(freqs), np.sin(freqs), scale


def tabs_for(tok0):
    cos, sin, scale = rotary_tables()
    idx = np.clip(np.arange(tok0, tok0 + TSE), 0, T - 1)
    sgn = np.where(np.arange(64) % 64 < 32, -1.0, 1.0)

    def dbl(tab):
        t_ = tab[idx].T.astype(np.float32)
        return np.concatenate([t_, t_], 0)
    csq = dbl(cos * scale) * QKS
    snq = dbl(sin * scale * sgn[None, :]) * QKS
    csk = dbl(cos / scale)
    snk = dbl(sin / scale * sgn[None, :])
    return np.ascontiguousarray(np.stack([csq, snq, csk, snk]))


def rotm_host():
    r = np.zeros((128, 128), np.float32)
    for d_ in range(128):
        rr = (d_ % 64 + 32) % 64 + 64 * (d_ // 64)
        r[rr, d_] = 1.0
    return r


def oneh_host():
    o = np.zeros((1, 2, 128), np.float32)
    o[0, 0, 0:64] = 1.0
    o[0, 1, 64:128] = 1.0
    return o


def masks_for(core):
    t0 = core * TSH - EXT
    out = np.empty((17, 128, STRIP), np.float32)
    for i in range(17):
        q0 = t0 + 128 * i
        q = q0 + np.arange(128)[:, None]
        k = q0 - WIN + np.arange(STRIP)[None, :]
        bad = (k > q) | (k < q - WIN) | (k < 0)
        m = np.where(bad, NEG, np.float32(0.0)).astype(np.float32)
        dead = (q[:, 0] < 0)
        m[dead, :] = NEG
        m[dead, 0] = 0.0
        out[i] = m
    return np.ascontiguousarray(out)


def permute_w1(w1, b1):
    a, g = w1[:, :HID], w1[:, HID:]
    ab, gb = b1[:HID], b1[HID:]
    ap_ = np.concatenate([a, np.zeros((D, HIDP - HID), w1.dtype)], 1)
    gp = np.concatenate([g, np.zeros((D, HIDP - HID), w1.dtype)], 1)
    abp = np.concatenate([ab, np.zeros(HIDP - HID, b1.dtype)])
    gbp = np.concatenate([gb, np.zeros(HIDP - HID, b1.dtype)])
    cols, bcols = [], []
    for j in range(HIDP // 64):
        cols += [ap_[:, j * 64:(j + 1) * 64], gp[:, j * 64:(j + 1) * 64]]
        bcols += [abp[j * 64:(j + 1) * 64], gbp[j * 64:(j + 1) * 64]]
    return (np.ascontiguousarray(np.concatenate(cols, 1)),
            np.ascontiguousarray(np.concatenate(bcols)))


def permute_w2(w2):
    return np.ascontiguousarray(
        np.concatenate([w2, np.zeros((2816 - HID, D), w2.dtype)], 0))


def _cc(a):
    return np.ascontiguousarray(a)


def kernel_unfused(**inputs):
    f32 = np.float32
    x = np.asarray(inputs["L0_attn_x"], f32)[0]
    w_in = np.asarray(inputs["L0_attn_w_in"], f32)
    ident = np.eye(128).astype(NPBF)
    iden32 = np.eye(128, dtype=f32)
    oneh = oneh_host()
    rotm = rotm_host()

    nc = build_A()
    maps = [dict(xa=_cc(x[c * TSH:(c + 1) * TSH]), ident=ident)
            for c in range(NC)]
    res = run_bass_kernel_spmd(nc, maps, core_ids=list(range(NC)))
    xn_full = np.concatenate([np.asarray(r["xn"]) for r in res.results], 1)

    nc = build_B0()
    maps = []
    for c in range(NC):
        sl = slice(c * ES, (c + 1) * ES)
        w_sl = np.concatenate([w_in[sl], w_in[E + c * ES:E + (c + 1) * ES]], 0)
        maps.append(dict(
            xnf=_cc(xn_full),
            w_in_slT=_cc(w_sl.T),
            lnw=np.asarray(inputs["L0_attn_ln_w"], f32),
            lnb=np.asarray(inputs["L0_attn_ln_b"], f32),
            cw=_cc(np.asarray(inputs["L0_attn_conv_w"], f32)[sl]),
            cb=_cc(np.asarray(inputs["L0_attn_conv_b"], f32)[sl]),
            wxpT=_cc(np.asarray(inputs["L0_attn_w_xproj"], f32)[:, sl].T),
        ))
    res = run_bass_kernel_spmd(nc, maps, core_ids=list(range(NC)))
    u32 = [np.asarray(r["u32"]) for r in res.results]
    z32 = [np.asarray(r["z32"]) for r in res.results]
    xdblp = np.stack([np.asarray(r["xdblp"]) for r in res.results])

    nc = build_B1()
    maps = []
    for c in range(NC):
        sl = slice(c * ES, (c + 1) * ES)
        maps.append(dict(
            u32=u32[c], z32=z32[c], xdblp=xdblp,
            wdtT=_cc(np.asarray(inputs["L0_attn_w_dt"], f32)[sl].T),
            bdt=_cc(np.asarray(inputs["L0_attn_b_dt"], f32)[sl]),
            alog=_cc(np.asarray(inputs["L0_attn_a_log"], f32)[sl]),
            dskip=_cc(np.asarray(inputs["L0_attn_d_skip"], f32)[sl]),
        ))
    res = run_bass_kernel_spmd(nc, maps, core_ids=list(range(NC)))
    yg_full = np.concatenate([np.asarray(r["yg"]) for r in res.results], 0)

    def ext_sliceT(full_T_first):
        outs = []
        for c in range(NC):
            t0 = c * TSH - EXT
            o = np.zeros((TSE,) + full_T_first.shape[1:], full_T_first.dtype)
            lo = max(t0, 0)
            o[lo - t0:] = full_T_first[lo:t0 + TSE]
            outs.append(_cc(o.T))
        return outs

    qs0 = np.asarray(inputs["L0_ffn_q_scale"], f32)
    ks0 = np.asarray(inputs["L0_ffn_k_scale"], f32)
    xe_s = ext_sliceT(x)
    yge_s = ext_sliceT(yg_full.T)
    nc = build_C()
    maps = []
    for c in range(NC):
        t0 = c * TSH - EXT
        maps.append(dict(
            xe=xe_s[c],
            yge=_cc(yge_s[c].astype(NPBF)),
            woutT=_cc(np.asarray(inputs["L0_attn_w_out"], f32).T),
            wqkv=_cc(np.asarray(inputs["L0_ffn_wqkv"], f32)[0]),
            qsr=_cc(np.tile(qs0[0], 2)[:, None]),
            ksr=_cc(np.tile(ks0[0], 2)[:, None]),
            tabs=tabs_for(t0), oneh=oneh, rotm=rotm, ident=ident,
        ))
    res = run_bass_kernel_spmd(nc, maps, core_ids=list(range(NC)))
    x1e = [np.asarray(r["x1e"]) for r in res.results]
    xbe = [np.asarray(r["xbe"]) for r in res.results]
    q0 = [np.asarray(r["q0"]) for r in res.results]
    k0 = [np.asarray(r["k0"]) for r in res.results]
    v0 = [np.asarray(r["vT0"]) for r in res.results]

    def run_layer(last, stream, qs_, ks_, vs_, layer, x1own=None):
        nc = build_D(last)
        w1f, b1p = permute_w1(np.asarray(inputs["L0_ffn_w1"], f32)[layer],
                              np.asarray(inputs["L0_ffn_b1"], f32)[layer])
        w2f = permute_w2(np.asarray(inputs["L0_ffn_w2"], f32)[layer])
        maps = []
        for c in range(NC):
            t0 = c * TSH - EXT
            nb = c - 1
            if nb >= 0:
                ktail = _cc(qs_[nb][0:0] if False else
                            ks_[nb][:, 1408:2176])
                vtail = _cc(vs_[nb][1408:2176, :])
            else:
                ktail = np.zeros((INNER, 768), NPBF)
                vtail = np.zeros((768, INNER), NPBF)
            m = dict(
                qi=qs_[c], ki=ks_[c], vTi=vs_[c],
                ktail=ktail, vtail=vtail,
                masks=masks_for(c),
                x1e=stream[c],
                woA=_cc(np.asarray(inputs["L0_ffn_wo"], f32)[layer]),
                w1f=w1f, b1p=b1p, w2f=w2f,
                b2v=_cc(np.asarray(inputs["L0_ffn_b2"], f32)[layer]),
                lgv=_cc(np.asarray(inputs["L0_ffn_ln_g"], f32)[layer]),
                lbv=_cc(np.asarray(inputs["L0_ffn_ln_b"], f32)[layer]),
                extm=np.full((128, 8, 128), 0.0 if c == 0 else 1.0, f32),
                ident=ident,
            )
            if not last:
                m.update(dict(
                    wqkv=_cc(np.asarray(inputs["L0_ffn_wqkv"], f32)[1]),
                    qsr=_cc(np.tile(qs0[1], 2)[:, None]),
                    ksr=_cc(np.tile(ks0[1], 2)[:, None]),
                    tabs=tabs_for(t0), oneh=oneh, rotm=rotm,
                ))
            else:
                m.update(dict(x1own=_cc(x1own[c][:, EXT:]), iden32=iden32))
            maps.append(m)
        return run_bass_kernel_spmd(nc, maps, core_ids=list(range(NC)))

    res = run_layer(False, xbe, q0, k0, v0, 0)
    x_1 = [np.asarray(r["x1o"]) for r in res.results]
    q1 = [np.asarray(r["qo1"]) for r in res.results]
    k1 = [np.asarray(r["ko1"]) for r in res.results]
    v1 = [np.asarray(r["vTo1"]) for r in res.results]

    res = run_layer(True, x_1, q1, k1, v1, 1, x1own=x1e)
    out = np.concatenate([np.asarray(r["outp"]) for r in res.results], 0)
    return out[None].astype(np.float32)


class Cursor:
    def __init__(self, tensor, limit):
        self.t = tensor
        self.limit = limit
        self.off = 0

    def take(self, *shape):
        n = int(np.prod(shape))
        off = self.off
        self.off += n
        assert self.off <= self.limit, f"arena overflow {self.off}>{self.limit}"
        v = self.t[:, off:off + n]
        if len(shape) == 1:
            return v
        return v.rearrange("p (a b) -> p a b", a=shape[0])

    def reset(self):
        self.off = 0


def build_FUSED():
    nc = bass.Bass("TRN2", target_bir_lowering=False, debug=False)
    nc.cache_partition_id()
    f32 = F32

    def ein(name, shape, dt=F32):
        return nc.dram_tensor(name, shape, dt, kind="ExternalInput").ap()

    xa_d = ein("xa", [TSH, D])
    xe_d = ein("xe", [D, TSE])
    w_d = ein("w_in_slT", [D, 2 * ES])
    lnw_d = ein("lnw", [D])
    lnb_d = ein("lnb", [D])
    cw_d = ein("cw", [ES, 4])
    cb_d = ein("cb", [ES])
    wxp_d = ein("wxpT", [ES, 96])
    wdt_d = ein("wdtT", [DTR, ES])
    bdt_d = ein("bdt", [ES])
    alog_d = ein("alog", [ES, NST])
    dsk_d = ein("dskip", [ES])
    wo_d = ein("woutT", [E, D])
    wq0_d = ein("wqkv0", [D, 3 * INNER])
    wq1_d = ein("wqkv1", [D, 3 * INNER])
    qs0_d = ein("qsr0", [128, 1])
    ks0_d = ein("ksr0", [128, 1])
    qs1_d = ein("qsr1", [128, 1])
    ks1_d = ein("ksr1", [128, 1])
    tab_d = ein("tabs", [4, 128, TSE])
    oneh_d = ein("oneh", [1, 2, 128])
    rot_d = ein("rotm", [128, 128])
    id_d = ein("ident", [128, 128], BF16)
    i32_d = ein("iden32", [128, 128])
    mk_d = ein("masks", [17, 128, STRIP])
    em_d = ein("extm", [128, 8, 128])
    idx_d = nc.dram_tensor("idxt", [128, IXN], mybir.dt.uint32,
                           kind="ExternalInput").ap()
    woA0_d = ein("woA0", [INNER, D])
    woA1_d = ein("woA1", [INNER, D])
    w1f0_d = ein("w1f0", [D, NH1])
    w1f1_d = ein("w1f1", [D, NH1])
    b1p0_d = ein("b1p0", [NH1])
    b1p1_d = ein("b1p1", [NH1])
    w2f0_d = ein("w2f0", [2816, D])
    w2f1_d = ein("w2f1", [2816, D])
    b2v0_d = ein("b2v0", [D])
    b2v1_d = ein("b2v1", [D])
    lgv0_d = ein("lgv0", [D])
    lgv1_d = ein("lgv1", [D])
    lbv0_d = ein("lbv0", [D])
    lbv1_d = ein("lbv1", [D])
    out_d = nc.dram_tensor("outp", [TSH, D], F32, kind="ExternalOutput").ap()

    xn_ai = nc.dram_tensor("xn_ai", [D, T], BF16).ap()
    xn_ao = nc.dram_tensor("xn_ao", [D, T], BF16).ap()
    xd_ai = nc.dram_tensor("xd_ai", [96, T], F32).ap()
    xd_ao = nc.dram_tensor("xd_ao", [96, T], F32).ap()
    y_ai = nc.dram_tensor("y_ai", [E, YW], BF16).ap()
    y_ao = nc.dram_tensor("y_ao", [E, YW], BF16).ap()
    kg_ai = [nc.dram_tensor(f"kg_ai{l}", [INNER, 9 * 768], BF16).ap()
             for l in range(2)]
    kg_ao = [nc.dram_tensor(f"kg_ao{l}", [INNER, 9 * 768], BF16).ap()
             for l in range(2)]
    vg_ai = [nc.dram_tensor(f"vg_ai{l}", [9 * 768, INNER], BF16).ap()
             for l in range(2)]
    vg_ao = [nc.dram_tensor(f"vg_ao{l}", [9 * 768, INNER], BF16).ap()
             for l in range(2)]
    kt_loc = [nc.dram_tensor(f"kt_loc{l}", [INNER, 768], BF16).ap()
              for l in range(2)]
    vt_loc = [nc.dram_tensor(f"vt_loc{l}", [768, INNER], BF16).ap()
              for l in range(2)]
    u_d = nc.dram_tensor("u32s", [ES, T], F32).ap()
    z_d = nc.dram_tensor("z32s", [ES, T], F32).ap()
    bc_d = nc.dram_tensor("bc_scr", [32, T], F32).ap()
    yg_d = None
    x1_d = nc.dram_tensor("x1s", [D, TSE], F32).ap()
    xb_d = nc.dram_tensor("xbs", [D, TSE], F32).ap()
    q_ds = [nc.dram_tensor(f"q{l}s", [INNER, TSE], BF16).ap() for l in range(2)]
    k_ds = [nc.dram_tensor(f"k{l}s", [INNER, TSE], BF16).ap() for l in range(2)]
    v_ds = [nc.dram_tensor(f"v{l}s", [TSE, INNER], BF16).ap() for l in range(2)]
    x1o_d = nc.dram_tensor("x1os", [D, TSE], F32).ap()
    x1b_d = nc.dram_tensor("x1bs", [D, TSE], BF16).ap()
    xf_d = nc.dram_tensor("xfs", [D, TSE], F32).ap()

    AF_EL = 34000
    AB_EL = 34000
    with ExitStack() as st:
        P = Prog(nc, st)
        Consts(P, nc, st)
        arf_t = st.enter_context(nc.sbuf_tensor("arena_f", [128, AF_EL], F32))
        arb_t = st.enter_context(nc.sbuf_tensor("arena_b", [128, AB_EL], BF16))
        ident = st.enter_context(nc.sbuf_tensor("ident_s", [128, 128], BF16))
        iden32 = st.enter_context(nc.sbuf_tensor("iden32_s", [128, 128], F32))
        rotm = st.enter_context(nc.sbuf_tensor("rotm_s", [128, 128], F32))
        oneh = st.enter_context(nc.sbuf_tensor("oneh_s", [1, 2, 128], F32))
        ones128 = st.enter_context(nc.sbuf_tensor("ones128", [128, 1], F32))
        onesr = st.enter_context(nc.sbuf_tensor("onesr", [1, 128], F32))
        colsml = st.enter_context(nc.sbuf_tensor("colsml", [128, 16], F32))
        carry = st.enter_context(nc.sbuf_tensor("carry", [128, 2, NST], F32))
        acol = st.enter_context(nc.sbuf_tensor("acol", [128, 2, NST], F32))
        smlA = st.enter_context(nc.sbuf_tensor("smlA", [128, 2, 8], F32))
        smlB = st.enter_context(nc.sbuf_tensor("smlB", [128, 2, 4], F32))
        lnbb = st.enter_context(nc.sbuf_tensor("lnbb", [128, 8], BF16))
        qsr = st.enter_context(nc.sbuf_tensor("qsr_s", [128, 2], F32))
        ksr = st.enter_context(nc.sbuf_tensor("ksr_s", [128, 2], F32))
        lgc = st.enter_context(nc.sbuf_tensor("lgc", [128, 8], F32))
        lbb = st.enter_context(nc.sbuf_tensor("lbb", [128, 8], BF16))
        b2c = st.enter_context(nc.sbuf_tensor("b2c", [128, 8], F32))
        b1e = st.enter_context(nc.sbuf_tensor("b1e", [128, 44], F32))
        b1pv = st.enter_context(nc.sbuf_tensor("b1pv", [128, 44], F32))
        ixs = st.enter_context(nc.sbuf_tensor("ixs", [128, IXN],
                                              mybir.dt.uint32))
        rows0 = st.enter_context(nc.sbuf_tensor("rows0", [1, 1024], F32))
        psA = st.enter_context(nc.psum_tensor("psA", [128, 3584], F32))
        ps_bf = st.enter_context(nc.psum_tensor("ps_bf", [128, 640], BF16))
        blk = st.enter_context(nc.Block())

        arf = Cursor(arf_t, AF_EL)
        arb = Cursor(arb_t, AB_EL)
        ps_mm = psA[:, 2048:2560]
        ps_bc = psA[:, 2560:3072]
        ps_row = psA[0:1, 1024:1536]
        ps_o = psA[:, 3072:3200]
        ps_sim = psA[:, 0:640]
        ps_t = ps_bf[:, 0:512]
        ps_pt = ps_bf[:, 0:640]

        def cc(ins_ap, outs_ap):
            def b(e):
                return e.collective_compute(
                    "AllReduce", ALU.add, replica_groups=[list(range(NC))],
                    ins=[ins_ap.opt()], outs=[outs_ap.opt()])
            P.g(b)

        P.dma(ident[:, :], id_d)
        P.dma(ixs[:, :], idx_d)
        P.dma(iden32[:, :], i32_d)
        P.dma(rotm[:, :], rot_d)
        P.dma(oneh[:, :, :], oneh_d)
        P.v(mset(ones128[:, :], 1.0))
        P.v(mset(onesr[:, :], 1.0))
        zrow = arb.take(16384)
        P.v(mset(zrow[:, :], 0.0))
        rows = rows0.ap().rearrange("p (a b) -> p a b", a=2)
        P.dma(qsr[:, 0:1], qs0_d)
        P.dma(qsr[:, 1:2], qs1_d)
        P.dma(ksr[:, 0:1], ks0_d)
        P.dma(ksr[:, 1:2], ks1_d)
        zv = lambda a_, b_: zrow[:, 0:a_ * b_].rearrange(
            "p (a b) -> p a b", a=a_)
        for j in range(0, T, 2048):
            P.dma(xn_ai.rearrange("(b p) l -> p b l", p=128)[:, :, j:j + 2048],
                  zv(8, 2048))
        for j in range(0, YW, 1024):
            P.dma(y_ai.rearrange("(b p) l -> p b l", p=128)[:, :, j:j + 1024],
                  zv(16, 1024))
        for l in range(2):
            for j in range(0, 9 * 768, 1728):
                P.dma(kg_ai[l].rearrange("(b p) l -> p b l", p=128)
                      [:, :, j:j + 1728], zv(4, 1728))
            for sl in range(0, 9, 3):
                P.dma(vg_ai[l][sl * 768:(sl + 3) * 768, :]
                      .rearrange("(a p) c -> p a c", p=128), zv(18, 512))
        arb.reset()
        x_s = arf.take(16, D)
        sqs = arf.take(D)
        xnb = arb.take(16, D)
        xna = arb.take(8, 2048)
        P.dma(x_s[:, :, :], xa_d.rearrange("(n p) d -> p n d", p=128))
        for i in range(16):
            xt = x_s[:, i, :]
            for rep in range(2):
                P.v(red(colsml[:, 0:1], xt, ALU.add))
                P.v(ts(colsml[:, 0:1], colsml[:, 0:1], 1.0 / D, None,
                       ALU.mult))
                P.v(ts(xt, xt, colsml[:, 0:1], None, ALU.subtract))
                P.s(act(sqs[:, :], xt, AF.Square, accum=colsml[:, 2:3]))
                P.s(act(colsml[:, 2:3], colsml[:, 2:3], AF.Ln, bias=EPS,
                        scale=1.0 / D))
                P.s(act(colsml[:, 2:3], colsml[:, 2:3], AF.Exp, scale=-0.5))
                if rep == 0:
                    P.v(ts(xt, xt, colsml[:, 2:3], None, ALU.mult))
                else:
                    P.v(ts(xnb[:, i, :], xt, colsml[:, 2:3], None, ALU.mult))
        for i in range(16):
            for dbq in range(2):
                for j in range(4):
                    db = dbq * 4 + j
                    P.t(trans(ps_t[:, j * 128:(j + 1) * 128],
                              xnb[:, i, db * 128:(db + 1) * 128], ident[:, :]))
                for j in range(4):
                    db = dbq * 4 + j
                    P.s(act(xna[:, db, 128 * i:128 * (i + 1)],
                            ps_t[:, j * 128:(j + 1) * 128], AF.Copy))
        xnai_f = xn_ai.rearrange("d (j l) -> (d j) l", l=TSH)
        for b in range(8):
            P.ind(xnai_f, ixs[:, IXA + b:IXA + b + 1], xna[:, b, :], None)
        cc(xn_ai, xn_ao)
        arf.reset()
        arb.reset()
        wt = arb.take(8, 512)
        xnc = arb.take(8, 2048)
        ub = arb.take(2, 2048)
        wxt = arb.take(2, 96)
        xs3 = arf.take(2, 2048 + 3)
        zb = arf.take(2, 2048)
        uc = arf.take(2, 2048)
        lnw_s = smlA[:, :, :].rearrange("p a b -> p (a b)")
        bias_s = smlB[:, 0, :]
        cw_s = arf.take(2, 4)
        cb_s = colsml[:, 8:10]
        P.dmag(wt[:, :, :], w_d.rearrange("(b p) r -> p b r", p=128))
        vec_cols(P, lnw_s, lnw_d, 8)
        vec_cols(P, lnbb, lnb_d, 8, cast=True)
        P.dma(cw_s[:, :, :], cw_d.rearrange("(b p) k -> p b k", p=128))
        vec_cols(P, cb_s, cb_d, 2)
        P.dmag(wxt[:, :, :], wxp_d.rearrange("(b p) r -> p b r", p=128))
        for mg in range(4):
            for k in range(8):
                P.t(mm(psA[0:128, 3200 + mg:3200 + mg + 1],
                       wt[:, k, mg * 128:(mg + 1) * 128],
                       lnbb[:, k:k + 1], k == 0, k == 7))
        P.s(act(bias_s[:, 0:4], psA[0:128, 3200:3204], AF.Copy))
        for db in range(8):
            P.v(ts(wt[:, db, :], wt[:, db, :], lnw_s[:, db:db + 1], None,
                   ALU.mult))
        P.v(mset(xs3[:, :, 0:3], 0.0))
        LC0 = 2048
        for c in range(T // LC0):
            P.dma(xnc[:, :, :],
                  xn_ao.rearrange("(b p) l -> p b l", p=128)
                  [:, :, c * LC0:(c + 1) * LC0])
            for m in range(4):
                for fo, fs in scan_f(LC0, 512):
                    for k in range(8):
                        P.t(mm(ps_mm[:, 0:fs], wt[:, k, m * 128:(m + 1) * 128],
                               xnc[:, k, fo:fo + fs], k == 0, k == 7))
                    if m < 2:
                        P.s(act(xs3[:, m, 3 + fo:3 + fo + fs], ps_mm[:, 0:fs],
                                AF.Identity, bias=bias_s[:, m:m + 1]))
                    else:
                        P.s(act(zb[:, m - 2, fo:fo + fs], ps_mm[:, 0:fs],
                                AF.Identity, bias=bias_s[:, m:m + 1]))
            for tb in range(2):
                P.v(ts(uc[:, tb, :], xs3[:, tb, 0:LC0], cw_s[:, tb, 0:1],
                       cb_s[:, tb:tb + 1], ALU.mult, ALU.add))
                for k in range(1, 4):
                    P.v(stt(uc[:, tb, :], xs3[:, tb, k:k + LC0],
                            cw_s[:, tb, k:k + 1], uc[:, tb, :],
                            ALU.mult, ALU.add))
                P.v(cp(xs3[:, tb, 0:3], xs3[:, tb, LC0:LC0 + 3]))
                P.s(act(uc[:, tb, :], uc[:, tb, :], AF.Silu))
                P.v(cp(ub[:, tb, :], uc[:, tb, :]))
            P.dma(u_d.rearrange("(b p) l -> p b l", p=128)
                  [:, :, c * LC0:(c + 1) * LC0], uc[:, :, :])
            P.dma(z_d.rearrange("(b p) l -> p b l", p=128)
                  [:, :, c * LC0:(c + 1) * LC0], zb[:, :, :])
            for fo, fs in scan_f(LC0, 512):
                for k in range(2):
                    P.t(mm(ps_mm[0:96, 0:fs], wxt[:, k, :],
                           ub[:, k, fo:fo + fs], k == 0, k == 1))
                P.s(act(zb[0:96, 0, fo:fo + fs], ps_mm[0:96, 0:fs], AF.Copy))
            P.dma(xd_ai[:, c * LC0:(c + 1) * LC0], zb[0:96, 0, 0:LC0])
        cc(xd_ai, xd_ao)
        arf.reset()
        arb.reset()

        wdtt0 = arb.take(2, 128)
        dtrb = arb.take(1, 1024)
        ygb = arb.take(1, 1024)
        xds = arf.take(1, 1024)
        dtpre = arf.take(2, 1024)
        spt = arf.take(1, 1024)
        dtb = arf.take(2, 1024)
        uch = arf.take(2, 1024)
        zch = arf.take(2, 1024)
        dtu = arf.take(2, 1024)
        bcf = arf.take(2 * 1024)
        dA = arf.take(2 * 1024)
        dBu = arf.take(2 * 1024)
        hb = arf.take(2 * 1024)
        yb4 = arf.take(8, 1024)
        yb = arf.take(1, 1024)
        fixc = colsml[:, 10:12]
        bdt_s = smlB[:, 1, 0:2]
        dsk_s = smlB[:, 1, 2:4]
        P.dmag(wdtt0.rearrange("p a b -> p (a b)")[0:64, :]
               .rearrange("r (b pp) -> r b pp", pp=128),
               wdt_d.rearrange("r (b p) -> r b p", p=128))
        wdtt = wdtt0.rearrange("p a b -> p (a b)")[0:64, :] \
            .rearrange("r (b pp) -> r b pp", pp=128)
        vec_cols(P, bdt_s, bdt_d, 2)
        P.dma(acol[:, :, :], alog_d.rearrange("(b p) n -> p b n", p=128))
        vec_cols(P, dsk_s, dsk_d, 2)
        P.v(mset(carry[:, :, :], 0.0))
        for tb in range(2):
            P.s(act(acol[:, tb, :], acol[:, tb, :], AF.Exp))
            P.v(ts(acol[:, tb, :], acol[:, tb, :], -1.0, None, ALU.mult))
        NG = 2
        LC1 = 1024
        for c in range(T // LC1):
            lo = c * LC1
            P.dma(xds[0:96, 0, :], xd_ao[:, lo:lo + LC1])
            P.dma(bc_d[:, lo:lo + LC1],
                  xds.rearrange("p a b -> p (a b)")[64:96, 0:LC1])
            P.v(cp(dtrb[:, 0, :][0:64, :],
                   xds.rearrange("p a b -> p (a b)")[0:64, 0:LC1]))
            for mg in range(2):
                for fo, fs in scan_f(LC1, 512):
                    P.t(mm(ps_mm[0:128, 0:fs], wdtt[:, mg, :],
                           dtrb[:, 0, :][0:64, fo:fo + fs], True, True))
                    P.s(act(dtpre[:, mg, fo:fo + fs], ps_mm[0:128, 0:fs],
                            AF.Identity, bias=bdt_s[:, mg:mg + 1]))
            for tb in range(2):
                P.s(act(spt[:, 0, :], dtpre[:, tb, :], AF.Abs))
                P.s(act(spt[:, 0, :], spt[:, 0, :], AF.Exp, scale=-1.0))
                P.s(act(spt[:, 0, :], spt[:, 0, :], AF.Ln, bias=1.0))
                P.s(act(dtb[:, tb, :], dtpre[:, tb, :], AF.Relu))
                P.v(tt(dtb[:, tb, :], dtb[:, tb, :], spt[:, 0, :], ALU.add))
            P.dma(uch[:, :, :], u_d.rearrange("(b p) l -> p b l", p=128)
                  [:, :, lo:lo + LC1])
            P.dma(zch[:, :, :], z_d.rearrange("(b p) l -> p b l", p=128)
                  [:, :, lo:lo + LC1])
            for tb in range(2):
                P.v(tt(dtu[:, tb, :], dtb[:, tb, :], uch[:, tb, :], ALU.mult))
            for tb in range(2):
                for g in range(NST // NG):
                    P.dma(bcf.rearrange("p (n l) -> p n l", n=NG)[0:1, :, :],
                          bc_d[g * NG:(g + 1) * NG, lo:lo + LC1][None, :, :])
                    for fo, fs in scan_f(NG * LC1, 512):
                        P.t(mm(psA[:, fo:fo + fs], onesr[0:1, :],
                               bcf[0:1, fo:fo + fs], True, True))
                    for n in range(NG):
                        P.s(act(dA[:, n * LC1:(n + 1) * LC1], dtb[:, tb, :],
                                AF.Exp,
                                scale=acol[:, tb, g * NG + n:g * NG + n + 1]))
                    P.v(tt(dBu.rearrange("p (n l) -> p n l", n=NG),
                           dtu[:, tb, None, :].broadcast_to([128, NG, LC1]),
                           psA[:, 0:NG * LC1].rearrange("p (n l) -> p n l",
                                                        n=NG),
                           ALU.mult))
                    P.v(tt(fixc[:, 0:NG],
                           dA.rearrange("p (n l) -> p n l", n=NG)[:, :, 0],
                           carry[:, tb, g * NG:(g + 1) * NG], ALU.mult))
                    P.v(tt(dBu.rearrange("p (n l) -> p n l", n=NG)[:, :, 0],
                           dBu.rearrange("p (n l) -> p n l", n=NG)[:, :, 0],
                           fixc[:, 0:NG], ALU.add))
                    P.v(mset(dA.rearrange("p (n l) -> p n l",
                                          n=NG)[:, :, 0:1], 0.0))
                    P.v(tts(hb[:, :], dA[:, :], dBu[:, :], 0.0))
                    P.v(cp(carry[:, tb, g * NG:(g + 1) * NG],
                           hb.rearrange("p (n l) -> p n l",
                                        n=NG)[:, :, LC1 - 1]))
                    P.dma(bcf.rearrange("p (n l) -> p n l", n=NG)[0:1, :, :],
                          bc_d[16 + g * NG:16 + (g + 1) * NG,
                               lo:lo + LC1][None, :, :])
                    for fo, fs in scan_f(NG * LC1, 512):
                        P.t(mm(psA[:, fo:fo + fs], onesr[0:1, :],
                               bcf[0:1, fo:fo + fs], True, True))
                    P.v(tt(dA[:, :], hb[:, :], psA[:, 0:NG * LC1], ALU.mult))
                    P.v(red(yb4[:, g, :],
                            dA.rearrange("p (n l) -> p l n", n=NG),
                            ALU.add))
                P.v(red(yb[:, 0, :], yb4.rearrange("p g l -> p l g"),
                        ALU.add))
                P.v(ts(spt[:, 0, :], uch[:, tb, :], dsk_s[:, tb:tb + 1],
                       None, ALU.mult))
                P.v(tt(yb[:, 0, :], yb[:, 0, :], spt[:, 0, :], ALU.add))
                P.s(act(spt[:, 0, :], zch[:, tb, :], AF.Silu))
                P.v(tt(ygb[:, 0, :], yb[:, 0, :], spt[:, 0, :], ALU.mult))
                P.ind(y_ai.rearrange("r (j l) -> (r j) l", l=1024),
                      ixs[:, IXB + c * 2 + tb:IXB + c * 2 + tb + 1],
                      ygb[:, 0, :], None)
        cc(y_ai, y_ao)
        arf.reset()
        arb.reset()
        xs = arf.take(8, TSE)
        dbuf = arf.take(8, 512)
        bobuf = arf.take(8, 512)
        sqb = arf.take(8, 512)
        t1 = arf.take(512)
        t2 = arf.take(512)
        t3 = arf.take(512)
        tabc = arf.take(4, 512)
        ygc = arb.take(16, 512)
        warena = arb.take(16384)
        xbpc = arb.take(8, 513)
        vbb = arb.take(4, 512)
        vtbb = arb.take(4, 512)
        qob = arb.take(512)
        B = dict(xbpc=xbpc, qf=dbuf[:, 0:4, :], sqh=sqb[:, 0:4, :],
                 vb=vbb, vtb=vtbb, rqr=rows, t1=t1, t2=t2, t3=t3, qo=qob,
                 tabc=tabc, ones128=ones128, onesr=onesr, oneh=oneh,
                 ident=ident, rotm=rotm, ps_mm=ps_mm, ps_bc=ps_bc,
                 ps_row=ps_row, ps_t=ps_t,
                 wq=warena[:, 0:12288].rearrange("p (b c) -> p b c", b=8))
        wo = warena.rearrange("p (b c) -> p b c", b=16)
        P.dma(xs[:, :, :], xe_d.rearrange("(b p) l -> p b l", p=128))
        P.dmag(wo, wo_d.rearrange("(b p) c -> p b c", p=128))
        y36 = y_ao.rearrange("r (j l) -> (r j) l", l=512)
        y144 = y_ao.rearrange("r (j l) -> (r j) l", l=128)
        for ci, (fo, fs) in enumerate(CHK_C):
            for k in range(16):
                P.ind(ygc[:, k, 0:fs], None, y144 if ci == 0 else y36,
                      ixs[:, IXC + ci * 16 + k:IXC + ci * 16 + k + 1])
            fm_ln(P, lambda k: xs[:, k, fo:fo + fs], fs, dbuf, sqb,
                  ones128, onesr, ps_row, ps_bc, rows,
                  nrm_view=lambda k: dbuf[:, k, 0:fs])
            for m in range(8):
                for k in range(16):
                    P.t(mm(ps_mm[:, 0:fs], wo[:, k, m * 128:(m + 1) * 128],
                           ygc[:, k, 0:fs], k == 0, k == 15))
                P.v(tt(bobuf[:, m, 0:fs], ps_mm[:, 0:fs], dbuf[:, m, 0:fs],
                       ALU.add))
            fm_ln(P, lambda k: bobuf[:, k, 0:fs], fs, bobuf, sqb,
                  ones128, onesr, ps_row, ps_bc, rows,
                  nrm_view=lambda k: bobuf[:, k, 0:fs])
            for m in range(8):
                P.v(tt(xs[:, m, fo:fo + fs], bobuf[:, m, 0:fs],
                       xs[:, m, fo:fo + fs], ALU.add))
            P.dma(x1_d.rearrange("(b p) l -> p b l", p=128)[:, :, fo:fo + fs],
                  xs[:, :, fo:fo + fs])
        P.dmag(B["wq"], wq0_d.rearrange("(b p) c -> p b c", p=128))
        P.v(mset(B["xbpc"][:, :, 0:1], 0.0))
        B["qsr"] = qsr[:, 0:1]
        B["ksr"] = ksr[:, 0:1]
        for ci, (fo, fs) in enumerate(CHK_C):
            P.dma(B["tabc"][:, :, 0:fs],
                  tab_d[:, :, fo:fo + fs].transpose([1, 0, 2]))
            fm_ln(P, lambda k: xs[:, k, fo:fo + fs], fs, dbuf, sqb,
                  ones128, onesr, ps_row, ps_bc, rows,
                  nrm_view=lambda k: dbuf[:, k, 0:fs])
            for k in range(8):
                P.v(cp(B["xbpc"][:, k, 1:1 + fs], dbuf[:, k, 0:fs]))
            P.dma(xb_d.rearrange("(b p) l -> p b l", p=128)[:, :, fo:fo + fs],
                  dbuf[:, :, 0:fs])
            qkv_chunk(P, B, fo, fs, q_ds[0], k_ds[0], v_ds[0])
        arf.reset()
        arb.reset()

        def kv_exchange(l):
            ktl = arb.take(4, 768)
            vtl = arb.take(6, 512)
            P.dma(ktl[:, :, :], k_ds[l][:, TSE - 768:TSE]
                  .rearrange("(a p) w -> p a w", p=128))
            P.dma(vtl[:, :, :], v_ds[l][TSE - 768:TSE, :]
                  .rearrange("(a p) w -> p a w", p=128))
            kgi_f = kg_ai[l].rearrange("r (j w) -> (r j) w", w=768)
            for kt in range(4):
                P.ind(kgi_f, ixs[:, IXK + kt:IXK + kt + 1], ktl[:, kt, :],
                      None)
            for it in range(6):
                P.ind(vg_ai[l], ixs[:, IXV + it:IXV + it + 1], vtl[:, it, :],
                      None)
            cc(kg_ai[l], kg_ao[l])
            cc(vg_ai[l], vg_ao[l])
            kgo_f = kg_ao[l].rearrange("r (j w) -> (r j) w", w=768)
            for kt in range(4):
                P.ind(ktl[:, kt, :], None, kgo_f,
                      ixs[:, IXKR + kt:IXKR + kt + 1])
            for it in range(6):
                P.ind(vtl[:, it, :], None, vg_ao[l],
                      ixs[:, IXVR + it:IXVR + it + 1])
            P.dma(kt_loc[l].rearrange("(a p) w -> p a w", p=128),
                  ktl[:, :, :])
            P.dma(vt_loc[l].rearrange("(a p) w -> p a w", p=128),
                  vtl[:, :, :])
            arb.reset()

        def emit_D(l, last):
            stream_d = xb_d if l == 0 else x1o_d
            woA_d = woA0_d if l == 0 else woA1_d
            w1_d = w1f0_d if l == 0 else w1f1_d
            b1_d = b1p0_d if l == 0 else b1p1_d
            w2_d = w2f0_d if l == 0 else w2f1_d
            b2_d = b2v0_d if l == 0 else b2v1_d
            lg_d = lgv0_d if l == 0 else lgv1_d
            lb_d = lbv0_d if l == 0 else lbv1_d
            xm = arf.take(8, TSE + 1)
            rbuf = arf.take(8, 512)
            dbuf = arf.take(8, 512)
            sqb = arf.take(8, 512)
            simb = arf.take(STRIP)
            maskc = arf.take(STRIP)
            h1b = arf.take(512)
            gbv = arf.take(512)
            extm = arf.take(8, 128)
            xb2flat = arb.take(8 * 513)
            qbt = xb2flat[:, 0:512].rearrange("p (a b) -> p a b", a=4)
            kstr = xb2flat[:, 512:512 + 4 * STRIP].rearrange(
                "p (a b) -> p a b", a=4)
            vstr = arb.take(5, 512)
            pbuf = arb.take(STRIP)
            ptb = arb.take(5, 128)
            obf = arb.take(4, 128)
            xbm = arb.take(8, 512)
            hbf = arb.take(22, 512)
            wmg = arb.take(8, 128)
            w2g = arb.take(22, 128)
            wob = arb.take(4, D)
            cols = colsml[:, 0:4]
            P.dma(xm[:, :, 1:TSE + 1],
                  stream_d.rearrange("(b p) l -> p b l", p=128))
            P.v(mset(xm[:, :, 0:1], 0.0))
            P.dmag(wob[:, :, :], woA_d.rearrange("(b p) c -> p b c", p=128))
            P.dma(extm[:, :, :], em_d)
            vec_cols(P, lgc, lg_d, 8)
            vec_cols(P, lbb, lb_d, 8, cast=True)
            vec_cols(P, b2c, b2_d, 8)
            vec_cols(P, b1pv[:, 0:43], b1_d, 43)
            for i in range(17):
                c0 = 128 * i + 128
                P.dma(maskc[:, :], mk_d[i, :, :])
                P.dma(qbt[:, :, :],
                      q_ds[l].rearrange("(b p) ll -> p b ll", p=128)
                      [:, :, 128 * i:128 * (i + 1)])
                nt = max(0, min(768 - c0, STRIP))
                if nt > 0:
                    P.dma(kstr[:, :, 0:nt],
                          kt_loc[l].rearrange("(b p) w -> p b w", p=128)
                          [:, :, c0:c0 + nt])
                if nt < STRIP:
                    o0 = c0 + nt - 768 + 128
                    P.dma(kstr[:, :, nt:STRIP],
                          k_ds[l].rearrange("(b p) w -> p b w", p=128)
                          [:, :, o0:o0 + STRIP - nt])
                for j in range(5):
                    t0_ = c0 + 128 * j
                    if t0_ < 768:
                        P.dma(vstr[:, j, :],
                              vt_loc[l][t0_:t0_ + 128, :]
                              .rearrange("(a p) c -> p a c", p=128)[:, 0, :])
                    else:
                        o0 = t0_ - 768 + 128
                        P.dma(vstr[:, j, :],
                              v_ds[l][o0:o0 + 128, :]
                              .rearrange("(a p) c -> p a c", p=128)[:, 0, :])
                for h in range(8):
                    tl, b = h // 2, 64 * (h % 2)
                    P.t(mm(ps_sim[:, 0:512], qbt[b:b + 64, tl, :],
                           kstr[b:b + 64, tl, 0:512], True, True))
                    P.t(mm(ps_sim[:, 512:STRIP], qbt[b:b + 64, tl, :],
                           kstr[b:b + 64, tl, 512:STRIP], True, True))
                    P.v(tt(simb[:, :], ps_sim[:, :], maskc[:, :], ALU.add))
                    P.v(red(cols[:, 0:1], simb[:, :], ALU.max, negate=True))
                    P.s(act(pbuf[:, :], simb[:, :], AF.Exp, bias=cols[:, 0:1],
                            accum=cols[:, 1:2]))
                    P.v(rcp(cols[:, 2:3], cols[:, 1:2]))
                    P.v(ts(pbuf[:, :], pbuf[:, :], cols[:, 2:3], None,
                           ALU.mult))
                    for j in range(5):
                        P.t(trans(ps_pt[:, 128 * j:128 * (j + 1)],
                                  pbuf[:, 128 * j:128 * (j + 1)], ident[:, :]))
                    for j in range(5):
                        P.s(act(ptb[:, j, :], ps_pt[:, 128 * j:128 * (j + 1)],
                                AF.Copy))
                    for j in range(5):
                        P.t(mm(ps_o[0:64, :], vstr[:, j, 64 * h:64 * h + 64],
                               ptb[:, j, :], j == 0, j == 4))
                    P.s(act(obf[b:b + 64, tl, :], ps_o[0:64, 0:128], AF.Copy))
                for m in range(8):
                    for k in range(4):
                        P.t(mm(ps_mm[:, 0:128],
                               wob[:, k, m * 128:(m + 1) * 128],
                               obf[:, k, :], k == 0, k == 3))
                    P.v(tt(xm[:, m, 1 + 128 * i:1 + 128 * (i + 1)],
                           ps_mm[:, 0:128],
                           xm[:, m, 1 + 128 * i:1 + 128 * (i + 1)], ALU.add))
            for ci, (fo, fs) in enumerate(scan_f(TSE, 512)):
                def xv(k, fo=fo, fs=fs):
                    return (xm[:, k, 1 + fo:1 + fo + fs] if k < 4
                            else xm[:, k, fo:fo + fs])
                fm_ln(P, xv, fs, dbuf, sqb, ones128, onesr, ps_row, ps_bc,
                      rows, nrm_view=None)
                for k in range(8):
                    P.v(tt(xbm[:, k, 0:fs], dbuf[:, k, 0:fs], ps_bc[:, 0:fs],
                           ALU.mult))
                P.v(mset(hbf[64:128, 21, 0:fs], 0.0))
                for mg in range(43):
                    P.dmag(wmg[:, :, :],
                           w1_d.rearrange("(b p) c -> p b c", p=128)
                           [:, :, mg * 128:(mg + 1) * 128])
                    if ci == 0:
                        for k in range(8):
                            P.t(mm(ps_o[:, 0:1], wmg[:, k, :],
                                   lbb[:, k:k + 1], k == 0, k == 7))
                        P.s(act(b1e[:, mg:mg + 1], ps_o[:, 0:1], AF.Copy))
                        P.v(tt(b1e[:, mg:mg + 1], b1e[:, mg:mg + 1],
                               b1pv[:, mg:mg + 1], ALU.add))
                    for k in range(8):
                        P.v(ts(wmg[:, k, :], wmg[:, k, :], lgc[:, k:k + 1],
                               None, ALU.mult))
                    for k in range(8):
                        P.t(mm(ps_mm[:, 0:fs], wmg[:, k, :], xbm[:, k, 0:fs],
                               k == 0, k == 7))
                    P.s(act(h1b[:, 0:fs], ps_mm[:, 0:fs], AF.Identity,
                            bias=b1e[:, mg:mg + 1]))
                    P.s(act(gbv[0:64, 0:fs], h1b[64:128, 0:fs], AF.Gelu))
                    P.v(tt(hbf[64 * (mg % 2):64 * (mg % 2) + 64, mg // 2,
                               0:fs],
                           h1b[0:64, 0:fs], gbv[0:64, 0:fs], ALU.mult))
                for m in range(8):
                    P.dmag(w2g[:, :, :],
                           w2_d.rearrange("(b p) c -> p b c", p=128)
                           [:, :, m * 128:(m + 1) * 128])
                    for k in range(22):
                        P.t(mm(ps_mm[:, 0:fs], w2g[:, k, :], hbf[:, k, 0:fs],
                               k == 0, k == 21))
                    P.s(act(rbuf[:, m, 0:fs], ps_mm[:, 0:fs], AF.Identity,
                            bias=b2c[:, m:m + 1]))
                    P.v(tt(rbuf[:, m, 0:fs], rbuf[:, m, 0:fs],
                           xm[:, m, 1 + fo:1 + fo + fs], ALU.add))
                if fo < EXT:
                    for m in range(8):
                        P.v(tt(rbuf[:, m, 0:EXT], rbuf[:, m, 0:EXT],
                               extm[:, m, :], ALU.mult))
                if not last:
                    P.dma(x1o_d.rearrange("(b p) l -> p b l", p=128)
                          [:, :, fo:fo + fs], rbuf[:, :, 0:fs])
                    P.dmag(x1b_d.rearrange("(b p) l -> p b l", p=128)
                           [:, :, fo:fo + fs], rbuf[:, :, 0:fs])
                else:
                    P.dma(xf_d.rearrange("(b p) l -> p b l", p=128)
                          [:, :, fo:fo + fs], rbuf[:, :, 0:fs])
            if not last:
                B2 = dict(xbpc=xb2flat.rearrange("p (a b) -> p a b", a=8),
                          qf=dbuf[:, 0:4, :],
                          sqh=sqb[:, 0:4, :], vb=xbm[:, 0:4, :],
                          vtb=xbm[:, 4:8, :], rqr=rows,
                          t1=rbuf[:, 4, :], t2=rbuf[:, 5, :],
                          t3=rbuf[:, 6, :], qo=hbf[:, 0, 0:512],
                          tabc=rbuf[:, 0:4, :], ones128=ones128,
                          onesr=onesr, oneh=oneh, ident=ident, rotm=rotm,
                          ps_mm=ps_mm, ps_bc=ps_bc, ps_row=ps_row,
                          ps_t=ps_t, qsr=qsr[:, 1:2], ksr=ksr[:, 1:2],
                          wq_d=wq1_d, wqm=wmg)
                for ci, (fo, fs) in enumerate(scan_f(TSE, 512)):
                    P.dma(B2["tabc"][:, :, 0:fs],
                          tab_d[:, :, fo:fo + fs].transpose([1, 0, 2]))
                    if fo == 0:
                        P.dma(B2["xbpc"][:, :, 1:1 + fs],
                              x1b_d.rearrange("(b p) l -> p b l", p=128)
                              [:, :, 0:fs])
                        P.v(mset(B2["xbpc"][:, :, 0:1], 0.0))
                    else:
                        P.dma(B2["xbpc"][:, :, 0:1 + fs],
                              x1b_d.rearrange("(b p) l -> p b l", p=128)
                              [:, :, fo - 1:fo + fs])
                    qkv_chunk(P, B2, fo, fs, q_ds[1], k_ds[1], v_ds[1])
            else:
                for fo, fs in scan_f(TSH, 512):
                    P.dma(rbuf[:, :, 0:fs],
                          xf_d.rearrange("(b p) l -> p b l", p=128)
                          [:, :, EXT + fo:EXT + fo + fs])
                    fm_ln(P, lambda k, fs=fs: rbuf[:, k, 0:fs], fs, dbuf,
                          sqb, ones128, onesr, ps_row, ps_bc, rows,
                          nrm_view=lambda k, fs=fs: dbuf[:, k, 0:fs])
                    P.dma(sqb[:, :, 0:fs],
                          x1_d.rearrange("(b p) l -> p b l", p=128)
                          [:, :, EXT + fo:EXT + fo + fs])
                    for m in range(8):
                        P.v(tt(dbuf[:, m, 0:fs], dbuf[:, m, 0:fs],
                               sqb[:, m, 0:fs], ALU.add))
                    for j in range(fs // 128):
                        for m in range(8):
                            P.t(trans(ps_mm[:, 0:128],
                                      dbuf[:, m, 128 * j:128 * (j + 1)],
                                      iden32[:, :]))
                            P.s(act(rbuf[:, m, 0:128] if False else
                                    rbuf.rearrange("p a b -> p (a b)")
                                    [:, m * 128:(m + 1) * 128],
                                    ps_mm[:, 0:128], AF.Copy))
                        P.dma(out_d[fo + 128 * j:fo + 128 * (j + 1), :],
                              rbuf.rearrange("p a b -> p (a b)")[:, 0:1024])
            arf.reset()
            arb.reset()

        kv_exchange(0)
        emit_D(0, False)
        kv_exchange(1)
        emit_D(1, True)
        P.finalize(blk)
    return nc


def kernel(**inputs):
    f32 = np.float32
    x = np.asarray(inputs["L0_attn_x"], f32)[0]
    w_in = np.asarray(inputs["L0_attn_w_in"], f32)
    ident = np.eye(128).astype(NPBF)
    iden32 = np.eye(128, dtype=f32)
    oneh = oneh_host()
    rotm = rotm_host()
    qs_ = np.asarray(inputs["L0_ffn_q_scale"], f32)
    ks_ = np.asarray(inputs["L0_ffn_k_scale"], f32)
    w1f0, b1p0 = permute_w1(np.asarray(inputs["L0_ffn_w1"], f32)[0],
                            np.asarray(inputs["L0_ffn_b1"], f32)[0])
    w1f1, b1p1 = permute_w1(np.asarray(inputs["L0_ffn_w1"], f32)[1],
                            np.asarray(inputs["L0_ffn_b1"], f32)[1])
    w2f0 = permute_w2(np.asarray(inputs["L0_ffn_w2"], f32)[0])
    w2f1 = permute_w2(np.asarray(inputs["L0_ffn_w2"], f32)[1])

    nc = build_FUSED()
    maps = []
    for c in range(NC):
        sl = slice(c * ES, (c + 1) * ES)
        t0 = c * TSH - EXT
        w_sl = np.concatenate([w_in[sl], w_in[E + c * ES:E + (c + 1) * ES]], 0)
        xe = np.zeros((TSE, D), f32)
        lo = max(t0, 0)
        xe[lo - t0:] = x[lo:t0 + TSE]
        maps.append(dict(
            xa=_cc(x[c * TSH:(c + 1) * TSH]),
            xe=_cc(xe.T),
            w_in_slT=_cc(w_sl.T),
            lnw=np.asarray(inputs["L0_attn_ln_w"], f32),
            lnb=np.asarray(inputs["L0_attn_ln_b"], f32),
            cw=_cc(np.asarray(inputs["L0_attn_conv_w"], f32)[sl]),
            cb=_cc(np.asarray(inputs["L0_attn_conv_b"], f32)[sl]),
            wxpT=_cc(np.asarray(inputs["L0_attn_w_xproj"], f32)[:, sl].T),
            wdtT=_cc(np.asarray(inputs["L0_attn_w_dt"], f32)[sl].T),
            bdt=_cc(np.asarray(inputs["L0_attn_b_dt"], f32)[sl]),
            alog=_cc(np.asarray(inputs["L0_attn_a_log"], f32)[sl]),
            dskip=_cc(np.asarray(inputs["L0_attn_d_skip"], f32)[sl]),
            woutT=_cc(np.asarray(inputs["L0_attn_w_out"], f32).T),
            wqkv0=_cc(np.asarray(inputs["L0_ffn_wqkv"], f32)[0]),
            wqkv1=_cc(np.asarray(inputs["L0_ffn_wqkv"], f32)[1]),
            qsr0=_cc(np.tile(qs_[0], 2)[:, None]),
            ksr0=_cc(np.tile(ks_[0], 2)[:, None]),
            qsr1=_cc(np.tile(qs_[1], 2)[:, None]),
            ksr1=_cc(np.tile(ks_[1], 2)[:, None]),
            tabs=tabs_for(t0),
            oneh=oneh, rotm=rotm, ident=ident, iden32=iden32,
            masks=masks_for(c),
            extm=np.full((128, 8, 128), 0.0 if c == 0 else 1.0, f32),
            idxt=idx_table_for(c),
            woA0=_cc(np.asarray(inputs["L0_ffn_wo"], f32)[0]),
            woA1=_cc(np.asarray(inputs["L0_ffn_wo"], f32)[1]),
            w1f0=w1f0, w1f1=w1f1, b1p0=b1p0, b1p1=b1p1,
            w2f0=w2f0, w2f1=w2f1,
            b2v0=_cc(np.asarray(inputs["L0_ffn_b2"], f32)[0]),
            b2v1=_cc(np.asarray(inputs["L0_ffn_b2"], f32)[1]),
            lgv0=_cc(np.asarray(inputs["L0_ffn_ln_g"], f32)[0]),
            lgv1=_cc(np.asarray(inputs["L0_ffn_ln_g"], f32)[1]),
            lbv0=_cc(np.asarray(inputs["L0_ffn_ln_b"], f32)[0]),
            lbv1=_cc(np.asarray(inputs["L0_ffn_ln_b"], f32)[1]),
        ))
    res = run_bass_kernel_spmd(nc, maps, core_ids=list(range(NC)))
    out = np.concatenate([np.asarray(r["outp"]) for r in res.results], 0)
    return out[None].astype(np.float32)


# version 06
# speedup vs baseline: 1.1601x; 1.1601x over previous
import numpy as np
import ml_dtypes
from contextlib import ExitStack

import concourse.bass as bass
import concourse.mybir as mybir
from concourse.bass import IndirectOffsetOnAxis
from concourse.bass_utils import run_bass_kernel_spmd

F32 = mybir.dt.float32
BF16 = mybir.dt.bfloat16
AF = mybir.ActivationFunctionType
ALU = mybir.AluOpType
AX = mybir.AxisListType
NPBF = ml_dtypes.bfloat16

NC = 8
T = 16384
D = 1024
E = 2048
ES = E // NC
TSH = T // NC
EXT = 128
TSE = TSH + EXT
NST = 16
DTR = 64
WIN = 512
HEADS = 8
DH = 64
INNER = 512
HID = 2730
HIDP = 2752
NH1 = 2 * HIDP
QKS = 8.0
EPS = 1e-5
STRIP = WIN + 128
NEG = np.float32(-3.0e38)

ENGINES = ("sync", "vector", "scalar", "tensor", "gpsimd")

YW = T + 2048
CHK_C = [(0, 128), (128, 512), (640, 512), (1152, 512), (1664, 512)]
IXA = 0
IXB = 8
IXC = 40
IXK = 120
IXV = 124
IXKR = 130
IXVR = 134
IXN = 140


def idx_table_for(c):
    p = np.arange(128, dtype=np.int64)
    t = np.zeros((128, IXN), np.int64)
    for b in range(8):
        t[:, IXA + b] = (b * 128 + p) * 8 + c
    for ch in range(16):
        for tb in range(2):
            t[:, IXB + ch * 2 + tb] = (c * ES + tb * 128 + p) * (YW // 1024) \
                + ch
    for k in range(16):
        t[:, IXC + k] = (k * 128 + p) * (YW // 128) + \
            (c * 16 - 1 if c > 0 else YW // 128 - 1)
        for ci in range(1, 5):
            t[:, IXC + ci * 16 + k] = (k * 128 + p) * (YW // 512) + c * 4 \
                + (ci - 1)
    for kt in range(4):
        t[:, IXK + kt] = (kt * 128 + p) * 9 + (c + 1)
        t[:, IXKR + kt] = (kt * 128 + p) * 9 + c
    for it in range(6):
        t[:, IXV + it] = (c + 1) * 768 + it * 128 + p
        t[:, IXVR + it] = c * 768 + it * 128 + p
    return np.ascontiguousarray(t.astype(np.uint32))


class Prog:

    LIM = 16000

    def __init__(self, nc, stack, nsem=10):
        self.nc = nc
        self.sems = [stack.enter_context(nc.semaphore(f"sq{i}"))
                     for i in range(nsem)]
        self.sem = self.sems[0]
        self.ops = []
        self.cnt_ = 0

    def op(self, ename, builder, dma=False, tag=None):
        inc = 16 if dma else 1
        self.ops.append((ename, builder, inc, inc,
                         "dma" if dma else (tag or "comp")))
        self.cnt_ += inc

    def site(self, ename, builders):
        for j, b in enumerate(builders):
            en = ename
            if isinstance(b, tuple):
                en, b = b
            cnt = 16 if j == len(builders) - 1 else 0
            self.ops.append((en, b, 16, cnt, "coll"))
            self.cnt_ += cnt

    def dma(self, out, in_):
        self.op("sync", lambda e, o=out, i=in_: e.dma_start(out=o, in_=i), dma=True)

    def dmag(self, out, in_):
        self.op("gpsimd", lambda e, o=out, i=in_: e.dma_start(out=o, in_=i), dma=True)

    def ind(self, out, out_idx, in_, in_idx):
        oo = IndirectOffsetOnAxis(ap=out_idx, axis=0) if out_idx is not None \
            else None
        io = IndirectOffsetOnAxis(ap=in_idx, axis=0) if in_idx is not None \
            else None
        self.op("gpsimd",
                lambda e, o=out, i=in_, oo=oo, io=io: e.indirect_dma_start(
                    out=o, out_offset=oo, in_=i, in_offset=io), dma=True)

    def v(self, builder):
        self.op("vector", builder)

    def s(self, builder):
        self.op("scalar", builder)

    def t(self, builder):
        self.op("tensor", builder)

    def g(self, builder):
        self.op("gpsimd", builder)

    def finalize(self, block):
        LIM = Prog.LIM
        recs = []
        ep = 0
        base = [0]
        c = 0
        prev_e = None
        for e, b, stamp, cnt, tag in self.ops:
            if c + cnt - base[ep] > LIM:
                ep += 1
                base.append(c)
                assert ep < len(self.sems), "need more epoch semaphores"
            recs.append((e, b, stamp, ep, c - base[ep], e == prev_e, tag))
            prev_e = e
            c += cnt
        totals = [0] * (ep + 1)
        for (e, b, stamp, cnt, tag), r in zip(self.ops, recs):
            totals[r[3]] += cnt
        for ename in ENGINES:
            mine = [(b, inc, k, lp, sw, tag)
                    for (e, b, inc, k, lp, sw, tag) in recs if e == ename]
            if not mine and ename != "sync":
                continue

            def body(eng, mine=mine, last=(ename == "sync"), ep=ep,
                     ename=ename):
                seen = -1
                unclean = set()
                for b, inc, k, lp, sw, tag in mine:
                    if k > seen:
                        for kk in range(max(seen, 0), k):
                            eng.wait_ge(self.sems[kk], totals[kk])
                        seen = k
                        sw = False
                        unclean = set()
                    skip = sw and ename in ("tensor",) \
                        and (tag == "comp" and not unclean)
                    if lp and not skip:
                        eng.wait_ge(self.sems[k], lp)
                        unclean = set()
                    if tag != "comp":
                        unclean.add(tag)
                    b(eng).then_inc(self.sems[k], inc)
                if last:
                    for kk in range(ep + 1):
                        eng.wait_ge(self.sems[kk], totals[kk])

            getattr(block, ename)(body)
        return c


def mm(ps_ap, lhsT_ap, rhs_ap, start, stop):
    return lambda e: e.matmul(ps_ap, lhsT_ap, rhs_ap, start=start, stop=stop)


class Consts:
    cur = None

    def __init__(self, P, nc, stack):
        self.P = P
        self.t = stack.enter_context(nc.sbuf_tensor("cpool", [128, 32], F32))
        self.vals = {}
        Consts.cur = self

    def col(self, v):
        v = float(v)
        if v not in self.vals:
            i = len(self.vals)
            assert i < 32
            self.vals[v] = i
            self.P.v(mset(self.t[:, i:i + 1], v))
        return self.t[:, self.vals[v]:self.vals[v] + 1]


def act(out, in_, func, bias=0.0, scale=1.0, accum=None):
    np_ = in_.shape[0]
    if isinstance(bias, float) and bias != 0.0:
        bias = Consts.cur.col(bias)[0:np_, :] if np_ < 128 else             Consts.cur.col(bias)
    if isinstance(scale, float) and scale != 1.0:
        scale = Consts.cur.col(scale)[0:np_, :] if np_ < 128 else             Consts.cur.col(scale)
    return lambda e: e.activation(out, in_, func, bias=bias, scale=scale,
                                  accum_out=accum)


def tt(out, a, b, op):
    return lambda e: e.tensor_tensor(out, a, b, op)


def ts(out, a, s1, s2, op0, op1=ALU.bypass):
    if s2 is None:
        return lambda e: e.tensor_scalar(out, a, s1, None, op0)
    return lambda e: e.tensor_scalar(out, a, s1, s2, op0, op1)


def stt(out, in0, scalar, in1, op0, op1):
    return lambda e: e.scalar_tensor_tensor(out, in0, scalar, in1, op0, op1)


def red(out, in_, op, negate=False):
    return lambda e: e.tensor_reduce(out, in_, AX.X, op, negate=negate)


def cp(out, in_):
    return lambda e: e.tensor_copy(out, in_)


def rcp(out, in_):
    return lambda e: e.reciprocal(out, in_)


def mset(ap_, v):
    return lambda e: e.memset(ap_, v)


def tts(out, d0, d1, init):
    return lambda e: e.tensor_tensor_scan(out, d0, d1, init, ALU.mult, ALU.add)


def trans(ps_ap, in_ap, ident_ap):
    return lambda e: e.transpose(ps_ap, in_ap, ident_ap)


def vec_cols(P, dst, src1d, nb, cast=False):
    for b in range(nb):
        src = src1d[b * 128:(b + 1) * 128][:, None]
        if cast:
            P.dmag(dst[:, b:b + 1], src)
        else:
            P.dma(dst[:, b:b + 1], src)


def scan_f(total, step):
    o = 0
    while o < total:
        s = min(step, total - o)
        yield o, s
        o += s


def emit_fm_ln(P, sc, x_view, nb, F, fo, d_out, n_out, ones128, sq_t, ps_row,
               ps_bc, row_sb, shift_upper_from=None, out_dtype_note=None):
    for k in range(nb):
        P.t(mm(ps_row[0:1, 0:F], ones128[:, 0:1], x_view(k),
               start=(k == 0), stop=(k == nb - 1)))
    P.s(act(row_sb[0:1, 0:F], ps_row[0:1, 0:F], AF.Identity, scale=1.0 / D))
    P.t(mm(ps_bc[:, 0:F], ones128[0:1, 0:128], row_sb[0:1, 0:F], True, True))
    for k in range(nb):
        P.v(tt(d_out[:, k, fo:fo + F], x_view(k), ps_bc[:, 0:F], ALU.subtract))
    for k in range(nb):
        P.s(act(sq_t[:, k, 0:F], d_out[:, k, fo:fo + F], AF.Square))
    for k in range(nb):
        P.t(mm(ps_row[0:1, 0:F], ones128[:, 0:1], sq_t[:, k, 0:F],
               start=(k == 0), stop=(k == nb - 1)))
    P.s(act(row_sb[1:2, 0:F], ps_row[0:1, 0:F], AF.Ln, bias=EPS, scale=1.0 / D))
    P.s(act(row_sb[1:2, 0:F], row_sb[1:2, 0:F], AF.Exp, scale=-0.5))
    P.t(mm(ps_bc[:, 0:F], ones128[0:1, 0:128], row_sb[1:2, 0:F], True, True))
    if n_out is not None:
        for k in range(nb):
            P.v(tt(n_out[:, k, fo:fo + F], d_out[:, k, fo:fo + F],
                   ps_bc[:, 0:F], ALU.mult))


def build_A():
    nc = bass.Bass("TRN2", target_bir_lowering=False, debug=False)
    xa_d = nc.dram_tensor("xa", [TSH, D], F32, kind="ExternalInput").ap()
    id_d = nc.dram_tensor("ident", [128, 128], BF16, kind="ExternalInput").ap()
    xn_d = nc.dram_tensor("xn", [D, TSH], BF16, kind="ExternalOutput").ap()
    with ExitStack() as st:
        P = Prog(nc, st)
        Consts(P, nc, st)
        x_s = st.enter_context(nc.sbuf_tensor("x_s", [128, 16, D], F32))
        sqs = st.enter_context(nc.sbuf_tensor("sqs", [128, D], F32))
        xnb = st.enter_context(nc.sbuf_tensor("xnb", [128, 16, D], BF16))
        colc = st.enter_context(nc.sbuf_tensor("colc", [128, 4], F32))
        ident = st.enter_context(nc.sbuf_tensor("ident_s", [128, 128], BF16))
        xnf = st.enter_context(nc.sbuf_tensor("xnf", [128, 8, TSH], BF16))
        pst = st.enter_context(nc.psum_tensor("pst", [128, 512], BF16))
        blk = st.enter_context(nc.Block())

        P.dma(x_s[:, :, :], xa_d.rearrange("(n p) d -> p n d", p=128))
        P.dma(ident[:, :], id_d)
        for i in range(16):
            xt = x_s[:, i, :]
            for rep in range(2):
                P.v(red(colc[:, 0:1], xt, ALU.add))
                P.v(ts(colc[:, 0:1], colc[:, 0:1], 1.0 / D, None, ALU.mult))
                P.v(ts(xt, xt, colc[:, 0:1], None, ALU.subtract))
                P.s(act(sqs[:, :], xt, AF.Square, accum=colc[:, 2:3]))
                P.s(act(colc[:, 2:3], colc[:, 2:3], AF.Ln, bias=EPS,
                        scale=1.0 / D))
                P.s(act(colc[:, 2:3], colc[:, 2:3], AF.Exp, scale=-0.5))
                if rep == 0:
                    P.v(ts(xt, xt, colc[:, 2:3], None, ALU.mult))
                else:
                    P.v(ts(xnb[:, i, :], xt, colc[:, 2:3], None, ALU.mult))
        for i in range(16):
            for dbq in range(2):
                for j in range(4):
                    db = dbq * 4 + j
                    P.t(trans(pst[:, j * 128:(j + 1) * 128],
                              xnb[:, i, db * 128:(db + 1) * 128], ident[:, :]))
                for j in range(4):
                    db = dbq * 4 + j
                    P.s(act(xnf[:, db, i * 128:(i + 1) * 128],
                            pst[:, j * 128:(j + 1) * 128], AF.Copy))
        P.dma(xn_d.rearrange("(b p) l -> p b l", p=128), xnf[:, :, :])
        P.finalize(blk)
    return nc


def build_B0():
    nc = bass.Bass("TRN2", target_bir_lowering=False, debug=False)
    xn_d = nc.dram_tensor("xnf", [D, T], BF16, kind="ExternalInput").ap()
    w_d = nc.dram_tensor("w_in_slT", [D, 2 * ES], F32, kind="ExternalInput").ap()
    lnw_d = nc.dram_tensor("lnw", [D], F32, kind="ExternalInput").ap()
    lnb_d = nc.dram_tensor("lnb", [D], F32, kind="ExternalInput").ap()
    cw_d = nc.dram_tensor("cw", [ES, 4], F32, kind="ExternalInput").ap()
    cb_d = nc.dram_tensor("cb", [ES], F32, kind="ExternalInput").ap()
    wxp_d = nc.dram_tensor("wxpT", [ES, 96], F32, kind="ExternalInput").ap()
    u_d = nc.dram_tensor("u32", [ES, T], F32, kind="ExternalOutput").ap()
    z_d = nc.dram_tensor("z32", [ES, T], F32, kind="ExternalOutput").ap()
    xp_d = nc.dram_tensor("xdblp", [96, T], F32, kind="ExternalOutput").ap()

    LC = 2048
    NCH = T // LC
    with ExitStack() as st:
        P = Prog(nc, st)
        Consts(P, nc, st)
        wt = st.enter_context(nc.sbuf_tensor("wt", [128, 8, 512], BF16))
        lnw_s = st.enter_context(nc.sbuf_tensor("lnw_s", [128, 8], F32))
        lnbb = st.enter_context(nc.sbuf_tensor("lnbb", [128, 8], BF16))
        bias_s = st.enter_context(nc.sbuf_tensor("bias_s", [128, 4], F32))
        cw_s = st.enter_context(nc.sbuf_tensor("cw_s", [128, 2, 4], F32))
        cb_s = st.enter_context(nc.sbuf_tensor("cb_s", [128, 2], F32))
        wxt = st.enter_context(nc.sbuf_tensor("wxt", [128, 2, 96], BF16))
        xnc = st.enter_context(nc.sbuf_tensor("xnc", [128, 8, LC], BF16))
        xs3 = st.enter_context(nc.sbuf_tensor("xs3", [128, 2, LC + 3], F32))
        zb = st.enter_context(nc.sbuf_tensor("zb", [128, 2, LC], F32))
        uc = st.enter_context(nc.sbuf_tensor("uc", [128, 2, LC], F32))
        ub = st.enter_context(nc.sbuf_tensor("ub", [128, 2, LC], BF16))
        ps = st.enter_context(nc.psum_tensor("ps", [128, 512], F32))
        blk = st.enter_context(nc.Block())

        P.dmag(wt[:, :, :], w_d.rearrange("(b p) r -> p b r", p=128))
        vec_cols(P, lnw_s, lnw_d, 8)
        vec_cols(P, lnbb, lnb_d, 8, cast=True)
        P.dma(cw_s[:, :, :], cw_d.rearrange("(b p) k -> p b k", p=128))
        vec_cols(P, cb_s, cb_d, 2)
        P.dmag(wxt[:, :, :], wxp_d.rearrange("(b p) r -> p b r", p=128))
        for mg in range(4):
            for k in range(8):
                P.t(mm(ps[0:128, mg:mg + 1],
                       wt[:, k, mg * 128:(mg + 1) * 128],
                       lnbb[:, k:k + 1], k == 0, k == 7))
        P.s(act(bias_s[:, :], ps[0:128, 0:4], AF.Copy))
        for db in range(8):
            P.v(ts(wt[:, db, :], wt[:, db, :], lnw_s[:, db:db + 1], None,
                   ALU.mult))
        P.v(mset(xs3[:, :, 0:3], 0.0))

        for c in range(NCH):
            P.dma(xnc[:, :, :],
                  xn_d.rearrange("(b p) l -> p b l", p=128)[:, :,
                                                           c * LC:(c + 1) * LC])
            for m in range(4):
                for fo, fs in scan_f(LC, 512):
                    for k in range(8):
                        P.t(mm(ps[:, 0:fs], wt[:, k, m * 128:(m + 1) * 128],
                               xnc[:, k, fo:fo + fs], k == 0, k == 7))
                    if m < 2:
                        P.s(act(xs3[:, m, 3 + fo:3 + fo + fs], ps[:, 0:fs],
                                AF.Identity, bias=bias_s[:, m:m + 1]))
                    else:
                        P.s(act(zb[:, m - 2, fo:fo + fs], ps[:, 0:fs],
                                AF.Identity, bias=bias_s[:, m:m + 1]))
            for tb in range(2):
                P.v(ts(uc[:, tb, :], xs3[:, tb, 0:LC], cw_s[:, tb, 0:1],
                       cb_s[:, tb:tb + 1], ALU.mult, ALU.add))
                for k in range(1, 4):
                    P.v(stt(uc[:, tb, :], xs3[:, tb, k:k + LC],
                            cw_s[:, tb, k:k + 1], uc[:, tb, :],
                            ALU.mult, ALU.add))
                P.v(cp(xs3[:, tb, 0:3], xs3[:, tb, LC:LC + 3]))
                P.s(act(uc[:, tb, :], uc[:, tb, :], AF.Silu))
                P.v(cp(ub[:, tb, :], uc[:, tb, :]))
            P.dma(u_d.rearrange("(b p) l -> p b l", p=128)[:, :,
                                                           c * LC:(c + 1) * LC],
                  uc[:, :, :])
            P.dma(z_d.rearrange("(b p) l -> p b l", p=128)[:, :,
                                                           c * LC:(c + 1) * LC],
                  zb[:, :, :])
            for fo, fs in scan_f(LC, 512):
                for k in range(2):
                    P.t(mm(ps[0:96, 0:fs], wxt[:, k, :], ub[:, k, fo:fo + fs],
                           k == 0, k == 1))
                P.s(act(zb[0:96, 0, fo:fo + fs], ps[0:96, 0:fs], AF.Copy))
            P.dma(xp_d[:, c * LC:(c + 1) * LC], zb[0:96, 0, 0:LC])
        P.finalize(blk)
    return nc


def build_B1():
    nc = bass.Bass("TRN2", target_bir_lowering=False, debug=False)
    u_d = nc.dram_tensor("u32", [ES, T], F32, kind="ExternalInput").ap()
    z_d = nc.dram_tensor("z32", [ES, T], F32, kind="ExternalInput").ap()
    xp_d = nc.dram_tensor("xdblp", [NC, 96, T], F32, kind="ExternalInput").ap()
    wdt_d = nc.dram_tensor("wdtT", [DTR, ES], F32, kind="ExternalInput").ap()
    bdt_d = nc.dram_tensor("bdt", [ES], F32, kind="ExternalInput").ap()
    alog_d = nc.dram_tensor("alog", [ES, NST], F32, kind="ExternalInput").ap()
    dsk_d = nc.dram_tensor("dskip", [ES], F32, kind="ExternalInput").ap()
    yg_d = nc.dram_tensor("yg", [ES, T], BF16, kind="ExternalOutput").ap()
    bc_d = nc.dram_tensor("bc_scr", [32, T], F32).ap()

    LC = 1024
    NCH = T // LC
    NG = 4
    with ExitStack() as st:
        P = Prog(nc, st)
        Consts(P, nc, st)
        wdtt = st.enter_context(nc.sbuf_tensor("wdtt", [64, 2, 128], BF16))
        bdt_s = st.enter_context(nc.sbuf_tensor("bdt_s", [128, 2], F32))
        acol = st.enter_context(nc.sbuf_tensor("acol", [128, 2, NST], F32))
        dsk_s = st.enter_context(nc.sbuf_tensor("dsk_s", [128, 2], F32))
        ones1 = st.enter_context(nc.sbuf_tensor("ones1", [1, 128], F32))
        xdb = st.enter_context(nc.sbuf_tensor("xdb", [96, 8, LC], F32))
        xds = st.enter_context(nc.sbuf_tensor("xds", [96, LC], F32))
        dtrb = st.enter_context(nc.sbuf_tensor("dtrb", [64, LC], BF16))
        dtpre = st.enter_context(nc.sbuf_tensor("dtpre", [128, 2, LC], F32))
        spt = st.enter_context(nc.sbuf_tensor("spt", [128, LC], F32))
        dtb = st.enter_context(nc.sbuf_tensor("dtb", [128, 2, LC], F32))
        uch = st.enter_context(nc.sbuf_tensor("uch", [128, 2, LC], F32))
        zch = st.enter_context(nc.sbuf_tensor("zch", [128, 2, LC], F32))
        dtu = st.enter_context(nc.sbuf_tensor("dtu", [128, 2, LC], F32))
        bcf = st.enter_context(nc.sbuf_tensor("bcf", [1, NG * LC], F32))
        dA = st.enter_context(nc.sbuf_tensor("dA", [128, NG * LC], F32))
        dBu = st.enter_context(nc.sbuf_tensor("dBu", [128, NG * LC], F32))
        hb = st.enter_context(nc.sbuf_tensor("hb", [128, NG * LC], F32))
        fixc = st.enter_context(nc.sbuf_tensor("fixc", [128, NG], F32))
        carry = st.enter_context(nc.sbuf_tensor("carry", [128, 2, NST], F32))
        yb4 = st.enter_context(nc.sbuf_tensor("yb4", [128, NG, LC], F32))
        yb = st.enter_context(nc.sbuf_tensor("yb", [128, LC], F32))
        ygb = st.enter_context(nc.sbuf_tensor("ygb", [128, LC], BF16))
        ps = st.enter_context(nc.psum_tensor("ps", [128, NG * LC], F32))
        blk = st.enter_context(nc.Block())

        P.dmag(wdtt[:, :, :], wdt_d.rearrange("r (b p) -> r b p", p=128))
        vec_cols(P, bdt_s, bdt_d, 2)
        P.dma(acol[:, :, :], alog_d.rearrange("(b p) n -> p b n", p=128))
        vec_cols(P, dsk_s, dsk_d, 2)
        P.v(mset(ones1[:, :], 1.0))
        P.v(mset(carry[:, :, :], 0.0))
        for tb in range(2):
            P.s(act(acol[:, tb, :], acol[:, tb, :], AF.Exp))
            P.v(ts(acol[:, tb, :], acol[:, tb, :], -1.0, None, ALU.mult))

        for c in range(NCH):
            lo = c * LC
            P.dma(xdb[:, :, :], xp_d.transpose([1, 0, 2])[:, :, lo:lo + LC])
            P.v(tt(xds[:, :], xdb[:, 0, :], xdb[:, 1, :], ALU.add))
            for j in range(2, 8):
                P.v(tt(xds[:, :], xds[:, :], xdb[:, j, :], ALU.add))
            P.dma(bc_d[:, lo:lo + LC], xds[64:96, :])
            P.v(cp(dtrb[:, :], xds[0:64, :]))
            for mg in range(2):
                for fo, fs in scan_f(LC, 512):
                    P.t(mm(ps[0:128, fo:fo + fs], wdtt[:, mg, :],
                           dtrb[:, fo:fo + fs], True, True))
                P.s(act(dtpre[:, mg, :], ps[0:128, 0:LC], AF.Identity,
                        bias=bdt_s[:, mg:mg + 1]))
            for tb in range(2):
                P.s(act(spt[:, :], dtpre[:, tb, :], AF.Abs))
                P.s(act(spt[:, :], spt[:, :], AF.Exp, scale=-1.0))
                P.s(act(spt[:, :], spt[:, :], AF.Ln, bias=1.0))
                P.s(act(dtb[:, tb, :], dtpre[:, tb, :], AF.Relu))
                P.v(tt(dtb[:, tb, :], dtb[:, tb, :], spt[:, :], ALU.add))
            P.dma(uch[:, :, :],
                  u_d.rearrange("(b p) l -> p b l", p=128)[:, :, lo:lo + LC])
            P.dma(zch[:, :, :],
                  z_d.rearrange("(b p) l -> p b l", p=128)[:, :, lo:lo + LC])
            for tb in range(2):
                P.v(tt(dtu[:, tb, :], dtb[:, tb, :], uch[:, tb, :], ALU.mult))
            for tb in range(2):
                for g in range(NST // NG):
                    P.dma(bcf.ap().rearrange("p (n l) -> p n l", n=NG),
                          bc_d[g * NG:(g + 1) * NG, lo:lo + LC][None, :, :])
                    for fo, fs in scan_f(NG * LC, 512):
                        P.t(mm(ps[:, fo:fo + fs], ones1[0:1, :],
                               bcf[0:1, fo:fo + fs], True, True))
                    for n in range(NG):
                        P.s(act(dA[:, n * LC:(n + 1) * LC], dtb[:, tb, :],
                                AF.Exp, scale=acol[:, tb, g * NG + n:
                                                   g * NG + n + 1]))
                    P.v(tt(dBu.ap().rearrange("p (n l) -> p n l", n=NG),
                           dtu[:, tb, None, :].broadcast_to([128, NG, LC]),
                           ps.ap().rearrange("p (n l) -> p n l", n=NG),
                           ALU.mult))
                    P.v(tt(fixc[:, 0:NG],
                           dA.ap().rearrange("p (n l) -> p n l", n=NG)[:, :, 0],
                           carry[:, tb, g * NG:(g + 1) * NG], ALU.mult))
                    P.v(tt(dBu.ap().rearrange("p (n l) -> p n l",
                                              n=NG)[:, :, 0],
                           dBu.ap().rearrange("p (n l) -> p n l",
                                              n=NG)[:, :, 0],
                           fixc[:, 0:NG], ALU.add))
                    P.v(mset(dA.ap().rearrange("p (n l) -> p n l",
                                               n=NG)[:, :, 0:1], 0.0))
                    P.v(tts(hb[:, :], dA[:, :], dBu[:, :], 0.0))
                    P.v(cp(carry[:, tb, g * NG:(g + 1) * NG],
                           hb.ap().rearrange("p (n l) -> p n l",
                                             n=NG)[:, :, LC - 1]))
                    P.dma(bcf.ap().rearrange("p (n l) -> p n l", n=NG),
                          bc_d[16 + g * NG:16 + (g + 1) * NG,
                               lo:lo + LC][None, :, :])
                    for fo, fs in scan_f(NG * LC, 512):
                        P.t(mm(ps[:, fo:fo + fs], ones1[0:1, :],
                               bcf[0:1, fo:fo + fs], True, True))
                    P.v(tt(dA[:, :], hb[:, :], ps[:, 0:NG * LC], ALU.mult))
                    P.v(red(yb4[:, g, :],
                            dA.ap().rearrange("p (n l) -> p l n", n=NG),
                            ALU.add))
                P.v(red(yb[:, :],
                        yb4.ap().rearrange("p g l -> p l g"), ALU.add))
                P.v(ts(spt[:, :], uch[:, tb, :], dsk_s[:, tb:tb + 1], None,
                       ALU.mult))
                P.v(tt(yb[:, :], yb[:, :], spt[:, :], ALU.add))
                P.s(act(spt[:, :], zch[:, tb, :], AF.Silu))
                P.v(tt(ygb[:, :], yb[:, :], spt[:, :], ALU.mult))
                P.dma(yg_d.rearrange("(b p) l -> p b l", p=128)[:, tb,
                                                               lo:lo + LC],
                      ygb[:, :])
        P.finalize(blk)
    return nc


def fm_ln(P, x_view, F, dbuf, sqb, ones128, onesr, ps_row, ps_bc, rows, nrm_view=None):
    for k in range(8):
        P.t(mm(ps_row[0:1, 0:F], ones128[:, 0:1], x_view(k), k == 0, k == 7))
    P.s(act(rows[0:1, 0, 0:F], ps_row[0:1, 0:F], AF.Identity, scale=1.0 / D))
    P.t(mm(ps_bc[:, 0:F], onesr[0:1, :], rows[0:1, 0, 0:F], True, True))
    for k in range(8):
        P.v(tt(dbuf[:, k, 0:F], x_view(k), ps_bc[:, 0:F], ALU.subtract))
    for k in range(8):
        P.s(act(sqb[:, k, 0:F], dbuf[:, k, 0:F], AF.Square))
    for k in range(8):
        P.t(mm(ps_row[0:1, 0:F], ones128[:, 0:1], sqb[:, k, 0:F], k == 0,
               k == 7))
    P.s(act(rows[0:1, 1, 0:F], ps_row[0:1, 0:F], AF.Ln, bias=EPS,
            scale=1.0 / D))
    P.s(act(rows[0:1, 1, 0:F], rows[0:1, 1, 0:F], AF.Exp, scale=-0.5))
    P.t(mm(ps_bc[:, 0:F], onesr[0:1, :], rows[0:1, 1, 0:F], True, True))
    if nrm_view is not None:
        for k in range(8):
            P.v(tt(nrm_view(k), dbuf[:, k, 0:F], ps_bc[:, 0:F], ALU.mult))


def qkv_chunk(P, B, fo, fs, q_d, k_d, vT_d):
    xbpc, qf, vb, vtb, sqh, rqr = (B["xbpc"], B["qf"], B["vb"],
                                   B["vtb"], B["sqh"], B["rqr"])
    wq = B.get("wq")
    t1, t2, t3, qo, tabc = B["t1"], B["t2"], B["t3"], B["qo"], B["tabc"]
    ones128, onesr, oneh, ident = (B["ones128"], B["onesr"], B["oneh"],
                                   B["ident"])
    ps_mm, ps_bc, ps_row, ps_t = B["ps_mm"], B["ps_bc"], B["ps_row"], B["ps_t"]

    def mm_group(m):
        if "wq_d" in B:
            P.dmag(B["wqm"][:, :, :],
                   B["wq_d"].rearrange("(b p) c -> p b c", p=128)
                   [:, :, m * 128:(m + 1) * 128])
            w_ap = lambda k: B["wqm"][:, k, :]
        else:
            w_ap = lambda k: wq[:, k, m * 128:(m + 1) * 128]
        for k in range(8):
            rhs = (xbpc[:, k, 1:1 + fs] if k < 4 else xbpc[:, k, 0:fs])
            P.t(mm(ps_mm[:, 0:fs], w_ap(k), rhs, k == 0, k == 7))

    def one_side(scol, csi, sni, out_d):
        for tl in range(4):
            P.s(act(sqh[:, tl, 0:fs], qf[:, tl, 0:fs], AF.Square))
        for tl in range(4):
            for hh in range(2):
                b = 64 * hh
                P.t(mm(ps_row[0:1, 0:fs], ones128[b:b + 64, 0:1],
                       sqh[b:b + 64, tl, 0:fs], True, True))
                P.s(act(rqr[0:1, hh, 0:fs], ps_row[0:1, 0:fs], AF.Copy))
            P.v(ts(rqr[0:1, :, 0:fs], rqr[0:1, :, 0:fs], 1e-24, None,
                   ALU.max))
            P.s(act(rqr[0:1, :, 0:fs], rqr[0:1, :, 0:fs], AF.Ln))
            P.s(act(rqr[0:1, :, 0:fs], rqr[0:1, :, 0:fs], AF.Exp, scale=-0.5))
            P.t(mm(ps_bc[:, 0:fs], oneh[0:1, 0, :], rqr[0:1, 0, 0:fs],
                   True, False))
            P.t(mm(ps_bc[:, 0:fs], oneh[0:1, 1, :],
                   rqr[0:1, 1, 0:fs], False, True))
            P.v(tt(t1[:, 0:fs], qf[:, tl, 0:fs], ps_bc[:, 0:fs], ALU.mult))
            P.v(ts(t1[:, 0:fs], t1[:, 0:fs], scol, None, ALU.mult))
            P.v(tt(t2[:, 0:fs], t1[:, 0:fs], tabc[:, csi, 0:fs], ALU.mult))
            P.t(mm(ps_bc[:, 0:fs], B["rotm"][:, :], t1[:, 0:fs], True, True))
            P.v(tt(t3[:, 0:fs], ps_bc[:, 0:fs], tabc[:, sni, 0:fs], ALU.mult))
            P.v(tt(qo[:, 0:fs], t2[:, 0:fs], t3[:, 0:fs], ALU.add))
            P.dma(out_d[tl * 128:(tl + 1) * 128, fo:fo + fs], qo[:, 0:fs])

    for m in range(4):
        mm_group(m)
        P.s(act(qf[:, m, 0:fs], ps_mm[:, 0:fs], AF.Copy))
    one_side(B["qsr"], 0, 1, q_d)
    for m in range(4, 8):
        mm_group(m)
        P.s(act(qf[:, m - 4, 0:fs], ps_mm[:, 0:fs], AF.Copy))
    one_side(B["ksr"], 2, 3, k_d)
    for m in range(8, 12):
        mm_group(m)
        P.s(act(vb[:, m - 8, 0:fs], ps_mm[:, 0:fs], AF.Copy))
    ntile = fs // 128
    for vt_ in range(4):
        for j in range(ntile):
            P.t(trans(ps_t[:, j * 128:(j + 1) * 128],
                      vb[:, vt_, j * 128:(j + 1) * 128], ident[:, :]))
        for j in range(ntile):
            P.s(act(vtb[:, j, vt_ * 128:(vt_ + 1) * 128],
                    ps_t[:, j * 128:(j + 1) * 128], AF.Copy))
    P.dma(vT_d[fo:fo + fs, :].rearrange("(j p) c -> p j c", p=128),
          vtb[:, 0:ntile, :])
    for k in range(8):
        P.v(cp(xbpc[:, k, 0:1], xbpc[:, k, fs:fs + 1]))


def alloc_qkv_bufs(nc, st, name, qf=None, sqh=None, vb=None, vtb=None,
                   rqr=None, t123=None, qo=None, tabc=None):
    B = {}
    B["xbpc"] = st.enter_context(
        nc.sbuf_tensor(name + "xbpc", [128, 8, 513], BF16))
    if qf is None:
        qf = st.enter_context(nc.sbuf_tensor(name + "qf", [128, 4, 512], F32))
    if sqh is None:
        sqh = st.enter_context(
            nc.sbuf_tensor(name + "sqh", [128, 4, 512], F32))
    B["qf"], B["sqh"] = qf, sqh
    if vb is None:
        vb = st.enter_context(nc.sbuf_tensor(name + "vb", [128, 4, 512], BF16))
    if vtb is None:
        vtb = st.enter_context(
            nc.sbuf_tensor(name + "vtb", [128, 4, 512], BF16))
    B["vb"], B["vtb"] = vb, vtb
    B["rqr"] = rqr if rqr is not None else st.enter_context(
        nc.sbuf_tensor(name + "rqr", [1, 2, 512], F32))
    if t123 is not None:
        B["t1"], B["t2"], B["t3"] = t123
    else:
        B["t1"] = st.enter_context(nc.sbuf_tensor(name + "t1", [128, 512], F32))
        B["t2"] = st.enter_context(nc.sbuf_tensor(name + "t2", [128, 512], F32))
        B["t3"] = st.enter_context(nc.sbuf_tensor(name + "t3", [128, 512], F32))
    B["qo"] = qo if qo is not None else st.enter_context(
        nc.sbuf_tensor(name + "qo", [128, 512], BF16))
    B["tabc"] = tabc if tabc is not None else st.enter_context(
        nc.sbuf_tensor(name + "tabc", [128, 4, 512], F32))
    return B


def build_C():
    nc = bass.Bass("TRN2", target_bir_lowering=False, debug=False)
    xe_d = nc.dram_tensor("xe", [D, TSE], F32, kind="ExternalInput").ap()
    yg_d = nc.dram_tensor("yge", [E, TSE], BF16, kind="ExternalInput").ap()
    wo_d = nc.dram_tensor("woutT", [E, D], F32, kind="ExternalInput").ap()
    wq_d = nc.dram_tensor("wqkv", [D, 3 * INNER], F32, kind="ExternalInput").ap()
    qs_d = nc.dram_tensor("qsr", [128, 1], F32, kind="ExternalInput").ap()
    ks_d = nc.dram_tensor("ksr", [128, 1], F32, kind="ExternalInput").ap()
    tab_d = nc.dram_tensor("tabs", [4, 128, TSE], F32, kind="ExternalInput").ap()
    oneh_d = nc.dram_tensor("oneh", [1, 2, 128], F32, kind="ExternalInput").ap()
    rot_d = nc.dram_tensor("rotm", [128, 128], F32, kind="ExternalInput").ap()
    id_d = nc.dram_tensor("ident", [128, 128], BF16, kind="ExternalInput").ap()
    x1_d = nc.dram_tensor("x1e", [D, TSE], F32, kind="ExternalOutput").ap()
    xb_d = nc.dram_tensor("xbe", [D, TSE], F32, kind="ExternalOutput").ap()
    q_od = nc.dram_tensor("q0", [INNER, TSE], BF16, kind="ExternalOutput").ap()
    k_od = nc.dram_tensor("k0", [INNER, TSE], BF16, kind="ExternalOutput").ap()
    v_od = nc.dram_tensor("vT0", [TSE, INNER], BF16, kind="ExternalOutput").ap()

    with ExitStack() as st:
        P = Prog(nc, st)
        Consts(P, nc, st)
        xs = st.enter_context(nc.sbuf_tensor("xs", [128, 8, TSE], F32))
        warena = st.enter_context(nc.sbuf_tensor("warena", [128, 16384], BF16))
        wo = warena.ap().rearrange("p (b c) -> p b c", b=16)
        ygc = st.enter_context(nc.sbuf_tensor("ygc", [128, 16, 512], BF16))
        dbuf = st.enter_context(nc.sbuf_tensor("dbuf", [128, 8, 512], F32))
        bobuf = st.enter_context(nc.sbuf_tensor("bobuf", [128, 8, 512], F32))
        sqb = st.enter_context(nc.sbuf_tensor("sqb", [128, 8, 512], F32))
        rows = st.enter_context(nc.sbuf_tensor("rows", [1, 2, 512], F32))
        ones128 = st.enter_context(nc.sbuf_tensor("ones128", [128, 1], F32))
        onesr = st.enter_context(nc.sbuf_tensor("onesr", [1, 128], F32))
        oneh = st.enter_context(nc.sbuf_tensor("oneh_s", [1, 2, 128], F32))
        ident = st.enter_context(nc.sbuf_tensor("ident_s", [128, 128], BF16))
        qsr = st.enter_context(nc.sbuf_tensor("qsr_s", [128, 1], F32))
        ksr = st.enter_context(nc.sbuf_tensor("ksr_s", [128, 1], F32))
        rotm = st.enter_context(nc.sbuf_tensor("rotm_s", [128, 128], F32))
        B = alloc_qkv_bufs(nc, st, "q0", qf=dbuf[:, 0:4, :],
                           sqh=sqb[:, 0:4, :])
        ps_mm = st.enter_context(nc.psum_tensor("ps_mm", [128, 512], F32))
        ps_bc = st.enter_context(nc.psum_tensor("ps_bc", [128, 512], F32))
        ps_row = st.enter_context(nc.psum_tensor("ps_row", [1, 512], F32))
        ps_t = st.enter_context(nc.psum_tensor("ps_t", [128, 512], BF16))
        blk = st.enter_context(nc.Block())

        B.update(dict(ones128=ones128, onesr=onesr, oneh=oneh, ident=ident,
                      rotm=rotm,
                      ps_mm=ps_mm, ps_bc=ps_bc, ps_row=ps_row, ps_t=ps_t,
                      qsr=qsr[:, 0:1], ksr=ksr[:, 0:1],
                      wq=warena[:, 0:12288].rearrange("p (b c) -> p b c", b=8)))
        P.dma(xs[:, :, :], xe_d.rearrange("(b p) l -> p b l", p=128))
        P.dmag(warena.ap().rearrange("p (b c) -> p b c", b=16),
               wo_d.rearrange("(b p) c -> p b c", p=128))
        P.v(mset(ones128[:, :], 1.0))
        P.v(mset(onesr[:, :], 1.0))
        P.dma(oneh[:, :, :], oneh_d)
        P.dma(ident[:, :], id_d)
        P.dma(qsr[:, :], qs_d)
        P.dma(ksr[:, :], ks_d)
        P.dma(rotm[:, :], rot_d)

        for fo, fs in scan_f(TSE, 512):
            P.dma(ygc[:, :, 0:fs],
                  yg_d.rearrange("(b p) l -> p b l", p=128)[:, :, fo:fo + fs])
            fm_ln(P, lambda k: xs[:, k, fo:fo + fs], fs, dbuf, sqb,
                  ones128, onesr, ps_row, ps_bc, rows,
                  nrm_view=lambda k: dbuf[:, k, 0:fs])
            for m in range(8):
                for k in range(16):
                    P.t(mm(ps_mm[:, 0:fs], wo[:, k, m * 128:(m + 1) * 128],
                           ygc[:, k, 0:fs], k == 0, k == 15))
                P.v(tt(bobuf[:, m, 0:fs], ps_mm[:, 0:fs], dbuf[:, m, 0:fs],
                       ALU.add))
            fm_ln(P, lambda k: bobuf[:, k, 0:fs], fs, bobuf, sqb,
                  ones128, onesr, ps_row, ps_bc, rows,
                  nrm_view=lambda k: bobuf[:, k, 0:fs])
            for m in range(8):
                P.v(tt(xs[:, m, fo:fo + fs], bobuf[:, m, 0:fs],
                       xs[:, m, fo:fo + fs], ALU.add))
            P.dma(x1_d.rearrange("(b p) l -> p b l", p=128)[:, :, fo:fo + fs],
                  xs[:, :, fo:fo + fs])
        P.dmag(B["wq"], wq_d.rearrange("(b p) c -> p b c", p=128))
        P.v(mset(B["xbpc"][:, :, 0:1], 0.0))
        for ci, (fo, fs) in enumerate(scan_f(TSE, 512)):
            P.dma(B["tabc"][:, :, 0:fs],
                  tab_d[:, :, fo:fo + fs].transpose([1, 0, 2]))
            fm_ln(P, lambda k: xs[:, k, fo:fo + fs], fs, dbuf, sqb,
                  ones128, onesr, ps_row, ps_bc, rows,
                  nrm_view=lambda k: dbuf[:, k, 0:fs])
            for k in range(8):
                P.v(cp(B["xbpc"][:, k, 1:1 + fs], dbuf[:, k, 0:fs]))
            P.dma(xb_d.rearrange("(b p) l -> p b l", p=128)[:, :, fo:fo + fs],
                  dbuf[:, :, 0:fs])
            qkv_chunk(P, B, fo, fs, q_od, k_od, v_od)
        P.finalize(blk)
    return nc


def build_D(last):
    nc = bass.Bass("TRN2", target_bir_lowering=False, debug=False)
    qi_d = nc.dram_tensor("qi", [INNER, TSE], BF16, kind="ExternalInput").ap()
    ki_d = nc.dram_tensor("ki", [INNER, TSE], BF16, kind="ExternalInput").ap()
    vi_d = nc.dram_tensor("vTi", [TSE, INNER], BF16, kind="ExternalInput").ap()
    kt_d = nc.dram_tensor("ktail", [INNER, 768], BF16, kind="ExternalInput").ap()
    vt_d = nc.dram_tensor("vtail", [768, INNER], BF16, kind="ExternalInput").ap()
    mk_d = nc.dram_tensor("masks", [17, 128, STRIP], F32,
                          kind="ExternalInput").ap()
    xe_d = nc.dram_tensor("x1e", [D, TSE], F32, kind="ExternalInput").ap()
    wo_d = nc.dram_tensor("woA", [INNER, D], F32, kind="ExternalInput").ap()
    w1_d = nc.dram_tensor("w1f", [D, NH1], F32, kind="ExternalInput").ap()
    b1_d = nc.dram_tensor("b1p", [NH1], F32, kind="ExternalInput").ap()
    w2_d = nc.dram_tensor("w2f", [2816, D], F32, kind="ExternalInput").ap()
    b2_d = nc.dram_tensor("b2v", [D], F32, kind="ExternalInput").ap()
    lg_d = nc.dram_tensor("lgv", [D], F32, kind="ExternalInput").ap()
    lb_d = nc.dram_tensor("lbv", [D], F32, kind="ExternalInput").ap()
    em_d = nc.dram_tensor("extm", [128, 8, 128], F32, kind="ExternalInput").ap()
    id_d = nc.dram_tensor("ident", [128, 128], BF16, kind="ExternalInput").ap()
    x1o_d = nc.dram_tensor("x1o", [D, TSE], F32, kind="ExternalOutput").ap()
    x1b_d = nc.dram_tensor("x1b_scr", [D, TSE], BF16).ap()
    if not last:
        wq_d = nc.dram_tensor("wqkv", [D, 3 * INNER], F32,
                              kind="ExternalInput").ap()
        qs_d = nc.dram_tensor("qsr", [128, 1], F32, kind="ExternalInput").ap()
        ks_d = nc.dram_tensor("ksr", [128, 1], F32, kind="ExternalInput").ap()
        tab_d = nc.dram_tensor("tabs", [4, 128, TSE], F32,
                               kind="ExternalInput").ap()
        oneh_d = nc.dram_tensor("oneh", [1, 2, 128], F32,
                                kind="ExternalInput").ap()
        rot_d = nc.dram_tensor("rotm", [128, 128], F32,
                               kind="ExternalInput").ap()
        q_od = nc.dram_tensor("qo1", [INNER, TSE], BF16,
                              kind="ExternalOutput").ap()
        k_od = nc.dram_tensor("ko1", [INNER, TSE], BF16,
                              kind="ExternalOutput").ap()
        v_od = nc.dram_tensor("vTo1", [TSE, INNER], BF16,
                              kind="ExternalOutput").ap()
    else:
        x0_d = nc.dram_tensor("x1own", [D, TSH], F32,
                              kind="ExternalInput").ap()
        i32_d = nc.dram_tensor("iden32", [128, 128], F32,
                               kind="ExternalInput").ap()
        out_d = nc.dram_tensor("outp", [TSH, D], F32,
                               kind="ExternalOutput").ap()
        xf_d = nc.dram_tensor("xf_scr", [D, TSE], F32).ap()

    with ExitStack() as st:
        P = Prog(nc, st)
        Consts(P, nc, st)
        xm = st.enter_context(nc.sbuf_tensor("xm", [128, 8, TSE + 1], F32))
        qbt = st.enter_context(nc.sbuf_tensor("qbt", [128, 4, 128], BF16))
        kstr = st.enter_context(nc.sbuf_tensor("kstr", [128, 4, STRIP], BF16))
        vstr = st.enter_context(nc.sbuf_tensor("vstr", [128, 5, 512], BF16))
        maskc = st.enter_context(nc.sbuf_tensor("maskc", [128, STRIP], F32))
        simb = st.enter_context(nc.sbuf_tensor("simb", [128, STRIP], F32))
        pbuf = st.enter_context(nc.sbuf_tensor("pbuf", [128, STRIP], BF16))
        ptb = st.enter_context(nc.sbuf_tensor("ptb", [128, 5, 128], BF16))
        cols = st.enter_context(nc.sbuf_tensor("colsd", [128, 4], F32))
        obf = st.enter_context(nc.sbuf_tensor("obf", [128, 4, 128], BF16))
        rbuf = st.enter_context(nc.sbuf_tensor("rbuf", [128, 8, 512], F32))
        dbuf = st.enter_context(nc.sbuf_tensor("dbuf", [128, 8, 512], F32))
        sqb = st.enter_context(nc.sbuf_tensor("sqb", [128, 8, 512], F32))
        xbm = st.enter_context(nc.sbuf_tensor("xbm", [128, 8, 512], BF16))
        hbf = st.enter_context(nc.sbuf_tensor("hbf", [128, 22, 512], BF16))
        wmg = st.enter_context(nc.sbuf_tensor("wmg", [128, 8, 128], BF16))
        w2g = st.enter_context(nc.sbuf_tensor("w2g", [128, 22, 128], BF16))
        h1b = st.enter_context(nc.sbuf_tensor("h1b", [128, 512], F32))
        gb = st.enter_context(nc.sbuf_tensor("gb", [64, 512], F32))
        b1e = st.enter_context(nc.sbuf_tensor("b1e", [128, 44], F32))
        b1pv = st.enter_context(nc.sbuf_tensor("b1pv", [128, 44], F32))
        lgc = st.enter_context(nc.sbuf_tensor("lgc", [128, 8], F32))
        lbb = st.enter_context(nc.sbuf_tensor("lbb", [128, 8], BF16))
        b2c = st.enter_context(nc.sbuf_tensor("b2c", [128, 8], F32))
        extm = st.enter_context(nc.sbuf_tensor("extm_s", [128, 8, 128], F32))
        rows = st.enter_context(nc.sbuf_tensor("rows", [1, 2, 512], F32))
        ones128 = st.enter_context(nc.sbuf_tensor("ones128", [128, 1], F32))
        onesr = st.enter_context(nc.sbuf_tensor("onesr", [1, 128], F32))
        ident = st.enter_context(nc.sbuf_tensor("ident_s", [128, 128], BF16))
        wob = st.enter_context(nc.sbuf_tensor("wob", [128, 4, D], BF16))
        ps_sim = st.enter_context(nc.psum_tensor("ps_sim", [128, STRIP], F32))
        ps_pt = st.enter_context(nc.psum_tensor("ps_pt", [128, STRIP], BF16))
        ps_o = st.enter_context(nc.psum_tensor("ps_o", [128, 128], F32))
        ps_mm = st.enter_context(nc.psum_tensor("ps_mm", [128, 512], F32))
        ps_bc = st.enter_context(nc.psum_tensor("ps_bc", [128, 512], F32))
        ps_row = st.enter_context(nc.psum_tensor("ps_row", [1, 512], F32))
        ps_t = st.enter_context(nc.psum_tensor("ps_t", [128, 512], BF16))

        if not last:
            qsr = st.enter_context(nc.sbuf_tensor("qsr_s", [128, 1], F32))
            ksr = st.enter_context(nc.sbuf_tensor("ksr_s", [128, 1], F32))
            rotm = st.enter_context(nc.sbuf_tensor("rotm_s", [128, 128], F32))
            oneh = st.enter_context(nc.sbuf_tensor("oneh_s", [1, 2, 128], F32))
            wqm = st.enter_context(nc.sbuf_tensor("wqm", [128, 8, 128], BF16))
            B = alloc_qkv_bufs(nc, st, "qn", qf=dbuf[:, 0:4, :],
                               sqh=sqb[:, 0:4, :], vb=xbm[:, 0:4, :],
                               vtb=xbm[:, 4:8, :], rqr=rows,
                               t123=(rbuf[:, 4, :], rbuf[:, 5, :],
                                     rbuf[:, 6, :]),
                               qo=hbf[:, 0, 0:512], tabc=rbuf[:, 0:4, :])
            B.update(dict(ones128=ones128, onesr=onesr, oneh=oneh,
                          ident=ident, rotm=rotm, ps_mm=ps_mm, ps_bc=ps_bc,
                          ps_row=ps_row, ps_t=ps_t, qsr=qsr[:, 0:1],
                          ksr=ksr[:, 0:1], wq_d=wq_d, wqm=wqm))
        else:
            iden32 = st.enter_context(
                nc.sbuf_tensor("iden32_s", [128, 128], F32))
            otb = st.enter_context(nc.sbuf_tensor("otb", [128, D], F32))
        blk = st.enter_context(nc.Block())

        P.dma(xm[:, :, 1:TSE + 1], xe_d.rearrange("(b p) l -> p b l", p=128))
        P.v(mset(xm[:, :, 0:1], 0.0))
        P.dmag(wob[:, :, :], wo_d.rearrange("(b p) c -> p b c", p=128))
        P.v(mset(ones128[:, :], 1.0))
        P.v(mset(onesr[:, :], 1.0))
        P.dma(ident[:, :], id_d)
        P.dma(extm[:, :, :], em_d)
        vec_cols(P, lgc, lg_d, 8)
        vec_cols(P, lbb, lb_d, 8, cast=True)
        vec_cols(P, b2c, b2_d, 8)
        vec_cols(P, b1pv[:, 0:43], b1_d, 43)
        if not last:
            P.dma(qsr[:, :], qs_d)
            P.dma(ksr[:, :], ks_d)
            P.dma(rotm[:, :], rot_d)
            P.dma(oneh[:, :, :], oneh_d)
        else:
            P.dma(iden32[:, :], i32_d)

        for i in range(17):
            c0 = 128 * i + 128
            P.dma(maskc[:, :], mk_d[i, :, :])
            P.dma(qbt[:, :, :],
                  qi_d.rearrange("(b p) l -> p b l", p=128)
                  [:, :, 128 * i:128 * (i + 1)])
            nt = max(0, min(768 - c0, STRIP))
            if nt > 0:
                P.dma(kstr[:, :, 0:nt],
                      kt_d.rearrange("(b p) l -> p b l", p=128)
                      [:, :, c0:c0 + nt])
            if nt < STRIP:
                o0 = c0 + nt - 768 + 128
                P.dma(kstr[:, :, nt:STRIP],
                      ki_d.rearrange("(b p) l -> p b l", p=128)
                      [:, :, o0:o0 + STRIP - nt])
            for j in range(5):
                t0 = c0 + 128 * j
                if t0 < 768:
                    P.dma(vstr[:, j, :],
                          vt_d[t0:t0 + 128, :].rearrange("(a p) c -> p a c",
                                                         p=128)[:, 0, :])
                else:
                    o0 = t0 - 768 + 128
                    P.dma(vstr[:, j, :],
                          vi_d[o0:o0 + 128, :].rearrange("(a p) c -> p a c",
                                                         p=128)[:, 0, :])
            for h in range(8):
                tl, b = h // 2, 64 * (h % 2)
                P.t(mm(ps_sim[:, 0:512], qbt[b:b + 64, tl, :],
                       kstr[b:b + 64, tl, 0:512], True, True))
                P.t(mm(ps_sim[:, 512:STRIP], qbt[b:b + 64, tl, :],
                       kstr[b:b + 64, tl, 512:STRIP], True, True))
                P.v(tt(simb[:, :], ps_sim[:, :], maskc[:, :], ALU.add))
                P.v(red(cols[:, 0:1], simb[:, :], ALU.max, negate=True))
                P.s(act(pbuf[:, :], simb[:, :], AF.Exp, bias=cols[:, 0:1],
                        accum=cols[:, 1:2]))
                P.v(rcp(cols[:, 2:3], cols[:, 1:2]))
                P.v(ts(pbuf[:, :], pbuf[:, :], cols[:, 2:3], None, ALU.mult))
                for j in range(5):
                    P.t(trans(ps_pt[:, 128 * j:128 * (j + 1)],
                              pbuf[:, 128 * j:128 * (j + 1)], ident[:, :]))
                for j in range(5):
                    P.s(act(ptb[:, j, :], ps_pt[:, 128 * j:128 * (j + 1)],
                            AF.Copy))
                for j in range(5):
                    P.t(mm(ps_o[0:64, :], vstr[:, j, 64 * h:64 * h + 64],
                           ptb[:, j, :], j == 0, j == 4))
                P.s(act(obf[b:b + 64, tl, :], ps_o[0:64, :], AF.Copy))
            for m in range(8):
                for k in range(4):
                    P.t(mm(ps_mm[:, 0:128], wob[:, k, m * 128:(m + 1) * 128],
                           obf[:, k, :], k == 0, k == 3))
                P.v(tt(xm[:, m, 1 + 128 * i:1 + 128 * (i + 1)],
                       ps_mm[:, 0:128],
                       xm[:, m, 1 + 128 * i:1 + 128 * (i + 1)], ALU.add))
        for ci, (fo, fs) in enumerate(scan_f(TSE, 512)):
            def xv(k, fo=fo, fs=fs):
                return (xm[:, k, 1 + fo:1 + fo + fs] if k < 4
                        else xm[:, k, fo:fo + fs])
            fm_ln(P, xv, fs, dbuf, sqb, ones128, onesr, ps_row, ps_bc, rows,
                  nrm_view=None)
            for k in range(8):
                P.v(tt(xbm[:, k, 0:fs], dbuf[:, k, 0:fs], ps_bc[:, 0:fs],
                       ALU.mult))
            P.v(mset(hbf[64:128, 21, 0:fs], 0.0))
            for mg in range(43):
                P.dmag(wmg[:, :, :],
                       w1_d.rearrange("(b p) c -> p b c", p=128)
                       [:, :, mg * 128:(mg + 1) * 128])
                if ci == 0:
                    for k in range(8):
                        P.t(mm(ps_o[:, 0:1], wmg[:, k, :], lbb[:, k:k + 1],
                               k == 0, k == 7))
                    P.s(act(b1e[:, mg:mg + 1], ps_o[:, 0:1], AF.Copy))
                    P.v(tt(b1e[:, mg:mg + 1], b1e[:, mg:mg + 1],
                           b1pv[:, mg:mg + 1], ALU.add))
                for k in range(8):
                    P.v(ts(wmg[:, k, :], wmg[:, k, :], lgc[:, k:k + 1], None,
                           ALU.mult))
                for k in range(8):
                    P.t(mm(ps_mm[:, 0:fs], wmg[:, k, :], xbm[:, k, 0:fs],
                           k == 0, k == 7))
                P.s(act(h1b[:, 0:fs], ps_mm[:, 0:fs], AF.Identity,
                        bias=b1e[:, mg:mg + 1]))
                P.s(act(gb[:, 0:fs], h1b[64:128, 0:fs], AF.Gelu))
                P.v(tt(hbf[64 * (mg % 2):64 * (mg % 2) + 64, mg // 2, 0:fs],
                       h1b[0:64, 0:fs], gb[:, 0:fs], ALU.mult))
            for m in range(8):
                P.dmag(w2g[:, :, :],
                       w2_d.rearrange("(b p) c -> p b c", p=128)
                       [:, :, m * 128:(m + 1) * 128])
                for k in range(22):
                    P.t(mm(ps_mm[:, 0:fs], w2g[:, k, :], hbf[:, k, 0:fs],
                           k == 0, k == 21))
                P.s(act(rbuf[:, m, 0:fs], ps_mm[:, 0:fs], AF.Identity,
                        bias=b2c[:, m:m + 1]))
                P.v(tt(rbuf[:, m, 0:fs], rbuf[:, m, 0:fs],
                       xm[:, m, 1 + fo:1 + fo + fs], ALU.add))
            if fo < EXT:
                for m in range(8):
                    P.v(tt(rbuf[:, m, 0:EXT], rbuf[:, m, 0:EXT],
                           extm[:, m, :], ALU.mult))
            P.dma(x1o_d.rearrange("(b p) l -> p b l", p=128)[:, :, fo:fo + fs],
                  rbuf[:, :, 0:fs])
            if not last:
                P.dmag(x1b_d.rearrange("(b p) l -> p b l", p=128)
                       [:, :, fo:fo + fs], rbuf[:, :, 0:fs])
            else:
                P.dma(xf_d.rearrange("(b p) l -> p b l", p=128)
                      [:, :, fo:fo + fs], rbuf[:, :, 0:fs])
        if not last:
            for ci, (fo, fs) in enumerate(scan_f(TSE, 512)):
                P.dma(B["tabc"][:, :, 0:fs],
                      tab_d[:, :, fo:fo + fs].transpose([1, 0, 2]))
                if fo == 0:
                    P.dma(B["xbpc"][:, :, 1:1 + fs],
                          x1b_d.rearrange("(b p) l -> p b l", p=128)
                          [:, :, 0:fs])
                    P.v(mset(B["xbpc"][:, :, 0:1], 0.0))
                else:
                    P.dma(B["xbpc"][:, :, 0:1 + fs],
                          x1b_d.rearrange("(b p) l -> p b l", p=128)
                          [:, :, fo - 1:fo + fs])
                qkv_chunk(P, B, fo, fs, q_od, k_od, v_od)
        else:
            for fo, fs in scan_f(TSH, 512):
                P.dma(rbuf[:, :, 0:fs],
                      xf_d.rearrange("(b p) l -> p b l", p=128)
                      [:, :, EXT + fo:EXT + fo + fs])
                fm_ln(P, lambda k, fs=fs: rbuf[:, k, 0:fs], fs, dbuf, sqb,
                      ones128, onesr, ps_row, ps_bc, rows,
                      nrm_view=lambda k, fs=fs: dbuf[:, k, 0:fs])
                P.dma(sqb[:, :, 0:fs],
                      x0_d.rearrange("(b p) l -> p b l", p=128)
                      [:, :, fo:fo + fs])
                for m in range(8):
                    P.v(tt(dbuf[:, m, 0:fs], dbuf[:, m, 0:fs],
                           sqb[:, m, 0:fs], ALU.add))
                for j in range(fs // 128):
                    for m in range(8):
                        P.t(mm(ps_mm[:, 0:128], iden32[:, :],
                               dbuf[:, m, 128 * j:128 * (j + 1)],
                               True, True) if False else
                            trans(ps_mm[:, 0:128],
                                  dbuf[:, m, 128 * j:128 * (j + 1)],
                                  iden32[:, :]))
                        P.s(act(otb[:, m * 128:(m + 1) * 128],
                                ps_mm[:, 0:128], AF.Copy))
                    P.dma(out_d[fo + 128 * j:fo + 128 * (j + 1), :],
                          otb[:, :])
        P.finalize(blk)
    return nc


def rotary_tables():
    half = np.arange(0, DH, 2, dtype=np.float64)
    inv_freq = 1.0 / 10000.0 ** (half / DH)
    t = np.arange(T, dtype=np.float64)
    freqs = t[:, None] * inv_freq[None, :]
    freqs = np.concatenate([freqs, freqs], -1)
    base = (half + 0.4 * DH) / (1.4 * DH)
    power = (t - T // 2) / (WIN // 2)
    scale = base[None, :] ** power[:, None]
    scale = np.concatenate([scale, scale], -1)
    return np.cos(freqs), np.sin(freqs), scale


def tabs_for(tok0):
    cos, sin, scale = rotary_tables()
    idx = np.clip(np.arange(tok0, tok0 + TSE), 0, T - 1)
    sgn = np.where(np.arange(64) % 64 < 32, -1.0, 1.0)

    def dbl(tab):
        t_ = tab[idx].T.astype(np.float32)
        return np.concatenate([t_, t_], 0)
    csq = dbl(cos * scale) * QKS
    snq = dbl(sin * scale * sgn[None, :]) * QKS
    csk = dbl(cos / scale)
    snk = dbl(sin / scale * sgn[None, :])
    return np.ascontiguousarray(np.stack([csq, snq, csk, snk]))


def rotm_host():
    r = np.zeros((128, 128), np.float32)
    for d_ in range(128):
        rr = (d_ % 64 + 32) % 64 + 64 * (d_ // 64)
        r[rr, d_] = 1.0
    return r


def oneh_host():
    o = np.zeros((1, 2, 128), np.float32)
    o[0, 0, 0:64] = 1.0
    o[0, 1, 64:128] = 1.0
    return o


def masks_for(core):
    t0 = core * TSH - EXT
    out = np.empty((17, 128, STRIP), np.float32)
    for i in range(17):
        q0 = t0 + 128 * i
        q = q0 + np.arange(128)[:, None]
        k = q0 - WIN + np.arange(STRIP)[None, :]
        bad = (k > q) | (k < q - WIN) | (k < 0)
        m = np.where(bad, NEG, np.float32(0.0)).astype(np.float32)
        dead = (q[:, 0] < 0)
        m[dead, :] = NEG
        m[dead, 0] = 0.0
        out[i] = m
    return np.ascontiguousarray(out)


def permute_w1(w1, b1):
    a, g = w1[:, :HID], w1[:, HID:]
    ab, gb = b1[:HID], b1[HID:]
    ap_ = np.concatenate([a, np.zeros((D, HIDP - HID), w1.dtype)], 1)
    gp = np.concatenate([g, np.zeros((D, HIDP - HID), w1.dtype)], 1)
    abp = np.concatenate([ab, np.zeros(HIDP - HID, b1.dtype)])
    gbp = np.concatenate([gb, np.zeros(HIDP - HID, b1.dtype)])
    cols, bcols = [], []
    for j in range(HIDP // 64):
        cols += [ap_[:, j * 64:(j + 1) * 64], gp[:, j * 64:(j + 1) * 64]]
        bcols += [abp[j * 64:(j + 1) * 64], gbp[j * 64:(j + 1) * 64]]
    return (np.ascontiguousarray(np.concatenate(cols, 1)),
            np.ascontiguousarray(np.concatenate(bcols)))


def permute_w2(w2):
    return np.ascontiguousarray(
        np.concatenate([w2, np.zeros((2816 - HID, D), w2.dtype)], 0))


def _cc(a):
    return np.ascontiguousarray(a)


def kernel_unfused(**inputs):
    f32 = np.float32
    x = np.asarray(inputs["L0_attn_x"], f32)[0]
    w_in = np.asarray(inputs["L0_attn_w_in"], f32)
    ident = np.eye(128).astype(NPBF)
    iden32 = np.eye(128, dtype=f32)
    oneh = oneh_host()
    rotm = rotm_host()

    nc = build_A()
    maps = [dict(xa=_cc(x[c * TSH:(c + 1) * TSH]), ident=ident)
            for c in range(NC)]
    res = run_bass_kernel_spmd(nc, maps, core_ids=list(range(NC)))
    xn_full = np.concatenate([np.asarray(r["xn"]) for r in res.results], 1)

    nc = build_B0()
    maps = []
    for c in range(NC):
        sl = slice(c * ES, (c + 1) * ES)
        w_sl = np.concatenate([w_in[sl], w_in[E + c * ES:E + (c + 1) * ES]], 0)
        maps.append(dict(
            xnf=_cc(xn_full),
            w_in_slT=_cc(w_sl.T),
            lnw=np.asarray(inputs["L0_attn_ln_w"], f32),
            lnb=np.asarray(inputs["L0_attn_ln_b"], f32),
            cw=_cc(np.asarray(inputs["L0_attn_conv_w"], f32)[sl]),
            cb=_cc(np.asarray(inputs["L0_attn_conv_b"], f32)[sl]),
            wxpT=_cc(np.asarray(inputs["L0_attn_w_xproj"], f32)[:, sl].T),
        ))
    res = run_bass_kernel_spmd(nc, maps, core_ids=list(range(NC)))
    u32 = [np.asarray(r["u32"]) for r in res.results]
    z32 = [np.asarray(r["z32"]) for r in res.results]
    xdblp = np.stack([np.asarray(r["xdblp"]) for r in res.results])

    nc = build_B1()
    maps = []
    for c in range(NC):
        sl = slice(c * ES, (c + 1) * ES)
        maps.append(dict(
            u32=u32[c], z32=z32[c], xdblp=xdblp,
            wdtT=_cc(np.asarray(inputs["L0_attn_w_dt"], f32)[sl].T),
            bdt=_cc(np.asarray(inputs["L0_attn_b_dt"], f32)[sl]),
            alog=_cc(np.asarray(inputs["L0_attn_a_log"], f32)[sl]),
            dskip=_cc(np.asarray(inputs["L0_attn_d_skip"], f32)[sl]),
        ))
    res = run_bass_kernel_spmd(nc, maps, core_ids=list(range(NC)))
    yg_full = np.concatenate([np.asarray(r["yg"]) for r in res.results], 0)

    def ext_sliceT(full_T_first):
        outs = []
        for c in range(NC):
            t0 = c * TSH - EXT
            o = np.zeros((TSE,) + full_T_first.shape[1:], full_T_first.dtype)
            lo = max(t0, 0)
            o[lo - t0:] = full_T_first[lo:t0 + TSE]
            outs.append(_cc(o.T))
        return outs

    qs0 = np.asarray(inputs["L0_ffn_q_scale"], f32)
    ks0 = np.asarray(inputs["L0_ffn_k_scale"], f32)
    xe_s = ext_sliceT(x)
    yge_s = ext_sliceT(yg_full.T)
    nc = build_C()
    maps = []
    for c in range(NC):
        t0 = c * TSH - EXT
        maps.append(dict(
            xe=xe_s[c],
            yge=_cc(yge_s[c].astype(NPBF)),
            woutT=_cc(np.asarray(inputs["L0_attn_w_out"], f32).T),
            wqkv=_cc(np.asarray(inputs["L0_ffn_wqkv"], f32)[0]),
            qsr=_cc(np.tile(qs0[0], 2)[:, None]),
            ksr=_cc(np.tile(ks0[0], 2)[:, None]),
            tabs=tabs_for(t0), oneh=oneh, rotm=rotm, ident=ident,
        ))
    res = run_bass_kernel_spmd(nc, maps, core_ids=list(range(NC)))
    x1e = [np.asarray(r["x1e"]) for r in res.results]
    xbe = [np.asarray(r["xbe"]) for r in res.results]
    q0 = [np.asarray(r["q0"]) for r in res.results]
    k0 = [np.asarray(r["k0"]) for r in res.results]
    v0 = [np.asarray(r["vT0"]) for r in res.results]

    def run_layer(last, stream, qs_, ks_, vs_, layer, x1own=None):
        nc = build_D(last)
        w1f, b1p = permute_w1(np.asarray(inputs["L0_ffn_w1"], f32)[layer],
                              np.asarray(inputs["L0_ffn_b1"], f32)[layer])
        w2f = permute_w2(np.asarray(inputs["L0_ffn_w2"], f32)[layer])
        maps = []
        for c in range(NC):
            t0 = c * TSH - EXT
            nb = c - 1
            if nb >= 0:
                ktail = _cc(qs_[nb][0:0] if False else
                            ks_[nb][:, 1408:2176])
                vtail = _cc(vs_[nb][1408:2176, :])
            else:
                ktail = np.zeros((INNER, 768), NPBF)
                vtail = np.zeros((768, INNER), NPBF)
            m = dict(
                qi=qs_[c], ki=ks_[c], vTi=vs_[c],
                ktail=ktail, vtail=vtail,
                masks=masks_for(c),
                x1e=stream[c],
                woA=_cc(np.asarray(inputs["L0_ffn_wo"], f32)[layer]),
                w1f=w1f, b1p=b1p, w2f=w2f,
                b2v=_cc(np.asarray(inputs["L0_ffn_b2"], f32)[layer]),
                lgv=_cc(np.asarray(inputs["L0_ffn_ln_g"], f32)[layer]),
                lbv=_cc(np.asarray(inputs["L0_ffn_ln_b"], f32)[layer]),
                extm=np.full((128, 8, 128), 0.0 if c == 0 else 1.0, f32),
                ident=ident,
            )
            if not last:
                m.update(dict(
                    wqkv=_cc(np.asarray(inputs["L0_ffn_wqkv"], f32)[1]),
                    qsr=_cc(np.tile(qs0[1], 2)[:, None]),
                    ksr=_cc(np.tile(ks0[1], 2)[:, None]),
                    tabs=tabs_for(t0), oneh=oneh, rotm=rotm,
                ))
            else:
                m.update(dict(x1own=_cc(x1own[c][:, EXT:]), iden32=iden32))
            maps.append(m)
        return run_bass_kernel_spmd(nc, maps, core_ids=list(range(NC)))

    res = run_layer(False, xbe, q0, k0, v0, 0)
    x_1 = [np.asarray(r["x1o"]) for r in res.results]
    q1 = [np.asarray(r["qo1"]) for r in res.results]
    k1 = [np.asarray(r["ko1"]) for r in res.results]
    v1 = [np.asarray(r["vTo1"]) for r in res.results]

    res = run_layer(True, x_1, q1, k1, v1, 1, x1own=x1e)
    out = np.concatenate([np.asarray(r["outp"]) for r in res.results], 0)
    return out[None].astype(np.float32)


class Cursor:
    def __init__(self, tensor, limit):
        self.t = tensor
        self.limit = limit
        self.off = 0

    def take(self, *shape):
        n = int(np.prod(shape))
        off = self.off
        self.off += n
        assert self.off <= self.limit, f"arena overflow {self.off}>{self.limit}"
        v = self.t[:, off:off + n]
        if len(shape) == 1:
            return v
        return v.rearrange("p (a b) -> p a b", a=shape[0])

    def reset(self):
        self.off = 0


def build_FUSED():
    nc = bass.Bass("TRN2", target_bir_lowering=False, debug=False)
    nc.cache_partition_id()
    f32 = F32

    def ein(name, shape, dt=F32):
        return nc.dram_tensor(name, shape, dt, kind="ExternalInput").ap()

    xa_d = ein("xa", [TSH, D])
    xe_d = ein("xe", [D, TSE])
    w_d = ein("w_in_slT", [D, 2 * ES])
    lnw_d = ein("lnw", [D])
    lnb_d = ein("lnb", [D])
    cw_d = ein("cw", [ES, 4])
    cb_d = ein("cb", [ES])
    wxp_d = ein("wxpT", [ES, 96])
    wdt_d = ein("wdtT", [DTR, ES])
    bdt_d = ein("bdt", [ES])
    alog_d = ein("alog", [ES, NST])
    dsk_d = ein("dskip", [ES])
    wo_d = ein("woutT", [E, D])
    wq0_d = ein("wqkv0", [D, 3 * INNER])
    wq1_d = ein("wqkv1", [D, 3 * INNER])
    qs0_d = ein("qsr0", [128, 1])
    ks0_d = ein("ksr0", [128, 1])
    qs1_d = ein("qsr1", [128, 1])
    ks1_d = ein("ksr1", [128, 1])
    tab_d = ein("tabs", [4, 128, TSE])
    oneh_d = ein("oneh", [1, 2, 128])
    rot_d = ein("rotm", [128, 128])
    id_d = ein("ident", [128, 128], BF16)
    i32_d = ein("iden32", [128, 128])
    mk_d = ein("masks", [17, 128, STRIP])
    em_d = ein("extm", [128, 8, 128])
    idx_d = nc.dram_tensor("idxt", [128, IXN], mybir.dt.uint32,
                           kind="ExternalInput").ap()
    woA0_d = ein("woA0", [INNER, D])
    woA1_d = ein("woA1", [INNER, D])
    w1f0_d = ein("w1f0", [D, NH1])
    w1f1_d = ein("w1f1", [D, NH1])
    b1p0_d = ein("b1p0", [NH1])
    b1p1_d = ein("b1p1", [NH1])
    w2f0_d = ein("w2f0", [2816, D])
    w2f1_d = ein("w2f1", [2816, D])
    b2v0_d = ein("b2v0", [D])
    b2v1_d = ein("b2v1", [D])
    lgv0_d = ein("lgv0", [D])
    lgv1_d = ein("lgv1", [D])
    lbv0_d = ein("lbv0", [D])
    lbv1_d = ein("lbv1", [D])
    out_d = nc.dram_tensor("outp", [TSH, D], F32, kind="ExternalOutput").ap()

    xn_ai = nc.dram_tensor("xn_ai", [D, T], BF16).ap()
    xn_ao = nc.dram_tensor("xn_ao", [D, T], BF16).ap()
    xd_ai = nc.dram_tensor("xd_ai", [96, T], F32).ap()
    xd_ao = nc.dram_tensor("xd_ao", [96, T], F32).ap()
    y_ai = nc.dram_tensor("y_ai", [E, YW], BF16).ap()
    y_ao = nc.dram_tensor("y_ao", [E, YW], BF16).ap()
    kg_ai = [nc.dram_tensor(f"kg_ai{l}", [INNER, 9 * 768], BF16).ap()
             for l in range(2)]
    kg_ao = [nc.dram_tensor(f"kg_ao{l}", [INNER, 9 * 768], BF16).ap()
             for l in range(2)]
    vg_ai = [nc.dram_tensor(f"vg_ai{l}", [9 * 768, INNER], BF16).ap()
             for l in range(2)]
    vg_ao = [nc.dram_tensor(f"vg_ao{l}", [9 * 768, INNER], BF16).ap()
             for l in range(2)]
    kt_loc = [nc.dram_tensor(f"kt_loc{l}", [INNER, 768], BF16).ap()
              for l in range(2)]
    vt_loc = [nc.dram_tensor(f"vt_loc{l}", [768, INNER], BF16).ap()
              for l in range(2)]
    u_d = nc.dram_tensor("u32s", [ES, T], F32).ap()
    z_d = nc.dram_tensor("z32s", [ES, T], F32).ap()
    bc_d = nc.dram_tensor("bc_scr", [32, T], F32).ap()
    yg_d = None
    x1_d = nc.dram_tensor("x1s", [D, TSE], F32).ap()
    xb_d = nc.dram_tensor("xbs", [D, TSE], F32).ap()
    q_ds = [nc.dram_tensor(f"q{l}s", [INNER, TSE], BF16).ap() for l in range(2)]
    k_ds = [nc.dram_tensor(f"k{l}s", [INNER, TSE], BF16).ap() for l in range(2)]
    v_ds = [nc.dram_tensor(f"v{l}s", [TSE, INNER], BF16).ap() for l in range(2)]
    x1o_d = nc.dram_tensor("x1os", [D, TSE], F32).ap()
    x1b_d = nc.dram_tensor("x1bs", [D, TSE], BF16).ap()
    xf_d = nc.dram_tensor("xfs", [D, TSE], F32).ap()

    AF_EL = 34000
    AB_EL = 34000
    with ExitStack() as st:
        P = Prog(nc, st)
        Consts(P, nc, st)
        arf_t = st.enter_context(nc.sbuf_tensor("arena_f", [128, AF_EL], F32))
        arb_t = st.enter_context(nc.sbuf_tensor("arena_b", [128, AB_EL], BF16))
        ident = st.enter_context(nc.sbuf_tensor("ident_s", [128, 128], BF16))
        iden32 = st.enter_context(nc.sbuf_tensor("iden32_s", [128, 128], F32))
        rotm = st.enter_context(nc.sbuf_tensor("rotm_s", [128, 128], F32))
        oneh = st.enter_context(nc.sbuf_tensor("oneh_s", [1, 2, 128], F32))
        ones128 = st.enter_context(nc.sbuf_tensor("ones128", [128, 1], F32))
        onesr = st.enter_context(nc.sbuf_tensor("onesr", [1, 128], F32))
        colsml = st.enter_context(nc.sbuf_tensor("colsml", [128, 16], F32))
        carry = st.enter_context(nc.sbuf_tensor("carry", [128, 2, NST], F32))
        acol = st.enter_context(nc.sbuf_tensor("acol", [128, 2, NST], F32))
        smlA = st.enter_context(nc.sbuf_tensor("smlA", [128, 2, 8], F32))
        smlB = st.enter_context(nc.sbuf_tensor("smlB", [128, 2, 4], F32))
        lnbb = st.enter_context(nc.sbuf_tensor("lnbb", [128, 8], BF16))
        qsr = st.enter_context(nc.sbuf_tensor("qsr_s", [128, 2], F32))
        ksr = st.enter_context(nc.sbuf_tensor("ksr_s", [128, 2], F32))
        lgc = st.enter_context(nc.sbuf_tensor("lgc", [128, 8], F32))
        lbb = st.enter_context(nc.sbuf_tensor("lbb", [128, 8], BF16))
        b2c = st.enter_context(nc.sbuf_tensor("b2c", [128, 8], F32))
        b1e = st.enter_context(nc.sbuf_tensor("b1e", [128, 44], F32))
        b1pv = st.enter_context(nc.sbuf_tensor("b1pv", [128, 44], F32))
        ixs = st.enter_context(nc.sbuf_tensor("ixs", [128, IXN],
                                              mybir.dt.uint32))
        rows0 = st.enter_context(nc.sbuf_tensor("rows0", [1, 1024], F32))
        psA = st.enter_context(nc.psum_tensor("psA", [128, 3584], F32))
        ps_bf = st.enter_context(nc.psum_tensor("ps_bf", [128, 640], BF16))
        blk = st.enter_context(nc.Block())

        arf = Cursor(arf_t, AF_EL)
        arb = Cursor(arb_t, AB_EL)
        ps_mm = psA[:, 2048:2560]
        ps_bc = psA[:, 2560:3072]
        ps_row = psA[0:1, 1024:1536]
        ps_o = psA[:, 3072:3200]
        ps_sim = psA[:, 0:640]
        ps_t = ps_bf[:, 0:512]
        ps_pt = ps_bf[:, 0:640]

        def cc(ins_ap, outs_ap):
            def b(e):
                return e.collective_compute(
                    "AllReduce", ALU.add, replica_groups=[list(range(NC))],
                    ins=[ins_ap.opt()], outs=[outs_ap.opt()])
            P.op("gpsimd", b, tag="coll")

        P.dma(ident[:, :], id_d)
        P.dma(ixs[:, :], idx_d)
        P.dma(iden32[:, :], i32_d)
        P.dma(rotm[:, :], rot_d)
        P.dma(oneh[:, :, :], oneh_d)
        P.v(mset(ones128[:, :], 1.0))
        P.v(mset(onesr[:, :], 1.0))
        zrow = arb.take(16384)
        P.v(mset(zrow[:, :], 0.0))
        rows = rows0.ap().rearrange("p (a b) -> p a b", a=2)
        P.dma(qsr[:, 0:1], qs0_d)
        P.dma(qsr[:, 1:2], qs1_d)
        P.dma(ksr[:, 0:1], ks0_d)
        P.dma(ksr[:, 1:2], ks1_d)
        zv = lambda a_, b_: zrow[:, 0:a_ * b_].rearrange(
            "p (a b) -> p a b", a=a_)
        for j in range(0, T, 2048):
            P.dma(xn_ai.rearrange("(b p) l -> p b l", p=128)[:, :, j:j + 2048],
                  zv(8, 2048))
        for j in range(0, YW, 1024):
            P.dma(y_ai.rearrange("(b p) l -> p b l", p=128)[:, :, j:j + 1024],
                  zv(16, 1024))
        for l in range(2):
            for j in range(0, 9 * 768, 1728):
                P.dma(kg_ai[l].rearrange("(b p) l -> p b l", p=128)
                      [:, :, j:j + 1728], zv(4, 1728))
            for sl in range(0, 9, 3):
                P.dma(vg_ai[l][sl * 768:(sl + 3) * 768, :]
                      .rearrange("(a p) c -> p a c", p=128), zv(18, 512))
        arb.reset()
        x_s = arf.take(16, D)
        sqs = arf.take(D)
        xnb = arb.take(16, D)
        xna = arb.take(8, 2048)
        P.dma(x_s[:, :, :], xa_d.rearrange("(n p) d -> p n d", p=128))
        for i in range(16):
            xt = x_s[:, i, :]
            for rep in range(2):
                P.v(red(colsml[:, 0:1], xt, ALU.add))
                P.v(ts(colsml[:, 0:1], colsml[:, 0:1], 1.0 / D, None,
                       ALU.mult))
                P.v(ts(xt, xt, colsml[:, 0:1], None, ALU.subtract))
                P.s(act(sqs[:, :], xt, AF.Square, accum=colsml[:, 2:3]))
                P.s(act(colsml[:, 2:3], colsml[:, 2:3], AF.Ln, bias=EPS,
                        scale=1.0 / D))
                P.s(act(colsml[:, 2:3], colsml[:, 2:3], AF.Exp, scale=-0.5))
                if rep == 0:
                    P.v(ts(xt, xt, colsml[:, 2:3], None, ALU.mult))
                else:
                    P.v(ts(xnb[:, i, :], xt, colsml[:, 2:3], None, ALU.mult))
        for i in range(16):
            for dbq in range(2):
                for j in range(4):
                    db = dbq * 4 + j
                    P.t(trans(ps_t[:, j * 128:(j + 1) * 128],
                              xnb[:, i, db * 128:(db + 1) * 128], ident[:, :]))
                for j in range(4):
                    db = dbq * 4 + j
                    P.s(act(xna[:, db, 128 * i:128 * (i + 1)],
                            ps_t[:, j * 128:(j + 1) * 128], AF.Copy))
        xnai_f = xn_ai.rearrange("d (j l) -> (d j) l", l=TSH)
        for b in range(8):
            P.ind(xnai_f, ixs[:, IXA + b:IXA + b + 1], xna[:, b, :], None)
        cc(xn_ai, xn_ao)
        arf.reset()
        arb.reset()
        wt = arb.take(8, 512)
        xnc = arb.take(8, 2048)
        ub = arb.take(2, 2048)
        wxt = arb.take(2, 96)
        xs3 = arf.take(2, 2048 + 3)
        zb = arf.take(2, 2048)
        uc = arf.take(2, 2048)
        lnw_s = smlA[:, :, :].rearrange("p a b -> p (a b)")
        bias_s = smlB[:, 0, :]
        cw_s = arf.take(2, 4)
        cb_s = colsml[:, 8:10]
        P.dmag(wt[:, :, :], w_d.rearrange("(b p) r -> p b r", p=128))
        vec_cols(P, lnw_s, lnw_d, 8)
        vec_cols(P, lnbb, lnb_d, 8, cast=True)
        P.dma(cw_s[:, :, :], cw_d.rearrange("(b p) k -> p b k", p=128))
        vec_cols(P, cb_s, cb_d, 2)
        P.dmag(wxt[:, :, :], wxp_d.rearrange("(b p) r -> p b r", p=128))
        for mg in range(4):
            for k in range(8):
                P.t(mm(psA[0:128, 3200 + mg:3200 + mg + 1],
                       wt[:, k, mg * 128:(mg + 1) * 128],
                       lnbb[:, k:k + 1], k == 0, k == 7))
        P.s(act(bias_s[:, 0:4], psA[0:128, 3200:3204], AF.Copy))
        for db in range(8):
            P.v(ts(wt[:, db, :], wt[:, db, :], lnw_s[:, db:db + 1], None,
                   ALU.mult))
        P.v(mset(xs3[:, :, 0:3], 0.0))
        LC0 = 2048
        for c in range(T // LC0):
            P.dma(xnc[:, :, :],
                  xn_ao.rearrange("(b p) l -> p b l", p=128)
                  [:, :, c * LC0:(c + 1) * LC0])
            for m in range(4):
                for fo, fs in scan_f(LC0, 512):
                    for k in range(8):
                        P.t(mm(ps_mm[:, 0:fs], wt[:, k, m * 128:(m + 1) * 128],
                               xnc[:, k, fo:fo + fs], k == 0, k == 7))
                    if m < 2:
                        P.s(act(xs3[:, m, 3 + fo:3 + fo + fs], ps_mm[:, 0:fs],
                                AF.Identity, bias=bias_s[:, m:m + 1]))
                    else:
                        P.s(act(zb[:, m - 2, fo:fo + fs], ps_mm[:, 0:fs],
                                AF.Identity, bias=bias_s[:, m:m + 1]))
            for tb in range(2):
                P.v(ts(uc[:, tb, :], xs3[:, tb, 0:LC0], cw_s[:, tb, 0:1],
                       cb_s[:, tb:tb + 1], ALU.mult, ALU.add))
                for k in range(1, 4):
                    P.v(stt(uc[:, tb, :], xs3[:, tb, k:k + LC0],
                            cw_s[:, tb, k:k + 1], uc[:, tb, :],
                            ALU.mult, ALU.add))
                P.v(cp(xs3[:, tb, 0:3], xs3[:, tb, LC0:LC0 + 3]))
                P.s(act(uc[:, tb, :], uc[:, tb, :], AF.Silu))
                P.v(cp(ub[:, tb, :], uc[:, tb, :]))
            P.dma(u_d.rearrange("(b p) l -> p b l", p=128)
                  [:, :, c * LC0:(c + 1) * LC0], uc[:, :, :])
            P.dma(z_d.rearrange("(b p) l -> p b l", p=128)
                  [:, :, c * LC0:(c + 1) * LC0], zb[:, :, :])
            for fo, fs in scan_f(LC0, 512):
                for k in range(2):
                    P.t(mm(ps_mm[0:96, 0:fs], wxt[:, k, :],
                           ub[:, k, fo:fo + fs], k == 0, k == 1))
                P.s(act(zb[0:96, 0, fo:fo + fs], ps_mm[0:96, 0:fs], AF.Copy))
            P.dma(xd_ai[:, c * LC0:(c + 1) * LC0], zb[0:96, 0, 0:LC0])
        cc(xd_ai, xd_ao)
        arf.reset()
        arb.reset()

        wdtt0 = arb.take(2, 128)
        dtrb = arb.take(1, 1024)
        ygb = arb.take(1, 1024)
        xds = arf.take(1, 1024)
        dtpre = arf.take(2, 1024)
        spt = arf.take(1, 1024)
        dtb = arf.take(2, 1024)
        uch = arf.take(2, 1024)
        zch = arf.take(2, 1024)
        dtu = arf.take(2, 1024)
        bcf = arf.take(2 * 1024)
        dA = arf.take(2 * 1024)
        dBu = arf.take(2 * 1024)
        hb = arf.take(2 * 1024)
        yb4 = arf.take(8, 1024)
        yb = arf.take(1, 1024)
        fixc = colsml[:, 10:12]
        bdt_s = smlB[:, 1, 0:2]
        dsk_s = smlB[:, 1, 2:4]
        P.dmag(wdtt0.rearrange("p a b -> p (a b)")[0:64, :]
               .rearrange("r (b pp) -> r b pp", pp=128),
               wdt_d.rearrange("r (b p) -> r b p", p=128))
        wdtt = wdtt0.rearrange("p a b -> p (a b)")[0:64, :] \
            .rearrange("r (b pp) -> r b pp", pp=128)
        vec_cols(P, bdt_s, bdt_d, 2)
        P.dma(acol[:, :, :], alog_d.rearrange("(b p) n -> p b n", p=128))
        vec_cols(P, dsk_s, dsk_d, 2)
        P.v(mset(carry[:, :, :], 0.0))
        for tb in range(2):
            P.s(act(acol[:, tb, :], acol[:, tb, :], AF.Exp))
            P.v(ts(acol[:, tb, :], acol[:, tb, :], -1.0, None, ALU.mult))
        NG = 2
        LC1 = 1024
        for c in range(T // LC1):
            lo = c * LC1
            P.dma(xds[0:96, 0, :], xd_ao[:, lo:lo + LC1])
            P.dma(bc_d[:, lo:lo + LC1],
                  xds.rearrange("p a b -> p (a b)")[64:96, 0:LC1])
            P.v(cp(dtrb[:, 0, :][0:64, :],
                   xds.rearrange("p a b -> p (a b)")[0:64, 0:LC1]))
            for mg in range(2):
                for fo, fs in scan_f(LC1, 512):
                    P.t(mm(ps_mm[0:128, 0:fs], wdtt[:, mg, :],
                           dtrb[:, 0, :][0:64, fo:fo + fs], True, True))
                    P.s(act(dtpre[:, mg, fo:fo + fs], ps_mm[0:128, 0:fs],
                            AF.Identity, bias=bdt_s[:, mg:mg + 1]))
            for tb in range(2):
                P.s(act(spt[:, 0, :], dtpre[:, tb, :], AF.Abs))
                P.s(act(spt[:, 0, :], spt[:, 0, :], AF.Exp, scale=-1.0))
                P.s(act(spt[:, 0, :], spt[:, 0, :], AF.Ln, bias=1.0))
                P.s(act(dtb[:, tb, :], dtpre[:, tb, :], AF.Relu))
                P.v(tt(dtb[:, tb, :], dtb[:, tb, :], spt[:, 0, :], ALU.add))
            P.dma(uch[:, :, :], u_d.rearrange("(b p) l -> p b l", p=128)
                  [:, :, lo:lo + LC1])
            P.dma(zch[:, :, :], z_d.rearrange("(b p) l -> p b l", p=128)
                  [:, :, lo:lo + LC1])
            for tb in range(2):
                P.v(tt(dtu[:, tb, :], dtb[:, tb, :], uch[:, tb, :], ALU.mult))
            for tb in range(2):
                for g in range(NST // NG):
                    P.dma(bcf.rearrange("p (n l) -> p n l", n=NG)[0:1, :, :],
                          bc_d[g * NG:(g + 1) * NG, lo:lo + LC1][None, :, :])
                    for fo, fs in scan_f(NG * LC1, 512):
                        P.t(mm(psA[:, fo:fo + fs], onesr[0:1, :],
                               bcf[0:1, fo:fo + fs], True, True))
                    for n in range(NG):
                        P.s(act(dA[:, n * LC1:(n + 1) * LC1], dtb[:, tb, :],
                                AF.Exp,
                                scale=acol[:, tb, g * NG + n:g * NG + n + 1]))
                    P.v(tt(dBu.rearrange("p (n l) -> p n l", n=NG),
                           dtu[:, tb, None, :].broadcast_to([128, NG, LC1]),
                           psA[:, 0:NG * LC1].rearrange("p (n l) -> p n l",
                                                        n=NG),
                           ALU.mult))
                    P.v(tt(fixc[:, 0:NG],
                           dA.rearrange("p (n l) -> p n l", n=NG)[:, :, 0],
                           carry[:, tb, g * NG:(g + 1) * NG], ALU.mult))
                    P.v(tt(dBu.rearrange("p (n l) -> p n l", n=NG)[:, :, 0],
                           dBu.rearrange("p (n l) -> p n l", n=NG)[:, :, 0],
                           fixc[:, 0:NG], ALU.add))
                    P.v(mset(dA.rearrange("p (n l) -> p n l",
                                          n=NG)[:, :, 0:1], 0.0))
                    P.v(tts(hb[:, :], dA[:, :], dBu[:, :], 0.0))
                    P.v(cp(carry[:, tb, g * NG:(g + 1) * NG],
                           hb.rearrange("p (n l) -> p n l",
                                        n=NG)[:, :, LC1 - 1]))
                    P.dma(bcf.rearrange("p (n l) -> p n l", n=NG)[0:1, :, :],
                          bc_d[16 + g * NG:16 + (g + 1) * NG,
                               lo:lo + LC1][None, :, :])
                    for fo, fs in scan_f(NG * LC1, 512):
                        P.t(mm(psA[:, fo:fo + fs], onesr[0:1, :],
                               bcf[0:1, fo:fo + fs], True, True))
                    P.v(tt(dA[:, :], hb[:, :], psA[:, 0:NG * LC1], ALU.mult))
                    P.v(red(yb4[:, g, :],
                            dA.rearrange("p (n l) -> p l n", n=NG),
                            ALU.add))
                P.v(red(yb[:, 0, :], yb4.rearrange("p g l -> p l g"),
                        ALU.add))
                P.v(ts(spt[:, 0, :], uch[:, tb, :], dsk_s[:, tb:tb + 1],
                       None, ALU.mult))
                P.v(tt(yb[:, 0, :], yb[:, 0, :], spt[:, 0, :], ALU.add))
                P.s(act(spt[:, 0, :], zch[:, tb, :], AF.Silu))
                P.v(tt(ygb[:, 0, :], yb[:, 0, :], spt[:, 0, :], ALU.mult))
                P.ind(y_ai.rearrange("r (j l) -> (r j) l", l=1024),
                      ixs[:, IXB + c * 2 + tb:IXB + c * 2 + tb + 1],
                      ygb[:, 0, :], None)
        cc(y_ai, y_ao)
        arf.reset()
        arb.reset()
        xs = arf.take(8, TSE)
        dbuf = arf.take(8, 512)
        bobuf = arf.take(8, 512)
        sqb = arf.take(8, 512)
        t1 = arf.take(512)
        t2 = arf.take(512)
        t3 = arf.take(512)
        tabc = arf.take(4, 512)
        ygc = arb.take(16, 512)
        warena = arb.take(16384)
        xbpc = arb.take(8, 513)
        vbb = arb.take(4, 512)
        vtbb = arb.take(4, 512)
        qob = arb.take(512)
        B = dict(xbpc=xbpc, qf=dbuf[:, 0:4, :], sqh=sqb[:, 0:4, :],
                 vb=vbb, vtb=vtbb, rqr=rows, t1=t1, t2=t2, t3=t3, qo=qob,
                 tabc=tabc, ones128=ones128, onesr=onesr, oneh=oneh,
                 ident=ident, rotm=rotm, ps_mm=ps_mm, ps_bc=ps_bc,
                 ps_row=ps_row, ps_t=ps_t,
                 wq=warena[:, 0:12288].rearrange("p (b c) -> p b c", b=8))
        wo = warena.rearrange("p (b c) -> p b c", b=16)
        P.dma(xs[:, :, :], xe_d.rearrange("(b p) l -> p b l", p=128))
        P.dmag(wo, wo_d.rearrange("(b p) c -> p b c", p=128))
        y36 = y_ao.rearrange("r (j l) -> (r j) l", l=512)
        y144 = y_ao.rearrange("r (j l) -> (r j) l", l=128)
        for ci, (fo, fs) in enumerate(CHK_C):
            for k in range(16):
                P.ind(ygc[:, k, 0:fs], None, y144 if ci == 0 else y36,
                      ixs[:, IXC + ci * 16 + k:IXC + ci * 16 + k + 1])
            fm_ln(P, lambda k: xs[:, k, fo:fo + fs], fs, dbuf, sqb,
                  ones128, onesr, ps_row, ps_bc, rows,
                  nrm_view=lambda k: dbuf[:, k, 0:fs])
            for m in range(8):
                for k in range(16):
                    P.t(mm(ps_mm[:, 0:fs], wo[:, k, m * 128:(m + 1) * 128],
                           ygc[:, k, 0:fs], k == 0, k == 15))
                P.v(tt(bobuf[:, m, 0:fs], ps_mm[:, 0:fs], dbuf[:, m, 0:fs],
                       ALU.add))
            fm_ln(P, lambda k: bobuf[:, k, 0:fs], fs, bobuf, sqb,
                  ones128, onesr, ps_row, ps_bc, rows,
                  nrm_view=lambda k: bobuf[:, k, 0:fs])
            for m in range(8):
                P.v(tt(xs[:, m, fo:fo + fs], bobuf[:, m, 0:fs],
                       xs[:, m, fo:fo + fs], ALU.add))
            P.dma(x1_d.rearrange("(b p) l -> p b l", p=128)[:, :, fo:fo + fs],
                  xs[:, :, fo:fo + fs])
        P.dmag(B["wq"], wq0_d.rearrange("(b p) c -> p b c", p=128))
        P.v(mset(B["xbpc"][:, :, 0:1], 0.0))
        B["qsr"] = qsr[:, 0:1]
        B["ksr"] = ksr[:, 0:1]
        for ci, (fo, fs) in enumerate(CHK_C):
            P.dma(B["tabc"][:, :, 0:fs],
                  tab_d[:, :, fo:fo + fs].transpose([1, 0, 2]))
            fm_ln(P, lambda k: xs[:, k, fo:fo + fs], fs, dbuf, sqb,
                  ones128, onesr, ps_row, ps_bc, rows,
                  nrm_view=lambda k: dbuf[:, k, 0:fs])
            for k in range(8):
                P.v(cp(B["xbpc"][:, k, 1:1 + fs], dbuf[:, k, 0:fs]))
            P.dma(xb_d.rearrange("(b p) l -> p b l", p=128)[:, :, fo:fo + fs],
                  dbuf[:, :, 0:fs])
            qkv_chunk(P, B, fo, fs, q_ds[0], k_ds[0], v_ds[0])
        arf.reset()
        arb.reset()

        def kv_exchange(l):
            ktl = arb.take(4, 768)
            vtl = arb.take(6, 512)
            P.dma(ktl[:, :, :], k_ds[l][:, TSE - 768:TSE]
                  .rearrange("(a p) w -> p a w", p=128))
            P.dma(vtl[:, :, :], v_ds[l][TSE - 768:TSE, :]
                  .rearrange("(a p) w -> p a w", p=128))
            kgi_f = kg_ai[l].rearrange("r (j w) -> (r j) w", w=768)
            for kt in range(4):
                P.ind(kgi_f, ixs[:, IXK + kt:IXK + kt + 1], ktl[:, kt, :],
                      None)
            for it in range(6):
                P.ind(vg_ai[l], ixs[:, IXV + it:IXV + it + 1], vtl[:, it, :],
                      None)
            cc(kg_ai[l], kg_ao[l])
            cc(vg_ai[l], vg_ao[l])
            kgo_f = kg_ao[l].rearrange("r (j w) -> (r j) w", w=768)
            for kt in range(4):
                P.ind(ktl[:, kt, :], None, kgo_f,
                      ixs[:, IXKR + kt:IXKR + kt + 1])
            for it in range(6):
                P.ind(vtl[:, it, :], None, vg_ao[l],
                      ixs[:, IXVR + it:IXVR + it + 1])
            P.dma(kt_loc[l].rearrange("(a p) w -> p a w", p=128),
                  ktl[:, :, :])
            P.dma(vt_loc[l].rearrange("(a p) w -> p a w", p=128),
                  vtl[:, :, :])
            arb.reset()

        def emit_D(l, last):
            stream_d = xb_d if l == 0 else x1o_d
            woA_d = woA0_d if l == 0 else woA1_d
            w1_d = w1f0_d if l == 0 else w1f1_d
            b1_d = b1p0_d if l == 0 else b1p1_d
            w2_d = w2f0_d if l == 0 else w2f1_d
            b2_d = b2v0_d if l == 0 else b2v1_d
            lg_d = lgv0_d if l == 0 else lgv1_d
            lb_d = lbv0_d if l == 0 else lbv1_d
            xm = arf.take(8, TSE + 1)
            rbuf = arf.take(8, 512)
            dbuf = arf.take(8, 512)
            sqb = arf.take(8, 512)
            simb = arf.take(STRIP)
            maskc = arf.take(STRIP)
            h1b = arf.take(512)
            gbv = arf.take(512)
            extm = arf.take(8, 128)
            xb2flat = arb.take(8 * 513)
            qbt = xb2flat[:, 0:512].rearrange("p (a b) -> p a b", a=4)
            kstr = xb2flat[:, 512:512 + 4 * STRIP].rearrange(
                "p (a b) -> p a b", a=4)
            vstr = arb.take(5, 512)
            pbuf = arb.take(STRIP)
            ptb = arb.take(5, 128)
            obf = arb.take(4, 128)
            xbm = arb.take(8, 512)
            hbf = arb.take(22, 512)
            wmg = arb.take(8, 128)
            w2g = arb.take(22, 128)
            wob = arb.take(4, D)
            cols = colsml[:, 0:4]
            P.dma(xm[:, :, 1:TSE + 1],
                  stream_d.rearrange("(b p) l -> p b l", p=128))
            P.v(mset(xm[:, :, 0:1], 0.0))
            P.dmag(wob[:, :, :], woA_d.rearrange("(b p) c -> p b c", p=128))
            P.dma(extm[:, :, :], em_d)
            vec_cols(P, lgc, lg_d, 8)
            vec_cols(P, lbb, lb_d, 8, cast=True)
            vec_cols(P, b2c, b2_d, 8)
            vec_cols(P, b1pv[:, 0:43], b1_d, 43)
            for i in range(17):
                c0 = 128 * i + 128
                P.dma(maskc[:, :], mk_d[i, :, :])
                P.dma(qbt[:, :, :],
                      q_ds[l].rearrange("(b p) ll -> p b ll", p=128)
                      [:, :, 128 * i:128 * (i + 1)])
                nt = max(0, min(768 - c0, STRIP))
                if nt > 0:
                    P.dma(kstr[:, :, 0:nt],
                          kt_loc[l].rearrange("(b p) w -> p b w", p=128)
                          [:, :, c0:c0 + nt])
                if nt < STRIP:
                    o0 = c0 + nt - 768 + 128
                    P.dma(kstr[:, :, nt:STRIP],
                          k_ds[l].rearrange("(b p) w -> p b w", p=128)
                          [:, :, o0:o0 + STRIP - nt])
                for j in range(5):
                    t0_ = c0 + 128 * j
                    if t0_ < 768:
                        P.dma(vstr[:, j, :],
                              vt_loc[l][t0_:t0_ + 128, :]
                              .rearrange("(a p) c -> p a c", p=128)[:, 0, :])
                    else:
                        o0 = t0_ - 768 + 128
                        P.dma(vstr[:, j, :],
                              v_ds[l][o0:o0 + 128, :]
                              .rearrange("(a p) c -> p a c", p=128)[:, 0, :])
                for h in range(8):
                    tl, b = h // 2, 64 * (h % 2)
                    P.t(mm(ps_sim[:, 0:512], qbt[b:b + 64, tl, :],
                           kstr[b:b + 64, tl, 0:512], True, True))
                    P.t(mm(ps_sim[:, 512:STRIP], qbt[b:b + 64, tl, :],
                           kstr[b:b + 64, tl, 512:STRIP], True, True))
                    P.v(tt(simb[:, :], ps_sim[:, :], maskc[:, :], ALU.add))
                    P.v(red(cols[:, 0:1], simb[:, :], ALU.max, negate=True))
                    P.s(act(pbuf[:, :], simb[:, :], AF.Exp, bias=cols[:, 0:1],
                            accum=cols[:, 1:2]))
                    P.v(rcp(cols[:, 2:3], cols[:, 1:2]))
                    P.v(ts(pbuf[:, :], pbuf[:, :], cols[:, 2:3], None,
                           ALU.mult))
                    for j in range(5):
                        P.t(trans(ps_pt[:, 128 * j:128 * (j + 1)],
                                  pbuf[:, 128 * j:128 * (j + 1)], ident[:, :]))
                    for j in range(5):
                        P.s(act(ptb[:, j, :], ps_pt[:, 128 * j:128 * (j + 1)],
                                AF.Copy))
                    for j in range(5):
                        P.t(mm(ps_o[0:64, :], vstr[:, j, 64 * h:64 * h + 64],
                               ptb[:, j, :], j == 0, j == 4))
                    P.s(act(obf[b:b + 64, tl, :], ps_o[0:64, 0:128], AF.Copy))
                for m in range(8):
                    for k in range(4):
                        P.t(mm(ps_mm[:, 0:128],
                               wob[:, k, m * 128:(m + 1) * 128],
                               obf[:, k, :], k == 0, k == 3))
                    P.v(tt(xm[:, m, 1 + 128 * i:1 + 128 * (i + 1)],
                           ps_mm[:, 0:128],
                           xm[:, m, 1 + 128 * i:1 + 128 * (i + 1)], ALU.add))
            for ci, (fo, fs) in enumerate(scan_f(TSE, 512)):
                def xv(k, fo=fo, fs=fs):
                    return (xm[:, k, 1 + fo:1 + fo + fs] if k < 4
                            else xm[:, k, fo:fo + fs])
                fm_ln(P, xv, fs, dbuf, sqb, ones128, onesr, ps_row, ps_bc,
                      rows, nrm_view=None)
                for k in range(8):
                    P.v(tt(xbm[:, k, 0:fs], dbuf[:, k, 0:fs], ps_bc[:, 0:fs],
                           ALU.mult))
                P.v(mset(hbf[64:128, 21, 0:fs], 0.0))
                for mg in range(43):
                    P.dmag(wmg[:, :, :],
                           w1_d.rearrange("(b p) c -> p b c", p=128)
                           [:, :, mg * 128:(mg + 1) * 128])
                    if ci == 0:
                        for k in range(8):
                            P.t(mm(ps_o[:, 0:1], wmg[:, k, :],
                                   lbb[:, k:k + 1], k == 0, k == 7))
                        P.s(act(b1e[:, mg:mg + 1], ps_o[:, 0:1], AF.Copy))
                        P.v(tt(b1e[:, mg:mg + 1], b1e[:, mg:mg + 1],
                               b1pv[:, mg:mg + 1], ALU.add))
                    for k in range(8):
                        P.v(ts(wmg[:, k, :], wmg[:, k, :], lgc[:, k:k + 1],
                               None, ALU.mult))
                    for k in range(8):
                        P.t(mm(ps_mm[:, 0:fs], wmg[:, k, :], xbm[:, k, 0:fs],
                               k == 0, k == 7))
                    P.s(act(h1b[:, 0:fs], ps_mm[:, 0:fs], AF.Identity,
                            bias=b1e[:, mg:mg + 1]))
                    P.s(act(gbv[0:64, 0:fs], h1b[64:128, 0:fs], AF.Gelu))
                    P.v(tt(hbf[64 * (mg % 2):64 * (mg % 2) + 64, mg // 2,
                               0:fs],
                           h1b[0:64, 0:fs], gbv[0:64, 0:fs], ALU.mult))
                for m in range(8):
                    P.dmag(w2g[:, :, :],
                           w2_d.rearrange("(b p) c -> p b c", p=128)
                           [:, :, m * 128:(m + 1) * 128])
                    for k in range(22):
                        P.t(mm(ps_mm[:, 0:fs], w2g[:, k, :], hbf[:, k, 0:fs],
                               k == 0, k == 21))
                    P.s(act(rbuf[:, m, 0:fs], ps_mm[:, 0:fs], AF.Identity,
                            bias=b2c[:, m:m + 1]))
                    P.v(tt(rbuf[:, m, 0:fs], rbuf[:, m, 0:fs],
                           xm[:, m, 1 + fo:1 + fo + fs], ALU.add))
                if fo < EXT:
                    for m in range(8):
                        P.v(tt(rbuf[:, m, 0:EXT], rbuf[:, m, 0:EXT],
                               extm[:, m, :], ALU.mult))
                if not last:
                    P.dma(x1o_d.rearrange("(b p) l -> p b l", p=128)
                          [:, :, fo:fo + fs], rbuf[:, :, 0:fs])
                    P.dmag(x1b_d.rearrange("(b p) l -> p b l", p=128)
                           [:, :, fo:fo + fs], rbuf[:, :, 0:fs])
                else:
                    P.dma(xf_d.rearrange("(b p) l -> p b l", p=128)
                          [:, :, fo:fo + fs], rbuf[:, :, 0:fs])
            if not last:
                B2 = dict(xbpc=xb2flat.rearrange("p (a b) -> p a b", a=8),
                          qf=dbuf[:, 0:4, :],
                          sqh=sqb[:, 0:4, :], vb=xbm[:, 0:4, :],
                          vtb=xbm[:, 4:8, :], rqr=rows,
                          t1=rbuf[:, 4, :], t2=rbuf[:, 5, :],
                          t3=rbuf[:, 6, :], qo=hbf[:, 0, 0:512],
                          tabc=rbuf[:, 0:4, :], ones128=ones128,
                          onesr=onesr, oneh=oneh, ident=ident, rotm=rotm,
                          ps_mm=ps_mm, ps_bc=ps_bc, ps_row=ps_row,
                          ps_t=ps_t, qsr=qsr[:, 1:2], ksr=ksr[:, 1:2],
                          wq_d=wq1_d, wqm=wmg)
                for ci, (fo, fs) in enumerate(scan_f(TSE, 512)):
                    P.dma(B2["tabc"][:, :, 0:fs],
                          tab_d[:, :, fo:fo + fs].transpose([1, 0, 2]))
                    if fo == 0:
                        P.dma(B2["xbpc"][:, :, 1:1 + fs],
                              x1b_d.rearrange("(b p) l -> p b l", p=128)
                              [:, :, 0:fs])
                        P.v(mset(B2["xbpc"][:, :, 0:1], 0.0))
                    else:
                        P.dma(B2["xbpc"][:, :, 0:1 + fs],
                              x1b_d.rearrange("(b p) l -> p b l", p=128)
                              [:, :, fo - 1:fo + fs])
                    qkv_chunk(P, B2, fo, fs, q_ds[1], k_ds[1], v_ds[1])
            else:
                for fo, fs in scan_f(TSH, 512):
                    P.dma(rbuf[:, :, 0:fs],
                          xf_d.rearrange("(b p) l -> p b l", p=128)
                          [:, :, EXT + fo:EXT + fo + fs])
                    fm_ln(P, lambda k, fs=fs: rbuf[:, k, 0:fs], fs, dbuf,
                          sqb, ones128, onesr, ps_row, ps_bc, rows,
                          nrm_view=lambda k, fs=fs: dbuf[:, k, 0:fs])
                    P.dma(sqb[:, :, 0:fs],
                          x1_d.rearrange("(b p) l -> p b l", p=128)
                          [:, :, EXT + fo:EXT + fo + fs])
                    for m in range(8):
                        P.v(tt(dbuf[:, m, 0:fs], dbuf[:, m, 0:fs],
                               sqb[:, m, 0:fs], ALU.add))
                    for j in range(fs // 128):
                        for m in range(8):
                            P.t(trans(ps_mm[:, 0:128],
                                      dbuf[:, m, 128 * j:128 * (j + 1)],
                                      iden32[:, :]))
                            P.s(act(rbuf[:, m, 0:128] if False else
                                    rbuf.rearrange("p a b -> p (a b)")
                                    [:, m * 128:(m + 1) * 128],
                                    ps_mm[:, 0:128], AF.Copy))
                        P.dma(out_d[fo + 128 * j:fo + 128 * (j + 1), :],
                              rbuf.rearrange("p a b -> p (a b)")[:, 0:1024])
            arf.reset()
            arb.reset()

        kv_exchange(0)
        emit_D(0, False)
        kv_exchange(1)
        emit_D(1, True)
        P.finalize(blk)
    return nc


def kernel(**inputs):
    f32 = np.float32
    x = np.asarray(inputs["L0_attn_x"], f32)[0]
    w_in = np.asarray(inputs["L0_attn_w_in"], f32)
    ident = np.eye(128).astype(NPBF)
    iden32 = np.eye(128, dtype=f32)
    oneh = oneh_host()
    rotm = rotm_host()
    qs_ = np.asarray(inputs["L0_ffn_q_scale"], f32)
    ks_ = np.asarray(inputs["L0_ffn_k_scale"], f32)
    w1f0, b1p0 = permute_w1(np.asarray(inputs["L0_ffn_w1"], f32)[0],
                            np.asarray(inputs["L0_ffn_b1"], f32)[0])
    w1f1, b1p1 = permute_w1(np.asarray(inputs["L0_ffn_w1"], f32)[1],
                            np.asarray(inputs["L0_ffn_b1"], f32)[1])
    w2f0 = permute_w2(np.asarray(inputs["L0_ffn_w2"], f32)[0])
    w2f1 = permute_w2(np.asarray(inputs["L0_ffn_w2"], f32)[1])

    nc = build_FUSED()
    maps = []
    for c in range(NC):
        sl = slice(c * ES, (c + 1) * ES)
        t0 = c * TSH - EXT
        w_sl = np.concatenate([w_in[sl], w_in[E + c * ES:E + (c + 1) * ES]], 0)
        xe = np.zeros((TSE, D), f32)
        lo = max(t0, 0)
        xe[lo - t0:] = x[lo:t0 + TSE]
        maps.append(dict(
            xa=_cc(x[c * TSH:(c + 1) * TSH]),
            xe=_cc(xe.T),
            w_in_slT=_cc(w_sl.T),
            lnw=np.asarray(inputs["L0_attn_ln_w"], f32),
            lnb=np.asarray(inputs["L0_attn_ln_b"], f32),
            cw=_cc(np.asarray(inputs["L0_attn_conv_w"], f32)[sl]),
            cb=_cc(np.asarray(inputs["L0_attn_conv_b"], f32)[sl]),
            wxpT=_cc(np.asarray(inputs["L0_attn_w_xproj"], f32)[:, sl].T),
            wdtT=_cc(np.asarray(inputs["L0_attn_w_dt"], f32)[sl].T),
            bdt=_cc(np.asarray(inputs["L0_attn_b_dt"], f32)[sl]),
            alog=_cc(np.asarray(inputs["L0_attn_a_log"], f32)[sl]),
            dskip=_cc(np.asarray(inputs["L0_attn_d_skip"], f32)[sl]),
            woutT=_cc(np.asarray(inputs["L0_attn_w_out"], f32).T),
            wqkv0=_cc(np.asarray(inputs["L0_ffn_wqkv"], f32)[0]),
            wqkv1=_cc(np.asarray(inputs["L0_ffn_wqkv"], f32)[1]),
            qsr0=_cc(np.tile(qs_[0], 2)[:, None]),
            ksr0=_cc(np.tile(ks_[0], 2)[:, None]),
            qsr1=_cc(np.tile(qs_[1], 2)[:, None]),
            ksr1=_cc(np.tile(ks_[1], 2)[:, None]),
            tabs=tabs_for(t0),
            oneh=oneh, rotm=rotm, ident=ident, iden32=iden32,
            masks=masks_for(c),
            extm=np.full((128, 8, 128), 0.0 if c == 0 else 1.0, f32),
            idxt=idx_table_for(c),
            woA0=_cc(np.asarray(inputs["L0_ffn_wo"], f32)[0]),
            woA1=_cc(np.asarray(inputs["L0_ffn_wo"], f32)[1]),
            w1f0=w1f0, w1f1=w1f1, b1p0=b1p0, b1p1=b1p1,
            w2f0=w2f0, w2f1=w2f1,
            b2v0=_cc(np.asarray(inputs["L0_ffn_b2"], f32)[0]),
            b2v1=_cc(np.asarray(inputs["L0_ffn_b2"], f32)[1]),
            lgv0=_cc(np.asarray(inputs["L0_ffn_ln_g"], f32)[0]),
            lgv1=_cc(np.asarray(inputs["L0_ffn_ln_g"], f32)[1]),
            lbv0=_cc(np.asarray(inputs["L0_ffn_ln_b"], f32)[0]),
            lbv1=_cc(np.asarray(inputs["L0_ffn_ln_b"], f32)[1]),
        ))
    res = run_bass_kernel_spmd(nc, maps, core_ids=list(range(NC)))
    out = np.concatenate([np.asarray(r["outp"]) for r in res.results], 0)
    return out[None].astype(np.float32)


# version 07
# speedup vs baseline: 1.1784x; 1.0158x over previous
import numpy as np
import ml_dtypes
from contextlib import ExitStack

import concourse.bass as bass
import concourse.mybir as mybir
from concourse.bass import IndirectOffsetOnAxis
from concourse.bass_utils import run_bass_kernel_spmd

F32 = mybir.dt.float32
BF16 = mybir.dt.bfloat16
AF = mybir.ActivationFunctionType
ALU = mybir.AluOpType
AX = mybir.AxisListType
NPBF = ml_dtypes.bfloat16

NC = 8
T = 16384
D = 1024
E = 2048
ES = E // NC
TSH = T // NC
EXT = 128
TSE = TSH + EXT
NST = 16
DTR = 64
WIN = 512
HEADS = 8
DH = 64
INNER = 512
HID = 2730
HIDP = 2752
NH1 = 2 * HIDP
QKS = 8.0
EPS = 1e-5
STRIP = WIN + 128
NEG = np.float32(-3.0e38)

ENGINES = ("sync", "vector", "scalar", "tensor", "gpsimd")

YW = T + 2048
CHK_C = [(0, 128), (128, 512), (640, 512), (1152, 512), (1664, 512)]
IXA = 0
IXB = 8
IXC = 40
IXK = 120
IXV = 124
IXKR = 130
IXVR = 134
IXN = 140


def idx_table_for(c):
    p = np.arange(128, dtype=np.int64)
    t = np.zeros((128, IXN), np.int64)
    for b in range(8):
        t[:, IXA + b] = (b * 128 + p) * 8 + c
    for ch in range(16):
        for tb in range(2):
            t[:, IXB + ch * 2 + tb] = (c * ES + tb * 128 + p) * (YW // 1024) \
                + ch
    for k in range(16):
        t[:, IXC + k] = (k * 128 + p) * (YW // 128) + \
            (c * 16 - 1 if c > 0 else YW // 128 - 1)
        for ci in range(1, 5):
            t[:, IXC + ci * 16 + k] = (k * 128 + p) * (YW // 512) + c * 4 \
                + (ci - 1)
    for kt in range(4):
        t[:, IXK + kt] = (kt * 128 + p) * 9 + (c + 1)
        t[:, IXKR + kt] = (kt * 128 + p) * 9 + c
    for it in range(6):
        t[:, IXV + it] = (c + 1) * 768 + it * 128 + p
        t[:, IXVR + it] = c * 768 + it * 128 + p
    return np.ascontiguousarray(t.astype(np.uint32))


class Prog:

    LIM = 16000

    def __init__(self, nc, stack, nsem=10):
        self.nc = nc
        self.sems = [stack.enter_context(nc.semaphore(f"sq{i}"))
                     for i in range(nsem)]
        self.sem = self.sems[0]
        self.ops = []
        self.cnt_ = 0

    def op(self, ename, builder, dma=False, tag=None):
        inc = 16 if dma else 1
        self.ops.append((ename, builder, inc, inc,
                         "dma" if dma else (tag or "comp")))
        self.cnt_ += inc

    def site(self, ename, builders):
        for j, b in enumerate(builders):
            en = ename
            if isinstance(b, tuple):
                en, b = b
            cnt = 16 if j == len(builders) - 1 else 0
            self.ops.append((en, b, 16, cnt, "coll"))
            self.cnt_ += cnt

    def dma(self, out, in_):
        self.op("sync", lambda e, o=out, i=in_: e.dma_start(out=o, in_=i), dma=True)

    def dmag(self, out, in_):
        self.op("gpsimd", lambda e, o=out, i=in_: e.dma_start(out=o, in_=i), dma=True)

    def ind(self, out, out_idx, in_, in_idx):
        oo = IndirectOffsetOnAxis(ap=out_idx, axis=0) if out_idx is not None \
            else None
        io = IndirectOffsetOnAxis(ap=in_idx, axis=0) if in_idx is not None \
            else None
        self.op("gpsimd",
                lambda e, o=out, i=in_, oo=oo, io=io: e.indirect_dma_start(
                    out=o, out_offset=oo, in_=i, in_offset=io), dma=True)

    def v(self, builder):
        self.op("vector", builder)

    def s(self, builder):
        self.op("scalar", builder)

    def t(self, builder):
        self.op("tensor", builder)

    def g(self, builder):
        self.op("gpsimd", builder)

    def finalize(self, block):
        LIM = Prog.LIM
        recs = []
        ep = 0
        base = [0]
        c = 0
        prev_e = None
        for e, b, stamp, cnt, tag in self.ops:
            if c + cnt - base[ep] > LIM:
                ep += 1
                base.append(c)
                assert ep < len(self.sems), "need more epoch semaphores"
            recs.append((e, b, stamp, ep, c - base[ep], e == prev_e, tag))
            prev_e = e
            c += cnt
        totals = [0] * (ep + 1)
        for (e, b, stamp, cnt, tag), r in zip(self.ops, recs):
            totals[r[3]] += cnt
        for ename in ENGINES:
            mine = [(b, inc, k, lp, sw, tag)
                    for (e, b, inc, k, lp, sw, tag) in recs if e == ename]
            if not mine and ename != "sync":
                continue

            def body(eng, mine=mine, last=(ename == "sync"), ep=ep,
                     ename=ename):
                seen = -1
                unclean = set()
                for b, inc, k, lp, sw, tag in mine:
                    if k > seen:
                        for kk in range(max(seen, 0), k):
                            eng.wait_ge(self.sems[kk], totals[kk])
                        seen = k
                        sw = False
                        unclean = set()
                    skip = sw and ename in ("tensor", "scalar") \
                        and (tag == "comp" and not unclean)
                    if lp and not skip:
                        eng.wait_ge(self.sems[k], lp)
                        unclean = set()
                    if tag != "comp":
                        unclean.add(tag)
                    b(eng).then_inc(self.sems[k], inc)
                if last:
                    for kk in range(ep + 1):
                        eng.wait_ge(self.sems[kk], totals[kk])

            getattr(block, ename)(body)
        return c


def mm(ps_ap, lhsT_ap, rhs_ap, start, stop):
    return lambda e: e.matmul(ps_ap, lhsT_ap, rhs_ap, start=start, stop=stop)


class Consts:
    cur = None

    def __init__(self, P, nc, stack):
        self.P = P
        self.t = stack.enter_context(nc.sbuf_tensor("cpool", [128, 32], F32))
        self.vals = {}
        Consts.cur = self

    def col(self, v):
        v = float(v)
        if v not in self.vals:
            i = len(self.vals)
            assert i < 32
            self.vals[v] = i
            self.P.v(mset(self.t[:, i:i + 1], v))
        return self.t[:, self.vals[v]:self.vals[v] + 1]


def act(out, in_, func, bias=0.0, scale=1.0, accum=None):
    np_ = in_.shape[0]
    if isinstance(bias, float) and bias != 0.0:
        bias = Consts.cur.col(bias)[0:np_, :] if np_ < 128 else             Consts.cur.col(bias)
    if isinstance(scale, float) and scale != 1.0:
        scale = Consts.cur.col(scale)[0:np_, :] if np_ < 128 else             Consts.cur.col(scale)
    return lambda e: e.activation(out, in_, func, bias=bias, scale=scale,
                                  accum_out=accum)


def tt(out, a, b, op):
    return lambda e: e.tensor_tensor(out, a, b, op)


def ts(out, a, s1, s2, op0, op1=ALU.bypass):
    if s2 is None:
        return lambda e: e.tensor_scalar(out, a, s1, None, op0)
    return lambda e: e.tensor_scalar(out, a, s1, s2, op0, op1)


def stt(out, in0, scalar, in1, op0, op1):
    return lambda e: e.scalar_tensor_tensor(out, in0, scalar, in1, op0, op1)


def red(out, in_, op, negate=False):
    return lambda e: e.tensor_reduce(out, in_, AX.X, op, negate=negate)


def cp(out, in_):
    return lambda e: e.tensor_copy(out, in_)


def rcp(out, in_):
    return lambda e: e.reciprocal(out, in_)


def mset(ap_, v):
    return lambda e: e.memset(ap_, v)


def tts(out, d0, d1, init):
    return lambda e: e.tensor_tensor_scan(out, d0, d1, init, ALU.mult, ALU.add)


def trans(ps_ap, in_ap, ident_ap):
    return lambda e: e.transpose(ps_ap, in_ap, ident_ap)


def vec_cols(P, dst, src1d, nb, cast=False):
    for b in range(nb):
        src = src1d[b * 128:(b + 1) * 128][:, None]
        if cast:
            P.dmag(dst[:, b:b + 1], src)
        else:
            P.dma(dst[:, b:b + 1], src)


def scan_f(total, step):
    o = 0
    while o < total:
        s = min(step, total - o)
        yield o, s
        o += s


def emit_fm_ln(P, sc, x_view, nb, F, fo, d_out, n_out, ones128, sq_t, ps_row,
               ps_bc, row_sb, shift_upper_from=None, out_dtype_note=None):
    for k in range(nb):
        P.t(mm(ps_row[0:1, 0:F], ones128[:, 0:1], x_view(k),
               start=(k == 0), stop=(k == nb - 1)))
    P.s(act(row_sb[0:1, 0:F], ps_row[0:1, 0:F], AF.Identity, scale=1.0 / D))
    P.t(mm(ps_bc[:, 0:F], ones128[0:1, 0:128], row_sb[0:1, 0:F], True, True))
    for k in range(nb):
        P.v(tt(d_out[:, k, fo:fo + F], x_view(k), ps_bc[:, 0:F], ALU.subtract))
    for k in range(nb):
        P.s(act(sq_t[:, k, 0:F], d_out[:, k, fo:fo + F], AF.Square))
    for k in range(nb):
        P.t(mm(ps_row[0:1, 0:F], ones128[:, 0:1], sq_t[:, k, 0:F],
               start=(k == 0), stop=(k == nb - 1)))
    P.s(act(row_sb[1:2, 0:F], ps_row[0:1, 0:F], AF.Ln, bias=EPS, scale=1.0 / D))
    P.s(act(row_sb[1:2, 0:F], row_sb[1:2, 0:F], AF.Exp, scale=-0.5))
    P.t(mm(ps_bc[:, 0:F], ones128[0:1, 0:128], row_sb[1:2, 0:F], True, True))
    if n_out is not None:
        for k in range(nb):
            P.v(tt(n_out[:, k, fo:fo + F], d_out[:, k, fo:fo + F],
                   ps_bc[:, 0:F], ALU.mult))


def build_A():
    nc = bass.Bass("TRN2", target_bir_lowering=False, debug=False)
    xa_d = nc.dram_tensor("xa", [TSH, D], F32, kind="ExternalInput").ap()
    id_d = nc.dram_tensor("ident", [128, 128], BF16, kind="ExternalInput").ap()
    xn_d = nc.dram_tensor("xn", [D, TSH], BF16, kind="ExternalOutput").ap()
    with ExitStack() as st:
        P = Prog(nc, st)
        Consts(P, nc, st)
        x_s = st.enter_context(nc.sbuf_tensor("x_s", [128, 16, D], F32))
        sqs = st.enter_context(nc.sbuf_tensor("sqs", [128, D], F32))
        xnb = st.enter_context(nc.sbuf_tensor("xnb", [128, 16, D], BF16))
        colc = st.enter_context(nc.sbuf_tensor("colc", [128, 4], F32))
        ident = st.enter_context(nc.sbuf_tensor("ident_s", [128, 128], BF16))
        xnf = st.enter_context(nc.sbuf_tensor("xnf", [128, 8, TSH], BF16))
        pst = st.enter_context(nc.psum_tensor("pst", [128, 512], BF16))
        blk = st.enter_context(nc.Block())

        P.dma(x_s[:, :, :], xa_d.rearrange("(n p) d -> p n d", p=128))
        P.dma(ident[:, :], id_d)
        for i in range(16):
            xt = x_s[:, i, :]
            for rep in range(2):
                P.v(red(colc[:, 0:1], xt, ALU.add))
                P.v(ts(colc[:, 0:1], colc[:, 0:1], 1.0 / D, None, ALU.mult))
                P.v(ts(xt, xt, colc[:, 0:1], None, ALU.subtract))
                P.s(act(sqs[:, :], xt, AF.Square, accum=colc[:, 2:3]))
                P.s(act(colc[:, 2:3], colc[:, 2:3], AF.Ln, bias=EPS,
                        scale=1.0 / D))
                P.s(act(colc[:, 2:3], colc[:, 2:3], AF.Exp, scale=-0.5))
                if rep == 0:
                    P.v(ts(xt, xt, colc[:, 2:3], None, ALU.mult))
                else:
                    P.v(ts(xnb[:, i, :], xt, colc[:, 2:3], None, ALU.mult))
        for i in range(16):
            for dbq in range(2):
                for j in range(4):
                    db = dbq * 4 + j
                    P.t(trans(pst[:, j * 128:(j + 1) * 128],
                              xnb[:, i, db * 128:(db + 1) * 128], ident[:, :]))
                for j in range(4):
                    db = dbq * 4 + j
                    P.s(act(xnf[:, db, i * 128:(i + 1) * 128],
                            pst[:, j * 128:(j + 1) * 128], AF.Copy))
        P.dma(xn_d.rearrange("(b p) l -> p b l", p=128), xnf[:, :, :])
        P.finalize(blk)
    return nc


def build_B0():
    nc = bass.Bass("TRN2", target_bir_lowering=False, debug=False)
    xn_d = nc.dram_tensor("xnf", [D, T], BF16, kind="ExternalInput").ap()
    w_d = nc.dram_tensor("w_in_slT", [D, 2 * ES], F32, kind="ExternalInput").ap()
    lnw_d = nc.dram_tensor("lnw", [D], F32, kind="ExternalInput").ap()
    lnb_d = nc.dram_tensor("lnb", [D], F32, kind="ExternalInput").ap()
    cw_d = nc.dram_tensor("cw", [ES, 4], F32, kind="ExternalInput").ap()
    cb_d = nc.dram_tensor("cb", [ES], F32, kind="ExternalInput").ap()
    wxp_d = nc.dram_tensor("wxpT", [ES, 96], F32, kind="ExternalInput").ap()
    u_d = nc.dram_tensor("u32", [ES, T], F32, kind="ExternalOutput").ap()
    z_d = nc.dram_tensor("z32", [ES, T], F32, kind="ExternalOutput").ap()
    xp_d = nc.dram_tensor("xdblp", [96, T], F32, kind="ExternalOutput").ap()

    LC = 2048
    NCH = T // LC
    with ExitStack() as st:
        P = Prog(nc, st)
        Consts(P, nc, st)
        wt = st.enter_context(nc.sbuf_tensor("wt", [128, 8, 512], BF16))
        lnw_s = st.enter_context(nc.sbuf_tensor("lnw_s", [128, 8], F32))
        lnbb = st.enter_context(nc.sbuf_tensor("lnbb", [128, 8], BF16))
        bias_s = st.enter_context(nc.sbuf_tensor("bias_s", [128, 4], F32))
        cw_s = st.enter_context(nc.sbuf_tensor("cw_s", [128, 2, 4], F32))
        cb_s = st.enter_context(nc.sbuf_tensor("cb_s", [128, 2], F32))
        wxt = st.enter_context(nc.sbuf_tensor("wxt", [128, 2, 96], BF16))
        xnc = st.enter_context(nc.sbuf_tensor("xnc", [128, 8, LC], BF16))
        xs3 = st.enter_context(nc.sbuf_tensor("xs3", [128, 2, LC + 3], F32))
        zb = st.enter_context(nc.sbuf_tensor("zb", [128, 2, LC], F32))
        uc = st.enter_context(nc.sbuf_tensor("uc", [128, 2, LC], F32))
        ub = st.enter_context(nc.sbuf_tensor("ub", [128, 2, LC], BF16))
        ps = st.enter_context(nc.psum_tensor("ps", [128, 512], F32))
        blk = st.enter_context(nc.Block())

        P.dmag(wt[:, :, :], w_d.rearrange("(b p) r -> p b r", p=128))
        vec_cols(P, lnw_s, lnw_d, 8)
        vec_cols(P, lnbb, lnb_d, 8, cast=True)
        P.dma(cw_s[:, :, :], cw_d.rearrange("(b p) k -> p b k", p=128))
        vec_cols(P, cb_s, cb_d, 2)
        P.dmag(wxt[:, :, :], wxp_d.rearrange("(b p) r -> p b r", p=128))
        for mg in range(4):
            for k in range(8):
                P.t(mm(ps[0:128, mg:mg + 1],
                       wt[:, k, mg * 128:(mg + 1) * 128],
                       lnbb[:, k:k + 1], k == 0, k == 7))
        P.s(act(bias_s[:, :], ps[0:128, 0:4], AF.Copy))
        for db in range(8):
            P.v(ts(wt[:, db, :], wt[:, db, :], lnw_s[:, db:db + 1], None,
                   ALU.mult))
        P.v(mset(xs3[:, :, 0:3], 0.0))

        for c in range(NCH):
            P.dma(xnc[:, :, :],
                  xn_d.rearrange("(b p) l -> p b l", p=128)[:, :,
                                                           c * LC:(c + 1) * LC])
            for m in range(4):
                for fo, fs in scan_f(LC, 512):
                    for k in range(8):
                        P.t(mm(ps[:, 0:fs], wt[:, k, m * 128:(m + 1) * 128],
                               xnc[:, k, fo:fo + fs], k == 0, k == 7))
                    if m < 2:
                        P.s(act(xs3[:, m, 3 + fo:3 + fo + fs], ps[:, 0:fs],
                                AF.Identity, bias=bias_s[:, m:m + 1]))
                    else:
                        P.s(act(zb[:, m - 2, fo:fo + fs], ps[:, 0:fs],
                                AF.Identity, bias=bias_s[:, m:m + 1]))
            for tb in range(2):
                P.v(ts(uc[:, tb, :], xs3[:, tb, 0:LC], cw_s[:, tb, 0:1],
                       cb_s[:, tb:tb + 1], ALU.mult, ALU.add))
                for k in range(1, 4):
                    P.v(stt(uc[:, tb, :], xs3[:, tb, k:k + LC],
                            cw_s[:, tb, k:k + 1], uc[:, tb, :],
                            ALU.mult, ALU.add))
                P.v(cp(xs3[:, tb, 0:3], xs3[:, tb, LC:LC + 3]))
                P.s(act(uc[:, tb, :], uc[:, tb, :], AF.Silu))
                P.v(cp(ub[:, tb, :], uc[:, tb, :]))
            P.dma(u_d.rearrange("(b p) l -> p b l", p=128)[:, :,
                                                           c * LC:(c + 1) * LC],
                  uc[:, :, :])
            P.dma(z_d.rearrange("(b p) l -> p b l", p=128)[:, :,
                                                           c * LC:(c + 1) * LC],
                  zb[:, :, :])
            for fo, fs in scan_f(LC, 512):
                for k in range(2):
                    P.t(mm(ps[0:96, 0:fs], wxt[:, k, :], ub[:, k, fo:fo + fs],
                           k == 0, k == 1))
                P.s(act(zb[0:96, 0, fo:fo + fs], ps[0:96, 0:fs], AF.Copy))
            P.dma(xp_d[:, c * LC:(c + 1) * LC], zb[0:96, 0, 0:LC])
        P.finalize(blk)
    return nc


def build_B1():
    nc = bass.Bass("TRN2", target_bir_lowering=False, debug=False)
    u_d = nc.dram_tensor("u32", [ES, T], F32, kind="ExternalInput").ap()
    z_d = nc.dram_tensor("z32", [ES, T], F32, kind="ExternalInput").ap()
    xp_d = nc.dram_tensor("xdblp", [NC, 96, T], F32, kind="ExternalInput").ap()
    wdt_d = nc.dram_tensor("wdtT", [DTR, ES], F32, kind="ExternalInput").ap()
    bdt_d = nc.dram_tensor("bdt", [ES], F32, kind="ExternalInput").ap()
    alog_d = nc.dram_tensor("alog", [ES, NST], F32, kind="ExternalInput").ap()
    dsk_d = nc.dram_tensor("dskip", [ES], F32, kind="ExternalInput").ap()
    yg_d = nc.dram_tensor("yg", [ES, T], BF16, kind="ExternalOutput").ap()
    bc_d = nc.dram_tensor("bc_scr", [32, T], F32).ap()

    LC = 1024
    NCH = T // LC
    NG = 4
    with ExitStack() as st:
        P = Prog(nc, st)
        Consts(P, nc, st)
        wdtt = st.enter_context(nc.sbuf_tensor("wdtt", [64, 2, 128], BF16))
        bdt_s = st.enter_context(nc.sbuf_tensor("bdt_s", [128, 2], F32))
        acol = st.enter_context(nc.sbuf_tensor("acol", [128, 2, NST], F32))
        dsk_s = st.enter_context(nc.sbuf_tensor("dsk_s", [128, 2], F32))
        ones1 = st.enter_context(nc.sbuf_tensor("ones1", [1, 128], F32))
        xdb = st.enter_context(nc.sbuf_tensor("xdb", [96, 8, LC], F32))
        xds = st.enter_context(nc.sbuf_tensor("xds", [96, LC], F32))
        dtrb = st.enter_context(nc.sbuf_tensor("dtrb", [64, LC], BF16))
        dtpre = st.enter_context(nc.sbuf_tensor("dtpre", [128, 2, LC], F32))
        spt = st.enter_context(nc.sbuf_tensor("spt", [128, LC], F32))
        dtb = st.enter_context(nc.sbuf_tensor("dtb", [128, 2, LC], F32))
        uch = st.enter_context(nc.sbuf_tensor("uch", [128, 2, LC], F32))
        zch = st.enter_context(nc.sbuf_tensor("zch", [128, 2, LC], F32))
        dtu = st.enter_context(nc.sbuf_tensor("dtu", [128, 2, LC], F32))
        bcf = st.enter_context(nc.sbuf_tensor("bcf", [1, NG * LC], F32))
        dA = st.enter_context(nc.sbuf_tensor("dA", [128, NG * LC], F32))
        dBu = st.enter_context(nc.sbuf_tensor("dBu", [128, NG * LC], F32))
        hb = st.enter_context(nc.sbuf_tensor("hb", [128, NG * LC], F32))
        fixc = st.enter_context(nc.sbuf_tensor("fixc", [128, NG], F32))
        carry = st.enter_context(nc.sbuf_tensor("carry", [128, 2, NST], F32))
        yb4 = st.enter_context(nc.sbuf_tensor("yb4", [128, NG, LC], F32))
        yb = st.enter_context(nc.sbuf_tensor("yb", [128, LC], F32))
        ygb = st.enter_context(nc.sbuf_tensor("ygb", [128, LC], BF16))
        ps = st.enter_context(nc.psum_tensor("ps", [128, NG * LC], F32))
        blk = st.enter_context(nc.Block())

        P.dmag(wdtt[:, :, :], wdt_d.rearrange("r (b p) -> r b p", p=128))
        vec_cols(P, bdt_s, bdt_d, 2)
        P.dma(acol[:, :, :], alog_d.rearrange("(b p) n -> p b n", p=128))
        vec_cols(P, dsk_s, dsk_d, 2)
        P.v(mset(ones1[:, :], 1.0))
        P.v(mset(carry[:, :, :], 0.0))
        for tb in range(2):
            P.s(act(acol[:, tb, :], acol[:, tb, :], AF.Exp))
            P.v(ts(acol[:, tb, :], acol[:, tb, :], -1.0, None, ALU.mult))

        for c in range(NCH):
            lo = c * LC
            P.dma(xdb[:, :, :], xp_d.transpose([1, 0, 2])[:, :, lo:lo + LC])
            P.v(tt(xds[:, :], xdb[:, 0, :], xdb[:, 1, :], ALU.add))
            for j in range(2, 8):
                P.v(tt(xds[:, :], xds[:, :], xdb[:, j, :], ALU.add))
            P.dma(bc_d[:, lo:lo + LC], xds[64:96, :])
            P.v(cp(dtrb[:, :], xds[0:64, :]))
            for mg in range(2):
                for fo, fs in scan_f(LC, 512):
                    P.t(mm(ps[0:128, fo:fo + fs], wdtt[:, mg, :],
                           dtrb[:, fo:fo + fs], True, True))
                P.s(act(dtpre[:, mg, :], ps[0:128, 0:LC], AF.Identity,
                        bias=bdt_s[:, mg:mg + 1]))
            for tb in range(2):
                P.s(act(spt[:, :], dtpre[:, tb, :], AF.Abs))
                P.s(act(spt[:, :], spt[:, :], AF.Exp, scale=-1.0))
                P.s(act(spt[:, :], spt[:, :], AF.Ln, bias=1.0))
                P.s(act(dtb[:, tb, :], dtpre[:, tb, :], AF.Relu))
                P.v(tt(dtb[:, tb, :], dtb[:, tb, :], spt[:, :], ALU.add))
            P.dma(uch[:, :, :],
                  u_d.rearrange("(b p) l -> p b l", p=128)[:, :, lo:lo + LC])
            P.dma(zch[:, :, :],
                  z_d.rearrange("(b p) l -> p b l", p=128)[:, :, lo:lo + LC])
            for tb in range(2):
                P.v(tt(dtu[:, tb, :], dtb[:, tb, :], uch[:, tb, :], ALU.mult))
            for tb in range(2):
                for g in range(NST // NG):
                    P.dma(bcf.ap().rearrange("p (n l) -> p n l", n=NG),
                          bc_d[g * NG:(g + 1) * NG, lo:lo + LC][None, :, :])
                    for fo, fs in scan_f(NG * LC, 512):
                        P.t(mm(ps[:, fo:fo + fs], ones1[0:1, :],
                               bcf[0:1, fo:fo + fs], True, True))
                    for n in range(NG):
                        P.s(act(dA[:, n * LC:(n + 1) * LC], dtb[:, tb, :],
                                AF.Exp, scale=acol[:, tb, g * NG + n:
                                                   g * NG + n + 1]))
                    P.v(tt(dBu.ap().rearrange("p (n l) -> p n l", n=NG),
                           dtu[:, tb, None, :].broadcast_to([128, NG, LC]),
                           ps.ap().rearrange("p (n l) -> p n l", n=NG),
                           ALU.mult))
                    P.v(tt(fixc[:, 0:NG],
                           dA.ap().rearrange("p (n l) -> p n l", n=NG)[:, :, 0],
                           carry[:, tb, g * NG:(g + 1) * NG], ALU.mult))
                    P.v(tt(dBu.ap().rearrange("p (n l) -> p n l",
                                              n=NG)[:, :, 0],
                           dBu.ap().rearrange("p (n l) -> p n l",
                                              n=NG)[:, :, 0],
                           fixc[:, 0:NG], ALU.add))
                    P.v(mset(dA.ap().rearrange("p (n l) -> p n l",
                                               n=NG)[:, :, 0:1], 0.0))
                    P.v(tts(hb[:, :], dA[:, :], dBu[:, :], 0.0))
                    P.v(cp(carry[:, tb, g * NG:(g + 1) * NG],
                           hb.ap().rearrange("p (n l) -> p n l",
                                             n=NG)[:, :, LC - 1]))
                    P.dma(bcf.ap().rearrange("p (n l) -> p n l", n=NG),
                          bc_d[16 + g * NG:16 + (g + 1) * NG,
                               lo:lo + LC][None, :, :])
                    for fo, fs in scan_f(NG * LC, 512):
                        P.t(mm(ps[:, fo:fo + fs], ones1[0:1, :],
                               bcf[0:1, fo:fo + fs], True, True))
                    P.v(tt(dA[:, :], hb[:, :], ps[:, 0:NG * LC], ALU.mult))
                    P.v(red(yb4[:, g, :],
                            dA.ap().rearrange("p (n l) -> p l n", n=NG),
                            ALU.add))
                P.v(red(yb[:, :],
                        yb4.ap().rearrange("p g l -> p l g"), ALU.add))
                P.v(ts(spt[:, :], uch[:, tb, :], dsk_s[:, tb:tb + 1], None,
                       ALU.mult))
                P.v(tt(yb[:, :], yb[:, :], spt[:, :], ALU.add))
                P.s(act(spt[:, :], zch[:, tb, :], AF.Silu))
                P.v(tt(ygb[:, :], yb[:, :], spt[:, :], ALU.mult))
                P.dma(yg_d.rearrange("(b p) l -> p b l", p=128)[:, tb,
                                                               lo:lo + LC],
                      ygb[:, :])
        P.finalize(blk)
    return nc


def fm_ln(P, x_view, F, dbuf, sqb, ones128, onesr, ps_row, ps_bc, rows, nrm_view=None):
    for k in range(8):
        P.t(mm(ps_row[0:1, 0:F], ones128[:, 0:1], x_view(k), k == 0, k == 7))
    P.s(act(rows[0:1, 0, 0:F], ps_row[0:1, 0:F], AF.Identity, scale=1.0 / D))
    P.t(mm(ps_bc[:, 0:F], onesr[0:1, :], rows[0:1, 0, 0:F], True, True))
    for k in range(8):
        P.v(tt(dbuf[:, k, 0:F], x_view(k), ps_bc[:, 0:F], ALU.subtract))
    for k in range(8):
        P.s(act(sqb[:, k, 0:F], dbuf[:, k, 0:F], AF.Square))
    for k in range(8):
        P.t(mm(ps_row[0:1, 0:F], ones128[:, 0:1], sqb[:, k, 0:F], k == 0,
               k == 7))
    P.s(act(rows[0:1, 1, 0:F], ps_row[0:1, 0:F], AF.Ln, bias=EPS,
            scale=1.0 / D))
    P.s(act(rows[0:1, 1, 0:F], rows[0:1, 1, 0:F], AF.Exp, scale=-0.5))
    P.t(mm(ps_bc[:, 0:F], onesr[0:1, :], rows[0:1, 1, 0:F], True, True))
    if nrm_view is not None:
        for k in range(8):
            P.v(tt(nrm_view(k), dbuf[:, k, 0:F], ps_bc[:, 0:F], ALU.mult))


def qkv_chunk(P, B, fo, fs, q_d, k_d, vT_d):
    xbpc, qf, vb, vtb, sqh, rqr = (B["xbpc"], B["qf"], B["vb"],
                                   B["vtb"], B["sqh"], B["rqr"])
    wq = B.get("wq")
    t1, t2, t3, qo, tabc = B["t1"], B["t2"], B["t3"], B["qo"], B["tabc"]
    ones128, onesr, oneh, ident = (B["ones128"], B["onesr"], B["oneh"],
                                   B["ident"])
    ps_mm, ps_bc, ps_row, ps_t = B["ps_mm"], B["ps_bc"], B["ps_row"], B["ps_t"]

    def mm_group(m):
        if "wq_d" in B:
            P.dmag(B["wqm"][:, :, :],
                   B["wq_d"].rearrange("(b p) c -> p b c", p=128)
                   [:, :, m * 128:(m + 1) * 128])
            w_ap = lambda k: B["wqm"][:, k, :]
        else:
            w_ap = lambda k: wq[:, k, m * 128:(m + 1) * 128]
        for k in range(8):
            rhs = (xbpc[:, k, 1:1 + fs] if k < 4 else xbpc[:, k, 0:fs])
            P.t(mm(ps_mm[:, 0:fs], w_ap(k), rhs, k == 0, k == 7))

    def one_side(scol, csi, sni, out_d):
        for tl in range(4):
            P.s(act(sqh[:, tl, 0:fs], qf[:, tl, 0:fs], AF.Square))
        for tl in range(4):
            for hh in range(2):
                b = 64 * hh
                P.t(mm(ps_row[0:1, 0:fs], ones128[b:b + 64, 0:1],
                       sqh[b:b + 64, tl, 0:fs], True, True))
                P.s(act(rqr[0:1, hh, 0:fs], ps_row[0:1, 0:fs], AF.Copy))
            P.v(ts(rqr[0:1, :, 0:fs], rqr[0:1, :, 0:fs], 1e-24, None,
                   ALU.max))
            P.s(act(rqr[0:1, :, 0:fs], rqr[0:1, :, 0:fs], AF.Ln))
            P.s(act(rqr[0:1, :, 0:fs], rqr[0:1, :, 0:fs], AF.Exp, scale=-0.5))
            P.t(mm(ps_bc[:, 0:fs], oneh[0:1, 0, :], rqr[0:1, 0, 0:fs],
                   True, False))
            P.t(mm(ps_bc[:, 0:fs], oneh[0:1, 1, :],
                   rqr[0:1, 1, 0:fs], False, True))
            P.v(tt(t1[:, 0:fs], qf[:, tl, 0:fs], ps_bc[:, 0:fs], ALU.mult))
            P.v(ts(t1[:, 0:fs], t1[:, 0:fs], scol, None, ALU.mult))
            P.v(tt(t2[:, 0:fs], t1[:, 0:fs], tabc[:, csi, 0:fs], ALU.mult))
            P.t(mm(ps_bc[:, 0:fs], B["rotm"][:, :], t1[:, 0:fs], True, True))
            P.v(tt(t3[:, 0:fs], ps_bc[:, 0:fs], tabc[:, sni, 0:fs], ALU.mult))
            P.v(tt(qo[:, 0:fs], t2[:, 0:fs], t3[:, 0:fs], ALU.add))
            P.dma(out_d[tl * 128:(tl + 1) * 128, fo:fo + fs], qo[:, 0:fs])

    for m in range(4):
        mm_group(m)
        P.s(act(qf[:, m, 0:fs], ps_mm[:, 0:fs], AF.Copy))
    one_side(B["qsr"], 0, 1, q_d)
    for m in range(4, 8):
        mm_group(m)
        P.s(act(qf[:, m - 4, 0:fs], ps_mm[:, 0:fs], AF.Copy))
    one_side(B["ksr"], 2, 3, k_d)
    for m in range(8, 12):
        mm_group(m)
        P.s(act(vb[:, m - 8, 0:fs], ps_mm[:, 0:fs], AF.Copy))
    ntile = fs // 128
    for vt_ in range(4):
        for j in range(ntile):
            P.t(trans(ps_t[:, j * 128:(j + 1) * 128],
                      vb[:, vt_, j * 128:(j + 1) * 128], ident[:, :]))
        for j in range(ntile):
            P.s(act(vtb[:, j, vt_ * 128:(vt_ + 1) * 128],
                    ps_t[:, j * 128:(j + 1) * 128], AF.Copy))
    P.dma(vT_d[fo:fo + fs, :].rearrange("(j p) c -> p j c", p=128),
          vtb[:, 0:ntile, :])
    for k in range(8):
        P.v(cp(xbpc[:, k, 0:1], xbpc[:, k, fs:fs + 1]))


def alloc_qkv_bufs(nc, st, name, qf=None, sqh=None, vb=None, vtb=None,
                   rqr=None, t123=None, qo=None, tabc=None):
    B = {}
    B["xbpc"] = st.enter_context(
        nc.sbuf_tensor(name + "xbpc", [128, 8, 513], BF16))
    if qf is None:
        qf = st.enter_context(nc.sbuf_tensor(name + "qf", [128, 4, 512], F32))
    if sqh is None:
        sqh = st.enter_context(
            nc.sbuf_tensor(name + "sqh", [128, 4, 512], F32))
    B["qf"], B["sqh"] = qf, sqh
    if vb is None:
        vb = st.enter_context(nc.sbuf_tensor(name + "vb", [128, 4, 512], BF16))
    if vtb is None:
        vtb = st.enter_context(
            nc.sbuf_tensor(name + "vtb", [128, 4, 512], BF16))
    B["vb"], B["vtb"] = vb, vtb
    B["rqr"] = rqr if rqr is not None else st.enter_context(
        nc.sbuf_tensor(name + "rqr", [1, 2, 512], F32))
    if t123 is not None:
        B["t1"], B["t2"], B["t3"] = t123
    else:
        B["t1"] = st.enter_context(nc.sbuf_tensor(name + "t1", [128, 512], F32))
        B["t2"] = st.enter_context(nc.sbuf_tensor(name + "t2", [128, 512], F32))
        B["t3"] = st.enter_context(nc.sbuf_tensor(name + "t3", [128, 512], F32))
    B["qo"] = qo if qo is not None else st.enter_context(
        nc.sbuf_tensor(name + "qo", [128, 512], BF16))
    B["tabc"] = tabc if tabc is not None else st.enter_context(
        nc.sbuf_tensor(name + "tabc", [128, 4, 512], F32))
    return B


def build_C():
    nc = bass.Bass("TRN2", target_bir_lowering=False, debug=False)
    xe_d = nc.dram_tensor("xe", [D, TSE], F32, kind="ExternalInput").ap()
    yg_d = nc.dram_tensor("yge", [E, TSE], BF16, kind="ExternalInput").ap()
    wo_d = nc.dram_tensor("woutT", [E, D], F32, kind="ExternalInput").ap()
    wq_d = nc.dram_tensor("wqkv", [D, 3 * INNER], F32, kind="ExternalInput").ap()
    qs_d = nc.dram_tensor("qsr", [128, 1], F32, kind="ExternalInput").ap()
    ks_d = nc.dram_tensor("ksr", [128, 1], F32, kind="ExternalInput").ap()
    tab_d = nc.dram_tensor("tabs", [4, 128, TSE], F32, kind="ExternalInput").ap()
    oneh_d = nc.dram_tensor("oneh", [1, 2, 128], F32, kind="ExternalInput").ap()
    rot_d = nc.dram_tensor("rotm", [128, 128], F32, kind="ExternalInput").ap()
    id_d = nc.dram_tensor("ident", [128, 128], BF16, kind="ExternalInput").ap()
    x1_d = nc.dram_tensor("x1e", [D, TSE], F32, kind="ExternalOutput").ap()
    xb_d = nc.dram_tensor("xbe", [D, TSE], F32, kind="ExternalOutput").ap()
    q_od = nc.dram_tensor("q0", [INNER, TSE], BF16, kind="ExternalOutput").ap()
    k_od = nc.dram_tensor("k0", [INNER, TSE], BF16, kind="ExternalOutput").ap()
    v_od = nc.dram_tensor("vT0", [TSE, INNER], BF16, kind="ExternalOutput").ap()

    with ExitStack() as st:
        P = Prog(nc, st)
        Consts(P, nc, st)
        xs = st.enter_context(nc.sbuf_tensor("xs", [128, 8, TSE], F32))
        warena = st.enter_context(nc.sbuf_tensor("warena", [128, 16384], BF16))
        wo = warena.ap().rearrange("p (b c) -> p b c", b=16)
        ygc = st.enter_context(nc.sbuf_tensor("ygc", [128, 16, 512], BF16))
        dbuf = st.enter_context(nc.sbuf_tensor("dbuf", [128, 8, 512], F32))
        bobuf = st.enter_context(nc.sbuf_tensor("bobuf", [128, 8, 512], F32))
        sqb = st.enter_context(nc.sbuf_tensor("sqb", [128, 8, 512], F32))
        rows = st.enter_context(nc.sbuf_tensor("rows", [1, 2, 512], F32))
        ones128 = st.enter_context(nc.sbuf_tensor("ones128", [128, 1], F32))
        onesr = st.enter_context(nc.sbuf_tensor("onesr", [1, 128], F32))
        oneh = st.enter_context(nc.sbuf_tensor("oneh_s", [1, 2, 128], F32))
        ident = st.enter_context(nc.sbuf_tensor("ident_s", [128, 128], BF16))
        qsr = st.enter_context(nc.sbuf_tensor("qsr_s", [128, 1], F32))
        ksr = st.enter_context(nc.sbuf_tensor("ksr_s", [128, 1], F32))
        rotm = st.enter_context(nc.sbuf_tensor("rotm_s", [128, 128], F32))
        B = alloc_qkv_bufs(nc, st, "q0", qf=dbuf[:, 0:4, :],
                           sqh=sqb[:, 0:4, :])
        ps_mm = st.enter_context(nc.psum_tensor("ps_mm", [128, 512], F32))
        ps_bc = st.enter_context(nc.psum_tensor("ps_bc", [128, 512], F32))
        ps_row = st.enter_context(nc.psum_tensor("ps_row", [1, 512], F32))
        ps_t = st.enter_context(nc.psum_tensor("ps_t", [128, 512], BF16))
        blk = st.enter_context(nc.Block())

        B.update(dict(ones128=ones128, onesr=onesr, oneh=oneh, ident=ident,
                      rotm=rotm,
                      ps_mm=ps_mm, ps_bc=ps_bc, ps_row=ps_row, ps_t=ps_t,
                      qsr=qsr[:, 0:1], ksr=ksr[:, 0:1],
                      wq=warena[:, 0:12288].rearrange("p (b c) -> p b c", b=8)))
        P.dma(xs[:, :, :], xe_d.rearrange("(b p) l -> p b l", p=128))
        P.dmag(warena.ap().rearrange("p (b c) -> p b c", b=16),
               wo_d.rearrange("(b p) c -> p b c", p=128))
        P.v(mset(ones128[:, :], 1.0))
        P.v(mset(onesr[:, :], 1.0))
        P.dma(oneh[:, :, :], oneh_d)
        P.dma(ident[:, :], id_d)
        P.dma(qsr[:, :], qs_d)
        P.dma(ksr[:, :], ks_d)
        P.dma(rotm[:, :], rot_d)

        for fo, fs in scan_f(TSE, 512):
            P.dma(ygc[:, :, 0:fs],
                  yg_d.rearrange("(b p) l -> p b l", p=128)[:, :, fo:fo + fs])
            fm_ln(P, lambda k: xs[:, k, fo:fo + fs], fs, dbuf, sqb,
                  ones128, onesr, ps_row, ps_bc, rows,
                  nrm_view=lambda k: dbuf[:, k, 0:fs])
            for m in range(8):
                for k in range(16):
                    P.t(mm(ps_mm[:, 0:fs], wo[:, k, m * 128:(m + 1) * 128],
                           ygc[:, k, 0:fs], k == 0, k == 15))
                P.v(tt(bobuf[:, m, 0:fs], ps_mm[:, 0:fs], dbuf[:, m, 0:fs],
                       ALU.add))
            fm_ln(P, lambda k: bobuf[:, k, 0:fs], fs, bobuf, sqb,
                  ones128, onesr, ps_row, ps_bc, rows,
                  nrm_view=lambda k: bobuf[:, k, 0:fs])
            for m in range(8):
                P.v(tt(xs[:, m, fo:fo + fs], bobuf[:, m, 0:fs],
                       xs[:, m, fo:fo + fs], ALU.add))
            P.dma(x1_d.rearrange("(b p) l -> p b l", p=128)[:, :, fo:fo + fs],
                  xs[:, :, fo:fo + fs])
        P.dmag(B["wq"], wq_d.rearrange("(b p) c -> p b c", p=128))
        P.v(mset(B["xbpc"][:, :, 0:1], 0.0))
        for ci, (fo, fs) in enumerate(scan_f(TSE, 512)):
            P.dma(B["tabc"][:, :, 0:fs],
                  tab_d[:, :, fo:fo + fs].transpose([1, 0, 2]))
            fm_ln(P, lambda k: xs[:, k, fo:fo + fs], fs, dbuf, sqb,
                  ones128, onesr, ps_row, ps_bc, rows,
                  nrm_view=lambda k: dbuf[:, k, 0:fs])
            for k in range(8):
                P.v(cp(B["xbpc"][:, k, 1:1 + fs], dbuf[:, k, 0:fs]))
            P.dma(xb_d.rearrange("(b p) l -> p b l", p=128)[:, :, fo:fo + fs],
                  dbuf[:, :, 0:fs])
            qkv_chunk(P, B, fo, fs, q_od, k_od, v_od)
        P.finalize(blk)
    return nc


def build_D(last):
    nc = bass.Bass("TRN2", target_bir_lowering=False, debug=False)
    qi_d = nc.dram_tensor("qi", [INNER, TSE], BF16, kind="ExternalInput").ap()
    ki_d = nc.dram_tensor("ki", [INNER, TSE], BF16, kind="ExternalInput").ap()
    vi_d = nc.dram_tensor("vTi", [TSE, INNER], BF16, kind="ExternalInput").ap()
    kt_d = nc.dram_tensor("ktail", [INNER, 768], BF16, kind="ExternalInput").ap()
    vt_d = nc.dram_tensor("vtail", [768, INNER], BF16, kind="ExternalInput").ap()
    mk_d = nc.dram_tensor("masks", [17, 128, STRIP], F32,
                          kind="ExternalInput").ap()
    xe_d = nc.dram_tensor("x1e", [D, TSE], F32, kind="ExternalInput").ap()
    wo_d = nc.dram_tensor("woA", [INNER, D], F32, kind="ExternalInput").ap()
    w1_d = nc.dram_tensor("w1f", [D, NH1], F32, kind="ExternalInput").ap()
    b1_d = nc.dram_tensor("b1p", [NH1], F32, kind="ExternalInput").ap()
    w2_d = nc.dram_tensor("w2f", [2816, D], F32, kind="ExternalInput").ap()
    b2_d = nc.dram_tensor("b2v", [D], F32, kind="ExternalInput").ap()
    lg_d = nc.dram_tensor("lgv", [D], F32, kind="ExternalInput").ap()
    lb_d = nc.dram_tensor("lbv", [D], F32, kind="ExternalInput").ap()
    em_d = nc.dram_tensor("extm", [128, 8, 128], F32, kind="ExternalInput").ap()
    id_d = nc.dram_tensor("ident", [128, 128], BF16, kind="ExternalInput").ap()
    x1o_d = nc.dram_tensor("x1o", [D, TSE], F32, kind="ExternalOutput").ap()
    x1b_d = nc.dram_tensor("x1b_scr", [D, TSE], BF16).ap()
    if not last:
        wq_d = nc.dram_tensor("wqkv", [D, 3 * INNER], F32,
                              kind="ExternalInput").ap()
        qs_d = nc.dram_tensor("qsr", [128, 1], F32, kind="ExternalInput").ap()
        ks_d = nc.dram_tensor("ksr", [128, 1], F32, kind="ExternalInput").ap()
        tab_d = nc.dram_tensor("tabs", [4, 128, TSE], F32,
                               kind="ExternalInput").ap()
        oneh_d = nc.dram_tensor("oneh", [1, 2, 128], F32,
                                kind="ExternalInput").ap()
        rot_d = nc.dram_tensor("rotm", [128, 128], F32,
                               kind="ExternalInput").ap()
        q_od = nc.dram_tensor("qo1", [INNER, TSE], BF16,
                              kind="ExternalOutput").ap()
        k_od = nc.dram_tensor("ko1", [INNER, TSE], BF16,
                              kind="ExternalOutput").ap()
        v_od = nc.dram_tensor("vTo1", [TSE, INNER], BF16,
                              kind="ExternalOutput").ap()
    else:
        x0_d = nc.dram_tensor("x1own", [D, TSH], F32,
                              kind="ExternalInput").ap()
        i32_d = nc.dram_tensor("iden32", [128, 128], F32,
                               kind="ExternalInput").ap()
        out_d = nc.dram_tensor("outp", [TSH, D], F32,
                               kind="ExternalOutput").ap()
        xf_d = nc.dram_tensor("xf_scr", [D, TSE], F32).ap()

    with ExitStack() as st:
        P = Prog(nc, st)
        Consts(P, nc, st)
        xm = st.enter_context(nc.sbuf_tensor("xm", [128, 8, TSE + 1], F32))
        qbt = st.enter_context(nc.sbuf_tensor("qbt", [128, 4, 128], BF16))
        kstr = st.enter_context(nc.sbuf_tensor("kstr", [128, 4, STRIP], BF16))
        vstr = st.enter_context(nc.sbuf_tensor("vstr", [128, 5, 512], BF16))
        maskc = st.enter_context(nc.sbuf_tensor("maskc", [128, STRIP], F32))
        simb = st.enter_context(nc.sbuf_tensor("simb", [128, STRIP], F32))
        pbuf = st.enter_context(nc.sbuf_tensor("pbuf", [128, STRIP], BF16))
        ptb = st.enter_context(nc.sbuf_tensor("ptb", [128, 5, 128], BF16))
        cols = st.enter_context(nc.sbuf_tensor("colsd", [128, 4], F32))
        obf = st.enter_context(nc.sbuf_tensor("obf", [128, 4, 128], BF16))
        rbuf = st.enter_context(nc.sbuf_tensor("rbuf", [128, 8, 512], F32))
        dbuf = st.enter_context(nc.sbuf_tensor("dbuf", [128, 8, 512], F32))
        sqb = st.enter_context(nc.sbuf_tensor("sqb", [128, 8, 512], F32))
        xbm = st.enter_context(nc.sbuf_tensor("xbm", [128, 8, 512], BF16))
        hbf = st.enter_context(nc.sbuf_tensor("hbf", [128, 22, 512], BF16))
        wmg = st.enter_context(nc.sbuf_tensor("wmg", [128, 8, 128], BF16))
        w2g = st.enter_context(nc.sbuf_tensor("w2g", [128, 22, 128], BF16))
        h1b = st.enter_context(nc.sbuf_tensor("h1b", [128, 512], F32))
        gb = st.enter_context(nc.sbuf_tensor("gb", [64, 512], F32))
        b1e = st.enter_context(nc.sbuf_tensor("b1e", [128, 44], F32))
        b1pv = st.enter_context(nc.sbuf_tensor("b1pv", [128, 44], F32))
        lgc = st.enter_context(nc.sbuf_tensor("lgc", [128, 8], F32))
        lbb = st.enter_context(nc.sbuf_tensor("lbb", [128, 8], BF16))
        b2c = st.enter_context(nc.sbuf_tensor("b2c", [128, 8], F32))
        extm = st.enter_context(nc.sbuf_tensor("extm_s", [128, 8, 128], F32))
        rows = st.enter_context(nc.sbuf_tensor("rows", [1, 2, 512], F32))
        ones128 = st.enter_context(nc.sbuf_tensor("ones128", [128, 1], F32))
        onesr = st.enter_context(nc.sbuf_tensor("onesr", [1, 128], F32))
        ident = st.enter_context(nc.sbuf_tensor("ident_s", [128, 128], BF16))
        wob = st.enter_context(nc.sbuf_tensor("wob", [128, 4, D], BF16))
        ps_sim = st.enter_context(nc.psum_tensor("ps_sim", [128, STRIP], F32))
        ps_pt = st.enter_context(nc.psum_tensor("ps_pt", [128, STRIP], BF16))
        ps_o = st.enter_context(nc.psum_tensor("ps_o", [128, 128], F32))
        ps_mm = st.enter_context(nc.psum_tensor("ps_mm", [128, 512], F32))
        ps_bc = st.enter_context(nc.psum_tensor("ps_bc", [128, 512], F32))
        ps_row = st.enter_context(nc.psum_tensor("ps_row", [1, 512], F32))
        ps_t = st.enter_context(nc.psum_tensor("ps_t", [128, 512], BF16))

        if not last:
            qsr = st.enter_context(nc.sbuf_tensor("qsr_s", [128, 1], F32))
            ksr = st.enter_context(nc.sbuf_tensor("ksr_s", [128, 1], F32))
            rotm = st.enter_context(nc.sbuf_tensor("rotm_s", [128, 128], F32))
            oneh = st.enter_context(nc.sbuf_tensor("oneh_s", [1, 2, 128], F32))
            wqm = st.enter_context(nc.sbuf_tensor("wqm", [128, 8, 128], BF16))
            B = alloc_qkv_bufs(nc, st, "qn", qf=dbuf[:, 0:4, :],
                               sqh=sqb[:, 0:4, :], vb=xbm[:, 0:4, :],
                               vtb=xbm[:, 4:8, :], rqr=rows,
                               t123=(rbuf[:, 4, :], rbuf[:, 5, :],
                                     rbuf[:, 6, :]),
                               qo=hbf[:, 0, 0:512], tabc=rbuf[:, 0:4, :])
            B.update(dict(ones128=ones128, onesr=onesr, oneh=oneh,
                          ident=ident, rotm=rotm, ps_mm=ps_mm, ps_bc=ps_bc,
                          ps_row=ps_row, ps_t=ps_t, qsr=qsr[:, 0:1],
                          ksr=ksr[:, 0:1], wq_d=wq_d, wqm=wqm))
        else:
            iden32 = st.enter_context(
                nc.sbuf_tensor("iden32_s", [128, 128], F32))
            otb = st.enter_context(nc.sbuf_tensor("otb", [128, D], F32))
        blk = st.enter_context(nc.Block())

        P.dma(xm[:, :, 1:TSE + 1], xe_d.rearrange("(b p) l -> p b l", p=128))
        P.v(mset(xm[:, :, 0:1], 0.0))
        P.dmag(wob[:, :, :], wo_d.rearrange("(b p) c -> p b c", p=128))
        P.v(mset(ones128[:, :], 1.0))
        P.v(mset(onesr[:, :], 1.0))
        P.dma(ident[:, :], id_d)
        P.dma(extm[:, :, :], em_d)
        vec_cols(P, lgc, lg_d, 8)
        vec_cols(P, lbb, lb_d, 8, cast=True)
        vec_cols(P, b2c, b2_d, 8)
        vec_cols(P, b1pv[:, 0:43], b1_d, 43)
        if not last:
            P.dma(qsr[:, :], qs_d)
            P.dma(ksr[:, :], ks_d)
            P.dma(rotm[:, :], rot_d)
            P.dma(oneh[:, :, :], oneh_d)
        else:
            P.dma(iden32[:, :], i32_d)

        for i in range(17):
            c0 = 128 * i + 128
            P.dma(maskc[:, :], mk_d[i, :, :])
            P.dma(qbt[:, :, :],
                  qi_d.rearrange("(b p) l -> p b l", p=128)
                  [:, :, 128 * i:128 * (i + 1)])
            nt = max(0, min(768 - c0, STRIP))
            if nt > 0:
                P.dma(kstr[:, :, 0:nt],
                      kt_d.rearrange("(b p) l -> p b l", p=128)
                      [:, :, c0:c0 + nt])
            if nt < STRIP:
                o0 = c0 + nt - 768 + 128
                P.dma(kstr[:, :, nt:STRIP],
                      ki_d.rearrange("(b p) l -> p b l", p=128)
                      [:, :, o0:o0 + STRIP - nt])
            for j in range(5):
                t0 = c0 + 128 * j
                if t0 < 768:
                    P.dma(vstr[:, j, :],
                          vt_d[t0:t0 + 128, :].rearrange("(a p) c -> p a c",
                                                         p=128)[:, 0, :])
                else:
                    o0 = t0 - 768 + 128
                    P.dma(vstr[:, j, :],
                          vi_d[o0:o0 + 128, :].rearrange("(a p) c -> p a c",
                                                         p=128)[:, 0, :])
            for h in range(8):
                tl, b = h // 2, 64 * (h % 2)
                P.t(mm(ps_sim[:, 0:512], qbt[b:b + 64, tl, :],
                       kstr[b:b + 64, tl, 0:512], True, True))
                P.t(mm(ps_sim[:, 512:STRIP], qbt[b:b + 64, tl, :],
                       kstr[b:b + 64, tl, 512:STRIP], True, True))
                P.v(tt(simb[:, :], ps_sim[:, :], maskc[:, :], ALU.add))
                P.v(red(cols[:, 0:1], simb[:, :], ALU.max, negate=True))
                P.s(act(pbuf[:, :], simb[:, :], AF.Exp, bias=cols[:, 0:1],
                        accum=cols[:, 1:2]))
                P.v(rcp(cols[:, 2:3], cols[:, 1:2]))
                P.v(ts(pbuf[:, :], pbuf[:, :], cols[:, 2:3], None, ALU.mult))
                for j in range(5):
                    P.t(trans(ps_pt[:, 128 * j:128 * (j + 1)],
                              pbuf[:, 128 * j:128 * (j + 1)], ident[:, :]))
                for j in range(5):
                    P.s(act(ptb[:, j, :], ps_pt[:, 128 * j:128 * (j + 1)],
                            AF.Copy))
                for j in range(5):
                    P.t(mm(ps_o[0:64, :], vstr[:, j, 64 * h:64 * h + 64],
                           ptb[:, j, :], j == 0, j == 4))
                P.s(act(obf[b:b + 64, tl, :], ps_o[0:64, :], AF.Copy))
            for m in range(8):
                for k in range(4):
                    P.t(mm(ps_mm[:, 0:128], wob[:, k, m * 128:(m + 1) * 128],
                           obf[:, k, :], k == 0, k == 3))
                P.v(tt(xm[:, m, 1 + 128 * i:1 + 128 * (i + 1)],
                       ps_mm[:, 0:128],
                       xm[:, m, 1 + 128 * i:1 + 128 * (i + 1)], ALU.add))
        for ci, (fo, fs) in enumerate(scan_f(TSE, 512)):
            def xv(k, fo=fo, fs=fs):
                return (xm[:, k, 1 + fo:1 + fo + fs] if k < 4
                        else xm[:, k, fo:fo + fs])
            fm_ln(P, xv, fs, dbuf, sqb, ones128, onesr, ps_row, ps_bc, rows,
                  nrm_view=None)
            for k in range(8):
                P.v(tt(xbm[:, k, 0:fs], dbuf[:, k, 0:fs], ps_bc[:, 0:fs],
                       ALU.mult))
            P.v(mset(hbf[64:128, 21, 0:fs], 0.0))
            for mg in range(43):
                P.dmag(wmg[:, :, :],
                       w1_d.rearrange("(b p) c -> p b c", p=128)
                       [:, :, mg * 128:(mg + 1) * 128])
                if ci == 0:
                    for k in range(8):
                        P.t(mm(ps_o[:, 0:1], wmg[:, k, :], lbb[:, k:k + 1],
                               k == 0, k == 7))
                    P.s(act(b1e[:, mg:mg + 1], ps_o[:, 0:1], AF.Copy))
                    P.v(tt(b1e[:, mg:mg + 1], b1e[:, mg:mg + 1],
                           b1pv[:, mg:mg + 1], ALU.add))
                for k in range(8):
                    P.v(ts(wmg[:, k, :], wmg[:, k, :], lgc[:, k:k + 1], None,
                           ALU.mult))
                for k in range(8):
                    P.t(mm(ps_mm[:, 0:fs], wmg[:, k, :], xbm[:, k, 0:fs],
                           k == 0, k == 7))
                P.s(act(h1b[:, 0:fs], ps_mm[:, 0:fs], AF.Identity,
                        bias=b1e[:, mg:mg + 1]))
                P.s(act(gb[:, 0:fs], h1b[64:128, 0:fs], AF.Gelu))
                P.v(tt(hbf[64 * (mg % 2):64 * (mg % 2) + 64, mg // 2, 0:fs],
                       h1b[0:64, 0:fs], gb[:, 0:fs], ALU.mult))
            for m in range(8):
                P.dmag(w2g[:, :, :],
                       w2_d.rearrange("(b p) c -> p b c", p=128)
                       [:, :, m * 128:(m + 1) * 128])
                for k in range(22):
                    P.t(mm(ps_mm[:, 0:fs], w2g[:, k, :], hbf[:, k, 0:fs],
                           k == 0, k == 21))
                P.s(act(rbuf[:, m, 0:fs], ps_mm[:, 0:fs], AF.Identity,
                        bias=b2c[:, m:m + 1]))
                P.v(tt(rbuf[:, m, 0:fs], rbuf[:, m, 0:fs],
                       xm[:, m, 1 + fo:1 + fo + fs], ALU.add))
            if fo < EXT:
                for m in range(8):
                    P.v(tt(rbuf[:, m, 0:EXT], rbuf[:, m, 0:EXT],
                           extm[:, m, :], ALU.mult))
            P.dma(x1o_d.rearrange("(b p) l -> p b l", p=128)[:, :, fo:fo + fs],
                  rbuf[:, :, 0:fs])
            if not last:
                P.dmag(x1b_d.rearrange("(b p) l -> p b l", p=128)
                       [:, :, fo:fo + fs], rbuf[:, :, 0:fs])
            else:
                P.dma(xf_d.rearrange("(b p) l -> p b l", p=128)
                      [:, :, fo:fo + fs], rbuf[:, :, 0:fs])
        if not last:
            for ci, (fo, fs) in enumerate(scan_f(TSE, 512)):
                P.dma(B["tabc"][:, :, 0:fs],
                      tab_d[:, :, fo:fo + fs].transpose([1, 0, 2]))
                if fo == 0:
                    P.dma(B["xbpc"][:, :, 1:1 + fs],
                          x1b_d.rearrange("(b p) l -> p b l", p=128)
                          [:, :, 0:fs])
                    P.v(mset(B["xbpc"][:, :, 0:1], 0.0))
                else:
                    P.dma(B["xbpc"][:, :, 0:1 + fs],
                          x1b_d.rearrange("(b p) l -> p b l", p=128)
                          [:, :, fo - 1:fo + fs])
                qkv_chunk(P, B, fo, fs, q_od, k_od, v_od)
        else:
            for fo, fs in scan_f(TSH, 512):
                P.dma(rbuf[:, :, 0:fs],
                      xf_d.rearrange("(b p) l -> p b l", p=128)
                      [:, :, EXT + fo:EXT + fo + fs])
                fm_ln(P, lambda k, fs=fs: rbuf[:, k, 0:fs], fs, dbuf, sqb,
                      ones128, onesr, ps_row, ps_bc, rows,
                      nrm_view=lambda k, fs=fs: dbuf[:, k, 0:fs])
                P.dma(sqb[:, :, 0:fs],
                      x0_d.rearrange("(b p) l -> p b l", p=128)
                      [:, :, fo:fo + fs])
                for m in range(8):
                    P.v(tt(dbuf[:, m, 0:fs], dbuf[:, m, 0:fs],
                           sqb[:, m, 0:fs], ALU.add))
                for j in range(fs // 128):
                    for m in range(8):
                        P.t(mm(ps_mm[:, 0:128], iden32[:, :],
                               dbuf[:, m, 128 * j:128 * (j + 1)],
                               True, True) if False else
                            trans(ps_mm[:, 0:128],
                                  dbuf[:, m, 128 * j:128 * (j + 1)],
                                  iden32[:, :]))
                        P.s(act(otb[:, m * 128:(m + 1) * 128],
                                ps_mm[:, 0:128], AF.Copy))
                    P.dma(out_d[fo + 128 * j:fo + 128 * (j + 1), :],
                          otb[:, :])
        P.finalize(blk)
    return nc


def rotary_tables():
    half = np.arange(0, DH, 2, dtype=np.float64)
    inv_freq = 1.0 / 10000.0 ** (half / DH)
    t = np.arange(T, dtype=np.float64)
    freqs = t[:, None] * inv_freq[None, :]
    freqs = np.concatenate([freqs, freqs], -1)
    base = (half + 0.4 * DH) / (1.4 * DH)
    power = (t - T // 2) / (WIN // 2)
    scale = base[None, :] ** power[:, None]
    scale = np.concatenate([scale, scale], -1)
    return np.cos(freqs), np.sin(freqs), scale


def tabs_for(tok0):
    cos, sin, scale = rotary_tables()
    idx = np.clip(np.arange(tok0, tok0 + TSE), 0, T - 1)
    sgn = np.where(np.arange(64) % 64 < 32, -1.0, 1.0)

    def dbl(tab):
        t_ = tab[idx].T.astype(np.float32)
        return np.concatenate([t_, t_], 0)
    csq = dbl(cos * scale) * QKS
    snq = dbl(sin * scale * sgn[None, :]) * QKS
    csk = dbl(cos / scale)
    snk = dbl(sin / scale * sgn[None, :])
    return np.ascontiguousarray(np.stack([csq, snq, csk, snk]))


def rotm_host():
    r = np.zeros((128, 128), np.float32)
    for d_ in range(128):
        rr = (d_ % 64 + 32) % 64 + 64 * (d_ // 64)
        r[rr, d_] = 1.0
    return r


def oneh_host():
    o = np.zeros((1, 2, 128), np.float32)
    o[0, 0, 0:64] = 1.0
    o[0, 1, 64:128] = 1.0
    return o


def masks_for(core):
    t0 = core * TSH - EXT
    out = np.empty((17, 128, STRIP), np.float32)
    for i in range(17):
        q0 = t0 + 128 * i
        q = q0 + np.arange(128)[:, None]
        k = q0 - WIN + np.arange(STRIP)[None, :]
        bad = (k > q) | (k < q - WIN) | (k < 0)
        m = np.where(bad, NEG, np.float32(0.0)).astype(np.float32)
        dead = (q[:, 0] < 0)
        m[dead, :] = NEG
        m[dead, 0] = 0.0
        out[i] = m
    return np.ascontiguousarray(out)


def permute_w1(w1, b1):
    a, g = w1[:, :HID], w1[:, HID:]
    ab, gb = b1[:HID], b1[HID:]
    ap_ = np.concatenate([a, np.zeros((D, HIDP - HID), w1.dtype)], 1)
    gp = np.concatenate([g, np.zeros((D, HIDP - HID), w1.dtype)], 1)
    abp = np.concatenate([ab, np.zeros(HIDP - HID, b1.dtype)])
    gbp = np.concatenate([gb, np.zeros(HIDP - HID, b1.dtype)])
    cols, bcols = [], []
    for j in range(HIDP // 64):
        cols += [ap_[:, j * 64:(j + 1) * 64], gp[:, j * 64:(j + 1) * 64]]
        bcols += [abp[j * 64:(j + 1) * 64], gbp[j * 64:(j + 1) * 64]]
    return (np.ascontiguousarray(np.concatenate(cols, 1)),
            np.ascontiguousarray(np.concatenate(bcols)))


def permute_w2(w2):
    return np.ascontiguousarray(
        np.concatenate([w2, np.zeros((2816 - HID, D), w2.dtype)], 0))


def _cc(a):
    return np.ascontiguousarray(a)


def kernel_unfused(**inputs):
    f32 = np.float32
    x = np.asarray(inputs["L0_attn_x"], f32)[0]
    w_in = np.asarray(inputs["L0_attn_w_in"], f32)
    ident = np.eye(128).astype(NPBF)
    iden32 = np.eye(128, dtype=f32)
    oneh = oneh_host()
    rotm = rotm_host()

    nc = build_A()
    maps = [dict(xa=_cc(x[c * TSH:(c + 1) * TSH]), ident=ident)
            for c in range(NC)]
    res = run_bass_kernel_spmd(nc, maps, core_ids=list(range(NC)))
    xn_full = np.concatenate([np.asarray(r["xn"]) for r in res.results], 1)

    nc = build_B0()
    maps = []
    for c in range(NC):
        sl = slice(c * ES, (c + 1) * ES)
        w_sl = np.concatenate([w_in[sl], w_in[E + c * ES:E + (c + 1) * ES]], 0)
        maps.append(dict(
            xnf=_cc(xn_full),
            w_in_slT=_cc(w_sl.T),
            lnw=np.asarray(inputs["L0_attn_ln_w"], f32),
            lnb=np.asarray(inputs["L0_attn_ln_b"], f32),
            cw=_cc(np.asarray(inputs["L0_attn_conv_w"], f32)[sl]),
            cb=_cc(np.asarray(inputs["L0_attn_conv_b"], f32)[sl]),
            wxpT=_cc(np.asarray(inputs["L0_attn_w_xproj"], f32)[:, sl].T),
        ))
    res = run_bass_kernel_spmd(nc, maps, core_ids=list(range(NC)))
    u32 = [np.asarray(r["u32"]) for r in res.results]
    z32 = [np.asarray(r["z32"]) for r in res.results]
    xdblp = np.stack([np.asarray(r["xdblp"]) for r in res.results])

    nc = build_B1()
    maps = []
    for c in range(NC):
        sl = slice(c * ES, (c + 1) * ES)
        maps.append(dict(
            u32=u32[c], z32=z32[c], xdblp=xdblp,
            wdtT=_cc(np.asarray(inputs["L0_attn_w_dt"], f32)[sl].T),
            bdt=_cc(np.asarray(inputs["L0_attn_b_dt"], f32)[sl]),
            alog=_cc(np.asarray(inputs["L0_attn_a_log"], f32)[sl]),
            dskip=_cc(np.asarray(inputs["L0_attn_d_skip"], f32)[sl]),
        ))
    res = run_bass_kernel_spmd(nc, maps, core_ids=list(range(NC)))
    yg_full = np.concatenate([np.asarray(r["yg"]) for r in res.results], 0)

    def ext_sliceT(full_T_first):
        outs = []
        for c in range(NC):
            t0 = c * TSH - EXT
            o = np.zeros((TSE,) + full_T_first.shape[1:], full_T_first.dtype)
            lo = max(t0, 0)
            o[lo - t0:] = full_T_first[lo:t0 + TSE]
            outs.append(_cc(o.T))
        return outs

    qs0 = np.asarray(inputs["L0_ffn_q_scale"], f32)
    ks0 = np.asarray(inputs["L0_ffn_k_scale"], f32)
    xe_s = ext_sliceT(x)
    yge_s = ext_sliceT(yg_full.T)
    nc = build_C()
    maps = []
    for c in range(NC):
        t0 = c * TSH - EXT
        maps.append(dict(
            xe=xe_s[c],
            yge=_cc(yge_s[c].astype(NPBF)),
            woutT=_cc(np.asarray(inputs["L0_attn_w_out"], f32).T),
            wqkv=_cc(np.asarray(inputs["L0_ffn_wqkv"], f32)[0]),
            qsr=_cc(np.tile(qs0[0], 2)[:, None]),
            ksr=_cc(np.tile(ks0[0], 2)[:, None]),
            tabs=tabs_for(t0), oneh=oneh, rotm=rotm, ident=ident,
        ))
    res = run_bass_kernel_spmd(nc, maps, core_ids=list(range(NC)))
    x1e = [np.asarray(r["x1e"]) for r in res.results]
    xbe = [np.asarray(r["xbe"]) for r in res.results]
    q0 = [np.asarray(r["q0"]) for r in res.results]
    k0 = [np.asarray(r["k0"]) for r in res.results]
    v0 = [np.asarray(r["vT0"]) for r in res.results]

    def run_layer(last, stream, qs_, ks_, vs_, layer, x1own=None):
        nc = build_D(last)
        w1f, b1p = permute_w1(np.asarray(inputs["L0_ffn_w1"], f32)[layer],
                              np.asarray(inputs["L0_ffn_b1"], f32)[layer])
        w2f = permute_w2(np.asarray(inputs["L0_ffn_w2"], f32)[layer])
        maps = []
        for c in range(NC):
            t0 = c * TSH - EXT
            nb = c - 1
            if nb >= 0:
                ktail = _cc(qs_[nb][0:0] if False else
                            ks_[nb][:, 1408:2176])
                vtail = _cc(vs_[nb][1408:2176, :])
            else:
                ktail = np.zeros((INNER, 768), NPBF)
                vtail = np.zeros((768, INNER), NPBF)
            m = dict(
                qi=qs_[c], ki=ks_[c], vTi=vs_[c],
                ktail=ktail, vtail=vtail,
                masks=masks_for(c),
                x1e=stream[c],
                woA=_cc(np.asarray(inputs["L0_ffn_wo"], f32)[layer]),
                w1f=w1f, b1p=b1p, w2f=w2f,
                b2v=_cc(np.asarray(inputs["L0_ffn_b2"], f32)[layer]),
                lgv=_cc(np.asarray(inputs["L0_ffn_ln_g"], f32)[layer]),
                lbv=_cc(np.asarray(inputs["L0_ffn_ln_b"], f32)[layer]),
                extm=np.full((128, 8, 128), 0.0 if c == 0 else 1.0, f32),
                ident=ident,
            )
            if not last:
                m.update(dict(
                    wqkv=_cc(np.asarray(inputs["L0_ffn_wqkv"], f32)[1]),
                    qsr=_cc(np.tile(qs0[1], 2)[:, None]),
                    ksr=_cc(np.tile(ks0[1], 2)[:, None]),
                    tabs=tabs_for(t0), oneh=oneh, rotm=rotm,
                ))
            else:
                m.update(dict(x1own=_cc(x1own[c][:, EXT:]), iden32=iden32))
            maps.append(m)
        return run_bass_kernel_spmd(nc, maps, core_ids=list(range(NC)))

    res = run_layer(False, xbe, q0, k0, v0, 0)
    x_1 = [np.asarray(r["x1o"]) for r in res.results]
    q1 = [np.asarray(r["qo1"]) for r in res.results]
    k1 = [np.asarray(r["ko1"]) for r in res.results]
    v1 = [np.asarray(r["vTo1"]) for r in res.results]

    res = run_layer(True, x_1, q1, k1, v1, 1, x1own=x1e)
    out = np.concatenate([np.asarray(r["outp"]) for r in res.results], 0)
    return out[None].astype(np.float32)


class Cursor:
    def __init__(self, tensor, limit):
        self.t = tensor
        self.limit = limit
        self.off = 0

    def take(self, *shape):
        n = int(np.prod(shape))
        off = self.off
        self.off += n
        assert self.off <= self.limit, f"arena overflow {self.off}>{self.limit}"
        v = self.t[:, off:off + n]
        if len(shape) == 1:
            return v
        return v.rearrange("p (a b) -> p a b", a=shape[0])

    def reset(self):
        self.off = 0


def build_FUSED():
    nc = bass.Bass("TRN2", target_bir_lowering=False, debug=False)
    nc.cache_partition_id()
    f32 = F32

    def ein(name, shape, dt=F32):
        return nc.dram_tensor(name, shape, dt, kind="ExternalInput").ap()

    xa_d = ein("xa", [TSH, D])
    xe_d = ein("xe", [D, TSE])
    w_d = ein("w_in_slT", [D, 2 * ES])
    lnw_d = ein("lnw", [D])
    lnb_d = ein("lnb", [D])
    cw_d = ein("cw", [ES, 4])
    cb_d = ein("cb", [ES])
    wxp_d = ein("wxpT", [ES, 96])
    wdt_d = ein("wdtT", [DTR, ES])
    bdt_d = ein("bdt", [ES])
    alog_d = ein("alog", [ES, NST])
    dsk_d = ein("dskip", [ES])
    wo_d = ein("woutT", [E, D])
    wq0_d = ein("wqkv0", [D, 3 * INNER])
    wq1_d = ein("wqkv1", [D, 3 * INNER])
    qs0_d = ein("qsr0", [128, 1])
    ks0_d = ein("ksr0", [128, 1])
    qs1_d = ein("qsr1", [128, 1])
    ks1_d = ein("ksr1", [128, 1])
    tab_d = ein("tabs", [4, 128, TSE])
    oneh_d = ein("oneh", [1, 2, 128])
    rot_d = ein("rotm", [128, 128])
    id_d = ein("ident", [128, 128], BF16)
    i32_d = ein("iden32", [128, 128])
    mk_d = ein("masks", [17, 128, STRIP])
    em_d = ein("extm", [128, 8, 128])
    idx_d = nc.dram_tensor("idxt", [128, IXN], mybir.dt.uint32,
                           kind="ExternalInput").ap()
    woA0_d = ein("woA0", [INNER, D])
    woA1_d = ein("woA1", [INNER, D])
    w1f0_d = ein("w1f0", [D, NH1])
    w1f1_d = ein("w1f1", [D, NH1])
    b1p0_d = ein("b1p0", [NH1])
    b1p1_d = ein("b1p1", [NH1])
    w2f0_d = ein("w2f0", [2816, D])
    w2f1_d = ein("w2f1", [2816, D])
    b2v0_d = ein("b2v0", [D])
    b2v1_d = ein("b2v1", [D])
    lgv0_d = ein("lgv0", [D])
    lgv1_d = ein("lgv1", [D])
    lbv0_d = ein("lbv0", [D])
    lbv1_d = ein("lbv1", [D])
    out_d = nc.dram_tensor("outp", [TSH, D], F32, kind="ExternalOutput").ap()

    xn_ai = nc.dram_tensor("xn_ai", [D, T], BF16).ap()
    xn_ao = nc.dram_tensor("xn_ao", [D, T], BF16).ap()
    xd_ai = nc.dram_tensor("xd_ai", [96, T], F32).ap()
    xd_ao = nc.dram_tensor("xd_ao", [96, T], F32).ap()
    y_ai = nc.dram_tensor("y_ai", [E, YW], BF16).ap()
    y_ao = nc.dram_tensor("y_ao", [E, YW], BF16).ap()
    kg_ai = [nc.dram_tensor(f"kg_ai{l}", [INNER, 9 * 768], BF16).ap()
             for l in range(2)]
    kg_ao = [nc.dram_tensor(f"kg_ao{l}", [INNER, 9 * 768], BF16).ap()
             for l in range(2)]
    vg_ai = [nc.dram_tensor(f"vg_ai{l}", [9 * 768, INNER], BF16).ap()
             for l in range(2)]
    vg_ao = [nc.dram_tensor(f"vg_ao{l}", [9 * 768, INNER], BF16).ap()
             for l in range(2)]
    kt_loc = [nc.dram_tensor(f"kt_loc{l}", [INNER, 768], BF16).ap()
              for l in range(2)]
    vt_loc = [nc.dram_tensor(f"vt_loc{l}", [768, INNER], BF16).ap()
              for l in range(2)]
    u_d = nc.dram_tensor("u32s", [ES, T], F32).ap()
    z_d = nc.dram_tensor("z32s", [ES, T], F32).ap()
    bc_d = nc.dram_tensor("bc_scr", [32, T], F32).ap()
    yg_d = None
    x1_d = nc.dram_tensor("x1s", [D, TSE], F32).ap()
    xb_d = nc.dram_tensor("xbs", [D, TSE], F32).ap()
    q_ds = [nc.dram_tensor(f"q{l}s", [INNER, TSE], BF16).ap() for l in range(2)]
    k_ds = [nc.dram_tensor(f"k{l}s", [INNER, TSE], BF16).ap() for l in range(2)]
    v_ds = [nc.dram_tensor(f"v{l}s", [TSE, INNER], BF16).ap() for l in range(2)]
    x1o_d = nc.dram_tensor("x1os", [D, TSE], F32).ap()
    x1b_d = nc.dram_tensor("x1bs", [D, TSE], BF16).ap()
    xf_d = nc.dram_tensor("xfs", [D, TSE], F32).ap()

    AF_EL = 34000
    AB_EL = 34000
    with ExitStack() as st:
        P = Prog(nc, st)
        Consts(P, nc, st)
        arf_t = st.enter_context(nc.sbuf_tensor("arena_f", [128, AF_EL], F32))
        arb_t = st.enter_context(nc.sbuf_tensor("arena_b", [128, AB_EL], BF16))
        ident = st.enter_context(nc.sbuf_tensor("ident_s", [128, 128], BF16))
        iden32 = st.enter_context(nc.sbuf_tensor("iden32_s", [128, 128], F32))
        rotm = st.enter_context(nc.sbuf_tensor("rotm_s", [128, 128], F32))
        oneh = st.enter_context(nc.sbuf_tensor("oneh_s", [1, 2, 128], F32))
        ones128 = st.enter_context(nc.sbuf_tensor("ones128", [128, 1], F32))
        onesr = st.enter_context(nc.sbuf_tensor("onesr", [1, 128], F32))
        colsml = st.enter_context(nc.sbuf_tensor("colsml", [128, 16], F32))
        carry = st.enter_context(nc.sbuf_tensor("carry", [128, 2, NST], F32))
        acol = st.enter_context(nc.sbuf_tensor("acol", [128, 2, NST], F32))
        smlA = st.enter_context(nc.sbuf_tensor("smlA", [128, 2, 8], F32))
        smlB = st.enter_context(nc.sbuf_tensor("smlB", [128, 2, 4], F32))
        lnbb = st.enter_context(nc.sbuf_tensor("lnbb", [128, 8], BF16))
        qsr = st.enter_context(nc.sbuf_tensor("qsr_s", [128, 2], F32))
        ksr = st.enter_context(nc.sbuf_tensor("ksr_s", [128, 2], F32))
        lgc = st.enter_context(nc.sbuf_tensor("lgc", [128, 8], F32))
        lbb = st.enter_context(nc.sbuf_tensor("lbb", [128, 8], BF16))
        b2c = st.enter_context(nc.sbuf_tensor("b2c", [128, 8], F32))
        b1e = st.enter_context(nc.sbuf_tensor("b1e", [128, 44], F32))
        b1pv = st.enter_context(nc.sbuf_tensor("b1pv", [128, 44], F32))
        ixs = st.enter_context(nc.sbuf_tensor("ixs", [128, IXN],
                                              mybir.dt.uint32))
        rows0 = st.enter_context(nc.sbuf_tensor("rows0", [1, 1024], F32))
        psA = st.enter_context(nc.psum_tensor("psA", [128, 3584], F32))
        ps_bf = st.enter_context(nc.psum_tensor("ps_bf", [128, 640], BF16))
        blk = st.enter_context(nc.Block())

        arf = Cursor(arf_t, AF_EL)
        arb = Cursor(arb_t, AB_EL)
        ps_mm = psA[:, 2048:2560]
        ps_bc = psA[:, 2560:3072]
        ps_row = psA[0:1, 1024:1536]
        ps_o = psA[:, 3072:3200]
        ps_sim = psA[:, 0:640]
        ps_t = ps_bf[:, 0:512]
        ps_pt = ps_bf[:, 0:640]

        def cc(ins_ap, outs_ap):
            def b(e):
                return e.collective_compute(
                    "AllReduce", ALU.add, replica_groups=[list(range(NC))],
                    ins=[ins_ap.opt()], outs=[outs_ap.opt()])
            P.op("gpsimd", b, tag="coll")

        P.dma(ident[:, :], id_d)
        P.dma(ixs[:, :], idx_d)
        P.dma(iden32[:, :], i32_d)
        P.dma(rotm[:, :], rot_d)
        P.dma(oneh[:, :, :], oneh_d)
        P.v(mset(ones128[:, :], 1.0))
        P.v(mset(onesr[:, :], 1.0))
        zrow = arb.take(16384)
        P.v(mset(zrow[:, :], 0.0))
        rows = rows0.ap().rearrange("p (a b) -> p a b", a=2)
        P.dma(qsr[:, 0:1], qs0_d)
        P.dma(qsr[:, 1:2], qs1_d)
        P.dma(ksr[:, 0:1], ks0_d)
        P.dma(ksr[:, 1:2], ks1_d)
        zv = lambda a_, b_: zrow[:, 0:a_ * b_].rearrange(
            "p (a b) -> p a b", a=a_)
        for j in range(0, T, 2048):
            P.dma(xn_ai.rearrange("(b p) l -> p b l", p=128)[:, :, j:j + 2048],
                  zv(8, 2048))
        for j in range(0, YW, 1024):
            P.dma(y_ai.rearrange("(b p) l -> p b l", p=128)[:, :, j:j + 1024],
                  zv(16, 1024))
        for l in range(2):
            for j in range(0, 9 * 768, 1728):
                P.dma(kg_ai[l].rearrange("(b p) l -> p b l", p=128)
                      [:, :, j:j + 1728], zv(4, 1728))
            for sl in range(0, 9, 3):
                P.dma(vg_ai[l][sl * 768:(sl + 3) * 768, :]
                      .rearrange("(a p) c -> p a c", p=128), zv(18, 512))
        arb.reset()
        x_s = arf.take(16, D)
        sqs = arf.take(D)
        xnb = arb.take(16, D)
        xna = arb.take(8, 2048)
        P.dma(x_s[:, :, :], xa_d.rearrange("(n p) d -> p n d", p=128))
        for i in range(16):
            xt = x_s[:, i, :]
            for rep in range(2):
                P.v(red(colsml[:, 0:1], xt, ALU.add))
                P.v(ts(colsml[:, 0:1], colsml[:, 0:1], 1.0 / D, None,
                       ALU.mult))
                P.v(ts(xt, xt, colsml[:, 0:1], None, ALU.subtract))
                P.s(act(sqs[:, :], xt, AF.Square, accum=colsml[:, 2:3]))
                P.s(act(colsml[:, 2:3], colsml[:, 2:3], AF.Ln, bias=EPS,
                        scale=1.0 / D))
                P.s(act(colsml[:, 2:3], colsml[:, 2:3], AF.Exp, scale=-0.5))
                if rep == 0:
                    P.v(ts(xt, xt, colsml[:, 2:3], None, ALU.mult))
                else:
                    P.v(ts(xnb[:, i, :], xt, colsml[:, 2:3], None, ALU.mult))
        for i in range(16):
            for dbq in range(2):
                for j in range(4):
                    db = dbq * 4 + j
                    P.t(trans(ps_t[:, j * 128:(j + 1) * 128],
                              xnb[:, i, db * 128:(db + 1) * 128], ident[:, :]))
                for j in range(4):
                    db = dbq * 4 + j
                    P.s(act(xna[:, db, 128 * i:128 * (i + 1)],
                            ps_t[:, j * 128:(j + 1) * 128], AF.Copy))
        xnai_f = xn_ai.rearrange("d (j l) -> (d j) l", l=TSH)
        for b in range(8):
            P.ind(xnai_f, ixs[:, IXA + b:IXA + b + 1], xna[:, b, :], None)
        cc(xn_ai, xn_ao)
        arf.reset()
        arb.reset()
        wt = arb.take(8, 512)
        xnc = arb.take(8, 2048)
        ub = arb.take(2, 2048)
        wxt = arb.take(2, 96)
        xs3 = arf.take(2, 2048 + 3)
        zb = arf.take(2, 2048)
        uc = arf.take(2, 2048)
        lnw_s = smlA[:, :, :].rearrange("p a b -> p (a b)")
        bias_s = smlB[:, 0, :]
        cw_s = arf.take(2, 4)
        cb_s = colsml[:, 8:10]
        P.dmag(wt[:, :, :], w_d.rearrange("(b p) r -> p b r", p=128))
        vec_cols(P, lnw_s, lnw_d, 8)
        vec_cols(P, lnbb, lnb_d, 8, cast=True)
        P.dma(cw_s[:, :, :], cw_d.rearrange("(b p) k -> p b k", p=128))
        vec_cols(P, cb_s, cb_d, 2)
        P.dmag(wxt[:, :, :], wxp_d.rearrange("(b p) r -> p b r", p=128))
        for mg in range(4):
            for k in range(8):
                P.t(mm(psA[0:128, 3200 + mg:3200 + mg + 1],
                       wt[:, k, mg * 128:(mg + 1) * 128],
                       lnbb[:, k:k + 1], k == 0, k == 7))
        P.s(act(bias_s[:, 0:4], psA[0:128, 3200:3204], AF.Copy))
        for db in range(8):
            P.v(ts(wt[:, db, :], wt[:, db, :], lnw_s[:, db:db + 1], None,
                   ALU.mult))
        P.v(mset(xs3[:, :, 0:3], 0.0))
        LC0 = 2048
        for c in range(T // LC0):
            P.dma(xnc[:, :, :],
                  xn_ao.rearrange("(b p) l -> p b l", p=128)
                  [:, :, c * LC0:(c + 1) * LC0])
            for m in range(4):
                for fo, fs in scan_f(LC0, 512):
                    for k in range(8):
                        P.t(mm(ps_mm[:, 0:fs], wt[:, k, m * 128:(m + 1) * 128],
                               xnc[:, k, fo:fo + fs], k == 0, k == 7))
                    if m < 2:
                        P.s(act(xs3[:, m, 3 + fo:3 + fo + fs], ps_mm[:, 0:fs],
                                AF.Identity, bias=bias_s[:, m:m + 1]))
                    else:
                        P.s(act(zb[:, m - 2, fo:fo + fs], ps_mm[:, 0:fs],
                                AF.Identity, bias=bias_s[:, m:m + 1]))
            for tb in range(2):
                P.v(ts(uc[:, tb, :], xs3[:, tb, 0:LC0], cw_s[:, tb, 0:1],
                       cb_s[:, tb:tb + 1], ALU.mult, ALU.add))
                for k in range(1, 4):
                    P.v(stt(uc[:, tb, :], xs3[:, tb, k:k + LC0],
                            cw_s[:, tb, k:k + 1], uc[:, tb, :],
                            ALU.mult, ALU.add))
                P.v(cp(xs3[:, tb, 0:3], xs3[:, tb, LC0:LC0 + 3]))
                P.s(act(uc[:, tb, :], uc[:, tb, :], AF.Silu))
                P.v(cp(ub[:, tb, :], uc[:, tb, :]))
            P.dma(u_d.rearrange("(b p) l -> p b l", p=128)
                  [:, :, c * LC0:(c + 1) * LC0], uc[:, :, :])
            P.dma(z_d.rearrange("(b p) l -> p b l", p=128)
                  [:, :, c * LC0:(c + 1) * LC0], zb[:, :, :])
            for fo, fs in scan_f(LC0, 512):
                for k in range(2):
                    P.t(mm(ps_mm[0:96, 0:fs], wxt[:, k, :],
                           ub[:, k, fo:fo + fs], k == 0, k == 1))
                P.s(act(zb[0:96, 0, fo:fo + fs], ps_mm[0:96, 0:fs], AF.Copy))
            P.dma(xd_ai[:, c * LC0:(c + 1) * LC0], zb[0:96, 0, 0:LC0])
        cc(xd_ai, xd_ao)
        arf.reset()
        arb.reset()

        wdtt0 = arb.take(2, 128)
        dtrb = arb.take(1, 1024)
        ygb = arb.take(1, 1024)
        xds = arf.take(1, 1024)
        dtpre = arf.take(2, 1024)
        spt = arf.take(1, 1024)
        dtb = arf.take(2, 1024)
        uch = arf.take(2, 1024)
        zch = arf.take(2, 1024)
        dtu = arf.take(2, 1024)
        bcf = arf.take(2 * 1024)
        dA = arf.take(2 * 1024)
        dBu = arf.take(2 * 1024)
        hb = arf.take(2 * 1024)
        yb4 = arf.take(8, 1024)
        yb = arf.take(1, 1024)
        fixc = colsml[:, 10:12]
        bdt_s = smlB[:, 1, 0:2]
        dsk_s = smlB[:, 1, 2:4]
        P.dmag(wdtt0.rearrange("p a b -> p (a b)")[0:64, :]
               .rearrange("r (b pp) -> r b pp", pp=128),
               wdt_d.rearrange("r (b p) -> r b p", p=128))
        wdtt = wdtt0.rearrange("p a b -> p (a b)")[0:64, :] \
            .rearrange("r (b pp) -> r b pp", pp=128)
        vec_cols(P, bdt_s, bdt_d, 2)
        P.dma(acol[:, :, :], alog_d.rearrange("(b p) n -> p b n", p=128))
        vec_cols(P, dsk_s, dsk_d, 2)
        P.v(mset(carry[:, :, :], 0.0))
        for tb in range(2):
            P.s(act(acol[:, tb, :], acol[:, tb, :], AF.Exp))
            P.v(ts(acol[:, tb, :], acol[:, tb, :], -1.0, None, ALU.mult))
        NG = 2
        LC1 = 1024
        for c in range(T // LC1):
            lo = c * LC1
            P.dma(xds[0:96, 0, :], xd_ao[:, lo:lo + LC1])
            P.dma(bc_d[:, lo:lo + LC1],
                  xds.rearrange("p a b -> p (a b)")[64:96, 0:LC1])
            P.v(cp(dtrb[:, 0, :][0:64, :],
                   xds.rearrange("p a b -> p (a b)")[0:64, 0:LC1]))
            for mg in range(2):
                for fo, fs in scan_f(LC1, 512):
                    P.t(mm(ps_mm[0:128, 0:fs], wdtt[:, mg, :],
                           dtrb[:, 0, :][0:64, fo:fo + fs], True, True))
                    P.s(act(dtpre[:, mg, fo:fo + fs], ps_mm[0:128, 0:fs],
                            AF.Identity, bias=bdt_s[:, mg:mg + 1]))
            for tb in range(2):
                P.s(act(spt[:, 0, :], dtpre[:, tb, :], AF.Abs))
                P.s(act(spt[:, 0, :], spt[:, 0, :], AF.Exp, scale=-1.0))
                P.s(act(spt[:, 0, :], spt[:, 0, :], AF.Ln, bias=1.0))
                P.s(act(dtb[:, tb, :], dtpre[:, tb, :], AF.Relu))
                P.v(tt(dtb[:, tb, :], dtb[:, tb, :], spt[:, 0, :], ALU.add))
            P.dma(uch[:, :, :], u_d.rearrange("(b p) l -> p b l", p=128)
                  [:, :, lo:lo + LC1])
            P.dma(zch[:, :, :], z_d.rearrange("(b p) l -> p b l", p=128)
                  [:, :, lo:lo + LC1])
            for tb in range(2):
                P.v(tt(dtu[:, tb, :], dtb[:, tb, :], uch[:, tb, :], ALU.mult))
            for tb in range(2):
                for g in range(NST // NG):
                    P.dma(bcf.rearrange("p (n l) -> p n l", n=NG)[0:1, :, :],
                          bc_d[g * NG:(g + 1) * NG, lo:lo + LC1][None, :, :])
                    for fo, fs in scan_f(NG * LC1, 512):
                        P.t(mm(psA[:, fo:fo + fs], onesr[0:1, :],
                               bcf[0:1, fo:fo + fs], True, True))
                    for n in range(NG):
                        P.s(act(dA[:, n * LC1:(n + 1) * LC1], dtb[:, tb, :],
                                AF.Exp,
                                scale=acol[:, tb, g * NG + n:g * NG + n + 1]))
                    P.v(tt(dBu.rearrange("p (n l) -> p n l", n=NG),
                           dtu[:, tb, None, :].broadcast_to([128, NG, LC1]),
                           psA[:, 0:NG * LC1].rearrange("p (n l) -> p n l",
                                                        n=NG),
                           ALU.mult))
                    P.v(tt(fixc[:, 0:NG],
                           dA.rearrange("p (n l) -> p n l", n=NG)[:, :, 0],
                           carry[:, tb, g * NG:(g + 1) * NG], ALU.mult))
                    P.v(tt(dBu.rearrange("p (n l) -> p n l", n=NG)[:, :, 0],
                           dBu.rearrange("p (n l) -> p n l", n=NG)[:, :, 0],
                           fixc[:, 0:NG], ALU.add))
                    P.v(mset(dA.rearrange("p (n l) -> p n l",
                                          n=NG)[:, :, 0:1], 0.0))
                    P.v(tts(hb[:, :], dA[:, :], dBu[:, :], 0.0))
                    P.v(cp(carry[:, tb, g * NG:(g + 1) * NG],
                           hb.rearrange("p (n l) -> p n l",
                                        n=NG)[:, :, LC1 - 1]))
                    P.dma(bcf.rearrange("p (n l) -> p n l", n=NG)[0:1, :, :],
                          bc_d[16 + g * NG:16 + (g + 1) * NG,
                               lo:lo + LC1][None, :, :])
                    for fo, fs in scan_f(NG * LC1, 512):
                        P.t(mm(psA[:, fo:fo + fs], onesr[0:1, :],
                               bcf[0:1, fo:fo + fs], True, True))
                    P.v(tt(dA[:, :], hb[:, :], psA[:, 0:NG * LC1], ALU.mult))
                    P.v(red(yb4[:, g, :],
                            dA.rearrange("p (n l) -> p l n", n=NG),
                            ALU.add))
                P.v(red(yb[:, 0, :], yb4.rearrange("p g l -> p l g"),
                        ALU.add))
                P.v(ts(spt[:, 0, :], uch[:, tb, :], dsk_s[:, tb:tb + 1],
                       None, ALU.mult))
                P.v(tt(yb[:, 0, :], yb[:, 0, :], spt[:, 0, :], ALU.add))
                P.s(act(spt[:, 0, :], zch[:, tb, :], AF.Silu))
                P.v(tt(ygb[:, 0, :], yb[:, 0, :], spt[:, 0, :], ALU.mult))
                P.ind(y_ai.rearrange("r (j l) -> (r j) l", l=1024),
                      ixs[:, IXB + c * 2 + tb:IXB + c * 2 + tb + 1],
                      ygb[:, 0, :], None)
        cc(y_ai, y_ao)
        arf.reset()
        arb.reset()
        xs = arf.take(8, TSE)
        dbuf = arf.take(8, 512)
        bobuf = arf.take(8, 512)
        sqb = arf.take(8, 512)
        t1 = arf.take(512)
        t2 = arf.take(512)
        t3 = arf.take(512)
        tabc = arf.take(4, 512)
        ygc = arb.take(16, 512)
        warena = arb.take(16384)
        xbpc = arb.take(8, 513)
        vbb = arb.take(4, 512)
        vtbb = arb.take(4, 512)
        qob = arb.take(512)
        B = dict(xbpc=xbpc, qf=dbuf[:, 0:4, :], sqh=sqb[:, 0:4, :],
                 vb=vbb, vtb=vtbb, rqr=rows, t1=t1, t2=t2, t3=t3, qo=qob,
                 tabc=tabc, ones128=ones128, onesr=onesr, oneh=oneh,
                 ident=ident, rotm=rotm, ps_mm=ps_mm, ps_bc=ps_bc,
                 ps_row=ps_row, ps_t=ps_t,
                 wq=warena[:, 0:12288].rearrange("p (b c) -> p b c", b=8))
        wo = warena.rearrange("p (b c) -> p b c", b=16)
        P.dma(xs[:, :, :], xe_d.rearrange("(b p) l -> p b l", p=128))
        P.dmag(wo, wo_d.rearrange("(b p) c -> p b c", p=128))
        y36 = y_ao.rearrange("r (j l) -> (r j) l", l=512)
        y144 = y_ao.rearrange("r (j l) -> (r j) l", l=128)
        for ci, (fo, fs) in enumerate(CHK_C):
            for k in range(16):
                P.ind(ygc[:, k, 0:fs], None, y144 if ci == 0 else y36,
                      ixs[:, IXC + ci * 16 + k:IXC + ci * 16 + k + 1])
            fm_ln(P, lambda k: xs[:, k, fo:fo + fs], fs, dbuf, sqb,
                  ones128, onesr, ps_row, ps_bc, rows,
                  nrm_view=lambda k: dbuf[:, k, 0:fs])
            for m in range(8):
                for k in range(16):
                    P.t(mm(ps_mm[:, 0:fs], wo[:, k, m * 128:(m + 1) * 128],
                           ygc[:, k, 0:fs], k == 0, k == 15))
                P.v(tt(bobuf[:, m, 0:fs], ps_mm[:, 0:fs], dbuf[:, m, 0:fs],
                       ALU.add))
            fm_ln(P, lambda k: bobuf[:, k, 0:fs], fs, bobuf, sqb,
                  ones128, onesr, ps_row, ps_bc, rows,
                  nrm_view=lambda k: bobuf[:, k, 0:fs])
            for m in range(8):
                P.v(tt(xs[:, m, fo:fo + fs], bobuf[:, m, 0:fs],
                       xs[:, m, fo:fo + fs], ALU.add))
            P.dma(x1_d.rearrange("(b p) l -> p b l", p=128)[:, :, fo:fo + fs],
                  xs[:, :, fo:fo + fs])
        P.dmag(B["wq"], wq0_d.rearrange("(b p) c -> p b c", p=128))
        P.v(mset(B["xbpc"][:, :, 0:1], 0.0))
        B["qsr"] = qsr[:, 0:1]
        B["ksr"] = ksr[:, 0:1]
        for ci, (fo, fs) in enumerate(CHK_C):
            P.dma(B["tabc"][:, :, 0:fs],
                  tab_d[:, :, fo:fo + fs].transpose([1, 0, 2]))
            fm_ln(P, lambda k: xs[:, k, fo:fo + fs], fs, dbuf, sqb,
                  ones128, onesr, ps_row, ps_bc, rows,
                  nrm_view=lambda k: dbuf[:, k, 0:fs])
            for k in range(8):
                P.v(cp(B["xbpc"][:, k, 1:1 + fs], dbuf[:, k, 0:fs]))
            P.dma(xb_d.rearrange("(b p) l -> p b l", p=128)[:, :, fo:fo + fs],
                  dbuf[:, :, 0:fs])
            qkv_chunk(P, B, fo, fs, q_ds[0], k_ds[0], v_ds[0])
        arf.reset()
        arb.reset()

        def kv_exchange(l):
            ktl = arb.take(4, 768)
            vtl = arb.take(6, 512)
            P.dma(ktl[:, :, :], k_ds[l][:, TSE - 768:TSE]
                  .rearrange("(a p) w -> p a w", p=128))
            P.dma(vtl[:, :, :], v_ds[l][TSE - 768:TSE, :]
                  .rearrange("(a p) w -> p a w", p=128))
            kgi_f = kg_ai[l].rearrange("r (j w) -> (r j) w", w=768)
            for kt in range(4):
                P.ind(kgi_f, ixs[:, IXK + kt:IXK + kt + 1], ktl[:, kt, :],
                      None)
            for it in range(6):
                P.ind(vg_ai[l], ixs[:, IXV + it:IXV + it + 1], vtl[:, it, :],
                      None)
            cc(kg_ai[l], kg_ao[l])
            cc(vg_ai[l], vg_ao[l])
            kgo_f = kg_ao[l].rearrange("r (j w) -> (r j) w", w=768)
            for kt in range(4):
                P.ind(ktl[:, kt, :], None, kgo_f,
                      ixs[:, IXKR + kt:IXKR + kt + 1])
            for it in range(6):
                P.ind(vtl[:, it, :], None, vg_ao[l],
                      ixs[:, IXVR + it:IXVR + it + 1])
            P.dma(kt_loc[l].rearrange("(a p) w -> p a w", p=128),
                  ktl[:, :, :])
            P.dma(vt_loc[l].rearrange("(a p) w -> p a w", p=128),
                  vtl[:, :, :])
            arb.reset()

        def emit_D(l, last):
            stream_d = xb_d if l == 0 else x1o_d
            woA_d = woA0_d if l == 0 else woA1_d
            w1_d = w1f0_d if l == 0 else w1f1_d
            b1_d = b1p0_d if l == 0 else b1p1_d
            w2_d = w2f0_d if l == 0 else w2f1_d
            b2_d = b2v0_d if l == 0 else b2v1_d
            lg_d = lgv0_d if l == 0 else lgv1_d
            lb_d = lbv0_d if l == 0 else lbv1_d
            xm = arf.take(8, TSE + 1)
            rbuf = arf.take(8, 512)
            dbuf = arf.take(8, 512)
            sqb = arf.take(8, 512)
            simb = arf.take(STRIP)
            maskc = arf.take(STRIP)
            h1b = arf.take(512)
            gbv = arf.take(512)
            extm = arf.take(8, 128)
            xb2flat = arb.take(8 * 513)
            qbt = xb2flat[:, 0:512].rearrange("p (a b) -> p a b", a=4)
            kstr = xb2flat[:, 512:512 + 4 * STRIP].rearrange(
                "p (a b) -> p a b", a=4)
            vstr = arb.take(5, 512)
            pbuf = arb.take(STRIP)
            ptb = arb.take(5, 128)
            obf = arb.take(4, 128)
            xbm = arb.take(8, 512)
            hbf = arb.take(22, 512)
            wmg = arb.take(8, 128)
            w2g = arb.take(22, 128)
            wob = arb.take(4, D)
            cols = colsml[:, 0:4]
            P.dma(xm[:, :, 1:TSE + 1],
                  stream_d.rearrange("(b p) l -> p b l", p=128))
            P.v(mset(xm[:, :, 0:1], 0.0))
            P.dmag(wob[:, :, :], woA_d.rearrange("(b p) c -> p b c", p=128))
            P.dma(extm[:, :, :], em_d)
            vec_cols(P, lgc, lg_d, 8)
            vec_cols(P, lbb, lb_d, 8, cast=True)
            vec_cols(P, b2c, b2_d, 8)
            vec_cols(P, b1pv[:, 0:43], b1_d, 43)
            for i in range(17):
                c0 = 128 * i + 128
                P.dma(maskc[:, :], mk_d[i, :, :])
                P.dma(qbt[:, :, :],
                      q_ds[l].rearrange("(b p) ll -> p b ll", p=128)
                      [:, :, 128 * i:128 * (i + 1)])
                nt = max(0, min(768 - c0, STRIP))
                if nt > 0:
                    P.dma(kstr[:, :, 0:nt],
                          kt_loc[l].rearrange("(b p) w -> p b w", p=128)
                          [:, :, c0:c0 + nt])
                if nt < STRIP:
                    o0 = c0 + nt - 768 + 128
                    P.dma(kstr[:, :, nt:STRIP],
                          k_ds[l].rearrange("(b p) w -> p b w", p=128)
                          [:, :, o0:o0 + STRIP - nt])
                for j in range(5):
                    t0_ = c0 + 128 * j
                    if t0_ < 768:
                        P.dma(vstr[:, j, :],
                              vt_loc[l][t0_:t0_ + 128, :]
                              .rearrange("(a p) c -> p a c", p=128)[:, 0, :])
                    else:
                        o0 = t0_ - 768 + 128
                        P.dma(vstr[:, j, :],
                              v_ds[l][o0:o0 + 128, :]
                              .rearrange("(a p) c -> p a c", p=128)[:, 0, :])
                for h in range(8):
                    tl, b = h // 2, 64 * (h % 2)
                    P.t(mm(ps_sim[:, 0:512], qbt[b:b + 64, tl, :],
                           kstr[b:b + 64, tl, 0:512], True, True))
                    P.t(mm(ps_sim[:, 512:STRIP], qbt[b:b + 64, tl, :],
                           kstr[b:b + 64, tl, 512:STRIP], True, True))
                    P.v(tt(simb[:, :], ps_sim[:, :], maskc[:, :], ALU.add))
                    P.v(red(cols[:, 0:1], simb[:, :], ALU.max, negate=True))
                    P.s(act(pbuf[:, :], simb[:, :], AF.Exp, bias=cols[:, 0:1],
                            accum=cols[:, 1:2]))
                    P.v(rcp(cols[:, 2:3], cols[:, 1:2]))
                    P.v(ts(pbuf[:, :], pbuf[:, :], cols[:, 2:3], None,
                           ALU.mult))
                    for j in range(5):
                        P.t(trans(ps_pt[:, 128 * j:128 * (j + 1)],
                                  pbuf[:, 128 * j:128 * (j + 1)], ident[:, :]))
                    for j in range(5):
                        P.s(act(ptb[:, j, :], ps_pt[:, 128 * j:128 * (j + 1)],
                                AF.Copy))
                    for j in range(5):
                        P.t(mm(ps_o[0:64, :], vstr[:, j, 64 * h:64 * h + 64],
                               ptb[:, j, :], j == 0, j == 4))
                    P.s(act(obf[b:b + 64, tl, :], ps_o[0:64, 0:128], AF.Copy))
                for m in range(8):
                    for k in range(4):
                        P.t(mm(ps_mm[:, 0:128],
                               wob[:, k, m * 128:(m + 1) * 128],
                               obf[:, k, :], k == 0, k == 3))
                    P.v(tt(xm[:, m, 1 + 128 * i:1 + 128 * (i + 1)],
                           ps_mm[:, 0:128],
                           xm[:, m, 1 + 128 * i:1 + 128 * (i + 1)], ALU.add))
            for ci, (fo, fs) in enumerate(scan_f(TSE, 512)):
                def xv(k, fo=fo, fs=fs):
                    return (xm[:, k, 1 + fo:1 + fo + fs] if k < 4
                            else xm[:, k, fo:fo + fs])
                fm_ln(P, xv, fs, dbuf, sqb, ones128, onesr, ps_row, ps_bc,
                      rows, nrm_view=None)
                for k in range(8):
                    P.v(tt(xbm[:, k, 0:fs], dbuf[:, k, 0:fs], ps_bc[:, 0:fs],
                           ALU.mult))
                P.v(mset(hbf[64:128, 21, 0:fs], 0.0))
                for mg in range(43):
                    P.dmag(wmg[:, :, :],
                           w1_d.rearrange("(b p) c -> p b c", p=128)
                           [:, :, mg * 128:(mg + 1) * 128])
                    if ci == 0:
                        for k in range(8):
                            P.t(mm(ps_o[:, 0:1], wmg[:, k, :],
                                   lbb[:, k:k + 1], k == 0, k == 7))
                        P.s(act(b1e[:, mg:mg + 1], ps_o[:, 0:1], AF.Copy))
                        P.v(tt(b1e[:, mg:mg + 1], b1e[:, mg:mg + 1],
                               b1pv[:, mg:mg + 1], ALU.add))
                    for k in range(8):
                        P.v(ts(wmg[:, k, :], wmg[:, k, :], lgc[:, k:k + 1],
                               None, ALU.mult))
                    for k in range(8):
                        P.t(mm(ps_mm[:, 0:fs], wmg[:, k, :], xbm[:, k, 0:fs],
                               k == 0, k == 7))
                    P.s(act(h1b[:, 0:fs], ps_mm[:, 0:fs], AF.Identity,
                            bias=b1e[:, mg:mg + 1]))
                    P.s(act(gbv[0:64, 0:fs], h1b[64:128, 0:fs], AF.Gelu))
                    P.v(tt(hbf[64 * (mg % 2):64 * (mg % 2) + 64, mg // 2,
                               0:fs],
                           h1b[0:64, 0:fs], gbv[0:64, 0:fs], ALU.mult))
                for m in range(8):
                    P.dmag(w2g[:, :, :],
                           w2_d.rearrange("(b p) c -> p b c", p=128)
                           [:, :, m * 128:(m + 1) * 128])
                    for k in range(22):
                        P.t(mm(ps_mm[:, 0:fs], w2g[:, k, :], hbf[:, k, 0:fs],
                               k == 0, k == 21))
                    P.s(act(rbuf[:, m, 0:fs], ps_mm[:, 0:fs], AF.Identity,
                            bias=b2c[:, m:m + 1]))
                    P.v(tt(rbuf[:, m, 0:fs], rbuf[:, m, 0:fs],
                           xm[:, m, 1 + fo:1 + fo + fs], ALU.add))
                if fo < EXT:
                    for m in range(8):
                        P.v(tt(rbuf[:, m, 0:EXT], rbuf[:, m, 0:EXT],
                               extm[:, m, :], ALU.mult))
                if not last:
                    P.dma(x1o_d.rearrange("(b p) l -> p b l", p=128)
                          [:, :, fo:fo + fs], rbuf[:, :, 0:fs])
                    P.dmag(x1b_d.rearrange("(b p) l -> p b l", p=128)
                           [:, :, fo:fo + fs], rbuf[:, :, 0:fs])
                else:
                    P.dma(xf_d.rearrange("(b p) l -> p b l", p=128)
                          [:, :, fo:fo + fs], rbuf[:, :, 0:fs])
            if not last:
                B2 = dict(xbpc=xb2flat.rearrange("p (a b) -> p a b", a=8),
                          qf=dbuf[:, 0:4, :],
                          sqh=sqb[:, 0:4, :], vb=xbm[:, 0:4, :],
                          vtb=xbm[:, 4:8, :], rqr=rows,
                          t1=rbuf[:, 4, :], t2=rbuf[:, 5, :],
                          t3=rbuf[:, 6, :], qo=hbf[:, 0, 0:512],
                          tabc=rbuf[:, 0:4, :], ones128=ones128,
                          onesr=onesr, oneh=oneh, ident=ident, rotm=rotm,
                          ps_mm=ps_mm, ps_bc=ps_bc, ps_row=ps_row,
                          ps_t=ps_t, qsr=qsr[:, 1:2], ksr=ksr[:, 1:2],
                          wq_d=wq1_d, wqm=wmg)
                for ci, (fo, fs) in enumerate(scan_f(TSE, 512)):
                    P.dma(B2["tabc"][:, :, 0:fs],
                          tab_d[:, :, fo:fo + fs].transpose([1, 0, 2]))
                    if fo == 0:
                        P.dma(B2["xbpc"][:, :, 1:1 + fs],
                              x1b_d.rearrange("(b p) l -> p b l", p=128)
                              [:, :, 0:fs])
                        P.v(mset(B2["xbpc"][:, :, 0:1], 0.0))
                    else:
                        P.dma(B2["xbpc"][:, :, 0:1 + fs],
                              x1b_d.rearrange("(b p) l -> p b l", p=128)
                              [:, :, fo - 1:fo + fs])
                    qkv_chunk(P, B2, fo, fs, q_ds[1], k_ds[1], v_ds[1])
            else:
                for fo, fs in scan_f(TSH, 512):
                    P.dma(rbuf[:, :, 0:fs],
                          xf_d.rearrange("(b p) l -> p b l", p=128)
                          [:, :, EXT + fo:EXT + fo + fs])
                    fm_ln(P, lambda k, fs=fs: rbuf[:, k, 0:fs], fs, dbuf,
                          sqb, ones128, onesr, ps_row, ps_bc, rows,
                          nrm_view=lambda k, fs=fs: dbuf[:, k, 0:fs])
                    P.dma(sqb[:, :, 0:fs],
                          x1_d.rearrange("(b p) l -> p b l", p=128)
                          [:, :, EXT + fo:EXT + fo + fs])
                    for m in range(8):
                        P.v(tt(dbuf[:, m, 0:fs], dbuf[:, m, 0:fs],
                               sqb[:, m, 0:fs], ALU.add))
                    for j in range(fs // 128):
                        for m in range(8):
                            P.t(trans(ps_mm[:, 0:128],
                                      dbuf[:, m, 128 * j:128 * (j + 1)],
                                      iden32[:, :]))
                            P.s(act(rbuf[:, m, 0:128] if False else
                                    rbuf.rearrange("p a b -> p (a b)")
                                    [:, m * 128:(m + 1) * 128],
                                    ps_mm[:, 0:128], AF.Copy))
                        P.dma(out_d[fo + 128 * j:fo + 128 * (j + 1), :],
                              rbuf.rearrange("p a b -> p (a b)")[:, 0:1024])
            arf.reset()
            arb.reset()

        kv_exchange(0)
        emit_D(0, False)
        kv_exchange(1)
        emit_D(1, True)
        P.finalize(blk)
    return nc


def kernel(**inputs):
    f32 = np.float32
    x = np.asarray(inputs["L0_attn_x"], f32)[0]
    w_in = np.asarray(inputs["L0_attn_w_in"], f32)
    ident = np.eye(128).astype(NPBF)
    iden32 = np.eye(128, dtype=f32)
    oneh = oneh_host()
    rotm = rotm_host()
    qs_ = np.asarray(inputs["L0_ffn_q_scale"], f32)
    ks_ = np.asarray(inputs["L0_ffn_k_scale"], f32)
    w1f0, b1p0 = permute_w1(np.asarray(inputs["L0_ffn_w1"], f32)[0],
                            np.asarray(inputs["L0_ffn_b1"], f32)[0])
    w1f1, b1p1 = permute_w1(np.asarray(inputs["L0_ffn_w1"], f32)[1],
                            np.asarray(inputs["L0_ffn_b1"], f32)[1])
    w2f0 = permute_w2(np.asarray(inputs["L0_ffn_w2"], f32)[0])
    w2f1 = permute_w2(np.asarray(inputs["L0_ffn_w2"], f32)[1])

    nc = build_FUSED()
    maps = []
    for c in range(NC):
        sl = slice(c * ES, (c + 1) * ES)
        t0 = c * TSH - EXT
        w_sl = np.concatenate([w_in[sl], w_in[E + c * ES:E + (c + 1) * ES]], 0)
        xe = np.zeros((TSE, D), f32)
        lo = max(t0, 0)
        xe[lo - t0:] = x[lo:t0 + TSE]
        maps.append(dict(
            xa=_cc(x[c * TSH:(c + 1) * TSH]),
            xe=_cc(xe.T),
            w_in_slT=_cc(w_sl.T),
            lnw=np.asarray(inputs["L0_attn_ln_w"], f32),
            lnb=np.asarray(inputs["L0_attn_ln_b"], f32),
            cw=_cc(np.asarray(inputs["L0_attn_conv_w"], f32)[sl]),
            cb=_cc(np.asarray(inputs["L0_attn_conv_b"], f32)[sl]),
            wxpT=_cc(np.asarray(inputs["L0_attn_w_xproj"], f32)[:, sl].T),
            wdtT=_cc(np.asarray(inputs["L0_attn_w_dt"], f32)[sl].T),
            bdt=_cc(np.asarray(inputs["L0_attn_b_dt"], f32)[sl]),
            alog=_cc(np.asarray(inputs["L0_attn_a_log"], f32)[sl]),
            dskip=_cc(np.asarray(inputs["L0_attn_d_skip"], f32)[sl]),
            woutT=_cc(np.asarray(inputs["L0_attn_w_out"], f32).T),
            wqkv0=_cc(np.asarray(inputs["L0_ffn_wqkv"], f32)[0]),
            wqkv1=_cc(np.asarray(inputs["L0_ffn_wqkv"], f32)[1]),
            qsr0=_cc(np.tile(qs_[0], 2)[:, None]),
            ksr0=_cc(np.tile(ks_[0], 2)[:, None]),
            qsr1=_cc(np.tile(qs_[1], 2)[:, None]),
            ksr1=_cc(np.tile(ks_[1], 2)[:, None]),
            tabs=tabs_for(t0),
            oneh=oneh, rotm=rotm, ident=ident, iden32=iden32,
            masks=masks_for(c),
            extm=np.full((128, 8, 128), 0.0 if c == 0 else 1.0, f32),
            idxt=idx_table_for(c),
            woA0=_cc(np.asarray(inputs["L0_ffn_wo"], f32)[0]),
            woA1=_cc(np.asarray(inputs["L0_ffn_wo"], f32)[1]),
            w1f0=w1f0, w1f1=w1f1, b1p0=b1p0, b1p1=b1p1,
            w2f0=w2f0, w2f1=w2f1,
            b2v0=_cc(np.asarray(inputs["L0_ffn_b2"], f32)[0]),
            b2v1=_cc(np.asarray(inputs["L0_ffn_b2"], f32)[1]),
            lgv0=_cc(np.asarray(inputs["L0_ffn_ln_g"], f32)[0]),
            lgv1=_cc(np.asarray(inputs["L0_ffn_ln_g"], f32)[1]),
            lbv0=_cc(np.asarray(inputs["L0_ffn_ln_b"], f32)[0]),
            lbv1=_cc(np.asarray(inputs["L0_ffn_ln_b"], f32)[1]),
        ))
    res = run_bass_kernel_spmd(nc, maps, core_ids=list(range(NC)))
    out = np.concatenate([np.asarray(r["outp"]) for r in res.results], 0)
    return out[None].astype(np.float32)


# version 10 (final)
# speedup vs baseline: 1.1981x; 1.0167x over previous
import numpy as np
import ml_dtypes
from contextlib import ExitStack

import concourse.bass as bass
import concourse.mybir as mybir
from concourse.bass import IndirectOffsetOnAxis
from concourse.bass_utils import run_bass_kernel_spmd

F32 = mybir.dt.float32
BF16 = mybir.dt.bfloat16
AF = mybir.ActivationFunctionType
ALU = mybir.AluOpType
AX = mybir.AxisListType
NPBF = ml_dtypes.bfloat16

NC = 8
T = 16384
D = 1024
E = 2048
ES = E // NC
TSH = T // NC
EXT = 128
TSE = TSH + EXT
NST = 16
DTR = 64
WIN = 512
HEADS = 8
DH = 64
INNER = 512
HID = 2730
HIDP = 2752
NH1 = 2 * HIDP
QKS = 8.0
EPS = 1e-5
STRIP = WIN + 128
NEG = np.float32(-3.0e38)

ENGINES = ("sync", "vector", "scalar", "tensor", "gpsimd")

YW = T + 2048
CHK_C = [(0, 128), (128, 512), (640, 512), (1152, 512), (1664, 512)]
IXA = 0
IXB = 8
IXC = 40
IXK = 120
IXV = 124
IXKR = 130
IXVR = 134
IXN = 140


def idx_table_for(c):
    p = np.arange(128, dtype=np.int64)
    t = np.zeros((128, IXN), np.int64)
    for b in range(8):
        t[:, IXA + b] = (b * 128 + p) * 8 + c
    for ch in range(16):
        for tb in range(2):
            t[:, IXB + ch * 2 + tb] = (c * ES + tb * 128 + p) * (YW // 1024) \
                + ch
    for k in range(16):
        t[:, IXC + k] = (k * 128 + p) * (YW // 128) + \
            (c * 16 - 1 if c > 0 else YW // 128 - 1)
        for ci in range(1, 5):
            t[:, IXC + ci * 16 + k] = (k * 128 + p) * (YW // 512) + c * 4 \
                + (ci - 1)
    for kt in range(4):
        t[:, IXK + kt] = (kt * 128 + p) * 9 + (c + 1)
        t[:, IXKR + kt] = (kt * 128 + p) * 9 + c
    for it in range(6):
        t[:, IXV + it] = (c + 1) * 768 + it * 128 + p
        t[:, IXVR + it] = c * 768 + it * 128 + p
    return np.ascontiguousarray(t.astype(np.uint32))


class Prog:

    LIM = 16000

    def __init__(self, nc, stack, nsem=10):
        self.nc = nc
        self.sems = [stack.enter_context(nc.semaphore(f"sq{i}"))
                     for i in range(nsem)]
        self.sem = self.sems[0]
        self.ops = []
        self.cnt_ = 0

    def op(self, ename, builder, dma=False, tag=None):
        inc = 16 if dma else 1
        self.ops.append((ename, builder, inc, inc,
                         "dma" if dma else (tag or "comp")))
        self.cnt_ += inc

    def site(self, ename, builders):
        for j, b in enumerate(builders):
            en = ename
            if isinstance(b, tuple):
                en, b = b
            cnt = 16 if j == len(builders) - 1 else 0
            self.ops.append((en, b, 16, cnt, "coll"))
            self.cnt_ += cnt

    def dma(self, out, in_):
        self.op("sync", lambda e, o=out, i=in_: e.dma_start(out=o, in_=i), dma=True)

    def dmag(self, out, in_):
        self.op("gpsimd", lambda e, o=out, i=in_: e.dma_start(out=o, in_=i), dma=True)

    def ind(self, out, out_idx, in_, in_idx):
        oo = IndirectOffsetOnAxis(ap=out_idx, axis=0) if out_idx is not None \
            else None
        io = IndirectOffsetOnAxis(ap=in_idx, axis=0) if in_idx is not None \
            else None
        self.op("gpsimd",
                lambda e, o=out, i=in_, oo=oo, io=io: e.indirect_dma_start(
                    out=o, out_offset=oo, in_=i, in_offset=io), dma=True)

    def v(self, builder):
        self.op("vector", builder)

    def s(self, builder):
        self.op("scalar", builder)

    def t(self, builder):
        self.op("tensor", builder)

    def g(self, builder):
        self.op("gpsimd", builder)

    def finalize(self, block):
        LIM = Prog.LIM
        recs = []
        ep = 0
        base = [0]
        c = 0
        prev_e = None
        for e, b, stamp, cnt, tag in self.ops:
            if c + cnt - base[ep] > LIM:
                ep += 1
                base.append(c)
                assert ep < len(self.sems), "need more epoch semaphores"
            recs.append((e, b, stamp, ep, c - base[ep], e == prev_e, tag))
            prev_e = e
            c += cnt
        totals = [0] * (ep + 1)
        for (e, b, stamp, cnt, tag), r in zip(self.ops, recs):
            totals[r[3]] += cnt
        for ename in ENGINES:
            mine = [(b, inc, k, lp, sw, tag)
                    for (e, b, inc, k, lp, sw, tag) in recs if e == ename]
            if not mine and ename != "sync":
                continue

            def body(eng, mine=mine, last=(ename == "sync"), ep=ep,
                     ename=ename):
                seen = -1
                unclean = set()
                for b, inc, k, lp, sw, tag in mine:
                    if k > seen:
                        for kk in range(max(seen, 0), k):
                            eng.wait_ge(self.sems[kk], totals[kk])
                        seen = k
                        sw = False
                        unclean = set()
                    skip = sw and (
                        (ename == "tensor" and tag == "comp"
                         and not unclean) or
                        (ename == "sync" and tag == "dma"
                         and unclean <= {"dma"}))
                    if lp and not skip:
                        eng.wait_ge(self.sems[k], lp)
                        unclean = set()
                    if tag != "comp":
                        unclean.add(tag)
                    b(eng).then_inc(self.sems[k], inc)
                if last:
                    for kk in range(ep + 1):
                        eng.wait_ge(self.sems[kk], totals[kk])

            getattr(block, ename)(body)
        return c


def mm(ps_ap, lhsT_ap, rhs_ap, start, stop):
    return lambda e: e.matmul(ps_ap, lhsT_ap, rhs_ap, start=start, stop=stop)


class Consts:
    cur = None

    def __init__(self, P, nc, stack):
        self.P = P
        self.t = stack.enter_context(nc.sbuf_tensor("cpool", [128, 32], F32))
        self.vals = {}
        Consts.cur = self

    def col(self, v):
        v = float(v)
        if v not in self.vals:
            i = len(self.vals)
            assert i < 32
            self.vals[v] = i
            self.P.v(mset(self.t[:, i:i + 1], v))
        return self.t[:, self.vals[v]:self.vals[v] + 1]


def act(out, in_, func, bias=0.0, scale=1.0, accum=None):
    np_ = in_.shape[0]
    if isinstance(bias, float) and bias != 0.0:
        bias = Consts.cur.col(bias)[0:np_, :] if np_ < 128 else             Consts.cur.col(bias)
    if isinstance(scale, float) and scale != 1.0:
        scale = Consts.cur.col(scale)[0:np_, :] if np_ < 128 else             Consts.cur.col(scale)
    return lambda e: e.activation(out, in_, func, bias=bias, scale=scale,
                                  accum_out=accum)


def tt(out, a, b, op):
    return lambda e: e.tensor_tensor(out, a, b, op)


def ts(out, a, s1, s2, op0, op1=ALU.bypass):
    if s2 is None:
        return lambda e: e.tensor_scalar(out, a, s1, None, op0)
    return lambda e: e.tensor_scalar(out, a, s1, s2, op0, op1)


def stt(out, in0, scalar, in1, op0, op1):
    return lambda e: e.scalar_tensor_tensor(out, in0, scalar, in1, op0, op1)


def red(out, in_, op, negate=False):
    return lambda e: e.tensor_reduce(out, in_, AX.X, op, negate=negate)


def cp(out, in_):
    return lambda e: e.tensor_copy(out, in_)


def rcp(out, in_):
    return lambda e: e.reciprocal(out, in_)


def mset(ap_, v):
    return lambda e: e.memset(ap_, v)


def tts(out, d0, d1, init):
    return lambda e: e.tensor_tensor_scan(out, d0, d1, init, ALU.mult, ALU.add)


def trans(ps_ap, in_ap, ident_ap):
    return lambda e: e.transpose(ps_ap, in_ap, ident_ap)


def vec_cols(P, dst, src1d, nb, cast=False):
    for b in range(nb):
        src = src1d[b * 128:(b + 1) * 128][:, None]
        if cast:
            P.dmag(dst[:, b:b + 1], src)
        else:
            P.dma(dst[:, b:b + 1], src)


def scan_f(total, step):
    o = 0
    while o < total:
        s = min(step, total - o)
        yield o, s
        o += s


def emit_fm_ln(P, sc, x_view, nb, F, fo, d_out, n_out, ones128, sq_t, ps_row,
               ps_bc, row_sb, shift_upper_from=None, out_dtype_note=None):
    for k in range(nb):
        P.t(mm(ps_row[0:1, 0:F], ones128[:, 0:1], x_view(k),
               start=(k == 0), stop=(k == nb - 1)))
    P.s(act(row_sb[0:1, 0:F], ps_row[0:1, 0:F], AF.Identity, scale=1.0 / D))
    P.t(mm(ps_bc[:, 0:F], ones128[0:1, 0:128], row_sb[0:1, 0:F], True, True))
    for k in range(nb):
        P.v(tt(d_out[:, k, fo:fo + F], x_view(k), ps_bc[:, 0:F], ALU.subtract))
    for k in range(nb):
        P.s(act(sq_t[:, k, 0:F], d_out[:, k, fo:fo + F], AF.Square))
    for k in range(nb):
        P.t(mm(ps_row[0:1, 0:F], ones128[:, 0:1], sq_t[:, k, 0:F],
               start=(k == 0), stop=(k == nb - 1)))
    P.s(act(row_sb[1:2, 0:F], ps_row[0:1, 0:F], AF.Ln, bias=EPS, scale=1.0 / D))
    P.s(act(row_sb[1:2, 0:F], row_sb[1:2, 0:F], AF.Exp, scale=-0.5))
    P.t(mm(ps_bc[:, 0:F], ones128[0:1, 0:128], row_sb[1:2, 0:F], True, True))
    if n_out is not None:
        for k in range(nb):
            P.v(tt(n_out[:, k, fo:fo + F], d_out[:, k, fo:fo + F],
                   ps_bc[:, 0:F], ALU.mult))


def build_A():
    nc = bass.Bass("TRN2", target_bir_lowering=False, debug=False)
    xa_d = nc.dram_tensor("xa", [TSH, D], F32, kind="ExternalInput").ap()
    id_d = nc.dram_tensor("ident", [128, 128], BF16, kind="ExternalInput").ap()
    xn_d = nc.dram_tensor("xn", [D, TSH], BF16, kind="ExternalOutput").ap()
    with ExitStack() as st:
        P = Prog(nc, st)
        Consts(P, nc, st)
        x_s = st.enter_context(nc.sbuf_tensor("x_s", [128, 16, D], F32))
        sqs = st.enter_context(nc.sbuf_tensor("sqs", [128, D], F32))
        xnb = st.enter_context(nc.sbuf_tensor("xnb", [128, 16, D], BF16))
        colc = st.enter_context(nc.sbuf_tensor("colc", [128, 4], F32))
        ident = st.enter_context(nc.sbuf_tensor("ident_s", [128, 128], BF16))
        xnf = st.enter_context(nc.sbuf_tensor("xnf", [128, 8, TSH], BF16))
        pst = st.enter_context(nc.psum_tensor("pst", [128, 512], BF16))
        blk = st.enter_context(nc.Block())

        P.dma(x_s[:, :, :], xa_d.rearrange("(n p) d -> p n d", p=128))
        P.dma(ident[:, :], id_d)
        for i in range(16):
            xt = x_s[:, i, :]
            for rep in range(2):
                P.v(red(colc[:, 0:1], xt, ALU.add))
                P.v(ts(colc[:, 0:1], colc[:, 0:1], 1.0 / D, None, ALU.mult))
                P.v(ts(xt, xt, colc[:, 0:1], None, ALU.subtract))
                P.s(act(sqs[:, :], xt, AF.Square, accum=colc[:, 2:3]))
                P.s(act(colc[:, 2:3], colc[:, 2:3], AF.Ln, bias=EPS,
                        scale=1.0 / D))
                P.s(act(colc[:, 2:3], colc[:, 2:3], AF.Exp, scale=-0.5))
                if rep == 0:
                    P.v(ts(xt, xt, colc[:, 2:3], None, ALU.mult))
                else:
                    P.v(ts(xnb[:, i, :], xt, colc[:, 2:3], None, ALU.mult))
        for i in range(16):
            for dbq in range(2):
                for j in range(4):
                    db = dbq * 4 + j
                    P.t(trans(pst[:, j * 128:(j + 1) * 128],
                              xnb[:, i, db * 128:(db + 1) * 128], ident[:, :]))
                for j in range(4):
                    db = dbq * 4 + j
                    P.s(act(xnf[:, db, i * 128:(i + 1) * 128],
                            pst[:, j * 128:(j + 1) * 128], AF.Copy))
        P.dma(xn_d.rearrange("(b p) l -> p b l", p=128), xnf[:, :, :])
        P.finalize(blk)
    return nc


def build_B0():
    nc = bass.Bass("TRN2", target_bir_lowering=False, debug=False)
    xn_d = nc.dram_tensor("xnf", [D, T], BF16, kind="ExternalInput").ap()
    w_d = nc.dram_tensor("w_in_slT", [D, 2 * ES], F32, kind="ExternalInput").ap()
    lnw_d = nc.dram_tensor("lnw", [D], F32, kind="ExternalInput").ap()
    lnb_d = nc.dram_tensor("lnb", [D], F32, kind="ExternalInput").ap()
    cw_d = nc.dram_tensor("cw", [ES, 4], F32, kind="ExternalInput").ap()
    cb_d = nc.dram_tensor("cb", [ES], F32, kind="ExternalInput").ap()
    wxp_d = nc.dram_tensor("wxpT", [ES, 96], F32, kind="ExternalInput").ap()
    u_d = nc.dram_tensor("u32", [ES, T], F32, kind="ExternalOutput").ap()
    z_d = nc.dram_tensor("z32", [ES, T], F32, kind="ExternalOutput").ap()
    xp_d = nc.dram_tensor("xdblp", [96, T], F32, kind="ExternalOutput").ap()

    LC = 2048
    NCH = T // LC
    with ExitStack() as st:
        P = Prog(nc, st)
        Consts(P, nc, st)
        wt = st.enter_context(nc.sbuf_tensor("wt", [128, 8, 512], BF16))
        lnw_s = st.enter_context(nc.sbuf_tensor("lnw_s", [128, 8], F32))
        lnbb = st.enter_context(nc.sbuf_tensor("lnbb", [128, 8], BF16))
        bias_s = st.enter_context(nc.sbuf_tensor("bias_s", [128, 4], F32))
        cw_s = st.enter_context(nc.sbuf_tensor("cw_s", [128, 2, 4], F32))
        cb_s = st.enter_context(nc.sbuf_tensor("cb_s", [128, 2], F32))
        wxt = st.enter_context(nc.sbuf_tensor("wxt", [128, 2, 96], BF16))
        xnc = st.enter_context(nc.sbuf_tensor("xnc", [128, 8, LC], BF16))
        xs3 = st.enter_context(nc.sbuf_tensor("xs3", [128, 2, LC + 3], F32))
        zb = st.enter_context(nc.sbuf_tensor("zb", [128, 2, LC], F32))
        uc = st.enter_context(nc.sbuf_tensor("uc", [128, 2, LC], F32))
        ub = st.enter_context(nc.sbuf_tensor("ub", [128, 2, LC], BF16))
        ps = st.enter_context(nc.psum_tensor("ps", [128, 512], F32))
        blk = st.enter_context(nc.Block())

        P.dmag(wt[:, :, :], w_d.rearrange("(b p) r -> p b r", p=128))
        vec_cols(P, lnw_s, lnw_d, 8)
        vec_cols(P, lnbb, lnb_d, 8, cast=True)
        P.dma(cw_s[:, :, :], cw_d.rearrange("(b p) k -> p b k", p=128))
        vec_cols(P, cb_s, cb_d, 2)
        P.dmag(wxt[:, :, :], wxp_d.rearrange("(b p) r -> p b r", p=128))
        for mg in range(4):
            for k in range(8):
                P.t(mm(ps[0:128, mg:mg + 1],
                       wt[:, k, mg * 128:(mg + 1) * 128],
                       lnbb[:, k:k + 1], k == 0, k == 7))
        P.s(act(bias_s[:, :], ps[0:128, 0:4], AF.Copy))
        for db in range(8):
            P.v(ts(wt[:, db, :], wt[:, db, :], lnw_s[:, db:db + 1], None,
                   ALU.mult))
        P.v(mset(xs3[:, :, 0:3], 0.0))

        for c in range(NCH):
            P.dma(xnc[:, :, :],
                  xn_d.rearrange("(b p) l -> p b l", p=128)[:, :,
                                                           c * LC:(c + 1) * LC])
            for m in range(4):
                for fo, fs in scan_f(LC, 512):
                    for k in range(8):
                        P.t(mm(ps[:, 0:fs], wt[:, k, m * 128:(m + 1) * 128],
                               xnc[:, k, fo:fo + fs], k == 0, k == 7))
                    if m < 2:
                        P.s(act(xs3[:, m, 3 + fo:3 + fo + fs], ps[:, 0:fs],
                                AF.Identity, bias=bias_s[:, m:m + 1]))
                    else:
                        P.s(act(zb[:, m - 2, fo:fo + fs], ps[:, 0:fs],
                                AF.Identity, bias=bias_s[:, m:m + 1]))
            for tb in range(2):
                P.v(ts(uc[:, tb, :], xs3[:, tb, 0:LC], cw_s[:, tb, 0:1],
                       cb_s[:, tb:tb + 1], ALU.mult, ALU.add))
                for k in range(1, 4):
                    P.v(stt(uc[:, tb, :], xs3[:, tb, k:k + LC],
                            cw_s[:, tb, k:k + 1], uc[:, tb, :],
                            ALU.mult, ALU.add))
                P.v(cp(xs3[:, tb, 0:3], xs3[:, tb, LC:LC + 3]))
                P.s(act(uc[:, tb, :], uc[:, tb, :], AF.Silu))
                P.v(cp(ub[:, tb, :], uc[:, tb, :]))
            P.dma(u_d.rearrange("(b p) l -> p b l", p=128)[:, :,
                                                           c * LC:(c + 1) * LC],
                  uc[:, :, :])
            P.dma(z_d.rearrange("(b p) l -> p b l", p=128)[:, :,
                                                           c * LC:(c + 1) * LC],
                  zb[:, :, :])
            for fo, fs in scan_f(LC, 512):
                for k in range(2):
                    P.t(mm(ps[0:96, 0:fs], wxt[:, k, :], ub[:, k, fo:fo + fs],
                           k == 0, k == 1))
                P.s(act(zb[0:96, 0, fo:fo + fs], ps[0:96, 0:fs], AF.Copy))
            P.dma(xp_d[:, c * LC:(c + 1) * LC], zb[0:96, 0, 0:LC])
        P.finalize(blk)
    return nc


def build_B1():
    nc = bass.Bass("TRN2", target_bir_lowering=False, debug=False)
    u_d = nc.dram_tensor("u32", [ES, T], F32, kind="ExternalInput").ap()
    z_d = nc.dram_tensor("z32", [ES, T], F32, kind="ExternalInput").ap()
    xp_d = nc.dram_tensor("xdblp", [NC, 96, T], F32, kind="ExternalInput").ap()
    wdt_d = nc.dram_tensor("wdtT", [DTR, ES], F32, kind="ExternalInput").ap()
    bdt_d = nc.dram_tensor("bdt", [ES], F32, kind="ExternalInput").ap()
    alog_d = nc.dram_tensor("alog", [ES, NST], F32, kind="ExternalInput").ap()
    dsk_d = nc.dram_tensor("dskip", [ES], F32, kind="ExternalInput").ap()
    yg_d = nc.dram_tensor("yg", [ES, T], BF16, kind="ExternalOutput").ap()
    bc_d = nc.dram_tensor("bc_scr", [32, T], F32).ap()

    LC = 1024
    NCH = T // LC
    NG = 4
    with ExitStack() as st:
        P = Prog(nc, st)
        Consts(P, nc, st)
        wdtt = st.enter_context(nc.sbuf_tensor("wdtt", [64, 2, 128], BF16))
        bdt_s = st.enter_context(nc.sbuf_tensor("bdt_s", [128, 2], F32))
        acol = st.enter_context(nc.sbuf_tensor("acol", [128, 2, NST], F32))
        dsk_s = st.enter_context(nc.sbuf_tensor("dsk_s", [128, 2], F32))
        ones1 = st.enter_context(nc.sbuf_tensor("ones1", [1, 128], F32))
        xdb = st.enter_context(nc.sbuf_tensor("xdb", [96, 8, LC], F32))
        xds = st.enter_context(nc.sbuf_tensor("xds", [96, LC], F32))
        dtrb = st.enter_context(nc.sbuf_tensor("dtrb", [64, LC], BF16))
        dtpre = st.enter_context(nc.sbuf_tensor("dtpre", [128, 2, LC], F32))
        spt = st.enter_context(nc.sbuf_tensor("spt", [128, LC], F32))
        dtb = st.enter_context(nc.sbuf_tensor("dtb", [128, 2, LC], F32))
        uch = st.enter_context(nc.sbuf_tensor("uch", [128, 2, LC], F32))
        zch = st.enter_context(nc.sbuf_tensor("zch", [128, 2, LC], F32))
        dtu = st.enter_context(nc.sbuf_tensor("dtu", [128, 2, LC], F32))
        bcf = st.enter_context(nc.sbuf_tensor("bcf", [1, NG * LC], F32))
        dA = st.enter_context(nc.sbuf_tensor("dA", [128, NG * LC], F32))
        dBu = st.enter_context(nc.sbuf_tensor("dBu", [128, NG * LC], F32))
        hb = st.enter_context(nc.sbuf_tensor("hb", [128, NG * LC], F32))
        fixc = st.enter_context(nc.sbuf_tensor("fixc", [128, NG], F32))
        carry = st.enter_context(nc.sbuf_tensor("carry", [128, 2, NST], F32))
        yb4 = st.enter_context(nc.sbuf_tensor("yb4", [128, NG, LC], F32))
        yb = st.enter_context(nc.sbuf_tensor("yb", [128, LC], F32))
        ygb = st.enter_context(nc.sbuf_tensor("ygb", [128, LC], BF16))
        ps = st.enter_context(nc.psum_tensor("ps", [128, NG * LC], F32))
        blk = st.enter_context(nc.Block())

        P.dmag(wdtt[:, :, :], wdt_d.rearrange("r (b p) -> r b p", p=128))
        vec_cols(P, bdt_s, bdt_d, 2)
        P.dma(acol[:, :, :], alog_d.rearrange("(b p) n -> p b n", p=128))
        vec_cols(P, dsk_s, dsk_d, 2)
        P.v(mset(ones1[:, :], 1.0))
        P.v(mset(carry[:, :, :], 0.0))
        for tb in range(2):
            P.s(act(acol[:, tb, :], acol[:, tb, :], AF.Exp))
            P.v(ts(acol[:, tb, :], acol[:, tb, :], -1.0, None, ALU.mult))

        for c in range(NCH):
            lo = c * LC
            P.dma(xdb[:, :, :], xp_d.transpose([1, 0, 2])[:, :, lo:lo + LC])
            P.v(tt(xds[:, :], xdb[:, 0, :], xdb[:, 1, :], ALU.add))
            for j in range(2, 8):
                P.v(tt(xds[:, :], xds[:, :], xdb[:, j, :], ALU.add))
            P.dma(bc_d[:, lo:lo + LC], xds[64:96, :])
            P.v(cp(dtrb[:, :], xds[0:64, :]))
            for mg in range(2):
                for fo, fs in scan_f(LC, 512):
                    P.t(mm(ps[0:128, fo:fo + fs], wdtt[:, mg, :],
                           dtrb[:, fo:fo + fs], True, True))
                P.s(act(dtpre[:, mg, :], ps[0:128, 0:LC], AF.Identity,
                        bias=bdt_s[:, mg:mg + 1]))
            for tb in range(2):
                P.s(act(spt[:, :], dtpre[:, tb, :], AF.Abs))
                P.s(act(spt[:, :], spt[:, :], AF.Exp, scale=-1.0))
                P.s(act(spt[:, :], spt[:, :], AF.Ln, bias=1.0))
                P.s(act(dtb[:, tb, :], dtpre[:, tb, :], AF.Relu))
                P.v(tt(dtb[:, tb, :], dtb[:, tb, :], spt[:, :], ALU.add))
            P.dma(uch[:, :, :],
                  u_d.rearrange("(b p) l -> p b l", p=128)[:, :, lo:lo + LC])
            P.dma(zch[:, :, :],
                  z_d.rearrange("(b p) l -> p b l", p=128)[:, :, lo:lo + LC])
            for tb in range(2):
                P.v(tt(dtu[:, tb, :], dtb[:, tb, :], uch[:, tb, :], ALU.mult))
            for tb in range(2):
                for g in range(NST // NG):
                    P.dma(bcf.ap().rearrange("p (n l) -> p n l", n=NG),
                          bc_d[g * NG:(g + 1) * NG, lo:lo + LC][None, :, :])
                    for fo, fs in scan_f(NG * LC, 512):
                        P.t(mm(ps[:, fo:fo + fs], ones1[0:1, :],
                               bcf[0:1, fo:fo + fs], True, True))
                    for n in range(NG):
                        P.s(act(dA[:, n * LC:(n + 1) * LC], dtb[:, tb, :],
                                AF.Exp, scale=acol[:, tb, g * NG + n:
                                                   g * NG + n + 1]))
                    P.v(tt(dBu.ap().rearrange("p (n l) -> p n l", n=NG),
                           dtu[:, tb, None, :].broadcast_to([128, NG, LC]),
                           ps.ap().rearrange("p (n l) -> p n l", n=NG),
                           ALU.mult))
                    P.v(tt(fixc[:, 0:NG],
                           dA.ap().rearrange("p (n l) -> p n l", n=NG)[:, :, 0],
                           carry[:, tb, g * NG:(g + 1) * NG], ALU.mult))
                    P.v(tt(dBu.ap().rearrange("p (n l) -> p n l",
                                              n=NG)[:, :, 0],
                           dBu.ap().rearrange("p (n l) -> p n l",
                                              n=NG)[:, :, 0],
                           fixc[:, 0:NG], ALU.add))
                    P.v(mset(dA.ap().rearrange("p (n l) -> p n l",
                                               n=NG)[:, :, 0:1], 0.0))
                    P.v(tts(hb[:, :], dA[:, :], dBu[:, :], 0.0))
                    P.v(cp(carry[:, tb, g * NG:(g + 1) * NG],
                           hb.ap().rearrange("p (n l) -> p n l",
                                             n=NG)[:, :, LC - 1]))
                    P.dma(bcf.ap().rearrange("p (n l) -> p n l", n=NG),
                          bc_d[16 + g * NG:16 + (g + 1) * NG,
                               lo:lo + LC][None, :, :])
                    for fo, fs in scan_f(NG * LC, 512):
                        P.t(mm(ps[:, fo:fo + fs], ones1[0:1, :],
                               bcf[0:1, fo:fo + fs], True, True))
                    P.v(tt(dA[:, :], hb[:, :], ps[:, 0:NG * LC], ALU.mult))
                    P.v(red(yb4[:, g, :],
                            dA.ap().rearrange("p (n l) -> p l n", n=NG),
                            ALU.add))
                P.v(red(yb[:, :],
                        yb4.ap().rearrange("p g l -> p l g"), ALU.add))
                P.v(ts(spt[:, :], uch[:, tb, :], dsk_s[:, tb:tb + 1], None,
                       ALU.mult))
                P.v(tt(yb[:, :], yb[:, :], spt[:, :], ALU.add))
                P.s(act(spt[:, :], zch[:, tb, :], AF.Silu))
                P.v(tt(ygb[:, :], yb[:, :], spt[:, :], ALU.mult))
                P.dma(yg_d.rearrange("(b p) l -> p b l", p=128)[:, tb,
                                                               lo:lo + LC],
                      ygb[:, :])
        P.finalize(blk)
    return nc


def fm_ln(P, x_view, F, dbuf, sqb, ones128, onesr, ps_row, ps_bc, rows, nrm_view=None):
    for k in range(8):
        P.t(mm(ps_row[0:1, 0:F], ones128[:, 0:1], x_view(k), k == 0, k == 7))
    P.s(act(rows[0:1, 0, 0:F], ps_row[0:1, 0:F], AF.Identity, scale=1.0 / D))
    P.t(mm(ps_bc[:, 0:F], onesr[0:1, :], rows[0:1, 0, 0:F], True, True))
    for k in range(8):
        P.v(tt(dbuf[:, k, 0:F], x_view(k), ps_bc[:, 0:F], ALU.subtract))
    for k in range(8):
        P.s(act(sqb[:, k, 0:F], dbuf[:, k, 0:F], AF.Square))
    for k in range(8):
        P.t(mm(ps_row[0:1, 0:F], ones128[:, 0:1], sqb[:, k, 0:F], k == 0,
               k == 7))
    P.s(act(rows[0:1, 1, 0:F], ps_row[0:1, 0:F], AF.Ln, bias=EPS,
            scale=1.0 / D))
    P.s(act(rows[0:1, 1, 0:F], rows[0:1, 1, 0:F], AF.Exp, scale=-0.5))
    P.t(mm(ps_bc[:, 0:F], onesr[0:1, :], rows[0:1, 1, 0:F], True, True))
    if nrm_view is not None:
        for k in range(8):
            P.v(tt(nrm_view(k), dbuf[:, k, 0:F], ps_bc[:, 0:F], ALU.mult))


def qkv_chunk(P, B, fo, fs, q_d, k_d, vT_d):
    xbpc, qf, vb, vtb, sqh, rqr = (B["xbpc"], B["qf"], B["vb"],
                                   B["vtb"], B["sqh"], B["rqr"])
    wq = B.get("wq")
    t1, t2, t3, qo, tabc = B["t1"], B["t2"], B["t3"], B["qo"], B["tabc"]
    ones128, onesr, oneh, ident = (B["ones128"], B["onesr"], B["oneh"],
                                   B["ident"])
    ps_mm, ps_bc, ps_row, ps_t = B["ps_mm"], B["ps_bc"], B["ps_row"], B["ps_t"]

    def mm_group(m):
        if "wq_d" in B:
            P.dmag(B["wqm"][:, :, :],
                   B["wq_d"].rearrange("(b p) c -> p b c", p=128)
                   [:, :, m * 128:(m + 1) * 128])
            w_ap = lambda k: B["wqm"][:, k, :]
        else:
            w_ap = lambda k: wq[:, k, m * 128:(m + 1) * 128]
        for k in range(8):
            rhs = (xbpc[:, k, 1:1 + fs] if k < 4 else xbpc[:, k, 0:fs])
            P.t(mm(ps_mm[:, 0:fs], w_ap(k), rhs, k == 0, k == 7))

    def one_side(scol, csi, sni, out_d):
        for tl in range(4):
            P.s(act(sqh[:, tl, 0:fs], qf[:, tl, 0:fs], AF.Square))
        for tl in range(4):
            for hh in range(2):
                b = 64 * hh
                P.t(mm(ps_row[0:1, 0:fs], ones128[b:b + 64, 0:1],
                       sqh[b:b + 64, tl, 0:fs], True, True))
                P.s(act(rqr[0:1, hh, 0:fs], ps_row[0:1, 0:fs], AF.Copy))
            P.v(ts(rqr[0:1, :, 0:fs], rqr[0:1, :, 0:fs], 1e-24, None,
                   ALU.max))
            P.s(act(rqr[0:1, :, 0:fs], rqr[0:1, :, 0:fs], AF.Ln))
            P.s(act(rqr[0:1, :, 0:fs], rqr[0:1, :, 0:fs], AF.Exp, scale=-0.5))
            P.t(mm(ps_bc[:, 0:fs], oneh[0:1, 0, :], rqr[0:1, 0, 0:fs],
                   True, False))
            P.t(mm(ps_bc[:, 0:fs], oneh[0:1, 1, :],
                   rqr[0:1, 1, 0:fs], False, True))
            P.v(tt(t1[:, 0:fs], qf[:, tl, 0:fs], ps_bc[:, 0:fs], ALU.mult))
            P.v(ts(t1[:, 0:fs], t1[:, 0:fs], scol, None, ALU.mult))
            P.v(tt(t2[:, 0:fs], t1[:, 0:fs], tabc[:, csi, 0:fs], ALU.mult))
            P.t(mm(ps_bc[:, 0:fs], B["rotm"][:, :], t1[:, 0:fs], True, True))
            P.v(tt(t3[:, 0:fs], ps_bc[:, 0:fs], tabc[:, sni, 0:fs], ALU.mult))
            P.v(tt(qo[:, 0:fs], t2[:, 0:fs], t3[:, 0:fs], ALU.add))
            P.dma(out_d[tl * 128:(tl + 1) * 128, fo:fo + fs], qo[:, 0:fs])

    for m in range(4):
        mm_group(m)
        P.s(act(qf[:, m, 0:fs], ps_mm[:, 0:fs], AF.Copy))
    one_side(B["qsr"], 0, 1, q_d)
    for m in range(4, 8):
        mm_group(m)
        P.s(act(qf[:, m - 4, 0:fs], ps_mm[:, 0:fs], AF.Copy))
    one_side(B["ksr"], 2, 3, k_d)
    for m in range(8, 12):
        mm_group(m)
        P.s(act(vb[:, m - 8, 0:fs], ps_mm[:, 0:fs], AF.Copy))
    ntile = fs // 128
    for vt_ in range(4):
        for j in range(ntile):
            P.t(trans(ps_t[:, j * 128:(j + 1) * 128],
                      vb[:, vt_, j * 128:(j + 1) * 128], ident[:, :]))
        for j in range(ntile):
            P.s(act(vtb[:, j, vt_ * 128:(vt_ + 1) * 128],
                    ps_t[:, j * 128:(j + 1) * 128], AF.Copy))
    P.dma(vT_d[fo:fo + fs, :].rearrange("(j p) c -> p j c", p=128),
          vtb[:, 0:ntile, :])
    for k in range(8):
        P.v(cp(xbpc[:, k, 0:1], xbpc[:, k, fs:fs + 1]))


def alloc_qkv_bufs(nc, st, name, qf=None, sqh=None, vb=None, vtb=None,
                   rqr=None, t123=None, qo=None, tabc=None):
    B = {}
    B["xbpc"] = st.enter_context(
        nc.sbuf_tensor(name + "xbpc", [128, 8, 513], BF16))
    if qf is None:
        qf = st.enter_context(nc.sbuf_tensor(name + "qf", [128, 4, 512], F32))
    if sqh is None:
        sqh = st.enter_context(
            nc.sbuf_tensor(name + "sqh", [128, 4, 512], F32))
    B["qf"], B["sqh"] = qf, sqh
    if vb is None:
        vb = st.enter_context(nc.sbuf_tensor(name + "vb", [128, 4, 512], BF16))
    if vtb is None:
        vtb = st.enter_context(
            nc.sbuf_tensor(name + "vtb", [128, 4, 512], BF16))
    B["vb"], B["vtb"] = vb, vtb
    B["rqr"] = rqr if rqr is not None else st.enter_context(
        nc.sbuf_tensor(name + "rqr", [1, 2, 512], F32))
    if t123 is not None:
        B["t1"], B["t2"], B["t3"] = t123
    else:
        B["t1"] = st.enter_context(nc.sbuf_tensor(name + "t1", [128, 512], F32))
        B["t2"] = st.enter_context(nc.sbuf_tensor(name + "t2", [128, 512], F32))
        B["t3"] = st.enter_context(nc.sbuf_tensor(name + "t3", [128, 512], F32))
    B["qo"] = qo if qo is not None else st.enter_context(
        nc.sbuf_tensor(name + "qo", [128, 512], BF16))
    B["tabc"] = tabc if tabc is not None else st.enter_context(
        nc.sbuf_tensor(name + "tabc", [128, 4, 512], F32))
    return B


def build_C():
    nc = bass.Bass("TRN2", target_bir_lowering=False, debug=False)
    xe_d = nc.dram_tensor("xe", [D, TSE], F32, kind="ExternalInput").ap()
    yg_d = nc.dram_tensor("yge", [E, TSE], BF16, kind="ExternalInput").ap()
    wo_d = nc.dram_tensor("woutT", [E, D], F32, kind="ExternalInput").ap()
    wq_d = nc.dram_tensor("wqkv", [D, 3 * INNER], F32, kind="ExternalInput").ap()
    qs_d = nc.dram_tensor("qsr", [128, 1], F32, kind="ExternalInput").ap()
    ks_d = nc.dram_tensor("ksr", [128, 1], F32, kind="ExternalInput").ap()
    tab_d = nc.dram_tensor("tabs", [4, 128, TSE], F32, kind="ExternalInput").ap()
    oneh_d = nc.dram_tensor("oneh", [1, 2, 128], F32, kind="ExternalInput").ap()
    rot_d = nc.dram_tensor("rotm", [128, 128], F32, kind="ExternalInput").ap()
    id_d = nc.dram_tensor("ident", [128, 128], BF16, kind="ExternalInput").ap()
    x1_d = nc.dram_tensor("x1e", [D, TSE], F32, kind="ExternalOutput").ap()
    xb_d = nc.dram_tensor("xbe", [D, TSE], F32, kind="ExternalOutput").ap()
    q_od = nc.dram_tensor("q0", [INNER, TSE], BF16, kind="ExternalOutput").ap()
    k_od = nc.dram_tensor("k0", [INNER, TSE], BF16, kind="ExternalOutput").ap()
    v_od = nc.dram_tensor("vT0", [TSE, INNER], BF16, kind="ExternalOutput").ap()

    with ExitStack() as st:
        P = Prog(nc, st)
        Consts(P, nc, st)
        xs = st.enter_context(nc.sbuf_tensor("xs", [128, 8, TSE], F32))
        warena = st.enter_context(nc.sbuf_tensor("warena", [128, 16384], BF16))
        wo = warena.ap().rearrange("p (b c) -> p b c", b=16)
        ygc = st.enter_context(nc.sbuf_tensor("ygc", [128, 16, 512], BF16))
        dbuf = st.enter_context(nc.sbuf_tensor("dbuf", [128, 8, 512], F32))
        bobuf = st.enter_context(nc.sbuf_tensor("bobuf", [128, 8, 512], F32))
        sqb = st.enter_context(nc.sbuf_tensor("sqb", [128, 8, 512], F32))
        rows = st.enter_context(nc.sbuf_tensor("rows", [1, 2, 512], F32))
        ones128 = st.enter_context(nc.sbuf_tensor("ones128", [128, 1], F32))
        onesr = st.enter_context(nc.sbuf_tensor("onesr", [1, 128], F32))
        oneh = st.enter_context(nc.sbuf_tensor("oneh_s", [1, 2, 128], F32))
        ident = st.enter_context(nc.sbuf_tensor("ident_s", [128, 128], BF16))
        qsr = st.enter_context(nc.sbuf_tensor("qsr_s", [128, 1], F32))
        ksr = st.enter_context(nc.sbuf_tensor("ksr_s", [128, 1], F32))
        rotm = st.enter_context(nc.sbuf_tensor("rotm_s", [128, 128], F32))
        B = alloc_qkv_bufs(nc, st, "q0", qf=dbuf[:, 0:4, :],
                           sqh=sqb[:, 0:4, :])
        ps_mm = st.enter_context(nc.psum_tensor("ps_mm", [128, 512], F32))
        ps_bc = st.enter_context(nc.psum_tensor("ps_bc", [128, 512], F32))
        ps_row = st.enter_context(nc.psum_tensor("ps_row", [1, 512], F32))
        ps_t = st.enter_context(nc.psum_tensor("ps_t", [128, 512], BF16))
        blk = st.enter_context(nc.Block())

        B.update(dict(ones128=ones128, onesr=onesr, oneh=oneh, ident=ident,
                      rotm=rotm,
                      ps_mm=ps_mm, ps_bc=ps_bc, ps_row=ps_row, ps_t=ps_t,
                      qsr=qsr[:, 0:1], ksr=ksr[:, 0:1],
                      wq=warena[:, 0:12288].rearrange("p (b c) -> p b c", b=8)))
        P.dma(xs[:, :, :], xe_d.rearrange("(b p) l -> p b l", p=128))
        P.dmag(warena.ap().rearrange("p (b c) -> p b c", b=16),
               wo_d.rearrange("(b p) c -> p b c", p=128))
        P.v(mset(ones128[:, :], 1.0))
        P.v(mset(onesr[:, :], 1.0))
        P.dma(oneh[:, :, :], oneh_d)
        P.dma(ident[:, :], id_d)
        P.dma(qsr[:, :], qs_d)
        P.dma(ksr[:, :], ks_d)
        P.dma(rotm[:, :], rot_d)

        for fo, fs in scan_f(TSE, 512):
            P.dma(ygc[:, :, 0:fs],
                  yg_d.rearrange("(b p) l -> p b l", p=128)[:, :, fo:fo + fs])
            fm_ln(P, lambda k: xs[:, k, fo:fo + fs], fs, dbuf, sqb,
                  ones128, onesr, ps_row, ps_bc, rows,
                  nrm_view=lambda k: dbuf[:, k, 0:fs])
            for m in range(8):
                for k in range(16):
                    P.t(mm(ps_mm[:, 0:fs], wo[:, k, m * 128:(m + 1) * 128],
                           ygc[:, k, 0:fs], k == 0, k == 15))
                P.v(tt(bobuf[:, m, 0:fs], ps_mm[:, 0:fs], dbuf[:, m, 0:fs],
                       ALU.add))
            fm_ln(P, lambda k: bobuf[:, k, 0:fs], fs, bobuf, sqb,
                  ones128, onesr, ps_row, ps_bc, rows,
                  nrm_view=lambda k: bobuf[:, k, 0:fs])
            for m in range(8):
                P.v(tt(xs[:, m, fo:fo + fs], bobuf[:, m, 0:fs],
                       xs[:, m, fo:fo + fs], ALU.add))
            P.dma(x1_d.rearrange("(b p) l -> p b l", p=128)[:, :, fo:fo + fs],
                  xs[:, :, fo:fo + fs])
        P.dmag(B["wq"], wq_d.rearrange("(b p) c -> p b c", p=128))
        P.v(mset(B["xbpc"][:, :, 0:1], 0.0))
        for ci, (fo, fs) in enumerate(scan_f(TSE, 512)):
            P.dma(B["tabc"][:, :, 0:fs],
                  tab_d[:, :, fo:fo + fs].transpose([1, 0, 2]))
            fm_ln(P, lambda k: xs[:, k, fo:fo + fs], fs, dbuf, sqb,
                  ones128, onesr, ps_row, ps_bc, rows,
                  nrm_view=lambda k: dbuf[:, k, 0:fs])
            for k in range(8):
                P.v(cp(B["xbpc"][:, k, 1:1 + fs], dbuf[:, k, 0:fs]))
            P.dma(xb_d.rearrange("(b p) l -> p b l", p=128)[:, :, fo:fo + fs],
                  dbuf[:, :, 0:fs])
            qkv_chunk(P, B, fo, fs, q_od, k_od, v_od)
        P.finalize(blk)
    return nc


def build_D(last):
    nc = bass.Bass("TRN2", target_bir_lowering=False, debug=False)
    qi_d = nc.dram_tensor("qi", [INNER, TSE], BF16, kind="ExternalInput").ap()
    ki_d = nc.dram_tensor("ki", [INNER, TSE], BF16, kind="ExternalInput").ap()
    vi_d = nc.dram_tensor("vTi", [TSE, INNER], BF16, kind="ExternalInput").ap()
    kt_d = nc.dram_tensor("ktail", [INNER, 768], BF16, kind="ExternalInput").ap()
    vt_d = nc.dram_tensor("vtail", [768, INNER], BF16, kind="ExternalInput").ap()
    mk_d = nc.dram_tensor("masks", [17, 128, STRIP], F32,
                          kind="ExternalInput").ap()
    xe_d = nc.dram_tensor("x1e", [D, TSE], F32, kind="ExternalInput").ap()
    wo_d = nc.dram_tensor("woA", [INNER, D], F32, kind="ExternalInput").ap()
    w1_d = nc.dram_tensor("w1f", [D, NH1], F32, kind="ExternalInput").ap()
    b1_d = nc.dram_tensor("b1p", [NH1], F32, kind="ExternalInput").ap()
    w2_d = nc.dram_tensor("w2f", [2816, D], F32, kind="ExternalInput").ap()
    b2_d = nc.dram_tensor("b2v", [D], F32, kind="ExternalInput").ap()
    lg_d = nc.dram_tensor("lgv", [D], F32, kind="ExternalInput").ap()
    lb_d = nc.dram_tensor("lbv", [D], F32, kind="ExternalInput").ap()
    em_d = nc.dram_tensor("extm", [128, 8, 128], F32, kind="ExternalInput").ap()
    id_d = nc.dram_tensor("ident", [128, 128], BF16, kind="ExternalInput").ap()
    x1o_d = nc.dram_tensor("x1o", [D, TSE], F32, kind="ExternalOutput").ap()
    x1b_d = nc.dram_tensor("x1b_scr", [D, TSE], BF16).ap()
    if not last:
        wq_d = nc.dram_tensor("wqkv", [D, 3 * INNER], F32,
                              kind="ExternalInput").ap()
        qs_d = nc.dram_tensor("qsr", [128, 1], F32, kind="ExternalInput").ap()
        ks_d = nc.dram_tensor("ksr", [128, 1], F32, kind="ExternalInput").ap()
        tab_d = nc.dram_tensor("tabs", [4, 128, TSE], F32,
                               kind="ExternalInput").ap()
        oneh_d = nc.dram_tensor("oneh", [1, 2, 128], F32,
                                kind="ExternalInput").ap()
        rot_d = nc.dram_tensor("rotm", [128, 128], F32,
                               kind="ExternalInput").ap()
        q_od = nc.dram_tensor("qo1", [INNER, TSE], BF16,
                              kind="ExternalOutput").ap()
        k_od = nc.dram_tensor("ko1", [INNER, TSE], BF16,
                              kind="ExternalOutput").ap()
        v_od = nc.dram_tensor("vTo1", [TSE, INNER], BF16,
                              kind="ExternalOutput").ap()
    else:
        x0_d = nc.dram_tensor("x1own", [D, TSH], F32,
                              kind="ExternalInput").ap()
        i32_d = nc.dram_tensor("iden32", [128, 128], F32,
                               kind="ExternalInput").ap()
        out_d = nc.dram_tensor("outp", [TSH, D], F32,
                               kind="ExternalOutput").ap()
        xf_d = nc.dram_tensor("xf_scr", [D, TSE], F32).ap()

    with ExitStack() as st:
        P = Prog(nc, st)
        Consts(P, nc, st)
        xm = st.enter_context(nc.sbuf_tensor("xm", [128, 8, TSE + 1], F32))
        qbt = st.enter_context(nc.sbuf_tensor("qbt", [128, 4, 128], BF16))
        kstr = st.enter_context(nc.sbuf_tensor("kstr", [128, 4, STRIP], BF16))
        vstr = st.enter_context(nc.sbuf_tensor("vstr", [128, 5, 512], BF16))
        maskc = st.enter_context(nc.sbuf_tensor("maskc", [128, STRIP], F32))
        simb = st.enter_context(nc.sbuf_tensor("simb", [128, STRIP], F32))
        pbuf = st.enter_context(nc.sbuf_tensor("pbuf", [128, STRIP], BF16))
        ptb = st.enter_context(nc.sbuf_tensor("ptb", [128, 5, 128], BF16))
        cols = st.enter_context(nc.sbuf_tensor("colsd", [128, 4], F32))
        obf = st.enter_context(nc.sbuf_tensor("obf", [128, 4, 128], BF16))
        rbuf = st.enter_context(nc.sbuf_tensor("rbuf", [128, 8, 512], F32))
        dbuf = st.enter_context(nc.sbuf_tensor("dbuf", [128, 8, 512], F32))
        sqb = st.enter_context(nc.sbuf_tensor("sqb", [128, 8, 512], F32))
        xbm = st.enter_context(nc.sbuf_tensor("xbm", [128, 8, 512], BF16))
        hbf = st.enter_context(nc.sbuf_tensor("hbf", [128, 22, 512], BF16))
        wmg = st.enter_context(nc.sbuf_tensor("wmg", [128, 8, 128], BF16))
        w2g = st.enter_context(nc.sbuf_tensor("w2g", [128, 22, 128], BF16))
        h1b = st.enter_context(nc.sbuf_tensor("h1b", [128, 512], F32))
        gb = st.enter_context(nc.sbuf_tensor("gb", [64, 512], F32))
        b1e = st.enter_context(nc.sbuf_tensor("b1e", [128, 44], F32))
        b1pv = st.enter_context(nc.sbuf_tensor("b1pv", [128, 44], F32))
        lgc = st.enter_context(nc.sbuf_tensor("lgc", [128, 8], F32))
        lbb = st.enter_context(nc.sbuf_tensor("lbb", [128, 8], BF16))
        b2c = st.enter_context(nc.sbuf_tensor("b2c", [128, 8], F32))
        extm = st.enter_context(nc.sbuf_tensor("extm_s", [128, 8, 128], F32))
        rows = st.enter_context(nc.sbuf_tensor("rows", [1, 2, 512], F32))
        ones128 = st.enter_context(nc.sbuf_tensor("ones128", [128, 1], F32))
        onesr = st.enter_context(nc.sbuf_tensor("onesr", [1, 128], F32))
        ident = st.enter_context(nc.sbuf_tensor("ident_s", [128, 128], BF16))
        wob = st.enter_context(nc.sbuf_tensor("wob", [128, 4, D], BF16))
        ps_sim = st.enter_context(nc.psum_tensor("ps_sim", [128, STRIP], F32))
        ps_pt = st.enter_context(nc.psum_tensor("ps_pt", [128, STRIP], BF16))
        ps_o = st.enter_context(nc.psum_tensor("ps_o", [128, 128], F32))
        ps_mm = st.enter_context(nc.psum_tensor("ps_mm", [128, 512], F32))
        ps_bc = st.enter_context(nc.psum_tensor("ps_bc", [128, 512], F32))
        ps_row = st.enter_context(nc.psum_tensor("ps_row", [1, 512], F32))
        ps_t = st.enter_context(nc.psum_tensor("ps_t", [128, 512], BF16))

        if not last:
            qsr = st.enter_context(nc.sbuf_tensor("qsr_s", [128, 1], F32))
            ksr = st.enter_context(nc.sbuf_tensor("ksr_s", [128, 1], F32))
            rotm = st.enter_context(nc.sbuf_tensor("rotm_s", [128, 128], F32))
            oneh = st.enter_context(nc.sbuf_tensor("oneh_s", [1, 2, 128], F32))
            wqm = st.enter_context(nc.sbuf_tensor("wqm", [128, 8, 128], BF16))
            B = alloc_qkv_bufs(nc, st, "qn", qf=dbuf[:, 0:4, :],
                               sqh=sqb[:, 0:4, :], vb=xbm[:, 0:4, :],
                               vtb=xbm[:, 4:8, :], rqr=rows,
                               t123=(rbuf[:, 4, :], rbuf[:, 5, :],
                                     rbuf[:, 6, :]),
                               qo=hbf[:, 0, 0:512], tabc=rbuf[:, 0:4, :])
            B.update(dict(ones128=ones128, onesr=onesr, oneh=oneh,
                          ident=ident, rotm=rotm, ps_mm=ps_mm, ps_bc=ps_bc,
                          ps_row=ps_row, ps_t=ps_t, qsr=qsr[:, 0:1],
                          ksr=ksr[:, 0:1], wq_d=wq_d, wqm=wqm))
        else:
            iden32 = st.enter_context(
                nc.sbuf_tensor("iden32_s", [128, 128], F32))
            otb = st.enter_context(nc.sbuf_tensor("otb", [128, D], F32))
        blk = st.enter_context(nc.Block())

        P.dma(xm[:, :, 1:TSE + 1], xe_d.rearrange("(b p) l -> p b l", p=128))
        P.v(mset(xm[:, :, 0:1], 0.0))
        P.dmag(wob[:, :, :], wo_d.rearrange("(b p) c -> p b c", p=128))
        P.v(mset(ones128[:, :], 1.0))
        P.v(mset(onesr[:, :], 1.0))
        P.dma(ident[:, :], id_d)
        P.dma(extm[:, :, :], em_d)
        vec_cols(P, lgc, lg_d, 8)
        vec_cols(P, lbb, lb_d, 8, cast=True)
        vec_cols(P, b2c, b2_d, 8)
        vec_cols(P, b1pv[:, 0:43], b1_d, 43)
        if not last:
            P.dma(qsr[:, :], qs_d)
            P.dma(ksr[:, :], ks_d)
            P.dma(rotm[:, :], rot_d)
            P.dma(oneh[:, :, :], oneh_d)
        else:
            P.dma(iden32[:, :], i32_d)

        for i in range(17):
            c0 = 128 * i + 128
            P.dma(maskc[:, :], mk_d[i, :, :])
            P.dma(qbt[:, :, :],
                  qi_d.rearrange("(b p) l -> p b l", p=128)
                  [:, :, 128 * i:128 * (i + 1)])
            nt = max(0, min(768 - c0, STRIP))
            if nt > 0:
                P.dma(kstr[:, :, 0:nt],
                      kt_d.rearrange("(b p) l -> p b l", p=128)
                      [:, :, c0:c0 + nt])
            if nt < STRIP:
                o0 = c0 + nt - 768 + 128
                P.dma(kstr[:, :, nt:STRIP],
                      ki_d.rearrange("(b p) l -> p b l", p=128)
                      [:, :, o0:o0 + STRIP - nt])
            for j in range(5):
                t0 = c0 + 128 * j
                if t0 < 768:
                    P.dma(vstr[:, j, :],
                          vt_d[t0:t0 + 128, :].rearrange("(a p) c -> p a c",
                                                         p=128)[:, 0, :])
                else:
                    o0 = t0 - 768 + 128
                    P.dma(vstr[:, j, :],
                          vi_d[o0:o0 + 128, :].rearrange("(a p) c -> p a c",
                                                         p=128)[:, 0, :])
            for h in range(8):
                tl, b = h // 2, 64 * (h % 2)
                P.t(mm(ps_sim[:, 0:512], qbt[b:b + 64, tl, :],
                       kstr[b:b + 64, tl, 0:512], True, True))
                P.t(mm(ps_sim[:, 512:STRIP], qbt[b:b + 64, tl, :],
                       kstr[b:b + 64, tl, 512:STRIP], True, True))
                P.v(tt(simb[:, :], ps_sim[:, :], maskc[:, :], ALU.add))
                P.v(red(cols[:, 0:1], simb[:, :], ALU.max, negate=True))
                P.s(act(pbuf[:, :], simb[:, :], AF.Exp, bias=cols[:, 0:1],
                        accum=cols[:, 1:2]))
                P.v(rcp(cols[:, 2:3], cols[:, 1:2]))
                P.v(ts(pbuf[:, :], pbuf[:, :], cols[:, 2:3], None, ALU.mult))
                for j in range(5):
                    P.t(trans(ps_pt[:, 128 * j:128 * (j + 1)],
                              pbuf[:, 128 * j:128 * (j + 1)], ident[:, :]))
                for j in range(5):
                    P.s(act(ptb[:, j, :], ps_pt[:, 128 * j:128 * (j + 1)],
                            AF.Copy))
                for j in range(5):
                    P.t(mm(ps_o[0:64, :], vstr[:, j, 64 * h:64 * h + 64],
                           ptb[:, j, :], j == 0, j == 4))
                P.s(act(obf[b:b + 64, tl, :], ps_o[0:64, :], AF.Copy))
            for m in range(8):
                for k in range(4):
                    P.t(mm(ps_mm[:, 0:128], wob[:, k, m * 128:(m + 1) * 128],
                           obf[:, k, :], k == 0, k == 3))
                P.v(tt(xm[:, m, 1 + 128 * i:1 + 128 * (i + 1)],
                       ps_mm[:, 0:128],
                       xm[:, m, 1 + 128 * i:1 + 128 * (i + 1)], ALU.add))
        for ci, (fo, fs) in enumerate(scan_f(TSE, 512)):
            def xv(k, fo=fo, fs=fs):
                return (xm[:, k, 1 + fo:1 + fo + fs] if k < 4
                        else xm[:, k, fo:fo + fs])
            fm_ln(P, xv, fs, dbuf, sqb, ones128, onesr, ps_row, ps_bc, rows,
                  nrm_view=None)
            for k in range(8):
                P.v(tt(xbm[:, k, 0:fs], dbuf[:, k, 0:fs], ps_bc[:, 0:fs],
                       ALU.mult))
            P.v(mset(hbf[64:128, 21, 0:fs], 0.0))
            for mg in range(43):
                P.dmag(wmg[:, :, :],
                       w1_d.rearrange("(b p) c -> p b c", p=128)
                       [:, :, mg * 128:(mg + 1) * 128])
                if ci == 0:
                    for k in range(8):
                        P.t(mm(ps_o[:, 0:1], wmg[:, k, :], lbb[:, k:k + 1],
                               k == 0, k == 7))
                    P.s(act(b1e[:, mg:mg + 1], ps_o[:, 0:1], AF.Copy))
                    P.v(tt(b1e[:, mg:mg + 1], b1e[:, mg:mg + 1],
                           b1pv[:, mg:mg + 1], ALU.add))
                for k in range(8):
                    P.v(ts(wmg[:, k, :], wmg[:, k, :], lgc[:, k:k + 1], None,
                           ALU.mult))
                for k in range(8):
                    P.t(mm(ps_mm[:, 0:fs], wmg[:, k, :], xbm[:, k, 0:fs],
                           k == 0, k == 7))
                P.s(act(h1b[:, 0:fs], ps_mm[:, 0:fs], AF.Identity,
                        bias=b1e[:, mg:mg + 1]))
                P.s(act(gb[:, 0:fs], h1b[64:128, 0:fs], AF.Gelu))
                P.v(tt(hbf[64 * (mg % 2):64 * (mg % 2) + 64, mg // 2, 0:fs],
                       h1b[0:64, 0:fs], gb[:, 0:fs], ALU.mult))
            for m in range(8):
                P.dmag(w2g[:, :, :],
                       w2_d.rearrange("(b p) c -> p b c", p=128)
                       [:, :, m * 128:(m + 1) * 128])
                for k in range(22):
                    P.t(mm(ps_mm[:, 0:fs], w2g[:, k, :], hbf[:, k, 0:fs],
                           k == 0, k == 21))
                P.s(act(rbuf[:, m, 0:fs], ps_mm[:, 0:fs], AF.Identity,
                        bias=b2c[:, m:m + 1]))
                P.v(tt(rbuf[:, m, 0:fs], rbuf[:, m, 0:fs],
                       xm[:, m, 1 + fo:1 + fo + fs], ALU.add))
            if fo < EXT:
                for m in range(8):
                    P.v(tt(rbuf[:, m, 0:EXT], rbuf[:, m, 0:EXT],
                           extm[:, m, :], ALU.mult))
            P.dma(x1o_d.rearrange("(b p) l -> p b l", p=128)[:, :, fo:fo + fs],
                  rbuf[:, :, 0:fs])
            if not last:
                P.dmag(x1b_d.rearrange("(b p) l -> p b l", p=128)
                       [:, :, fo:fo + fs], rbuf[:, :, 0:fs])
            else:
                P.dma(xf_d.rearrange("(b p) l -> p b l", p=128)
                      [:, :, fo:fo + fs], rbuf[:, :, 0:fs])
        if not last:
            for ci, (fo, fs) in enumerate(scan_f(TSE, 512)):
                P.dma(B["tabc"][:, :, 0:fs],
                      tab_d[:, :, fo:fo + fs].transpose([1, 0, 2]))
                if fo == 0:
                    P.dma(B["xbpc"][:, :, 1:1 + fs],
                          x1b_d.rearrange("(b p) l -> p b l", p=128)
                          [:, :, 0:fs])
                    P.v(mset(B["xbpc"][:, :, 0:1], 0.0))
                else:
                    P.dma(B["xbpc"][:, :, 0:1 + fs],
                          x1b_d.rearrange("(b p) l -> p b l", p=128)
                          [:, :, fo - 1:fo + fs])
                qkv_chunk(P, B, fo, fs, q_od, k_od, v_od)
        else:
            for fo, fs in scan_f(TSH, 512):
                P.dma(rbuf[:, :, 0:fs],
                      xf_d.rearrange("(b p) l -> p b l", p=128)
                      [:, :, EXT + fo:EXT + fo + fs])
                fm_ln(P, lambda k, fs=fs: rbuf[:, k, 0:fs], fs, dbuf, sqb,
                      ones128, onesr, ps_row, ps_bc, rows,
                      nrm_view=lambda k, fs=fs: dbuf[:, k, 0:fs])
                P.dma(sqb[:, :, 0:fs],
                      x0_d.rearrange("(b p) l -> p b l", p=128)
                      [:, :, fo:fo + fs])
                for m in range(8):
                    P.v(tt(dbuf[:, m, 0:fs], dbuf[:, m, 0:fs],
                           sqb[:, m, 0:fs], ALU.add))
                for j in range(fs // 128):
                    for m in range(8):
                        P.t(mm(ps_mm[:, 0:128], iden32[:, :],
                               dbuf[:, m, 128 * j:128 * (j + 1)],
                               True, True) if False else
                            trans(ps_mm[:, 0:128],
                                  dbuf[:, m, 128 * j:128 * (j + 1)],
                                  iden32[:, :]))
                        P.s(act(otb[:, m * 128:(m + 1) * 128],
                                ps_mm[:, 0:128], AF.Copy))
                    P.dma(out_d[fo + 128 * j:fo + 128 * (j + 1), :],
                          otb[:, :])
        P.finalize(blk)
    return nc


def rotary_tables():
    half = np.arange(0, DH, 2, dtype=np.float64)
    inv_freq = 1.0 / 10000.0 ** (half / DH)
    t = np.arange(T, dtype=np.float64)
    freqs = t[:, None] * inv_freq[None, :]
    freqs = np.concatenate([freqs, freqs], -1)
    base = (half + 0.4 * DH) / (1.4 * DH)
    power = (t - T // 2) / (WIN // 2)
    scale = base[None, :] ** power[:, None]
    scale = np.concatenate([scale, scale], -1)
    return np.cos(freqs), np.sin(freqs), scale


def tabs_for(tok0):
    cos, sin, scale = rotary_tables()
    idx = np.clip(np.arange(tok0, tok0 + TSE), 0, T - 1)
    sgn = np.where(np.arange(64) % 64 < 32, -1.0, 1.0)

    def dbl(tab):
        t_ = tab[idx].T.astype(np.float32)
        return np.concatenate([t_, t_], 0)
    csq = dbl(cos * scale) * QKS
    snq = dbl(sin * scale * sgn[None, :]) * QKS
    csk = dbl(cos / scale)
    snk = dbl(sin / scale * sgn[None, :])
    return np.ascontiguousarray(np.stack([csq, snq, csk, snk]))


def rotm_host():
    r = np.zeros((128, 128), np.float32)
    for d_ in range(128):
        rr = (d_ % 64 + 32) % 64 + 64 * (d_ // 64)
        r[rr, d_] = 1.0
    return r


def oneh_host():
    o = np.zeros((1, 2, 128), np.float32)
    o[0, 0, 0:64] = 1.0
    o[0, 1, 64:128] = 1.0
    return o


def masks_for(core):
    t0 = core * TSH - EXT
    out = np.empty((17, 128, STRIP), np.float32)
    for i in range(17):
        q0 = t0 + 128 * i
        q = q0 + np.arange(128)[:, None]
        k = q0 - WIN + np.arange(STRIP)[None, :]
        bad = (k > q) | (k < q - WIN) | (k < 0)
        m = np.where(bad, NEG, np.float32(0.0)).astype(np.float32)
        dead = (q[:, 0] < 0)
        m[dead, :] = NEG
        m[dead, 0] = 0.0
        out[i] = m
    return np.ascontiguousarray(out)


def permute_w1(w1, b1):
    a, g = w1[:, :HID], w1[:, HID:]
    ab, gb = b1[:HID], b1[HID:]
    ap_ = np.concatenate([a, np.zeros((D, HIDP - HID), w1.dtype)], 1)
    gp = np.concatenate([g, np.zeros((D, HIDP - HID), w1.dtype)], 1)
    abp = np.concatenate([ab, np.zeros(HIDP - HID, b1.dtype)])
    gbp = np.concatenate([gb, np.zeros(HIDP - HID, b1.dtype)])
    cols, bcols = [], []
    for j in range(HIDP // 64):
        cols += [ap_[:, j * 64:(j + 1) * 64], gp[:, j * 64:(j + 1) * 64]]
        bcols += [abp[j * 64:(j + 1) * 64], gbp[j * 64:(j + 1) * 64]]
    return (np.ascontiguousarray(np.concatenate(cols, 1)),
            np.ascontiguousarray(np.concatenate(bcols)))


def permute_w2(w2):
    return np.ascontiguousarray(
        np.concatenate([w2, np.zeros((2816 - HID, D), w2.dtype)], 0))


def _cc(a):
    return np.ascontiguousarray(a)


def kernel_unfused(**inputs):
    f32 = np.float32
    x = np.asarray(inputs["L0_attn_x"], f32)[0]
    w_in = np.asarray(inputs["L0_attn_w_in"], f32)
    ident = np.eye(128).astype(NPBF)
    iden32 = np.eye(128, dtype=f32)
    oneh = oneh_host()
    rotm = rotm_host()

    nc = build_A()
    maps = [dict(xa=_cc(x[c * TSH:(c + 1) * TSH]), ident=ident)
            for c in range(NC)]
    res = run_bass_kernel_spmd(nc, maps, core_ids=list(range(NC)))
    xn_full = np.concatenate([np.asarray(r["xn"]) for r in res.results], 1)

    nc = build_B0()
    maps = []
    for c in range(NC):
        sl = slice(c * ES, (c + 1) * ES)
        w_sl = np.concatenate([w_in[sl], w_in[E + c * ES:E + (c + 1) * ES]], 0)
        maps.append(dict(
            xnf=_cc(xn_full),
            w_in_slT=_cc(w_sl.T),
            lnw=np.asarray(inputs["L0_attn_ln_w"], f32),
            lnb=np.asarray(inputs["L0_attn_ln_b"], f32),
            cw=_cc(np.asarray(inputs["L0_attn_conv_w"], f32)[sl]),
            cb=_cc(np.asarray(inputs["L0_attn_conv_b"], f32)[sl]),
            wxpT=_cc(np.asarray(inputs["L0_attn_w_xproj"], f32)[:, sl].T),
        ))
    res = run_bass_kernel_spmd(nc, maps, core_ids=list(range(NC)))
    u32 = [np.asarray(r["u32"]) for r in res.results]
    z32 = [np.asarray(r["z32"]) for r in res.results]
    xdblp = np.stack([np.asarray(r["xdblp"]) for r in res.results])

    nc = build_B1()
    maps = []
    for c in range(NC):
        sl = slice(c * ES, (c + 1) * ES)
        maps.append(dict(
            u32=u32[c], z32=z32[c], xdblp=xdblp,
            wdtT=_cc(np.asarray(inputs["L0_attn_w_dt"], f32)[sl].T),
            bdt=_cc(np.asarray(inputs["L0_attn_b_dt"], f32)[sl]),
            alog=_cc(np.asarray(inputs["L0_attn_a_log"], f32)[sl]),
            dskip=_cc(np.asarray(inputs["L0_attn_d_skip"], f32)[sl]),
        ))
    res = run_bass_kernel_spmd(nc, maps, core_ids=list(range(NC)))
    yg_full = np.concatenate([np.asarray(r["yg"]) for r in res.results], 0)

    def ext_sliceT(full_T_first):
        outs = []
        for c in range(NC):
            t0 = c * TSH - EXT
            o = np.zeros((TSE,) + full_T_first.shape[1:], full_T_first.dtype)
            lo = max(t0, 0)
            o[lo - t0:] = full_T_first[lo:t0 + TSE]
            outs.append(_cc(o.T))
        return outs

    qs0 = np.asarray(inputs["L0_ffn_q_scale"], f32)
    ks0 = np.asarray(inputs["L0_ffn_k_scale"], f32)
    xe_s = ext_sliceT(x)
    yge_s = ext_sliceT(yg_full.T)
    nc = build_C()
    maps = []
    for c in range(NC):
        t0 = c * TSH - EXT
        maps.append(dict(
            xe=xe_s[c],
            yge=_cc(yge_s[c].astype(NPBF)),
            woutT=_cc(np.asarray(inputs["L0_attn_w_out"], f32).T),
            wqkv=_cc(np.asarray(inputs["L0_ffn_wqkv"], f32)[0]),
            qsr=_cc(np.tile(qs0[0], 2)[:, None]),
            ksr=_cc(np.tile(ks0[0], 2)[:, None]),
            tabs=tabs_for(t0), oneh=oneh, rotm=rotm, ident=ident,
        ))
    res = run_bass_kernel_spmd(nc, maps, core_ids=list(range(NC)))
    x1e = [np.asarray(r["x1e"]) for r in res.results]
    xbe = [np.asarray(r["xbe"]) for r in res.results]
    q0 = [np.asarray(r["q0"]) for r in res.results]
    k0 = [np.asarray(r["k0"]) for r in res.results]
    v0 = [np.asarray(r["vT0"]) for r in res.results]

    def run_layer(last, stream, qs_, ks_, vs_, layer, x1own=None):
        nc = build_D(last)
        w1f, b1p = permute_w1(np.asarray(inputs["L0_ffn_w1"], f32)[layer],
                              np.asarray(inputs["L0_ffn_b1"], f32)[layer])
        w2f = permute_w2(np.asarray(inputs["L0_ffn_w2"], f32)[layer])
        maps = []
        for c in range(NC):
            t0 = c * TSH - EXT
            nb = c - 1
            if nb >= 0:
                ktail = _cc(qs_[nb][0:0] if False else
                            ks_[nb][:, 1408:2176])
                vtail = _cc(vs_[nb][1408:2176, :])
            else:
                ktail = np.zeros((INNER, 768), NPBF)
                vtail = np.zeros((768, INNER), NPBF)
            m = dict(
                qi=qs_[c], ki=ks_[c], vTi=vs_[c],
                ktail=ktail, vtail=vtail,
                masks=masks_for(c),
                x1e=stream[c],
                woA=_cc(np.asarray(inputs["L0_ffn_wo"], f32)[layer]),
                w1f=w1f, b1p=b1p, w2f=w2f,
                b2v=_cc(np.asarray(inputs["L0_ffn_b2"], f32)[layer]),
                lgv=_cc(np.asarray(inputs["L0_ffn_ln_g"], f32)[layer]),
                lbv=_cc(np.asarray(inputs["L0_ffn_ln_b"], f32)[layer]),
                extm=np.full((128, 8, 128), 0.0 if c == 0 else 1.0, f32),
                ident=ident,
            )
            if not last:
                m.update(dict(
                    wqkv=_cc(np.asarray(inputs["L0_ffn_wqkv"], f32)[1]),
                    qsr=_cc(np.tile(qs0[1], 2)[:, None]),
                    ksr=_cc(np.tile(ks0[1], 2)[:, None]),
                    tabs=tabs_for(t0), oneh=oneh, rotm=rotm,
                ))
            else:
                m.update(dict(x1own=_cc(x1own[c][:, EXT:]), iden32=iden32))
            maps.append(m)
        return run_bass_kernel_spmd(nc, maps, core_ids=list(range(NC)))

    res = run_layer(False, xbe, q0, k0, v0, 0)
    x_1 = [np.asarray(r["x1o"]) for r in res.results]
    q1 = [np.asarray(r["qo1"]) for r in res.results]
    k1 = [np.asarray(r["ko1"]) for r in res.results]
    v1 = [np.asarray(r["vTo1"]) for r in res.results]

    res = run_layer(True, x_1, q1, k1, v1, 1, x1own=x1e)
    out = np.concatenate([np.asarray(r["outp"]) for r in res.results], 0)
    return out[None].astype(np.float32)


class Cursor:
    def __init__(self, tensor, limit):
        self.t = tensor
        self.limit = limit
        self.off = 0

    def take(self, *shape):
        n = int(np.prod(shape))
        off = self.off
        self.off += n
        assert self.off <= self.limit, f"arena overflow {self.off}>{self.limit}"
        v = self.t[:, off:off + n]
        if len(shape) == 1:
            return v
        return v.rearrange("p (a b) -> p a b", a=shape[0])

    def reset(self):
        self.off = 0


def build_FUSED():
    nc = bass.Bass("TRN2", target_bir_lowering=False, debug=False)
    nc.cache_partition_id()
    f32 = F32

    def ein(name, shape, dt=F32):
        return nc.dram_tensor(name, shape, dt, kind="ExternalInput").ap()

    xa_d = ein("xa", [TSH, D])
    xe_d = ein("xe", [D, TSE])
    w_d = ein("w_in_slT", [D, 2 * ES])
    lnw_d = ein("lnw", [D])
    lnb_d = ein("lnb", [D])
    cw_d = ein("cw", [ES, 4])
    cb_d = ein("cb", [ES])
    wxp_d = ein("wxpT", [ES, 96])
    wdt_d = ein("wdtT", [DTR, ES])
    bdt_d = ein("bdt", [ES])
    alog_d = ein("alog", [ES, NST])
    dsk_d = ein("dskip", [ES])
    wo_d = ein("woutT", [E, D])
    wq0_d = ein("wqkv0", [D, 3 * INNER])
    wq1_d = ein("wqkv1", [D, 3 * INNER])
    qs0_d = ein("qsr0", [128, 1])
    ks0_d = ein("ksr0", [128, 1])
    qs1_d = ein("qsr1", [128, 1])
    ks1_d = ein("ksr1", [128, 1])
    tab_d = ein("tabs", [4, 128, TSE])
    oneh_d = ein("oneh", [1, 2, 128])
    rot_d = ein("rotm", [128, 128])
    id_d = ein("ident", [128, 128], BF16)
    i32_d = ein("iden32", [128, 128])
    mk_d = ein("masks", [17, 128, STRIP])
    em_d = ein("extm", [128, 8, 128])
    idx_d = nc.dram_tensor("idxt", [128, IXN], mybir.dt.uint32,
                           kind="ExternalInput").ap()
    woA0_d = ein("woA0", [INNER, D])
    woA1_d = ein("woA1", [INNER, D])
    w1f0_d = ein("w1f0", [D, NH1])
    w1f1_d = ein("w1f1", [D, NH1])
    b1p0_d = ein("b1p0", [NH1])
    b1p1_d = ein("b1p1", [NH1])
    w2f0_d = ein("w2f0", [2816, D])
    w2f1_d = ein("w2f1", [2816, D])
    b2v0_d = ein("b2v0", [D])
    b2v1_d = ein("b2v1", [D])
    lgv0_d = ein("lgv0", [D])
    lgv1_d = ein("lgv1", [D])
    lbv0_d = ein("lbv0", [D])
    lbv1_d = ein("lbv1", [D])
    out_d = nc.dram_tensor("outp", [TSH, D], F32, kind="ExternalOutput").ap()

    xn_ai = nc.dram_tensor("xn_ai", [D, T], BF16).ap()
    xn_ao = nc.dram_tensor("xn_ao", [D, T], BF16).ap()
    xd_ai = nc.dram_tensor("xd_ai", [96, T], F32).ap()
    xd_ao = nc.dram_tensor("xd_ao", [96, T], F32).ap()
    y_ai = nc.dram_tensor("y_ai", [E, YW], BF16).ap()
    y_ao = nc.dram_tensor("y_ao", [E, YW], BF16).ap()
    kg_ai = [nc.dram_tensor(f"kg_ai{l}", [INNER, 9 * 768], BF16).ap()
             for l in range(2)]
    kg_ao = [nc.dram_tensor(f"kg_ao{l}", [INNER, 9 * 768], BF16).ap()
             for l in range(2)]
    vg_ai = [nc.dram_tensor(f"vg_ai{l}", [9 * 768, INNER], BF16).ap()
             for l in range(2)]
    vg_ao = [nc.dram_tensor(f"vg_ao{l}", [9 * 768, INNER], BF16).ap()
             for l in range(2)]
    kt_loc = [nc.dram_tensor(f"kt_loc{l}", [INNER, 768], BF16).ap()
              for l in range(2)]
    vt_loc = [nc.dram_tensor(f"vt_loc{l}", [768, INNER], BF16).ap()
              for l in range(2)]
    u_d = nc.dram_tensor("u32s", [ES, T], F32).ap()
    z_d = nc.dram_tensor("z32s", [ES, T], F32).ap()
    bc_d = nc.dram_tensor("bc_scr", [32, T], F32).ap()
    yg_d = None
    x1_d = nc.dram_tensor("x1s", [D, TSE], F32).ap()
    xb_d = nc.dram_tensor("xbs", [D, TSE], F32).ap()
    q_ds = [nc.dram_tensor(f"q{l}s", [INNER, TSE], BF16).ap() for l in range(2)]
    k_ds = [nc.dram_tensor(f"k{l}s", [INNER, TSE], BF16).ap() for l in range(2)]
    v_ds = [nc.dram_tensor(f"v{l}s", [TSE, INNER], BF16).ap() for l in range(2)]
    x1o_d = nc.dram_tensor("x1os", [D, TSE], F32).ap()
    x1b_d = nc.dram_tensor("x1bs", [D, TSE], BF16).ap()
    xf_d = nc.dram_tensor("xfs", [D, TSE], F32).ap()

    AF_EL = 34000
    AB_EL = 34000
    with ExitStack() as st:
        P = Prog(nc, st)
        Consts(P, nc, st)
        arf_t = st.enter_context(nc.sbuf_tensor("arena_f", [128, AF_EL], F32))
        arb_t = st.enter_context(nc.sbuf_tensor("arena_b", [128, AB_EL], BF16))
        ident = st.enter_context(nc.sbuf_tensor("ident_s", [128, 128], BF16))
        iden32 = st.enter_context(nc.sbuf_tensor("iden32_s", [128, 128], F32))
        rotm = st.enter_context(nc.sbuf_tensor("rotm_s", [128, 128], F32))
        oneh = st.enter_context(nc.sbuf_tensor("oneh_s", [1, 2, 128], F32))
        ones128 = st.enter_context(nc.sbuf_tensor("ones128", [128, 1], F32))
        onesr = st.enter_context(nc.sbuf_tensor("onesr", [1, 128], F32))
        colsml = st.enter_context(nc.sbuf_tensor("colsml", [128, 16], F32))
        carry = st.enter_context(nc.sbuf_tensor("carry", [128, 2, NST], F32))
        acol = st.enter_context(nc.sbuf_tensor("acol", [128, 2, NST], F32))
        smlA = st.enter_context(nc.sbuf_tensor("smlA", [128, 2, 8], F32))
        smlB = st.enter_context(nc.sbuf_tensor("smlB", [128, 2, 4], F32))
        lnbb = st.enter_context(nc.sbuf_tensor("lnbb", [128, 8], BF16))
        qsr = st.enter_context(nc.sbuf_tensor("qsr_s", [128, 2], F32))
        ksr = st.enter_context(nc.sbuf_tensor("ksr_s", [128, 2], F32))
        lgc = st.enter_context(nc.sbuf_tensor("lgc", [128, 8], F32))
        lbb = st.enter_context(nc.sbuf_tensor("lbb", [128, 8], BF16))
        b2c = st.enter_context(nc.sbuf_tensor("b2c", [128, 8], F32))
        b1e = st.enter_context(nc.sbuf_tensor("b1e", [128, 44], F32))
        b1pv = st.enter_context(nc.sbuf_tensor("b1pv", [128, 44], F32))
        ixs = st.enter_context(nc.sbuf_tensor("ixs", [128, IXN],
                                              mybir.dt.uint32))
        rows0 = st.enter_context(nc.sbuf_tensor("rows0", [1, 1024], F32))
        psA = st.enter_context(nc.psum_tensor("psA", [128, 3584], F32))
        ps_bf = st.enter_context(nc.psum_tensor("ps_bf", [128, 640], BF16))
        blk = st.enter_context(nc.Block())

        arf = Cursor(arf_t, AF_EL)
        arb = Cursor(arb_t, AB_EL)
        ps_mm = psA[:, 2048:2560]
        ps_bc = psA[:, 2560:3072]
        ps_row = psA[0:1, 1024:1536]
        ps_o = psA[:, 3072:3200]
        ps_sim = psA[:, 0:640]
        ps_t = ps_bf[:, 0:512]
        ps_pt = ps_bf[:, 0:640]

        def cc(ins_ap, outs_ap):
            def b(e):
                return e.collective_compute(
                    "AllReduce", ALU.add, replica_groups=[list(range(NC))],
                    ins=[ins_ap.opt()], outs=[outs_ap.opt()])
            P.op("gpsimd", b, tag="coll")

        P.dma(ident[:, :], id_d)
        P.dma(ixs[:, :], idx_d)
        P.dma(iden32[:, :], i32_d)
        P.dma(rotm[:, :], rot_d)
        P.dma(oneh[:, :, :], oneh_d)
        P.v(mset(ones128[:, :], 1.0))
        P.v(mset(onesr[:, :], 1.0))
        zrow = arb.take(16384)
        P.v(mset(zrow[:, :], 0.0))
        rows = rows0.ap().rearrange("p (a b) -> p a b", a=2)
        P.dma(qsr[:, 0:1], qs0_d)
        P.dma(qsr[:, 1:2], qs1_d)
        P.dma(ksr[:, 0:1], ks0_d)
        P.dma(ksr[:, 1:2], ks1_d)
        zv = lambda a_, b_: zrow[:, 0:a_ * b_].rearrange(
            "p (a b) -> p a b", a=a_)
        for j in range(0, T, 2048):
            P.dma(xn_ai.rearrange("(b p) l -> p b l", p=128)[:, :, j:j + 2048],
                  zv(8, 2048))
        for j in range(0, YW, 1024):
            P.dma(y_ai.rearrange("(b p) l -> p b l", p=128)[:, :, j:j + 1024],
                  zv(16, 1024))
        for l in range(2):
            for j in range(0, 9 * 768, 1728):
                P.dma(kg_ai[l].rearrange("(b p) l -> p b l", p=128)
                      [:, :, j:j + 1728], zv(4, 1728))
            for sl in range(0, 9, 3):
                P.dma(vg_ai[l][sl * 768:(sl + 3) * 768, :]
                      .rearrange("(a p) c -> p a c", p=128), zv(18, 512))
        arb.reset()
        x_s = arf.take(16, D)
        sqs = arf.take(D)
        xnb = arb.take(16, D)
        xna = arb.take(8, 2048)
        P.dma(x_s[:, :, :], xa_d.rearrange("(n p) d -> p n d", p=128))
        for i in range(16):
            xt = x_s[:, i, :]
            for rep in range(2):
                P.v(red(colsml[:, 0:1], xt, ALU.add))
                P.v(ts(colsml[:, 0:1], colsml[:, 0:1], 1.0 / D, None,
                       ALU.mult))
                P.v(ts(xt, xt, colsml[:, 0:1], None, ALU.subtract))
                P.s(act(sqs[:, :], xt, AF.Square, accum=colsml[:, 2:3]))
                P.s(act(colsml[:, 2:3], colsml[:, 2:3], AF.Ln, bias=EPS,
                        scale=1.0 / D))
                P.s(act(colsml[:, 2:3], colsml[:, 2:3], AF.Exp, scale=-0.5))
                if rep == 0:
                    P.v(ts(xt, xt, colsml[:, 2:3], None, ALU.mult))
                else:
                    P.v(ts(xnb[:, i, :], xt, colsml[:, 2:3], None, ALU.mult))
        for i in range(16):
            for dbq in range(2):
                for j in range(4):
                    db = dbq * 4 + j
                    P.t(trans(ps_t[:, j * 128:(j + 1) * 128],
                              xnb[:, i, db * 128:(db + 1) * 128], ident[:, :]))
                for j in range(4):
                    db = dbq * 4 + j
                    P.s(act(xna[:, db, 128 * i:128 * (i + 1)],
                            ps_t[:, j * 128:(j + 1) * 128], AF.Copy))
        xnai_f = xn_ai.rearrange("d (j l) -> (d j) l", l=TSH)
        for b in range(8):
            P.ind(xnai_f, ixs[:, IXA + b:IXA + b + 1], xna[:, b, :], None)
        cc(xn_ai, xn_ao)
        arf.reset()
        arb.reset()
        wt = arb.take(8, 512)
        xnc = arb.take(8, 2048)
        ub = arb.take(2, 2048)
        wxt = arb.take(2, 96)
        xs3 = arf.take(2, 2048 + 3)
        zb = arf.take(2, 2048)
        uc = arf.take(2, 2048)
        lnw_s = smlA[:, :, :].rearrange("p a b -> p (a b)")
        bias_s = smlB[:, 0, :]
        cw_s = arf.take(2, 4)
        cb_s = colsml[:, 8:10]
        P.dmag(wt[:, :, :], w_d.rearrange("(b p) r -> p b r", p=128))
        vec_cols(P, lnw_s, lnw_d, 8)
        vec_cols(P, lnbb, lnb_d, 8, cast=True)
        P.dma(cw_s[:, :, :], cw_d.rearrange("(b p) k -> p b k", p=128))
        vec_cols(P, cb_s, cb_d, 2)
        P.dmag(wxt[:, :, :], wxp_d.rearrange("(b p) r -> p b r", p=128))
        for mg in range(4):
            for k in range(8):
                P.t(mm(psA[0:128, 3200 + mg:3200 + mg + 1],
                       wt[:, k, mg * 128:(mg + 1) * 128],
                       lnbb[:, k:k + 1], k == 0, k == 7))
        P.s(act(bias_s[:, 0:4], psA[0:128, 3200:3204], AF.Copy))
        for db in range(8):
            P.v(ts(wt[:, db, :], wt[:, db, :], lnw_s[:, db:db + 1], None,
                   ALU.mult))
        P.v(mset(xs3[:, :, 0:3], 0.0))
        LC0 = 2048
        for c in range(T // LC0):
            P.dma(xnc[:, :, :],
                  xn_ao.rearrange("(b p) l -> p b l", p=128)
                  [:, :, c * LC0:(c + 1) * LC0])
            for m in range(4):
                for fo, fs in scan_f(LC0, 512):
                    for k in range(8):
                        P.t(mm(ps_mm[:, 0:fs], wt[:, k, m * 128:(m + 1) * 128],
                               xnc[:, k, fo:fo + fs], k == 0, k == 7))
                    if m < 2:
                        P.s(act(xs3[:, m, 3 + fo:3 + fo + fs], ps_mm[:, 0:fs],
                                AF.Identity, bias=bias_s[:, m:m + 1]))
                    else:
                        P.s(act(zb[:, m - 2, fo:fo + fs], ps_mm[:, 0:fs],
                                AF.Identity, bias=bias_s[:, m:m + 1]))
            for tb in range(2):
                P.v(ts(uc[:, tb, :], xs3[:, tb, 0:LC0], cw_s[:, tb, 0:1],
                       cb_s[:, tb:tb + 1], ALU.mult, ALU.add))
                for k in range(1, 4):
                    P.v(stt(uc[:, tb, :], xs3[:, tb, k:k + LC0],
                            cw_s[:, tb, k:k + 1], uc[:, tb, :],
                            ALU.mult, ALU.add))
                P.v(cp(xs3[:, tb, 0:3], xs3[:, tb, LC0:LC0 + 3]))
                P.s(act(uc[:, tb, :], uc[:, tb, :], AF.Silu))
                P.v(cp(ub[:, tb, :], uc[:, tb, :]))
            P.dma(u_d.rearrange("(b p) l -> p b l", p=128)
                  [:, :, c * LC0:(c + 1) * LC0], uc[:, :, :])
            P.dma(z_d.rearrange("(b p) l -> p b l", p=128)
                  [:, :, c * LC0:(c + 1) * LC0], zb[:, :, :])
            for fo, fs in scan_f(LC0, 512):
                for k in range(2):
                    P.t(mm(ps_mm[0:96, 0:fs], wxt[:, k, :],
                           ub[:, k, fo:fo + fs], k == 0, k == 1))
                P.s(act(zb[0:96, 0, fo:fo + fs], ps_mm[0:96, 0:fs], AF.Copy))
            P.dma(xd_ai[:, c * LC0:(c + 1) * LC0], zb[0:96, 0, 0:LC0])
        cc(xd_ai, xd_ao)
        arf.reset()
        arb.reset()

        wdtt0 = arb.take(2, 128)
        dtrb = arb.take(1, 1024)
        ygb = arb.take(1, 1024)
        xds = arf.take(1, 1024)
        dtpre = arf.take(2, 1024)
        spt = arf.take(1, 1024)
        dtb = arf.take(2, 1024)
        uch = arf.take(2, 1024)
        zch = arf.take(2, 1024)
        dtu = arf.take(2, 1024)
        bcf = arf.take(2 * 1024)
        dA = arf.take(2 * 1024)
        dBu = arf.take(2 * 1024)
        hb = arf.take(2 * 1024)
        yb4 = arf.take(8, 1024)
        yb = arf.take(1, 1024)
        fixc = colsml[:, 10:12]
        bdt_s = smlB[:, 1, 0:2]
        dsk_s = smlB[:, 1, 2:4]
        P.dmag(wdtt0.rearrange("p a b -> p (a b)")[0:64, :]
               .rearrange("r (b pp) -> r b pp", pp=128),
               wdt_d.rearrange("r (b p) -> r b p", p=128))
        wdtt = wdtt0.rearrange("p a b -> p (a b)")[0:64, :] \
            .rearrange("r (b pp) -> r b pp", pp=128)
        vec_cols(P, bdt_s, bdt_d, 2)
        P.dma(acol[:, :, :], alog_d.rearrange("(b p) n -> p b n", p=128))
        vec_cols(P, dsk_s, dsk_d, 2)
        P.v(mset(carry[:, :, :], 0.0))
        for tb in range(2):
            P.s(act(acol[:, tb, :], acol[:, tb, :], AF.Exp))
            P.v(ts(acol[:, tb, :], acol[:, tb, :], -1.0, None, ALU.mult))
        NG = 2
        LC1 = 1024
        for c in range(T // LC1):
            lo = c * LC1
            P.dma(xds[0:96, 0, :], xd_ao[:, lo:lo + LC1])
            P.v(cp(dtrb[:, 0, :][0:64, :],
                   xds.rearrange("p a b -> p (a b)")[0:64, 0:LC1]))
            P.dma(bc_d[:, lo:lo + LC1],
                  xds.rearrange("p a b -> p (a b)")[64:96, 0:LC1])
            for mg in range(2):
                for fo, fs in scan_f(LC1, 512):
                    P.t(mm(ps_mm[0:128, 0:fs], wdtt[:, mg, :],
                           dtrb[:, 0, :][0:64, fo:fo + fs], True, True))
                    P.s(act(dtpre[:, mg, fo:fo + fs], ps_mm[0:128, 0:fs],
                            AF.Identity, bias=bdt_s[:, mg:mg + 1]))
            for tb in range(2):
                P.s(act(spt[:, 0, :], dtpre[:, tb, :], AF.Abs))
                P.s(act(spt[:, 0, :], spt[:, 0, :], AF.Exp, scale=-1.0))
                P.s(act(spt[:, 0, :], spt[:, 0, :], AF.Ln, bias=1.0))
                P.s(act(dtb[:, tb, :], dtpre[:, tb, :], AF.Relu))
                P.v(tt(dtb[:, tb, :], dtb[:, tb, :], spt[:, 0, :], ALU.add))
            P.dma(uch[:, :, :], u_d.rearrange("(b p) l -> p b l", p=128)
                  [:, :, lo:lo + LC1])
            P.dma(zch[:, :, :], z_d.rearrange("(b p) l -> p b l", p=128)
                  [:, :, lo:lo + LC1])
            for tb in range(2):
                P.v(tt(dtu[:, tb, :], dtb[:, tb, :], uch[:, tb, :], ALU.mult))
            for tb in range(2):
                for g in range(NST // NG):
                    P.dma(bcf.rearrange("p (n l) -> p n l", n=NG)[0:1, :, :],
                          bc_d[g * NG:(g + 1) * NG, lo:lo + LC1][None, :, :])
                    for fo, fs in scan_f(NG * LC1, 512):
                        P.t(mm(psA[:, fo:fo + fs], onesr[0:1, :],
                               bcf[0:1, fo:fo + fs], True, True))
                    for n in range(NG):
                        P.s(act(dA[:, n * LC1:(n + 1) * LC1], dtb[:, tb, :],
                                AF.Exp,
                                scale=acol[:, tb, g * NG + n:g * NG + n + 1]))
                    P.v(tt(dBu.rearrange("p (n l) -> p n l", n=NG),
                           dtu[:, tb, None, :].broadcast_to([128, NG, LC1]),
                           psA[:, 0:NG * LC1].rearrange("p (n l) -> p n l",
                                                        n=NG),
                           ALU.mult))
                    P.v(tt(fixc[:, 0:NG],
                           dA.rearrange("p (n l) -> p n l", n=NG)[:, :, 0],
                           carry[:, tb, g * NG:(g + 1) * NG], ALU.mult))
                    P.v(tt(dBu.rearrange("p (n l) -> p n l", n=NG)[:, :, 0],
                           dBu.rearrange("p (n l) -> p n l", n=NG)[:, :, 0],
                           fixc[:, 0:NG], ALU.add))
                    P.v(mset(dA.rearrange("p (n l) -> p n l",
                                          n=NG)[:, :, 0:1], 0.0))
                    P.v(tts(hb[:, :], dA[:, :], dBu[:, :], 0.0))
                    P.v(cp(carry[:, tb, g * NG:(g + 1) * NG],
                           hb.rearrange("p (n l) -> p n l",
                                        n=NG)[:, :, LC1 - 1]))
                    P.dma(bcf.rearrange("p (n l) -> p n l", n=NG)[0:1, :, :],
                          bc_d[16 + g * NG:16 + (g + 1) * NG,
                               lo:lo + LC1][None, :, :])
                    for fo, fs in scan_f(NG * LC1, 512):
                        P.t(mm(psA[:, fo:fo + fs], onesr[0:1, :],
                               bcf[0:1, fo:fo + fs], True, True))
                    P.v(tt(dA[:, :], hb[:, :], psA[:, 0:NG * LC1], ALU.mult))
                    P.v(red(yb4[:, g, :],
                            dA.rearrange("p (n l) -> p l n", n=NG),
                            ALU.add))
                P.v(red(yb[:, 0, :], yb4.rearrange("p g l -> p l g"),
                        ALU.add))
                P.v(ts(spt[:, 0, :], uch[:, tb, :], dsk_s[:, tb:tb + 1],
                       None, ALU.mult))
                P.v(tt(yb[:, 0, :], yb[:, 0, :], spt[:, 0, :], ALU.add))
                P.s(act(spt[:, 0, :], zch[:, tb, :], AF.Silu))
                P.v(tt(ygb[:, 0, :], yb[:, 0, :], spt[:, 0, :], ALU.mult))
                P.ind(y_ai.rearrange("r (j l) -> (r j) l", l=1024),
                      ixs[:, IXB + c * 2 + tb:IXB + c * 2 + tb + 1],
                      ygb[:, 0, :], None)
        cc(y_ai, y_ao)
        arf.reset()
        arb.reset()
        xs = arf.take(8, TSE)
        dbuf = arf.take(8, 512)
        bobuf = arf.take(8, 512)
        sqb = arf.take(8, 512)
        t1 = arf.take(512)
        t2 = arf.take(512)
        t3 = arf.take(512)
        tabc = arf.take(4, 512)
        ygc = arb.take(16, 512)
        warena = arb.take(16384)
        xbpc = arb.take(8, 513)
        vbb = arb.take(4, 512)
        vtbb = arb.take(4, 512)
        qob = arb.take(512)
        B = dict(xbpc=xbpc, qf=dbuf[:, 0:4, :], sqh=sqb[:, 0:4, :],
                 vb=vbb, vtb=vtbb, rqr=rows, t1=t1, t2=t2, t3=t3, qo=qob,
                 tabc=tabc, ones128=ones128, onesr=onesr, oneh=oneh,
                 ident=ident, rotm=rotm, ps_mm=ps_mm, ps_bc=ps_bc,
                 ps_row=ps_row, ps_t=ps_t,
                 wq=warena[:, 0:12288].rearrange("p (b c) -> p b c", b=8))
        wo = warena.rearrange("p (b c) -> p b c", b=16)
        P.dma(xs[:, :, :], xe_d.rearrange("(b p) l -> p b l", p=128))
        P.dmag(wo, wo_d.rearrange("(b p) c -> p b c", p=128))
        y36 = y_ao.rearrange("r (j l) -> (r j) l", l=512)
        y144 = y_ao.rearrange("r (j l) -> (r j) l", l=128)
        for ci, (fo, fs) in enumerate(CHK_C):
            for k in range(16):
                P.ind(ygc[:, k, 0:fs], None, y144 if ci == 0 else y36,
                      ixs[:, IXC + ci * 16 + k:IXC + ci * 16 + k + 1])
            fm_ln(P, lambda k: xs[:, k, fo:fo + fs], fs, dbuf, sqb,
                  ones128, onesr, ps_row, ps_bc, rows,
                  nrm_view=lambda k: dbuf[:, k, 0:fs])
            for m in range(8):
                for k in range(16):
                    P.t(mm(ps_mm[:, 0:fs], wo[:, k, m * 128:(m + 1) * 128],
                           ygc[:, k, 0:fs], k == 0, k == 15))
                P.v(tt(bobuf[:, m, 0:fs], ps_mm[:, 0:fs], dbuf[:, m, 0:fs],
                       ALU.add))
            fm_ln(P, lambda k: bobuf[:, k, 0:fs], fs, bobuf, sqb,
                  ones128, onesr, ps_row, ps_bc, rows,
                  nrm_view=lambda k: bobuf[:, k, 0:fs])
            for m in range(8):
                P.v(tt(xs[:, m, fo:fo + fs], bobuf[:, m, 0:fs],
                       xs[:, m, fo:fo + fs], ALU.add))
            P.dma(x1_d.rearrange("(b p) l -> p b l", p=128)[:, :, fo:fo + fs],
                  xs[:, :, fo:fo + fs])
        P.dmag(B["wq"], wq0_d.rearrange("(b p) c -> p b c", p=128))
        P.v(mset(B["xbpc"][:, :, 0:1], 0.0))
        B["qsr"] = qsr[:, 0:1]
        B["ksr"] = ksr[:, 0:1]
        for ci, (fo, fs) in enumerate(CHK_C):
            P.dma(B["tabc"][:, :, 0:fs],
                  tab_d[:, :, fo:fo + fs].transpose([1, 0, 2]))
            fm_ln(P, lambda k: xs[:, k, fo:fo + fs], fs, dbuf, sqb,
                  ones128, onesr, ps_row, ps_bc, rows,
                  nrm_view=lambda k: dbuf[:, k, 0:fs])
            for k in range(8):
                P.v(cp(B["xbpc"][:, k, 1:1 + fs], dbuf[:, k, 0:fs]))
            P.dma(xb_d.rearrange("(b p) l -> p b l", p=128)[:, :, fo:fo + fs],
                  dbuf[:, :, 0:fs])
            qkv_chunk(P, B, fo, fs, q_ds[0], k_ds[0], v_ds[0])
        arf.reset()
        arb.reset()

        def kv_exchange(l):
            ktl = arb.take(4, 768)
            vtl = arb.take(6, 512)
            P.dma(ktl[:, :, :], k_ds[l][:, TSE - 768:TSE]
                  .rearrange("(a p) w -> p a w", p=128))
            P.dma(vtl[:, :, :], v_ds[l][TSE - 768:TSE, :]
                  .rearrange("(a p) w -> p a w", p=128))
            kgi_f = kg_ai[l].rearrange("r (j w) -> (r j) w", w=768)
            for kt in range(4):
                P.ind(kgi_f, ixs[:, IXK + kt:IXK + kt + 1], ktl[:, kt, :],
                      None)
            for it in range(6):
                P.ind(vg_ai[l], ixs[:, IXV + it:IXV + it + 1], vtl[:, it, :],
                      None)
            cc(kg_ai[l], kg_ao[l])
            cc(vg_ai[l], vg_ao[l])
            kgo_f = kg_ao[l].rearrange("r (j w) -> (r j) w", w=768)
            for kt in range(4):
                P.ind(ktl[:, kt, :], None, kgo_f,
                      ixs[:, IXKR + kt:IXKR + kt + 1])
            for it in range(6):
                P.ind(vtl[:, it, :], None, vg_ao[l],
                      ixs[:, IXVR + it:IXVR + it + 1])
            P.dma(kt_loc[l].rearrange("(a p) w -> p a w", p=128),
                  ktl[:, :, :])
            P.dma(vt_loc[l].rearrange("(a p) w -> p a w", p=128),
                  vtl[:, :, :])
            arb.reset()

        def emit_D(l, last):
            stream_d = xb_d if l == 0 else x1o_d
            woA_d = woA0_d if l == 0 else woA1_d
            w1_d = w1f0_d if l == 0 else w1f1_d
            b1_d = b1p0_d if l == 0 else b1p1_d
            w2_d = w2f0_d if l == 0 else w2f1_d
            b2_d = b2v0_d if l == 0 else b2v1_d
            lg_d = lgv0_d if l == 0 else lgv1_d
            lb_d = lbv0_d if l == 0 else lbv1_d
            xm = arf.take(8, TSE + 1)
            rbuf = arf.take(8, 512)
            dbuf = arf.take(8, 512)
            sqb = arf.take(8, 512)
            simb = arf.take(STRIP)
            maskc = arf.take(STRIP)
            h1b = arf.take(512)
            gbv = arf.take(512)
            extm = arf.take(8, 128)
            xb2flat = arb.take(8 * 513)
            qbt = xb2flat[:, 0:512].rearrange("p (a b) -> p a b", a=4)
            kstr = xb2flat[:, 512:512 + 4 * STRIP].rearrange(
                "p (a b) -> p a b", a=4)
            vstr = arb.take(5, 512)
            pbuf = arb.take(STRIP)
            ptb = arb.take(5, 128)
            obf = arb.take(4, 128)
            xbm = arb.take(8, 512)
            hbf = arb.take(22, 512)
            wmg = arb.take(8, 128)
            w2g = arb.take(22, 128)
            wob = arb.take(4, D)
            cols = colsml[:, 0:4]
            P.dma(xm[:, :, 1:TSE + 1],
                  stream_d.rearrange("(b p) l -> p b l", p=128))
            P.v(mset(xm[:, :, 0:1], 0.0))
            P.dmag(wob[:, :, :], woA_d.rearrange("(b p) c -> p b c", p=128))
            P.dma(extm[:, :, :], em_d)
            vec_cols(P, lgc, lg_d, 8)
            vec_cols(P, lbb, lb_d, 8, cast=True)
            vec_cols(P, b2c, b2_d, 8)
            vec_cols(P, b1pv[:, 0:43], b1_d, 43)
            for i in range(17):
                c0 = 128 * i + 128
                P.dma(maskc[:, :], mk_d[i, :, :])
                P.dma(qbt[:, :, :],
                      q_ds[l].rearrange("(b p) ll -> p b ll", p=128)
                      [:, :, 128 * i:128 * (i + 1)])
                nt = max(0, min(768 - c0, STRIP))
                if nt > 0:
                    P.dma(kstr[:, :, 0:nt],
                          kt_loc[l].rearrange("(b p) w -> p b w", p=128)
                          [:, :, c0:c0 + nt])
                if nt < STRIP:
                    o0 = c0 + nt - 768 + 128
                    P.dma(kstr[:, :, nt:STRIP],
                          k_ds[l].rearrange("(b p) w -> p b w", p=128)
                          [:, :, o0:o0 + STRIP - nt])
                for j in range(5):
                    t0_ = c0 + 128 * j
                    if t0_ < 768:
                        P.dma(vstr[:, j, :],
                              vt_loc[l][t0_:t0_ + 128, :]
                              .rearrange("(a p) c -> p a c", p=128)[:, 0, :])
                    else:
                        o0 = t0_ - 768 + 128
                        P.dma(vstr[:, j, :],
                              v_ds[l][o0:o0 + 128, :]
                              .rearrange("(a p) c -> p a c", p=128)[:, 0, :])
                for h in range(8):
                    tl, b = h // 2, 64 * (h % 2)
                    P.t(mm(ps_sim[:, 0:512], qbt[b:b + 64, tl, :],
                           kstr[b:b + 64, tl, 0:512], True, True))
                    P.t(mm(ps_sim[:, 512:STRIP], qbt[b:b + 64, tl, :],
                           kstr[b:b + 64, tl, 512:STRIP], True, True))
                    P.v(tt(simb[:, :], ps_sim[:, :], maskc[:, :], ALU.add))
                    P.v(red(cols[:, 0:1], simb[:, :], ALU.max, negate=True))
                    P.s(act(pbuf[:, :], simb[:, :], AF.Exp, bias=cols[:, 0:1],
                            accum=cols[:, 1:2]))
                    P.v(rcp(cols[:, 2:3], cols[:, 1:2]))
                    P.v(ts(pbuf[:, :], pbuf[:, :], cols[:, 2:3], None,
                           ALU.mult))
                    for j in range(5):
                        P.t(trans(ps_pt[:, 128 * j:128 * (j + 1)],
                                  pbuf[:, 128 * j:128 * (j + 1)], ident[:, :]))
                    for j in range(5):
                        P.s(act(ptb[:, j, :], ps_pt[:, 128 * j:128 * (j + 1)],
                                AF.Copy))
                    for j in range(5):
                        P.t(mm(ps_o[0:64, :], vstr[:, j, 64 * h:64 * h + 64],
                               ptb[:, j, :], j == 0, j == 4))
                    P.s(act(obf[b:b + 64, tl, :], ps_o[0:64, 0:128], AF.Copy))
                for m in range(8):
                    for k in range(4):
                        P.t(mm(ps_mm[:, 0:128],
                               wob[:, k, m * 128:(m + 1) * 128],
                               obf[:, k, :], k == 0, k == 3))
                    P.v(tt(xm[:, m, 1 + 128 * i:1 + 128 * (i + 1)],
                           ps_mm[:, 0:128],
                           xm[:, m, 1 + 128 * i:1 + 128 * (i + 1)], ALU.add))
            for ci, (fo, fs) in enumerate(scan_f(TSE, 512)):
                def xv(k, fo=fo, fs=fs):
                    return (xm[:, k, 1 + fo:1 + fo + fs] if k < 4
                            else xm[:, k, fo:fo + fs])
                fm_ln(P, xv, fs, dbuf, sqb, ones128, onesr, ps_row, ps_bc,
                      rows, nrm_view=None)
                for k in range(8):
                    P.v(tt(xbm[:, k, 0:fs], dbuf[:, k, 0:fs], ps_bc[:, 0:fs],
                           ALU.mult))
                P.v(mset(hbf[64:128, 21, 0:fs], 0.0))
                for mg in range(43):
                    P.dmag(wmg[:, :, :],
                           w1_d.rearrange("(b p) c -> p b c", p=128)
                           [:, :, mg * 128:(mg + 1) * 128])
                    if ci == 0:
                        for k in range(8):
                            P.t(mm(ps_o[:, 0:1], wmg[:, k, :],
                                   lbb[:, k:k + 1], k == 0, k == 7))
                        P.s(act(b1e[:, mg:mg + 1], ps_o[:, 0:1], AF.Copy))
                        P.v(tt(b1e[:, mg:mg + 1], b1e[:, mg:mg + 1],
                               b1pv[:, mg:mg + 1], ALU.add))
                    for k in range(8):
                        P.v(ts(wmg[:, k, :], wmg[:, k, :], lgc[:, k:k + 1],
                               None, ALU.mult))
                    for k in range(8):
                        P.t(mm(ps_mm[:, 0:fs], wmg[:, k, :], xbm[:, k, 0:fs],
                               k == 0, k == 7))
                    P.s(act(h1b[:, 0:fs], ps_mm[:, 0:fs], AF.Identity,
                            bias=b1e[:, mg:mg + 1]))
                    P.s(act(gbv[0:64, 0:fs], h1b[64:128, 0:fs], AF.Gelu))
                    P.v(tt(hbf[64 * (mg % 2):64 * (mg % 2) + 64, mg // 2,
                               0:fs],
                           h1b[0:64, 0:fs], gbv[0:64, 0:fs], ALU.mult))
                for m in range(8):
                    P.dmag(w2g[:, :, :],
                           w2_d.rearrange("(b p) c -> p b c", p=128)
                           [:, :, m * 128:(m + 1) * 128])
                    for k in range(22):
                        P.t(mm(ps_mm[:, 0:fs], w2g[:, k, :], hbf[:, k, 0:fs],
                               k == 0, k == 21))
                    P.s(act(rbuf[:, m, 0:fs], ps_mm[:, 0:fs], AF.Identity,
                            bias=b2c[:, m:m + 1]))
                    P.v(tt(rbuf[:, m, 0:fs], rbuf[:, m, 0:fs],
                           xm[:, m, 1 + fo:1 + fo + fs], ALU.add))
                if fo < EXT:
                    for m in range(8):
                        P.v(tt(rbuf[:, m, 0:EXT], rbuf[:, m, 0:EXT],
                               extm[:, m, :], ALU.mult))
                if not last:
                    P.dma(x1o_d.rearrange("(b p) l -> p b l", p=128)
                          [:, :, fo:fo + fs], rbuf[:, :, 0:fs])
                    P.dmag(x1b_d.rearrange("(b p) l -> p b l", p=128)
                           [:, :, fo:fo + fs], rbuf[:, :, 0:fs])
                else:
                    P.dma(xf_d.rearrange("(b p) l -> p b l", p=128)
                          [:, :, fo:fo + fs], rbuf[:, :, 0:fs])
            if not last:
                B2 = dict(xbpc=xb2flat.rearrange("p (a b) -> p a b", a=8),
                          qf=dbuf[:, 0:4, :],
                          sqh=sqb[:, 0:4, :], vb=xbm[:, 0:4, :],
                          vtb=xbm[:, 4:8, :], rqr=rows,
                          t1=rbuf[:, 4, :], t2=rbuf[:, 5, :],
                          t3=rbuf[:, 6, :], qo=hbf[:, 0, 0:512],
                          tabc=rbuf[:, 0:4, :], ones128=ones128,
                          onesr=onesr, oneh=oneh, ident=ident, rotm=rotm,
                          ps_mm=ps_mm, ps_bc=ps_bc, ps_row=ps_row,
                          ps_t=ps_t, qsr=qsr[:, 1:2], ksr=ksr[:, 1:2],
                          wq_d=wq1_d, wqm=wmg)
                for ci, (fo, fs) in enumerate(scan_f(TSE, 512)):
                    P.dma(B2["tabc"][:, :, 0:fs],
                          tab_d[:, :, fo:fo + fs].transpose([1, 0, 2]))
                    if fo == 0:
                        P.dma(B2["xbpc"][:, :, 1:1 + fs],
                              x1b_d.rearrange("(b p) l -> p b l", p=128)
                              [:, :, 0:fs])
                        P.v(mset(B2["xbpc"][:, :, 0:1], 0.0))
                    else:
                        P.dma(B2["xbpc"][:, :, 0:1 + fs],
                              x1b_d.rearrange("(b p) l -> p b l", p=128)
                              [:, :, fo - 1:fo + fs])
                    qkv_chunk(P, B2, fo, fs, q_ds[1], k_ds[1], v_ds[1])
            else:
                for fo, fs in scan_f(TSH, 512):
                    P.dma(rbuf[:, :, 0:fs],
                          xf_d.rearrange("(b p) l -> p b l", p=128)
                          [:, :, EXT + fo:EXT + fo + fs])
                    fm_ln(P, lambda k, fs=fs: rbuf[:, k, 0:fs], fs, dbuf,
                          sqb, ones128, onesr, ps_row, ps_bc, rows,
                          nrm_view=lambda k, fs=fs: dbuf[:, k, 0:fs])
                    P.dma(sqb[:, :, 0:fs],
                          x1_d.rearrange("(b p) l -> p b l", p=128)
                          [:, :, EXT + fo:EXT + fo + fs])
                    for m in range(8):
                        P.v(tt(dbuf[:, m, 0:fs], dbuf[:, m, 0:fs],
                               sqb[:, m, 0:fs], ALU.add))
                    for j in range(fs // 128):
                        for m in range(8):
                            P.t(trans(ps_mm[:, 0:128],
                                      dbuf[:, m, 128 * j:128 * (j + 1)],
                                      iden32[:, :]))
                            P.s(act(rbuf[:, m, 0:128] if False else
                                    rbuf.rearrange("p a b -> p (a b)")
                                    [:, m * 128:(m + 1) * 128],
                                    ps_mm[:, 0:128], AF.Copy))
                        P.dma(out_d[fo + 128 * j:fo + 128 * (j + 1), :],
                              rbuf.rearrange("p a b -> p (a b)")[:, 0:1024])
            arf.reset()
            arb.reset()

        kv_exchange(0)
        emit_D(0, False)
        kv_exchange(1)
        emit_D(1, True)
        P.finalize(blk)
    return nc


def kernel(**inputs):
    f32 = np.float32
    x = np.asarray(inputs["L0_attn_x"], f32)[0]
    w_in = np.asarray(inputs["L0_attn_w_in"], f32)
    ident = np.eye(128).astype(NPBF)
    iden32 = np.eye(128, dtype=f32)
    oneh = oneh_host()
    rotm = rotm_host()
    qs_ = np.asarray(inputs["L0_ffn_q_scale"], f32)
    ks_ = np.asarray(inputs["L0_ffn_k_scale"], f32)
    w1f0, b1p0 = permute_w1(np.asarray(inputs["L0_ffn_w1"], f32)[0],
                            np.asarray(inputs["L0_ffn_b1"], f32)[0])
    w1f1, b1p1 = permute_w1(np.asarray(inputs["L0_ffn_w1"], f32)[1],
                            np.asarray(inputs["L0_ffn_b1"], f32)[1])
    w2f0 = permute_w2(np.asarray(inputs["L0_ffn_w2"], f32)[0])
    w2f1 = permute_w2(np.asarray(inputs["L0_ffn_w2"], f32)[1])

    nc = build_FUSED()
    maps = []
    for c in range(NC):
        sl = slice(c * ES, (c + 1) * ES)
        t0 = c * TSH - EXT
        w_sl = np.concatenate([w_in[sl], w_in[E + c * ES:E + (c + 1) * ES]], 0)
        xe = np.zeros((TSE, D), f32)
        lo = max(t0, 0)
        xe[lo - t0:] = x[lo:t0 + TSE]
        maps.append(dict(
            xa=_cc(x[c * TSH:(c + 1) * TSH]),
            xe=_cc(xe.T),
            w_in_slT=_cc(w_sl.T),
            lnw=np.asarray(inputs["L0_attn_ln_w"], f32),
            lnb=np.asarray(inputs["L0_attn_ln_b"], f32),
            cw=_cc(np.asarray(inputs["L0_attn_conv_w"], f32)[sl]),
            cb=_cc(np.asarray(inputs["L0_attn_conv_b"], f32)[sl]),
            wxpT=_cc(np.asarray(inputs["L0_attn_w_xproj"], f32)[:, sl].T),
            wdtT=_cc(np.asarray(inputs["L0_attn_w_dt"], f32)[sl].T),
            bdt=_cc(np.asarray(inputs["L0_attn_b_dt"], f32)[sl]),
            alog=_cc(np.asarray(inputs["L0_attn_a_log"], f32)[sl]),
            dskip=_cc(np.asarray(inputs["L0_attn_d_skip"], f32)[sl]),
            woutT=_cc(np.asarray(inputs["L0_attn_w_out"], f32).T),
            wqkv0=_cc(np.asarray(inputs["L0_ffn_wqkv"], f32)[0]),
            wqkv1=_cc(np.asarray(inputs["L0_ffn_wqkv"], f32)[1]),
            qsr0=_cc(np.tile(qs_[0], 2)[:, None]),
            ksr0=_cc(np.tile(ks_[0], 2)[:, None]),
            qsr1=_cc(np.tile(qs_[1], 2)[:, None]),
            ksr1=_cc(np.tile(ks_[1], 2)[:, None]),
            tabs=tabs_for(t0),
            oneh=oneh, rotm=rotm, ident=ident, iden32=iden32,
            masks=masks_for(c),
            extm=np.full((128, 8, 128), 0.0 if c == 0 else 1.0, f32),
            idxt=idx_table_for(c),
            woA0=_cc(np.asarray(inputs["L0_ffn_wo"], f32)[0]),
            woA1=_cc(np.asarray(inputs["L0_ffn_wo"], f32)[1]),
            w1f0=w1f0, w1f1=w1f1, b1p0=b1p0, b1p1=b1p1,
            w2f0=w2f0, w2f1=w2f1,
            b2v0=_cc(np.asarray(inputs["L0_ffn_b2"], f32)[0]),
            b2v1=_cc(np.asarray(inputs["L0_ffn_b2"], f32)[1]),
            lgv0=_cc(np.asarray(inputs["L0_ffn_ln_g"], f32)[0]),
            lgv1=_cc(np.asarray(inputs["L0_ffn_ln_g"], f32)[1]),
            lbv0=_cc(np.asarray(inputs["L0_ffn_ln_b"], f32)[0]),
            lbv1=_cc(np.asarray(inputs["L0_ffn_ln_b"], f32)[1]),
        ))
    res = run_bass_kernel_spmd(nc, maps, core_ids=list(range(NC)))
    out = np.concatenate([np.asarray(r["outp"]) for r in res.results], 0)
    return out[None].astype(np.float32)
